# version 34
# baseline (speedup 1.0000x reference)
import contextlib
import numpy as np
import ml_dtypes
import concourse.bass as bass
import concourse.mybir as mybir
from concourse.bass_utils import run_bass_kernel_spmd

F32 = mybir.dt.float32
BF16 = mybir.dt.bfloat16
I32 = mybir.dt.int32
AF = mybir.ActivationFunctionType
ALU = mybir.AluOpType
AX = mybir.AxisListType

T = 2048
D = 1024
NDC = D // 128
TQ = 512
NQ = T // TQ
ENGS = ("sync", "gpsimd", "tensor", "vector", "scalar")
BF = ml_dtypes.bfloat16


class _Eng:
    def __init__(self, sem):
        self.sem = sem
        self.n = 0
        self.q = []
        self.selfwait = 0


class Prog:
    def __init__(self, nc, es):
        self.nc = nc
        self.es = es
        self.e = {name: _Eng(es.enter_context(nc.semaphore(f"sem_{name}")))
                  for name in ENGS}

    def do(self, eng, fn, inc=0):
        E = self.e[eng]
        if eng != "tensor" and E.n > E.selfwait:
            E.q.append(lambda e, s=E.sem, v=E.n: e.wait_ge(s, v))
            E.selfwait = E.n
        if inc:
            E.n += inc

            def thunk(e, fn=fn, sem=E.sem, inc=inc):
                fn(e).then_inc(sem, inc)
            E.q.append(thunk)
        else:
            E.q.append(lambda e, fn=fn: fn(e))
        return E.n

    def dma(self, eng, out, in_, **kw):
        return self.do(eng, lambda e, o=out, i=in_, kw=kw:
                       e.dma_start(out=o, in_=i, **kw), inc=16)

    def wait(self, eng, other, val=None):
        O = self.e[other]
        v = O.n if val is None else val
        if v > 0:
            self.e[eng].q.append(lambda e, s=O.sem, v=v: e.wait_ge(s, v))

    def barrier(self):
        counts = {n: self.e[n].n for n in ENGS}
        for a in ENGS:
            for b in ENGS:
                if a != b and counts[b] > 0:
                    self.e[a].q.append(
                        lambda e, s=self.e[b].sem, v=counts[b]: e.wait_ge(s, v))

    def emit(self):
        with self.nc.Block() as blk:
            for name in ENGS:
                E = self.e[name]
                if E.q:
                    getattr(blk, name)(lambda e, q=E.q: [f(e) for f in q])


class Ctx:
    pass


def make_ctx(nc, es, P, weights):
    c = Ctx()
    c.nc, c.es, c.P = nc, es, P
    c.x_in = nc.dram_tensor("x_in", [D, T], F32, kind="ExternalInput").ap()
    c.x_out = nc.dram_tensor("x_out", [D, T], F32, kind="ExternalOutput").ap()
    c.w = {nm: nc.dram_tensor(nm, list(shape), dt, kind="ExternalInput").ap()
           for nm, (shape, dt) in weights.items()}

    c.xr = es.enter_context(nc.sbuf_tensor("xr", [128, NDC * T], F32))
    c.xln = es.enter_context(nc.sbuf_tensor("xln", [128, NDC * T], BF16))
    c.scr = es.enter_context(nc.sbuf_tensor("scr", [128, 16 * 1024 + 256], BF16))
    c.wst = es.enter_context(nc.sbuf_tensor("wst", [128, 16 * 1024], BF16))
    c.aux = es.enter_context(nc.sbuf_tensor("aux", [128, 512], F32))
    c.strow = es.enter_context(nc.sbuf_tensor("strow", [1, 1536], F32))
    c.ones_b = es.enter_context(nc.sbuf_tensor("ones_b", [128, 128], BF16))
    c.ones_f = es.enter_context(nc.sbuf_tensor("ones_f", [1, 128], F32))
    c.consts = es.enter_context(nc.sbuf_tensor("consts", [128, 12], F32))
    P.do("vector", lambda e: e.memset(c.ones_b[:], 1.0), inc=1)
    P.do("vector", lambda e: e.memset(c.ones_f[:], 1.0), inc=1)
    P.do("vector", lambda e: e.memset(c.consts[:, 0:1], 1e-5), inc=1)
    P.do("vector", lambda e: e.memset(c.consts[:, 1:2], 1e-5 * 2.0 ** -64), inc=1)
    P.do("vector", lambda e: e.memset(c.consts[:, 2:3], -32.0 * float(np.log(2.0))), inc=1)

    c.ps = [es.enter_context(nc.psum_tensor(f"ps{i}", [128, 512], F32))
            for i in range(8)]
    c.ps_n = [0] * 8
    c.ps_f = [None] * 8

    if "IDENT128" in c.w:
        c.ident_b = es.enter_context(nc.sbuf_tensor("ident_b", [128, 128], BF16))
        P.dma("gpsimd", c.ident_b[:], c.w["IDENT128"])
        c.ident_f = es.enter_context(nc.sbuf_tensor("ident_f", [128, 128], F32))
        P.dma("sync", c.ident_f[:], c.w["IDENT128"])
    if "TRI128" in c.w:
        c.tri_f = es.enter_context(nc.sbuf_tensor("tri_f", [128, 128], F32))
        P.dma("sync", c.tri_f[:], c.w["TRI128"])
    c.ones_cf = es.enter_context(nc.sbuf_tensor("ones_cf", [128, 1], F32))
    P.do("vector", lambda e: e.memset(c.ones_cf[:], 1.0), inc=1)
    P.do("vector", lambda e: e.memset(c.consts[:, 3:4], 0.0), inc=1)
    P.dma("sync", c.xr[:].rearrange("p (ch t) -> p ch t", ch=NDC),
          c.x_in.rearrange("(ch p) t -> p ch t", p=128))
    P.wait("tensor", "sync")
    P.wait("vector", "sync")
    P.wait("scalar", "sync")
    return c


def bank_acquire(c, b):
    if c.ps_f[b] is not None:
        eng, cnt = c.ps_f[b]
        c.P.wait("tensor", eng, cnt)


def bank_done(c, b):
    c.ps_n[b] = c.P.e["tensor"].n


def bank_consume(c, b, eng):
    c.P.wait(eng, "tensor", c.ps_n[b])


def bank_release(c, b, eng):
    c.ps_f[b] = (eng, c.P.e[eng].n)


def xr_ap(c, dc, t0=0, tn=T):
    return c.xr[:, dc * T + t0: dc * T + t0 + tn]


def xln_ap(c, dc, t0=0, tn=T):
    return c.xln[:, dc * T + t0: dc * T + t0 + tn]


def emit_ln(c):
    P = c.P
    slot_pe = [0, 0]

    def stage(q):
        t0 = q * TQ
        off = (q % 2) * 1024
        P.wait("vector", "tensor", slot_pe[q % 2])
        P.wait("scalar", "tensor", slot_pe[q % 2])
        for dc in range(NDC):
            P.do("vector", lambda e, dc=dc, t0=t0, off=off: e.tensor_copy(
                c.scr[:, dc * 2048 + off:dc * 2048 + off + 512],
                xr_ap(c, dc, t0, TQ)), inc=1)
            P.do("scalar", lambda e, dc=dc, t0=t0, off=off: e.activation(
                c.scr[:, dc * 2048 + off + 512:dc * 2048 + off + 1024],
                xr_ap(c, dc, t0, TQ), AF.Square), inc=1)
        return P.e["vector"].n, P.e["scalar"].n

    staged = stage(0)
    for q in range(NQ):
        t0 = q * TQ
        soff = (q % 2) * 1024
        r_mean = c.strow[0:1, 0:TQ]
        r_vb = c.strow[0:1, TQ:2 * TQ]
        r_a = c.strow[0:1, 2 * TQ:3 * TQ]
        v_n, s_n = staged
        for b, off in ((0, soff), (1, soff + 512)):
            bank_acquire(c, b)
            P.wait("tensor", "vector", v_n)
            P.wait("tensor", "scalar", s_n)
            for dc in range(NDC):
                P.do("tensor", lambda e, b=b, dc=dc, off=off: e.matmul(
                    c.ps[b][0:1, :], c.ones_b[:, 0:1],
                    c.scr[:, dc * 2048 + off:dc * 2048 + off + 512],
                    start=(dc == 0), stop=(dc == NDC - 1)),
                    inc=(1 if dc == NDC - 1 else 0))
            bank_done(c, b)
        slot_pe[q % 2] = P.e["tensor"].n
        if q + 1 < NQ:
            staged = stage(q + 1)
        bank_consume(c, 0, "vector")
        bank_consume(c, 1, "vector")
        P.do("vector", lambda e, r=r_mean: e.tensor_scalar(
            out=r, in0=c.ps[0][0:1, :], scalar1=1.0 / D,
            scalar2=None, op0=ALU.mult), inc=1)
        P.do("vector", lambda e, r=r_vb: e.tensor_scalar(
            out=r, in0=c.ps[1][0:1, :], scalar1=1.0 / D,
            scalar2=None, op0=ALU.mult), inc=1)
        bank_release(c, 0, "vector")
        bank_release(c, 1, "vector")
        P.do("vector", lambda e, r_a=r_a, r_mean=r_mean: e.tensor_tensor(
            out=r_a, in0=r_mean, in1=r_mean, op=ALU.mult), inc=1)
        P.do("vector", lambda e, r_vb=r_vb, r_a=r_a: e.tensor_tensor(
            out=r_vb, in0=r_vb, in1=r_a, op=ALU.subtract), inc=1)
        P.wait("scalar", "vector")
        P.do("scalar", lambda e, r_a=r_a, r_vb=r_vb: e.activation(
            r_a, r_vb, AF.Ln, bias=c.consts[0:1, 1:2], scale=2.0 ** -64), inc=1)
        P.do("scalar", lambda e, r_a=r_a: e.activation(
            r_a, r_a, AF.Exp, scale=-0.5, bias=c.consts[0:1, 2:3]), inc=1)
        P.wait("vector", "scalar")
        P.do("vector", lambda e, r_vb=r_vb, r_mean=r_mean, r_a=r_a:
             e.scalar_tensor_tensor(out=r_vb, in0=r_mean, scalar=-1.0,
                                    in1=r_a, op0=ALU.mult, op1=ALU.mult), inc=1)
        v_n2 = P.e["vector"].n
        for b, cb in ((2, 2 * TQ), (3, TQ)):
            bank_acquire(c, b)
            P.wait("tensor", "vector", v_n2)
            P.do("tensor", lambda e, b=b, cb=cb: e.matmul(
                c.ps[b][:, :], c.ones_f[:, :], c.strow[0:1, cb:cb + TQ],
                start=True, stop=True), inc=1)
            bank_done(c, b)
        bank_consume(c, 2, "vector")
        bank_consume(c, 3, "vector")
        for dc in range(NDC):
            P.do("vector", lambda e, dc=dc, t0=t0: e.tensor_tensor(
                out=c.aux[:, 0:TQ], in0=xr_ap(c, dc, t0, TQ), in1=c.ps[2][:, :],
                op=ALU.mult), inc=1)
            P.do("vector", lambda e, dc=dc, t0=t0: e.tensor_tensor(
                out=xln_ap(c, dc, t0, TQ), in0=c.aux[:, 0:TQ], in1=c.ps[3][:, :],
                op=ALU.add), inc=1)
        bank_release(c, 2, "vector")
        bank_release(c, 3, "vector")
    P.barrier()


def stage_weight_bf16(c, dram_ap, k0, m0, kc, mc, dst_col, mtile=128, dst=None,
                      q="gpsimd"):
    src = dram_ap[k0:k0 + kc * 128, m0:m0 + mc * mtile]
    src = src.rearrange("(ki p) mm -> p ki mm", p=128)
    base = c.wst if dst is None else dst
    d = base[:, dst_col:dst_col + kc * mc * mtile]
    d = d.rearrange("p (ki mm) -> p ki mm", ki=kc)
    return c.P.dma(q, d, src)


def wtile(c, dst_col, mc, ki, mi, mtile=128):
    col = dst_col + (ki * mc + mi) * mtile
    return c.wst[:, col:col + mtile]



def b_l0_attn(c):
    P = c.P
    emit_ln(c)
    bes = contextlib.ExitStack()
    ygq = None
    nc = c.nc
    A, B = 0, 8192
    TWO_PI = 6.283185307179586
    MAGIC = 12582912.0
    C1 = 6.28125
    C2 = 1.9353072e-03
    C2B = float(np.float32(np.float64(TWO_PI) - 6.28125))

    svst = None
    theta = bes.enter_context(nc.sbuf_tensor("l0a_th", [128, 16], F32))
    taps = bes.enter_context(nc.sbuf_tensor("l0a_tap", [128, 132], F32))
    convb = bes.enter_context(nc.sbuf_tensor("l0a_cb", [128, 33], F32))
    ssc = bes.enter_context(nc.sbuf_tensor("l0a_ssc", [32, 1], F32))
    slp = bes.enter_context(nc.sbuf_tensor("l0a_slp", [32, 1], F32))
    selk = bes.enter_context(nc.sbuf_tensor("l0a_selk", [32, 2048], BF16))
    hsel = bes.enter_context(nc.sbuf_tensor("l0a_hsel", [128, 2], BF16))
    hsel2 = bes.enter_context(nc.sbuf_tensor("l0a_hsel2", [2, 128], BF16))
    bdre = bes.enter_context(nc.sbuf_tensor("l0a_bdre", [128, 128], BF16))
    bdim = bes.enter_context(nc.sbuf_tensor("l0a_bdim", [128, 128], BF16))
    wres = bes.enter_context(nc.sbuf_tensor("l0a_wres", [64, 64], F32))
    nsc = bes.enter_context(nc.sbuf_tensor("l0a_nsc", [128, 1], F32))
    wpoly = bes.enter_context(nc.sbuf_tensor("l0a_wp", [128, 4], F32))
    wrow = bes.enter_context(nc.sbuf_tensor("l0a_wrow", [1, 3], F32))
    wsum = bes.enter_context(nc.sbuf_tensor("l0a_wsum", [1, 1], F32))
    ztail = bes.enter_context(nc.sbuf_tensor("l0a_zt", [128, 51], F32))
    _f32v = c.scr[:].bitcast(F32)
    zbuf2 = [bes.enter_context(nc.sbuf_tensor(f"l0a_zb{i}", [128, 515], F32))
             for i in range(2)]
    pwb = zbuf2[0][0:32, 0:512]
    svst = zbuf2[1][:].bitcast(BF16)[0:128, 0:256]
    xvb2 = [_f32v[:, 4096:4608], _f32v[:, 4608:5120]]
    kb2 = [_f32v[:, 5120:5632], _f32v[:, 5632:6144]]
    prd2 = [_f32v[:, 6144:6656], _f32v[:, 6656:7168]]
    phib2 = [_f32v[:, 7168:7680], _f32v[:, 7680:8192]]
    sinb2 = [bes.enter_context(nc.sbuf_tensor(f"l0a_sin{i}", [128, 512], BF16))
             for i in range(2)]
    cosb2 = [bes.enter_context(nc.sbuf_tensor(f"l0a_cos{i}", [128, 512], BF16))
             for i in range(2)]
    polyb2 = [bes.enter_context(nc.sbuf_tensor(f"l0a_poly{i}", [128, 512], BF16))
              for i in range(2)]
    rem2 = [bes.enter_context(nc.sbuf_tensor(f"l0a_rem{i}", [128, 512], BF16))
            for i in range(2)]
    imm2 = [bes.enter_context(nc.sbuf_tensor(f"l0a_imm{i}", [128, 512], BF16))
            for i in range(2)]
    npr1 = bes.enter_context(nc.sbuf_tensor("l0a_npr", [128, 1024], F32))
    npr2 = [npr1, npr1]
    ren2 = [bes.enter_context(nc.sbuf_tensor(f"l0a_ren{i}", [128, 512], BF16))
            for i in range(2)]
    imn2 = [bes.enter_context(nc.sbuf_tensor(f"l0a_imn{i}", [128, 512], BF16))
            for i in range(2)]
    rsbb = None
    carp = bes.enter_context(nc.sbuf_tensor("l0a_crp", [128, 32], F32))
    sqb2 = [kb2[1], kb2[0]]
    carden = bes.enter_context(nc.sbuf_tensor("l0a_cden", [32, 1], F32))
    srawb = bes.enter_context(nc.sbuf_tensor("l0a_sraw", [32, 515], F32))
    rsbb = srawb[:].bitcast(BF16)[0:2, 0:512]
    pwbf = bes.enter_context(nc.sbuf_tensor("l0a_pwbf", [32, 512], BF16))
    invb = bes.enter_context(nc.sbuf_tensor("l0a_inv", [32, 512], BF16))
    tmps = c.aux[0:32, 0:512]
    gzs2 = [bes.enter_context(nc.sbuf_tensor(f"l0a_gzs{i}", [128, 515], BF16))
            for i in range(2)]
    diagb2 = [bes.enter_context(nc.sbuf_tensor(f"l0a_diag{i}", [128, 512], BF16))
              for i in range(2)]
    gcv2 = [c.scr[:, 12288:12800], c.scr[:, 13312:13824]]
    gzt = bes.enter_context(nc.sbuf_tensor("l0a_gzt", [128, 48], BF16))

    ygq = c.scr
    n1 = P.dma("sync", theta[:], c.w["L0THETA"])
    n2 = P.dma("sync", taps[:], c.w["L0TAPS"])
    n3 = P.dma("sync", convb[:], c.w["L0CONVB"])
    n4 = P.dma("sync", ssc[:], c.w["L0SSC"])
    n5 = 0
    n6 = P.dma("gpsimd", selk[:], c.w["L0SEL"])
    n7 = P.dma("gpsimd", hsel[:], c.w["L0HSEL"])
    n8 = P.dma("gpsimd", hsel2[:], c.w["L0HSEL2"])
    n9 = P.dma("sync", wres[:], c.w["L0_attn_W_re"])
    n10 = P.dma("sync", nsc[:], c.w["L0NSC"])
    n11 = P.dma("sync", wrow[:], c.w["L0DERIV"])
    n12 = P.dma("sync", c.aux[0:32, 0:1], c.w["L0SLOPES"])
    P.wait("vector", "sync", n12)
    P.wait("vector", "gpsimd", n8)
    P.do("vector", lambda e: e.memset(c.consts[:, 4:5], 1.0), inc=1)
    P.do("vector", lambda e: e.memset(c.consts[:, 5:6], float(np.pi / 2)), inc=1)
    P.do("vector", lambda e: e.memset(c.consts[:, 6:7], MAGIC), inc=1)
    P.do("vector", lambda e: e.memset(c.consts[:, 7:8], -MAGIC), inc=1)
    P.do("vector", lambda e: e.memset(c.consts[:, 8:9], 0.25), inc=1)
    P.do("vector", lambda e: e.memset(ztail[:], 0.0), inc=1)
    P.do("vector", lambda e: e.memset(gzt[:], 0.0), inc=1)
    P.do("vector", lambda e: e.memset(carp[:], 0.0), inc=1)
    P.do("vector", lambda e: e.memset(carden[:], 0.0), inc=1)
    P.wait("scalar", "vector")
    P.do("scalar", lambda e: e.activation(slp[:], c.aux[0:32, 0:1], AF.Abs), inc=1)
    P.do("scalar", lambda e: e.activation(slp[:], slp[:], AF.Exp, scale=-1.0), inc=1)
    P.do("scalar", lambda e: e.activation(slp[:], slp[:], AF.Ln,
                                          bias=c.consts[0:32, 4:5]), inc=1)
    P.wait("vector", "scalar")
    P.do("vector", lambda e: e.tensor_scalar(
        out=c.aux[0:32, 1:2], in0=c.aux[0:32, 0:1], scalar1=0.0, scalar2=None,
        op0=ALU.max), inc=1)
    P.do("vector", lambda e: e.tensor_tensor(
        out=slp[:], in0=slp[:], in1=c.aux[0:32, 1:2], op=ALU.add), inc=1)
    P.do("vector", lambda e: e.tensor_scalar(
        out=slp[:], in0=slp[:], scalar1=-1.0, scalar2=None, op0=ALU.mult), inc=1)
    P.wait("scalar", "sync", n11)
    P.do("scalar", lambda e: e.activation(wrow[:], wrow[:], AF.Exp,
                                          accum_out=wsum[:]), inc=1)
    P.wait("vector", "scalar")
    P.do("vector", lambda e: e.reciprocal(wsum[:], wsum[:]), inc=1)
    P.do("vector", lambda e: e.tensor_tensor(
        out=wrow[:], in0=wrow[:], in1=wsum[:].broadcast_to((1, 3)),
        op=ALU.mult), inc=1)
    v_n = P.e["vector"].n
    bank_acquire(c, 7)
    P.wait("tensor", "vector", v_n)
    P.do("tensor", lambda e: e.matmul(c.ps[7][:, 0:3], c.ones_f[:, :], wrow[:],
                                      start=True, stop=True), inc=1)
    bank_done(c, 7)
    bank_consume(c, 7, "vector")
    P.do("vector", lambda e: e.tensor_copy(wpoly[:, 0:3], c.ps[7][:, 0:3]), inc=1)
    P.do("vector", lambda e: e.tensor_scalar(
        out=wpoly[:, 3:4], in0=wpoly[:, 2:3], scalar1=-1.0, scalar2=None,
        op0=ALU.mult), inc=1)
    bank_release(c, 7, "vector")
    P.wait("vector", "sync", n10)
    P.do("vector", lambda e: e.memset(bdre[:], 0.0), inc=1)
    P.do("vector", lambda e: e.memset(bdim[:], 0.0), inc=1)
    for nm, bd, ncol in (("L0_attn_W_re", bdre, 0), ("L0_attn_W_im", bdim, 64)):
        if ncol:
            P.wait("sync", "vector")
            nn = P.dma("sync", wres[:], c.w[nm])
            P.wait("vector", "sync", nn)
        else:
            P.wait("vector", "sync", n9)
        P.do("vector", lambda e, bd=bd, ncol=ncol: e.tensor_scalar(
            out=bd[0:64, 0:64], in0=wres[:], scalar1=nsc[ncol:ncol + 64, 0:1],
            scalar2=None, op0=ALU.mult), inc=1)
        P.do("vector", lambda e, bd=bd: e.tensor_copy(
            bd[64:128, 64:128], bd[0:64, 0:64]), inc=1)

    inproj_pe = [0, 0, 0, 0]
    sraw_pe = 0
    dist_rd_v = 0
    msq_pe2 = [0, 0]
    y_pe2 = [0, 0]
    zb_rd_v = [0, 0]
    gz_rd_pe = [0]
    gz_wr_s = [0, 0]
    s1dve = [0, 0]
    aff_s = [0, 0]
    diag_rd_pe2 = [0, 0]
    diag_v2 = [0, 0]
    sq_gp = [0]

    def emit_s1_head(ct, t0):
        p = ct % 2
        bx = ct % 2
        zbuf = zbuf2[p]
        xvb = xvb2[p]
        gct = 16 + ct
        P.wait("vector", "tensor", diag_rd_pe2[p])
        P.do("vector", lambda e, gct=gct, p=p: e.tensor_tensor(
            out=diagb2[p][:].rearrange("p (k n) -> p k n", k=4),
            in0=c.ident_b[:].rearrange("p (u n) -> p u n", u=1).broadcast_to(
                (128, 4, 128)),
            in1=taps[:, gct * 4:gct * 4 + 4].rearrange(
                "p (k u) -> p k u", u=1).broadcast_to((128, 4, 128)),
            op=ALU.mult), inc=1)
        diag_v2[p] = P.e["vector"].n
        bank_consume(c, bx, "scalar")
        P.wait("scalar", "vector", zb_rd_v[p])
        P.do("scalar", lambda e, bx=bx, zbuf=zbuf: e.activation(
            zbuf[:, 3:515], c.ps[bx][:, :], AF.Copy), inc=1)
        bank_release(c, bx, "scalar")
        P.do("scalar", lambda e, ct=ct, zbuf=zbuf: e.activation(
            zbuf[:, 0:3], ztail[:, ct * 3:ct * 3 + 3], AF.Copy), inc=1)
        P.do("scalar", lambda e, ct=ct, zbuf=zbuf: e.activation(
            ztail[:, ct * 3:ct * 3 + 3], zbuf[:, 512:515], AF.Copy), inc=1)
        s1_act = P.e["scalar"].n
        P.wait("vector", "scalar", s1_act)
        P.wait("vector", "scalar", aff_s[p])
        P.do("vector", lambda e, ct=ct, zbuf=zbuf, xvb=xvb: e.tensor_scalar(
            out=xvb, in0=zbuf[:, 0:512], scalar1=taps[:, ct * 4:ct * 4 + 1],
            scalar2=convb[:, ct:ct + 1], op0=ALU.mult, op1=ALU.add), inc=1)
        for kk in range(1, 4):
            P.do("vector", lambda e, kk=kk, ct=ct, zbuf=zbuf, xvb=xvb:
                 e.scalar_tensor_tensor(
                     out=xvb, in0=zbuf[:, kk:kk + 512],
                     scalar=taps[:, ct * 4 + kk:ct * 4 + kk + 1], in1=xvb,
                     op0=ALU.mult, op1=ALU.add), inc=1)
        zb_rd_v[p] = P.e["vector"].n
        return zb_rd_v[p]

    def emit_s1_trig(ct, conv_n):
        p = ct % 2
        xvb = xvb2[p]
        kb = kb2[p]
        sqb = sqb2[p]
        P.wait("scalar", "vector", s1dve[p])
        P.wait("scalar", "vector", conv_n)
        P.do("scalar", lambda e, ct=ct, p=p, xvb=xvb: e.activation(
            phib2[p][:], xvb, AF.Copy, scale=theta[:, ct:ct + 1]), inc=1)
        P.do("scalar", lambda e, p=p, kb=kb: e.activation(
            kb, phib2[p][:], AF.Identity, scale=1.0 / TWO_PI,
            bias=c.consts[:, 6:7]), inc=1)
        P.do("scalar", lambda e, kb=kb: e.activation(
            kb, kb, AF.Identity, bias=c.consts[:, 7:8]), inc=1)
        act_k1 = P.e["scalar"].n
        P.wait("vector", "scalar", act_k1)
        for i, cc_ in enumerate((C1, C2B)):
            P.do("vector", lambda e, cc_=cc_, i=i, p=p, kb=kb:
                 e.scalar_tensor_tensor(
                     out=prd2[p][:], in0=kb, scalar=-cc_,
                     in1=(phib2[p][:] if i == 0 else prd2[p][:]),
                     op0=ALU.mult, op1=ALU.add), inc=1)
        P.do("vector", lambda e, xvb=xvb, sqb=sqb: e.tensor_tensor(
            out=sqb, in0=xvb, in1=xvb, op=ALU.mult), inc=1)
        casc1_n = P.e["vector"].n
        P.wait("scalar", "vector", casc1_n)
        P.do("scalar", lambda e, p=p: e.activation(
            sinb2[p][:], prd2[p][:], AF.Sin), inc=1)
        P.do("scalar", lambda e, p=p: e.activation(
            cosb2[p][:], prd2[p][:], AF.Sin, scale=0.5), inc=1)
        P.do("scalar", lambda e, p=p, kb=kb: e.activation(
            kb, cosb2[p][:], AF.Square), inc=1)
        P.do("scalar", lambda e, xvb=xvb: e.activation(
            xvb, xvb, AF.Identity, scale=wpoly[:, 1:2],
            bias=wpoly[:, 0:1]), inc=1)
        act_k2 = P.e["scalar"].n
        aff_s[p] = act_k2
        P.wait("vector", "scalar", act_k2)
        P.do("vector", lambda e, p=p, kb=kb: e.tensor_scalar(
            out=cosb2[p][:], in0=kb, scalar1=-2.0, scalar2=1.0,
            op0=ALU.mult, op1=ALU.add), inc=1)
        P.do("vector", lambda e, p=p, xvb=xvb, sqb=sqb: e.scalar_tensor_tensor(
            out=polyb2[p][:], in0=sqb, scalar=wpoly[:, 3:4], in1=xvb,
            op0=ALU.mult, op1=ALU.add), inc=1)
        s1dve[p] = P.e["vector"].n
        return act_k2


    def emit_tail_A(ct, t0, q, s2_n, first):
        p = ct % 2
        remb = rem2[p]
        immb = imm2[p]
        numpair = npr2[p]
        bank_acquire(c, 4)
        if first:
            P.wait("tensor", "vector", v_sr)
            P.wait("tensor", "scalar", sr_s)
        P.do("tensor", lambda e, ct=ct: e.matmul(
            c.ps[4][:, :], selk[:, ct * 128:ct * 128 + 128], pwbf[:],
            start=True, stop=True), inc=1)
        bank_done(c, 4)
        bank_consume(c, 4, "vector")
        P.wait("vector", "scalar", s2_n)
        P.wait("vector", "tensor", msq_pe2[p])
        P.do("vector", lambda e, p=p: e.tensor_tensor(
            out=polyb2[p][:], in0=polyb2[p][:], in1=c.ps[4][:, :],
            op=ALU.mult), inc=1)
        P.do("vector", lambda e, p=p, remb=remb: e.tensor_tensor(
            out=remb[:], in0=polyb2[p][:], in1=cosb2[p][:], op=ALU.mult), inc=1)
        P.do("vector", lambda e, p=p, immb=immb: e.tensor_tensor(
            out=immb[:], in0=polyb2[p][:], in1=sinb2[p][:], op=ALU.mult), inc=1)
        bank_release(c, 4, "vector")
        P.do("vector", lambda e, q=q, ct=ct, numpair=numpair, remb=remb:
             e.tensor_tensor_scan(
                 out=numpair[:, 0:512],
                 data0=c.ones_b[:, 0:1].broadcast_to((128, 512)),
                 data1=remb[:],
                 initial=(0.0 if q == 0 else carp[:, 2 * ct:2 * ct + 1]),
                 op0=ALU.mult, op1=ALU.add), inc=1)
        P.do("vector", lambda e, q=q, ct=ct, numpair=numpair, immb=immb:
             e.tensor_tensor_scan(
                 out=numpair[:, 512:1024],
                 data0=c.ones_b[:, 0:1].broadcast_to((128, 512)),
                 data1=immb[:],
                 initial=(0.0 if q == 0 else carp[:, 2 * ct + 1:2 * ct + 2]),
                 op0=ALU.mult, op1=ALU.add), inc=1)
        P.do("vector", lambda e, ct=ct, numpair=numpair: e.tensor_copy(
            carp[:, 2 * ct:2 * ct + 2], numpair[:, 511:1024:512]), inc=1)
        bank_acquire(c, 5)
        P.do("tensor", lambda e, ct=ct: e.matmul(
            c.ps[5][:, :], selk[:, ct * 128:ct * 128 + 128], invb[:],
            start=True, stop=True), inc=1)
        bank_done(c, 5)
        bank_consume(c, 5, "vector")
        P.wait("vector", "tensor", y_pe2[p])
        P.do("vector", lambda e, p=p, numpair=numpair: e.tensor_tensor(
            out=ren2[p][:], in0=numpair[:, 0:512], in1=c.ps[5][:, :],
            op=ALU.mult), inc=1)
        P.do("vector", lambda e, p=p, numpair=numpair: e.tensor_tensor(
            out=imn2[p][:], in0=numpair[:, 512:1024], in1=c.ps[5][:, :],
            op=ALU.mult), inc=1)
        bank_release(c, 5, "vector")
        norm_v = P.e["vector"].n
        P.do("vector", lambda e, p=p, remb=remb: e.tensor_tensor(
            out=remb[:], in0=ren2[p][:], in1=ren2[p][:], op=ALU.mult), inc=1)
        P.do("vector", lambda e, p=p, immb=immb: e.tensor_tensor(
            out=immb[:], in0=imn2[p][:], in1=imn2[p][:], op=ALU.mult), inc=1)
        sq_gp[0] = P.e["vector"].n
        return norm_v

    def emit_tail_B1(ct, norm_n):
        p = ct % 2
        remb = rem2[p]
        immb = imm2[p]
        renb = ren2[p]
        imnb = imn2[p]
        bank_acquire(c, 6)
        P.wait("tensor", "vector", norm_n)
        P.do("tensor", lambda e, renb=renb: e.matmul(
            c.ps[6][:, :], bdre[:], renb[:], start=True, stop=False), inc=0)
        P.do("tensor", lambda e, imnb=imnb: e.matmul(
            c.ps[6][:, :], bdim[:], imnb[:], start=False, stop=True), inc=1)
        bank_done(c, 6)
        y_pe2[p] = P.e["tensor"].n
        bank_acquire(c, 7)
        P.wait("tensor", "vector", sq_gp[0])
        P.do("tensor", lambda e, remb=remb: e.matmul(
            c.ps[7][0:2, :], hsel[:], remb[:], start=True, stop=False), inc=0)
        P.do("tensor", lambda e, immb=immb: e.matmul(
            c.ps[7][0:2, :], hsel[:], immb[:], start=False, stop=True), inc=1)
        bank_done(c, 7)
        msq_pe2[p] = P.e["tensor"].n

    def emit_tail_B2(ct, t0):
        p = ct % 2
        gct = 16 + ct
        p_ = ct % 2
        bank_consume(c, 7, "scalar")
        P.do("scalar", lambda e: e.activation(
            tmps[0:2, :], c.ps[7][0:2, :], AF.Ln, scale=1.0 / 128.0,
            bias=c.consts[0:2, 0:1]), inc=1)
        P.do("scalar", lambda e: e.activation(
            rsbb, tmps[0:2, :], AF.Exp, scale=-0.5), inc=1)
        s_rs = P.e["scalar"].n
        bg = 2 + ct % 2
        bank_acquire(c, bg)
        P.wait("tensor", "vector", diag_v2[p_])
        P.wait("tensor", "scalar", gz_wr_s[p_])
        for kk in range(4):
            P.do("tensor", lambda e, kk=kk, p_=p_, bg=bg: e.matmul(
                c.ps[bg][:, :], diagb2[p_][:, kk * 128:kk * 128 + 128],
                gzs2[p_][:, kk:kk + 512],
                start=(kk == 0), stop=(kk == 3)), inc=(1 if kk == 3 else 0))
        bank_done(c, bg)
        diag_rd_pe2[p_] = P.e["tensor"].n
        gz_rd_pe[0] = P.e["tensor"].n
        P.wait("tensor", "scalar", s_rs)
        P.do("tensor", lambda e: e.matmul(
            c.ps[7][:, :], hsel2[:], rsbb, start=True, stop=True), inc=1)
        bank_done(c, 7)
        bank_consume(c, bg, "scalar")
        gcv = gcv2[p]
        P.do("scalar", lambda e, gct=gct, bg=bg, gcv=gcv: e.activation(
            gcv[:], c.ps[bg][:, :], AF.Silu, bias=convb[:, gct:gct + 1]), inc=1)
        bank_release(c, bg, "scalar")
        s_n = P.e["scalar"].n
        tjoin = kb2[1 - p]
        bank_consume(c, 7, "vector")
        bank_consume(c, 6, "vector")
        P.wait("vector", "scalar", s_n)
        P.do("vector", lambda e, gcv=gcv, tjoin=tjoin: e.tensor_tensor(
            out=tjoin, in0=c.ps[7][:, :], in1=gcv[:], op=ALU.mult), inc=1)
        bank_release(c, 7, "vector")
        P.do("vector", lambda e, ct=ct, tjoin=tjoin: e.tensor_tensor(
            out=ygq[:, ct * 512:ct * 512 + 512], in0=tjoin,
            in1=c.ps[6][:, :], op=ALU.mult), inc=1)
        bank_release(c, 6, "vector")

    q_first = [True]
    for q in range(NQ):
        q_first[0] = True
        t0 = q * TQ
        P.wait("gpsimd", "tensor", sraw_pe)
        P.wait("gpsimd", "vector", zb_rd_v[1])
        lds = stage_weight_bf16(c, c.w["L0_attn_in_proj_w"], 0, 4096, 8, 1,
                                0, mtile=32, dst=svst)
        b = 6
        bank_acquire(c, b)
        P.wait("tensor", "gpsimd", lds)
        for dc in range(NDC):
            P.do("tensor", lambda e, b=b, dc=dc, t0=t0: e.matmul(
                c.ps[b][0:32, :], svst[:, dc * 32:dc * 32 + 32],
                xln_ap(c, dc, t0, TQ),
                start=(dc == 0), stop=(dc == NDC - 1)),
                inc=(1 if dc == NDC - 1 else 0))
        bank_done(c, b)
        sraw_pe = P.e["tensor"].n
        bank_consume(c, b, "vector")
        P.wait("vector", "scalar")
        P.do("vector", lambda e: e.tensor_copy(
            srawb[:, 0:3], ztail[0:32, 48:51]), inc=1)
        P.do("vector", lambda e, b=b: e.tensor_copy(
            srawb[:, 3:515], c.ps[b][0:32, :]), inc=1)
        bank_release(c, b, "vector")
        P.do("vector", lambda e: e.tensor_copy(
            ztail[0:32, 48:51], srawb[:, 512:515]), inc=1)
        P.do("vector", lambda e: e.tensor_scalar(
            out=pwb, in0=srawb[:, 0:512], scalar1=taps[0:32, 128:129],
            scalar2=None, op0=ALU.mult), inc=1)
        for kk in range(1, 4):
            P.do("vector", lambda e, kk=kk: e.scalar_tensor_tensor(
                out=pwb, in0=srawb[:, kk:kk + 512],
                scalar=taps[0:32, 128 + kk:129 + kk], in1=pwb,
                op0=ALU.mult, op1=ALU.add), inc=1)
        P.do("vector", lambda e: e.tensor_scalar(
            out=pwb, in0=pwb, scalar1=convb[0:32, 32:33], scalar2=None,
            op0=ALU.add), inc=1)
        P.do("vector", lambda e: e.tensor_scalar(
            out=pwb, in0=pwb, scalar1=ssc[:], scalar2=20.0,
            op0=ALU.mult, op1=ALU.min), inc=1)
        P.do("vector", lambda e: e.tensor_scalar(
            out=pwb, in0=pwb, scalar1=-20.0, scalar2=None,
            op0=ALU.max), inc=1)
        distc = srawb[:].bitcast(mybir.dt.float16)[0:32, 0:512]
        P.wait("gpsimd", "vector")
        nd = P.dma("gpsimd", distc, c.w["L0DIST"][:, t0:t0 + TQ])
        P.wait("vector", "gpsimd", nd)
        P.do("vector", lambda e, distc=distc: e.tensor_scalar(
            out=tmps, in0=distc, scalar1=slp[:],
            scalar2=None, op0=ALU.mult), inc=1)
        dist_rd_v = P.e["vector"].n
        P.wait("scalar", "vector")
        P.do("scalar", lambda e: e.activation(pwb, pwb, AF.Exp), inc=1)
        P.do("scalar", lambda e: e.activation(tmps, tmps, AF.Exp), inc=1)
        P.wait("vector", "scalar")
        P.do("vector", lambda e: e.tensor_tensor(
            out=pwb, in0=pwb, in1=tmps, op=ALU.mult), inc=1)
        P.do("vector", lambda e: e.tensor_copy(pwbf[:], pwb), inc=1)
        P.do("vector", lambda e, q=q: e.tensor_tensor_scan(
            out=tmps, data0=c.ones_b[0:32, 0:1].broadcast_to((32, 512)), data1=pwb,
            initial=(0.0 if q == 0 else carden[:]),
            op0=ALU.mult, op1=ALU.add), inc=1)
        P.do("vector", lambda e: e.tensor_copy(carden[:], tmps[0:32, 511:512]), inc=1)
        P.do("vector", lambda e: e.tensor_scalar(
            out=tmps, in0=tmps, scalar1=1e-4, scalar2=None,
            op0=ALU.max), inc=1)
        P.wait("scalar", "vector")
        P.do("scalar", lambda e: e.activation(tmps, tmps, AF.Ln), inc=1)
        P.do("scalar", lambda e: e.activation(invb[:], tmps, AF.Exp,
                                              scale=-1.0), inc=1)
        v_sr = P.e["vector"].n
        sr_s = P.e["scalar"].n
        zb_rd_v[0] = max(zb_rd_v[0], v_sr)

        def stage_group(g):
            half, part = g // 2, g % 2
            rg = (g % 2) * 2
            ru = rg + 1
            P.wait("gpsimd", "tensor", inproj_pe[rg])
            stage_weight_bf16(c, c.w["L0_attn_in_proj_w"], 0,
                              half * 1024 + part * 512, 8, 4, rg * 4096)
            P.wait("gpsimd", "tensor", inproj_pe[ru])
            ld = stage_weight_bf16(c, c.w["L0_attn_in_proj_w"], 0,
                                   2048 + half * 1024 + part * 512, 8, 4,
                                   ru * 4096)
            return rg, ru, ld

        def emit_zmm(ct, rg, ru, ld, t0):
            P.wait("tensor", "gpsimd", ld)
            cl4 = ct % 4
            for reg, bb in ((rg, ct % 2), (ru, 2 + ct % 2)):
                bank_acquire(c, bb)
                for dc in range(NDC):
                    P.do("tensor", lambda e, bb=bb, dc=dc, reg=reg,
                         cl4=cl4, t0=t0:
                         e.matmul(c.ps[bb][:, :],
                                  wtile(c, reg * 4096, 4, dc, cl4),
                                  xln_ap(c, dc, t0, TQ),
                                  start=(dc == 0), stop=(dc == NDC - 1)),
                         inc=(1 if dc == NDC - 1 else 0))
                bank_done(c, bb)
            p_ = ct % 2
            bg_ = 2 + ct % 2
            bank_consume(c, bg_, "scalar")
            P.wait("scalar", "tensor", gz_rd_pe[0])
            P.do("scalar", lambda e, bg_=bg_, p_=p_: e.activation(
                gzs2[p_][:, 3:515], c.ps[bg_][:, :], AF.Copy), inc=1)
            bank_release(c, bg_, "scalar")
            P.do("scalar", lambda e, ct=ct, p_=p_: e.activation(
                gzs2[p_][:, 0:3], gzt[:, ct * 3:ct * 3 + 3], AF.Copy), inc=1)
            P.do("scalar", lambda e, ct=ct, p_=p_: e.activation(
                gzt[:, ct * 3:ct * 3 + 3], gzs2[p_][:, 512:515], AF.Copy), inc=1)
            gz_wr_s[p_] = P.e["scalar"].n

        rg, ru, ld = stage_group(0)
        grp = [(rg, ru, ld)]
        emit_zmm(0, rg, ru, ld, t0)
        pend = None
        for ct in range(16):
            if pend is not None:
                normA = emit_tail_A(pend[0], t0, q, pend[1], pend[2])
                emit_tail_B1(pend[0], normA)
            conv_n = emit_s1_head(ct, t0)
            s2_n = emit_s1_trig(ct, conv_n)
            if pend is not None:
                emit_tail_B2(pend[0], t0)
            pend = (ct, s2_n, q_first[0])
            q_first[0] = False
            if ct < 15:
                nct = ct + 1
                if nct % 4 == 0:
                    g = nct // 4
                    rgp, rup, _ = grp[-1]
                    inproj_pe[rgp] = P.e["tensor"].n
                    inproj_pe[rup] = P.e["tensor"].n
                    rg, ru, ld = stage_group(g)
                    grp.append((rg, ru, ld))
                emit_zmm(nct, rg, ru, ld, t0)
        normA = emit_tail_A(pend[0], t0, q, pend[1], pend[2])
        emit_tail_B1(pend[0], normA)
        emit_tail_B2(pend[0], t0)
        pend = None
        pe_n = P.e["tensor"].n
        P.wait("gpsimd", "tensor", pe_n)
        ldo = stage_weight_bf16(c, c.w["L0_attn_out_proj_w"], 0, 0, 16, 8, 0)
        P.wait("tensor", "gpsimd", ldo)
        P.wait("tensor", "vector")
        for dct in range(8):
            b = dct % 2
            bank_acquire(c, b)
            for ct in range(16):
                P.do("tensor", lambda e, b=b, ct=ct, dct=dct: e.matmul(
                    c.ps[b][:, :], wtile(c, 0, 8, ct, dct),
                    ygq[:, ct * 512:ct * 512 + 512],
                    start=(ct == 0), stop=(ct == 15)),
                    inc=(1 if ct == 15 else 0))
            bank_done(c, b)
            bank_consume(c, b, "vector")
            P.do("vector", lambda e, b=b, dct=dct, t0=t0: e.tensor_tensor(
                out=xr_ap(c, dct, t0, TQ), in0=xr_ap(c, dct, t0, TQ),
                in1=c.ps[b][:, :], op=ALU.add), inc=1)
            bank_release(c, b, "vector")
        inproj_pe = [P.e["tensor"].n] * 4
        sraw_pe = P.e["tensor"].n
    P.barrier()
    bes.close()


def b_l0_ffn(c):
    P = c.P
    emit_ln(c)
    bes = contextlib.ExitStack()
    S = 1.0 / 127.0
    tmpb2 = [bes.enter_context(c.nc.sbuf_tensor(f"l0f_tmp{i}", [128, 512], BF16))
             for i in range(2)]
    rpe = [0, 0, 0, 0]
    tmp_read_v2 = [0, 0]

    for q in range(NQ):
        t0 = q * TQ
        def l0f_stage(idx):
            hs, part = divmod(idx, 2)
            rg = (idx % 2) * 2
            ru = rg + 1
            P.wait("gpsimd", "tensor", rpe[rg])
            stage_weight_bf16(c, c.w["L0_ffn_w_gate"], 0,
                              hs * 1024 + part * 512, 8, 4, rg * 4096)
            P.wait("gpsimd", "tensor", rpe[ru])
            ldb = stage_weight_bf16(c, c.w["L0_ffn_w_up"], 0,
                                    hs * 1024 + part * 512, 8, 4, ru * 4096)
            return rg, ru, ldb

        nxt = l0f_stage(0)
        for hs in range(3):
            for part in range(2):
                idx = hs * 2 + part
                rg, ru, ldb = nxt
                P.wait("tensor", "gpsimd", ldb)
                if idx + 1 < 6:
                    nxt = l0f_stage(idx + 1)
                for hl4 in range(4):
                    hc = hs * 8 + part * 4 + hl4
                    bg = hc % 2
                    bu = 2 + hc % 2
                    bank_acquire(c, bg)
                    for dc in range(NDC):
                        P.do("tensor", lambda e, bg=bg, dc=dc, hl4=hl4, t0=t0, rg=rg:
                             e.matmul(c.ps[bg][:, :], wtile(c, rg * 4096, 4, dc, hl4),
                                      xln_ap(c, dc, t0, TQ),
                                      start=(dc == 0), stop=(dc == NDC - 1)),
                             inc=(1 if dc == NDC - 1 else 0))
                    bank_done(c, bg)
                    bank_acquire(c, bu)
                    for dc in range(NDC):
                        P.do("tensor", lambda e, bu=bu, dc=dc, hl4=hl4, t0=t0, ru=ru:
                             e.matmul(c.ps[bu][:, :], wtile(c, ru * 4096, 4, dc, hl4),
                                      xln_ap(c, dc, t0, TQ),
                                      start=(dc == 0), stop=(dc == NDC - 1)),
                             inc=(1 if dc == NDC - 1 else 0))
                    bank_done(c, bu)
                    tmpb = tmpb2[hc % 2]
                    bank_consume(c, bg, "scalar")
                    P.wait("scalar", "vector", tmp_read_v2[hc % 2])
                    P.do("scalar", lambda e, bg=bg, tmpb=tmpb: e.activation(
                        tmpb[:], c.ps[bg][:, :], AF.Silu, scale=S), inc=1)
                    bank_release(c, bg, "scalar")
                    P.wait("vector", "scalar")
                    bank_consume(c, bu, "vector")
                    P.do("vector", lambda e, bu=bu, hc=hc, tmpb=tmpb:
                         e.tensor_tensor(
                             out=c.scr[:, hc * 512:hc * 512 + 512], in0=tmpb[:],
                             in1=c.ps[bu][:, :], op=ALU.mult), inc=1)
                    tmp_read_v2[hc % 2] = P.e["vector"].n
                    bank_release(c, bu, "vector")
                rpe[rg] = rpe[ru] = P.e["tensor"].n
        sc_n = P.e["scalar"].n
        v_n = P.e["vector"].n

        def l0f_stage_down(hs):
            rp = (hs % 2) * 2
            P.wait("gpsimd", "tensor", max(rpe[rp], rpe[rp + 1]))
            ld = stage_weight_bf16(c, c.w["L0_ffn_w_down"], hs * 1024, 0, 8, 8,
                                   rp * 4096)
            return rp, ld

        nxtb = l0f_stage_down(0)
        for hs in range(3):
            rp, ld = nxtb
            P.wait("tensor", "gpsimd", ld)
            if hs + 1 < 3:
                nxtb = l0f_stage_down(hs + 1)
            if hs == 0:
                for b in range(8):
                    bank_acquire(c, b)
                P.wait("tensor", "scalar", sc_n)
                P.wait("tensor", "vector", v_n)
            for dc in range(NDC):
                for hl in range(8):
                    hc = hs * 8 + hl
                    P.do("tensor", lambda e, dc=dc, rp=rp, hl=hl, hc=hc, hs=hs:
                         e.matmul(c.ps[dc][:, :], wtile(c, rp * 4096, 8, hl, dc),
                                  c.scr[:, hc * 512:hc * 512 + 512],
                                  start=(hs == 0 and hl == 0),
                                  stop=(hs == 2 and hl == 7)),
                         inc=(1 if hl == 7 else 0))
            rpe[rp] = rpe[rp + 1] = P.e["tensor"].n
        for dc in range(NDC):
            bank_done(c, dc)
            bank_consume(c, dc, "vector")
            P.do("vector", lambda e, dc=dc, t0=t0: e.scalar_tensor_tensor(
                out=xr_ap(c, dc, t0, TQ), in0=c.ps[dc][:, :],
                scalar=S * S, in1=xr_ap(c, dc, t0, TQ),
                op0=ALU.mult, op1=ALU.add), inc=1)
            bank_release(c, dc, "vector")
    P.barrier()
    bes.close()



def b_l2_attn(c):
    P = c.P
    emit_ln(c)
    bes = contextlib.ExitStack()
    kv = bes.enter_context(c.nc.sbuf_tensor("l2_kv", [128, 8192], BF16))
    o1 = bes.enter_context(c.nc.sbuf_tensor("l2_o1", [128, 4096], BF16))
    A, B = 0, 8192

    lda = stage_weight_bf16(c, c.w["L2_attn_w_k"], 0, 0, 8, 8, A)
    ldb = stage_weight_bf16(c, c.w["L2_attn_w_v"], 0, 0, 8, 8, B)
    P.wait("tensor", "gpsimd", ldb)
    for th in range(2):
        for tc8 in range(8):
            t0 = th * 1024 + tc8 * 128
            for src_reg, dst0, b0 in ((A, 0, 0), (B, 8192, 2)):
                for et in range(2):
                    b = b0 + et % 2
                    bank_acquire(c, b)
                    for dc in range(NDC):
                        P.do("tensor", lambda e, b=b, dc=dc, t0=t0, src_reg=src_reg, et=et:
                             e.matmul(c.ps[b][:, :],
                                      c.xln[:, dc * T + t0:dc * T + t0 + 128],
                                      c.wst[:, src_reg + dc * 1024 + et * 512:
                                            src_reg + dc * 1024 + et * 512 + 512],
                                      start=(dc == 0), stop=(dc == NDC - 1)),
                             inc=(1 if dc == NDC - 1 else 0))
                    bank_done(c, b)
                    bank_consume(c, b, "scalar")
                    P.do("scalar", lambda e, b=b, dst0=dst0, tc8=tc8, et=et:
                         e.activation(c.scr[:, dst0 + tc8 * 1024 + et * 512:
                                            dst0 + tc8 * 1024 + et * 512 + 512],
                                      c.ps[b][:, :], AF.Copy), inc=1)
                    bank_release(c, b, "scalar")
        sc_n = P.e["scalar"].n
        for dvh in range(2):
            for dkt in range(8):
                bank_acquire(c, dkt)
            P.wait("tensor", "scalar", sc_n)
            for tc8 in range(8):
                for dkt in range(8):
                    P.do("tensor", lambda e, dkt=dkt, tc8=tc8, dvh=dvh:
                         e.matmul(c.ps[dkt][:, :],
                                  c.scr[:, tc8 * 1024 + dkt * 128:
                                        tc8 * 1024 + dkt * 128 + 128],
                                  c.scr[:, 8192 + tc8 * 1024 + dvh * 512:
                                        8192 + tc8 * 1024 + dvh * 512 + 512],
                                  start=(tc8 == 0), stop=(tc8 == 7)),
                         inc=(1 if tc8 == 7 else 0))
            for dkt in range(8):
                bank_done(c, dkt)
                col = dkt * 1024 + dvh * 512
                if th == 0:
                    bank_consume(c, dkt, "scalar")
                    P.do("scalar", lambda e, dkt=dkt, col=col: e.activation(
                        kv[:, col:col + 512], c.ps[dkt][:, :], AF.Copy), inc=1)
                    bank_release(c, dkt, "scalar")
                else:
                    bank_consume(c, dkt, "vector")
                    P.do("vector", lambda e, dkt=dkt, col=col: e.tensor_tensor(
                        out=kv[:, col:col + 512], in0=kv[:, col:col + 512],
                        in1=c.ps[dkt][:, :], op=ALU.add), inc=1)
                    bank_release(c, dkt, "vector")
    pe_n = P.e["tensor"].n
    P.wait("gpsimd", "tensor", pe_n)
    lda = stage_weight_bf16(c, c.w["L2_attn_w_q"], 0, 0, 8, 8, A)
    ldb = stage_weight_bf16(c, c.w["L2_attn_w_o"], 0, 0, 8, 8, B)
    P.wait("tensor", "gpsimd", lda)
    for mt in range(NDC):
        for qc in range(NQ):
            b = (mt * NQ + qc) % 4
            bank_acquire(c, b)
            for dc in range(NDC):
                P.do("tensor", lambda e, b=b, dc=dc, mt=mt, qc=qc:
                     e.matmul(c.ps[b][:, :], wtile(c, A, 8, dc, mt),
                              xln_ap(c, dc, qc * TQ, TQ),
                              start=(dc == 0), stop=(dc == NDC - 1)),
                     inc=(1 if dc == NDC - 1 else 0))
            bank_done(c, b)
            bank_consume(c, b, "scalar")
            P.do("scalar", lambda e, b=b, mt=mt, qc=qc: e.activation(
                c.scr[:, mt * 2048 + qc * 512:mt * 2048 + qc * 512 + 512],
                c.ps[b][:, :], AF.Copy), inc=1)
            bank_release(c, b, "scalar")
    P.wait("tensor", "gpsimd", ldb)
    P.wait("tensor", "scalar")
    P.wait("tensor", "vector")
    for qc in range(NQ):
        for dvt in range(8):
            b = 4 + dvt % 2
            bank_acquire(c, b)
            for dkt in range(8):
                P.do("tensor", lambda e, b=b, dkt=dkt, dvt=dvt, qc=qc:
                     e.matmul(c.ps[b][:, :],
                              kv[:, dkt * 1024 + dvt * 128:dkt * 1024 + dvt * 128 + 128],
                              c.scr[:, dkt * 2048 + qc * 512:dkt * 2048 + qc * 512 + 512],
                              start=(dkt == 0), stop=(dkt == 7)),
                     inc=(1 if dkt == 7 else 0))
            bank_done(c, b)
            bank_consume(c, b, "scalar")
            P.do("scalar", lambda e, b=b, dvt=dvt: e.activation(
                o1[:, dvt * 512:dvt * 512 + 512], c.ps[b][:, :], AF.Copy), inc=1)
            bank_release(c, b, "scalar")
        sc_n = P.e["scalar"].n
        for dct in range(8):
            b = 6 + dct % 2
            bank_acquire(c, b)
            if dct == 0:
                P.wait("tensor", "scalar", sc_n)
            for dvt in range(8):
                P.do("tensor", lambda e, b=b, dvt=dvt, dct=dct:
                     e.matmul(c.ps[b][:, :], wtile(c, B, 8, dvt, dct),
                              o1[:, dvt * 512:dvt * 512 + 512],
                              start=(dvt == 0), stop=(dvt == 7)),
                     inc=(1 if dvt == 7 else 0))
            bank_done(c, b)
            bank_consume(c, b, "vector")
            P.do("vector", lambda e, b=b, dct=dct, qc=qc: e.tensor_tensor(
                out=xr_ap(c, dct, qc * TQ, TQ), in0=xr_ap(c, dct, qc * TQ, TQ),
                in1=c.ps[b][:, :], op=ALU.add), inc=1)
            bank_release(c, b, "vector")
    P.barrier()
    bes.close()



def b_l3_ffn(c):
    P = c.P
    emit_ln(c)
    bes = contextlib.ExitStack()
    nc = c.nc
    lg = bes.enter_context(nc.sbuf_tensor("l3f_lg", [128, 128], F32))
    mx = bes.enter_context(nc.sbuf_tensor("l3f_mx", [128, 16], F32))
    mtok = bes.enter_context(nc.sbuf_tensor("l3f_mtok", [128, 128], BF16))
    mfm = bes.enter_context(nc.sbuf_tensor("l3f_mfm", [8, 2048], BF16))
    mrow = bes.enter_context(nc.sbuf_tensor("l3f_mrow", [1, 1024], BF16))
    xinb = [bes.enter_context(nc.sbuf_tensor(f"l3f_xin{i}", [128, 4096], BF16))
            for i in range(2)]
    gw = bes.enter_context(nc.sbuf_tensor("l3f_gw", [128, 64], BF16))
    ld = stage_weight_bf16(c, c.w["L3_ffn_gate_w"], 0, 0, 8, 1, 0, mtile=8, dst=gw)
    P.wait("tensor", "gpsimd", ld)
    for tc in range(16):
        b = tc % 2
        bank_acquire(c, b)
        for dc in range(NDC):
            P.do("tensor", lambda e, b=b, dc=dc, tc=tc: e.matmul(
                c.ps[b][:, 0:8], c.xln[:, dc * T + tc * 128:dc * T + tc * 128 + 128],
                gw[:, dc * 8:dc * 8 + 8],
                start=(dc == 0), stop=(dc == NDC - 1)),
                inc=(1 if dc == NDC - 1 else 0))
        bank_done(c, b)
        bank_consume(c, b, "vector")
        P.do("vector", lambda e, b=b, tc=tc: e.tensor_copy(
            lg[:, tc * 8:tc * 8 + 8], c.ps[b][:, 0:8]), inc=1)
        bank_release(c, b, "vector")
        P.do("vector", lambda e, tc=tc: e.tensor_reduce(
            out=mx[:, tc:tc + 1], in_=lg[:, tc * 8:tc * 8 + 8],
            axis=AX.X, op=ALU.max), inc=1)
        P.do("vector", lambda e, tc=tc: e.tensor_tensor(
            out=mtok[:, tc * 8:tc * 8 + 8], in0=lg[:, tc * 8:tc * 8 + 8],
            in1=mx[:, tc:tc + 1].broadcast_to((128, 8)), op=ALU.is_ge), inc=1)
    v_n = P.e["vector"].n
    P.wait("tensor", "vector", v_n)
    for tc in range(16):
        b = 2 + tc % 2
        bank_acquire(c, b)
        P.do("tensor", lambda e, b=b, tc=tc: e.transpose(
            c.ps[b].bitcast(BF16)[0:8, 0:128], mtok[:, tc * 8:tc * 8 + 8],
            c.ident_b[:]), inc=1)
        bank_done(c, b)
        bank_consume(c, b, "vector")
        P.do("vector", lambda e, b=b, tc=tc: e.tensor_copy(
            mfm[:, tc * 128:tc * 128 + 128], c.ps[b].bitcast(BF16)[0:8, 0:128]),
            inc=1)
        bank_release(c, b, "vector")
    v_n = P.e["vector"].n
    P.wait("sync", "vector", v_n)
    mrow_pe = [0, 0]
    for e_ in range(8):
        for qc in range(NQ):
            i = e_ * NQ + qc
            mr = mrow[0:1, (i % 2) * 512:(i % 2) * 512 + 512]
            P.wait("sync", "tensor", mrow_pe[i % 2])
            n = P.dma("sync", mr, mfm[e_:e_ + 1, qc * TQ:qc * TQ + TQ])
            b = 4 + i % 2
            bank_acquire(c, b)
            P.wait("tensor", "sync", n)
            if i == 0:
                P.wait("tensor", "vector", v_n)
            P.do("tensor", lambda e, b=b, mr=mr: e.matmul(
                c.ps[b][:, :], c.ones_b[0:1, :], mr,
                start=True, stop=True), inc=1)
            bank_done(c, b)
            mrow_pe[i % 2] = P.e["tensor"].n
            bank_consume(c, b, "scalar")
            P.do("scalar", lambda e, e_=e_, qc=qc, b=b: e.activation(
                c.scr[:, e_ * 2048 + qc * 512:e_ * 2048 + qc * 512 + 512],
                c.ps[b][:, :], AF.Copy), inc=1)
            bank_release(c, b, "scalar")
    for dc in range(NDC):
        P.do("vector", lambda e, dc=dc: e.tensor_tensor(
            out=xr_ap(c, dc), in0=xr_ap(c, dc), in1=xln_ap(c, dc),
            op=ALU.add), inc=1)
    regs = [0, 8192]
    reg_pe = [0, 0]
    xin_rd_pe = [0, 0]
    xin_wr_v = [0, 0]
    sc_n = P.e["scalar"].n
    P.wait("vector", "scalar", sc_n)

    def l3f_emit_xin(i):
        qc_, e2 = divmod(i, 8)
        t0_ = qc_ * TQ
        buf = xinb[i % 2]
        xin3 = buf[:].rearrange("p (dcx t) -> p dcx t", dcx=NDC)
        xln3 = c.xln[:].rearrange(
            "p (dcx t) -> p dcx t", dcx=NDC)[:, :, t0_:t0_ + TQ]
        mbc = c.scr[:, e2 * 2048 + qc_ * 512:e2 * 2048 + qc_ * 512 + 512]
        mbc3 = mbc.rearrange("p (u t) -> p u t", u=1).broadcast_to(
            (128, NDC, TQ))
        P.wait("vector", "tensor", xin_rd_pe[i % 2])
        P.do("vector", lambda e, xin3=xin3, xln3=xln3, mbc3=mbc3:
             e.tensor_tensor(out=xin3, in0=xln3, in1=mbc3, op=ALU.mult), inc=1)
        xin_wr_v[i % 2] = P.e["vector"].n

    l3f_emit_xin(0)
    for qc in range(NQ):
        t0 = qc * TQ
        for b in range(8):
            bank_acquire(c, b)
        for e_ in range(8):
            i = qc * 8 + e_
            cur = i % 2
            xin = xinb[cur]
            reg = regs[e_ % 2]
            P.wait("gpsimd", "tensor", reg_pe[e_ % 2])
            ld = stage_weight_bf16(c, c.w["L3_ffn_expert_w"][e_], 0, 0, 8, 8, reg)
            P.wait("tensor", "gpsimd", ld)
            P.wait("tensor", "vector", xin_wr_v[cur])
            for dct in range(8):
                for hc in range(NDC):
                    P.do("tensor", lambda e, dct=dct, hc=hc, reg=reg, e_=e_,
                         xin=xin:
                         e.matmul(c.ps[dct][:, :], wtile(c, reg, 8, hc, dct),
                                  xin[:, hc * 512:hc * 512 + 512],
                                  start=(e_ == 0 and hc == 0),
                                  stop=(e_ == 7 and hc == 7)),
                         inc=(1 if hc == 7 else 0))
            reg_pe[e_ % 2] = P.e["tensor"].n
            xin_rd_pe[cur] = P.e["tensor"].n
            if i + 1 < 32:
                l3f_emit_xin(i + 1)
        for dct in range(8):
            bank_done(c, dct)
            bank_consume(c, dct, "vector")
            P.do("vector", lambda e, dct=dct, t0=t0: e.tensor_tensor(
                out=xr_ap(c, dct, t0, TQ), in0=xr_ap(c, dct, t0, TQ),
                in1=c.ps[dct][:, :], op=ALU.add), inc=1)
            bank_release(c, dct, "vector")
    P.barrier()
    bes.close()



def b_l2_ffn(c):
    P = c.P
    emit_ln(c)
    bes = contextlib.ExitStack()
    nc = c.nc
    lg = bes.enter_context(nc.sbuf_tensor("l2f_lg", [128, 128], F32))
    mxn = bes.enter_context(nc.sbuf_tensor("l2f_mxn", [128, 16], F32))
    den = bes.enter_context(nc.sbuf_tensor("l2f_den", [128, 16], F32))
    gate = bes.enter_context(nc.sbuf_tensor("l2f_gate", [128, 16], F32))
    msk = bes.enter_context(nc.sbuf_tensor("l2f_msk", [128, 128], F32))
    sel = bes.enter_context(nc.sbuf_tensor("l2f_sel", [128, 128], F32))
    selfm = bes.enter_context(nc.sbuf_tensor("l2f_selfm", [8, 2048], BF16))
    mrow = bes.enter_context(nc.sbuf_tensor("l2f_mrow", [1, 1024], BF16))
    carry = bes.enter_context(nc.sbuf_tensor("l2f_carry", [1, 8], F32))
    wrst = bes.enter_context(nc.sbuf_tensor("l2f_wr", [128, 64], BF16))
    brbc = bes.enter_context(nc.sbuf_tensor("l2f_br", [128, 8], F32))
    b1sb = bes.enter_context(nc.sbuf_tensor("l2f_b1", [128, 8], F32))
    b2t = bes.enter_context(nc.sbuf_tensor("l2f_b2t", [128, 8], F32))
    xin0 = bes.enter_context(nc.sbuf_tensor("l2f_xin0", [128, 4096], BF16))
    xin1 = bes.enter_context(nc.sbuf_tensor("l2f_xin1", [128, 4096], BF16))
    tsel = bes.enter_context(nc.sbuf_tensor("l2f_ts", [128, 512], BF16))
    hbuf = bes.enter_context(nc.sbuf_tensor("l2f_h", [128, 4096], BF16))
    tmp = bes.enter_context(nc.sbuf_tensor("l2f_tmp", [128, 8], F32))

    ld = stage_weight_bf16(c, c.w["L2_ffn_Wr"], 0, 0, 8, 1, 0, mtile=8, dst=wrst)
    nbr = P.dma("sync", c.aux[0:1, 0:8], c.w["L2_ffn_br"])
    P.do("vector", lambda e: e.memset(carry[:], 0.0), inc=1)
    bank_acquire(c, 6)
    P.wait("tensor", "sync", nbr)
    P.do("tensor", lambda e: e.matmul(c.ps[6][:, 0:8], c.ones_f[:, :],
                                      c.aux[0:1, 0:8], start=True, stop=True), inc=1)
    bank_done(c, 6)
    bank_consume(c, 6, "vector")
    P.do("vector", lambda e: e.tensor_copy(brbc[:], c.ps[6][:, 0:8]), inc=1)
    bank_release(c, 6, "vector")
    P.wait("tensor", "gpsimd", ld)

    for tc in range(16):
        b = tc % 2
        bank_acquire(c, b)
        for dc in range(NDC):
            P.do("tensor", lambda e, b=b, dc=dc, tc=tc: e.matmul(
                c.ps[b][:, 0:8], c.xln[:, dc * T + tc * 128:dc * T + tc * 128 + 128],
                wrst[:, dc * 8:dc * 8 + 8],
                start=(dc == 0), stop=(dc == NDC - 1)),
                inc=(1 if dc == NDC - 1 else 0))
        bank_done(c, b)
        bank_consume(c, b, "vector")
        P.do("vector", lambda e, b=b, tc=tc: e.tensor_tensor(
            out=lg[:, tc * 8:tc * 8 + 8], in0=c.ps[b][:, 0:8], in1=brbc[:],
            op=ALU.add), inc=1)
        bank_release(c, b, "vector")
        P.do("vector", lambda e, tc=tc: e.tensor_reduce(
            out=mxn[:, tc:tc + 1], in_=lg[:, tc * 8:tc * 8 + 8],
            axis=AX.X, op=ALU.max, negate=True), inc=1)
        P.wait("scalar", "vector")
        P.do("scalar", lambda e, tc=tc: e.activation(
            tmp[:, 0:8], lg[:, tc * 8:tc * 8 + 8], AF.Exp,
            bias=mxn[:, tc:tc + 1], accum_out=den[:, tc:tc + 1]), inc=1)
        P.wait("vector", "scalar")
        P.do("vector", lambda e, tc=tc: e.reciprocal(
            gate[:, tc:tc + 1], den[:, tc:tc + 1]), inc=1)
        P.do("vector", lambda e, tc=tc: e.scalar_tensor_tensor(
            out=msk[:, tc * 8:tc * 8 + 8], in0=lg[:, tc * 8:tc * 8 + 8],
            scalar=mxn[:, tc:tc + 1], in1=c.consts[:, 3:4].broadcast_to((128, 8)),
            op0=ALU.add, op1=ALU.is_ge), inc=1)
        v_n = P.e["vector"].n
        b2_ = 2 + tc % 2
        bank_acquire(c, b2_)
        P.wait("tensor", "vector", v_n)
        P.do("tensor", lambda e, b2_=b2_, tc=tc: e.matmul(
            c.ps[b2_][:, 0:8], c.tri_f[:], msk[:, tc * 8:tc * 8 + 8],
            start=True, stop=False), inc=0)
        P.do("tensor", lambda e, b2_=b2_: e.matmul(
            c.ps[b2_][:, 0:8], c.ones_f[:, :], carry[:],
            start=False, stop=True), inc=1)
        bank_done(c, b2_)
        b3 = 4 + tc % 2
        bank_acquire(c, b3)
        P.do("tensor", lambda e, b3=b3, tc=tc: e.matmul(
            c.ps[b3][0:1, 0:8], c.ones_cf[:], msk[:, tc * 8:tc * 8 + 8],
            start=True, stop=True), inc=1)
        bank_done(c, b3)
        bank_consume(c, b2_, "vector")
        P.do("vector", lambda e, b2_=b2_, tc=tc: e.tensor_scalar(
            out=sel[:, tc * 8:tc * 8 + 8], in0=c.ps[b2_][:, 0:8],
            scalar1=2047.5, scalar2=None, op0=ALU.is_le), inc=1)
        bank_release(c, b2_, "vector")
        P.do("vector", lambda e, tc=tc: e.tensor_tensor(
            out=sel[:, tc * 8:tc * 8 + 8], in0=sel[:, tc * 8:tc * 8 + 8],
            in1=msk[:, tc * 8:tc * 8 + 8], op=ALU.mult), inc=1)
        P.do("vector", lambda e, tc=tc: e.tensor_tensor(
            out=sel[:, tc * 8:tc * 8 + 8], in0=sel[:, tc * 8:tc * 8 + 8],
            in1=gate[:, tc:tc + 1].broadcast_to((128, 8)), op=ALU.mult), inc=1)
        bank_consume(c, b3, "vector")
        P.do("vector", lambda e, b3=b3: e.tensor_tensor(
            out=carry[:], in0=carry[:], in1=c.ps[b3][0:1, 0:8], op=ALU.add), inc=1)
        bank_release(c, b3, "vector")
    v_n = P.e["vector"].n
    P.wait("tensor", "vector", v_n)
    for tc in range(16):
        b = 6 + tc % 2
        bank_acquire(c, b)
        P.do("tensor", lambda e, b=b, tc=tc: e.transpose(
            c.ps[b][0:8, 0:128], sel[:, tc * 8:tc * 8 + 8], c.ident_f[:]), inc=1)
        bank_done(c, b)
        bank_consume(c, b, "vector")
        P.do("vector", lambda e, b=b, tc=tc: e.tensor_copy(
            selfm[:, tc * 128:tc * 128 + 128], c.ps[b][0:8, 0:128]), inc=1)
        bank_release(c, b, "vector")
    v_n = P.e["vector"].n
    P.wait("sync", "vector", v_n)
    mrow_pe = [0, 0]
    for e_ in range(8):
        for qc in range(NQ):
            i = e_ * NQ + qc
            mr = mrow[0:1, (i % 2) * 512:(i % 2) * 512 + 512]
            P.wait("sync", "tensor", mrow_pe[i % 2])
            n = P.dma("sync", mr, selfm[e_:e_ + 1, qc * TQ:qc * TQ + TQ])
            b = i % 2
            bank_acquire(c, b)
            P.wait("tensor", "sync", n)
            P.do("tensor", lambda e, b=b, mr=mr: e.matmul(
                c.ps[b][:, :], c.ones_b[0:1, :], mr,
                start=True, stop=True), inc=1)
            bank_done(c, b)
            mrow_pe[i % 2] = P.e["tensor"].n
            bank_consume(c, b, "scalar")
            P.do("scalar", lambda e, e_=e_, qc=qc, b=b: e.activation(
                c.scr[:, e_ * 2048 + qc * 512:e_ * 2048 + qc * 512 + 512],
                c.ps[b][:, :], AF.Copy), inc=1)
            bank_release(c, b, "scalar")
    A2, B2 = 0, 8192
    w1_rd_pe = 0
    w2_rd_pe = 0
    b1_rd_s = 0
    b2r_rd_v = 0
    xinb = [xin0, xin1]
    xin_rd_pe = [0, 0]
    xin_wr_v = [0, 0]
    sc_pb = P.e["scalar"].n
    P.wait("vector", "scalar", sc_pb)

    def l2f_mbc3(i):
        e_, qc = divmod(i, NQ)
        mbc = c.scr[:, e_ * 2048 + qc * 512:e_ * 2048 + qc * 512 + 512]
        return mbc.rearrange("p (u t) -> p u t", u=1).broadcast_to(
            (128, NDC, TQ))

    def l2f_emit_xin(i):
        qc = i % NQ
        t0 = qc * TQ
        buf = xinb[i % 2]
        xin3 = buf[:].rearrange("p (dcx t) -> p dcx t", dcx=NDC)
        xln3 = c.xln[:].rearrange(
            "p (dcx t) -> p dcx t", dcx=NDC)[:, :, t0:t0 + TQ]
        P.wait("vector", "tensor", xin_rd_pe[i % 2])
        P.do("vector", lambda e, xin3=xin3, xln3=xln3, m3_=l2f_mbc3(i):
             e.tensor_tensor(out=xin3, in0=xln3, in1=m3_, op=ALU.mult), inc=1)
        xin_wr_v[i % 2] = P.e["vector"].n

    l2f_emit_xin(0)
    for e_ in range(8):
        P.wait("gpsimd", "tensor", w1_rd_pe)
        ld1 = stage_weight_bf16(c, c.w["L2_ffn_W1"][e_], 0, 0, 8, 8, A2)
        P.wait("gpsimd", "scalar", b1_rd_s)
        nb = P.dma("gpsimd", b1sb[:], c.w["L2_ffn_b1"][e_])
        P.wait("gpsimd", "tensor", w2_rd_pe)
        ld2 = stage_weight_bf16(c, c.w["L2_ffn_W2"][e_], 0, 0, 8, 8, B2)
        P.wait("gpsimd", "vector", b2r_rd_v)
        nb2 = P.dma("gpsimd", b2t[:], c.w["L2B2T"][e_])
        for qc in range(NQ):
            t0 = qc * TQ
            i = e_ * NQ + qc
            cur = i % 2
            xin = xinb[cur]
            if qc == 0:
                P.wait("tensor", "gpsimd", ld1)
                P.wait("scalar", "gpsimd", nb)
            P.wait("tensor", "vector", xin_wr_v[cur])
            for ft in range(8):
                b = ft % 2
                bank_acquire(c, b)
                for dc in range(NDC):
                    P.do("tensor", lambda e, b=b, dc=dc, ft=ft, xin=xin:
                         e.matmul(c.ps[b][:, :], wtile(c, A2, 8, dc, ft),
                                  xin[:, dc * 512:dc * 512 + 512],
                                  start=(dc == 0), stop=(dc == NDC - 1)),
                         inc=(1 if dc == NDC - 1 else 0))
                bank_done(c, b)
                bank_consume(c, b, "scalar")
                P.do("scalar", lambda e, b=b, ft=ft: e.activation(
                    hbuf[:, ft * 512:ft * 512 + 512], c.ps[b][:, :], AF.Relu,
                    bias=b1sb[:, ft:ft + 1]), inc=1)
                bank_release(c, b, "scalar")
            xin_rd_pe[cur] = P.e["tensor"].n
            if qc == NQ - 1:
                w1_rd_pe = P.e["tensor"].n
            sc_n = P.e["scalar"].n
            b1_rd_s = sc_n
            if i + 1 < 32:
                l2f_emit_xin(i + 1)
            if qc == 0:
                P.wait("vector", "gpsimd", nb2)
                P.wait("tensor", "gpsimd", ld2)
            P.wait("tensor", "scalar", sc_n)
            mbc = c.scr[:, e_ * 2048 + qc * 512:e_ * 2048 + qc * 512 + 512]
            for dct in range(8):
                b = 2 + dct % 2
                bank_acquire(c, b)
                for fc in range(8):
                    P.do("tensor", lambda e, b=b, fc=fc, dct=dct:
                         e.matmul(c.ps[b][:, :],
                                  wtile(c, B2, 8, fc, dct),
                                  hbuf[:, fc * 512:fc * 512 + 512],
                                  start=(fc == 0), stop=(fc == 7)),
                         inc=(1 if fc == 7 else 0))
                bank_done(c, b)
                bank_consume(c, b, "vector")
                P.do("vector", lambda e, b=b, mbc=mbc, dct=dct:
                     e.scalar_tensor_tensor(
                         out=tsel[:], in0=c.ps[b][:, :],
                         scalar=b2t[:, dct:dct + 1], in1=mbc,
                         op0=ALU.add, op1=ALU.mult), inc=1)
                P.do("vector", lambda e, dct=dct, t0=t0: e.tensor_tensor(
                    out=xr_ap(c, dct, t0, TQ), in0=xr_ap(c, dct, t0, TQ),
                    in1=tsel[:], op=ALU.add), inc=1)
                bank_release(c, b, "vector")
        w2_rd_pe = P.e["tensor"].n
        b2r_rd_v = P.e["vector"].n
    P.barrier()
    bes.close()



def b_l1_attn(c):
    P = c.P
    emit_ln(c)
    bes = contextlib.ExitStack()
    nc = c.nc
    A, B = 0, 8192
    m3 = bes.enter_context(nc.sbuf_tensor("l1a_m3", [16, 3], F32))
    cnt = bes.enter_context(nc.sbuf_tensor("l1a_cnt", [16, 1], F32))
    gts = bes.enter_context(nc.sbuf_tensor("l1a_gt", [16, 1], F32))
    df2 = bes.enter_context(nc.sbuf_tensor("l1a_df2", [16, 2], F32))
    ktok = bes.enter_context(nc.sbuf_tensor("l1a_ktok", [128, 256], BF16))
    dnvf = bes.enter_context(nc.sbuf_tensor("l1a_dnvf", [1, 512], F32))
    dnvb = bes.enter_context(nc.sbuf_tensor("l1a_dnvb", [1, 512], BF16))
    dbcs = bes.enter_context(nc.sbuf_tensor("l1a_dbc", [128, 512], BF16))
    tsb = bes.enter_context(nc.sbuf_tensor("l1a_tsb", [128, 512], F32))
    qkvb = bes.enter_context(nc.sbuf_tensor("l1a_qkvb", [128, 24], F32))
    rb1 = bes.enter_context(nc.sbuf_tensor("l1a_rb1", [128, 2], F32))
    rb2 = bes.enter_context(nc.sbuf_tensor("l1a_rb2", [16, 1], F32))
    vbias = bes.enter_context(nc.sbuf_tensor("l1a_vbias", [1, 1024], BF16))

    n1 = P.dma("sync", qkvb[:], c.w["L1_attn_qkv_b"])
    n2 = P.dma("sync", rb1[:], c.w["L1_attn_r_b1"])
    n3 = P.dma("sync", rb2[:], c.w["L1_attn_r_b2"])
    n4 = P.dma("gpsimd", vbias[:], c.w["L1A_VBROW"])
    ldv = stage_weight_bf16(c, c.w["L1_attn_qkv_w"], 0, 2048, 8, 8, A)
    ldr1 = stage_weight_bf16(c, c.w["L1_attn_r_w1"], 0, 0, 8, 2, B)
    ldr2 = stage_weight_bf16(c, c.w["L1_attn_r_w2"], 0, 0, 2, 1, B + 2048, mtile=16)
    P.wait("tensor", "gpsimd", n4)
    P.wait("tensor", "gpsimd", ldv)
    P.wait("tensor", "sync", n3)
    ph1 = contextlib.ExitStack()
    rsb = ph1.enter_context(nc.sbuf_tensor("l1a_rs", [16, 2048], F32))
    h1b = ph1.enter_context(nc.sbuf_tensor("l1a_h1", [128, 4096], BF16))
    kfm = ph1.enter_context(nc.sbuf_tensor("l1a_kfm", [16, 2048], BF16))
    ones65 = c.scr[:, 0:16 * 1040].rearrange(
        "p (s w) -> p s w", w=65)[:, :, 64:65]
    P.do("vector", lambda e: e.tensor_copy(
        ones65, c.ones_b[:, 0:1].rearrange(
            "p (s w) -> p s w", w=1).broadcast_to((128, 256, 1))), inc=1)
    v_ones65 = P.e["vector"].n
    P.wait("tensor", "vector", v_ones65)
    for tc in range(16):
        for et in range(2):
            b = et % 2
            bank_acquire(c, b)
            for dc in range(NDC):
                P.do("tensor", lambda e, b=b, dc=dc, tc=tc, et=et: e.matmul(
                    c.ps[b][:, :],
                    c.xln[:, dc * T + tc * 128:dc * T + tc * 128 + 128],
                    c.wst[:, A + dc * 1024 + et * 512:A + dc * 1024 + et * 512 + 512],
                    start=(dc == 0), stop=False), inc=0)
            P.do("tensor", lambda e, b=b, et=et: e.matmul(
                c.ps[b][:, :], c.ones_b[0:1, :],
                vbias[:, et * 512:et * 512 + 512],
                start=False, stop=True), inc=1)
            bank_done(c, b)
            bank_consume(c, b, "scalar")
            P.do("scalar", lambda e, b=b, tc=tc, et=et: e.activation(
                c.scr[:, tc * 1040 + et * 520:tc * 1040 + et * 520 + 520]
                .rearrange("p (h w) -> p h w", w=65)[:, :, 0:64],
                c.ps[b][:, :].rearrange("p (h w) -> p h w", w=64),
                AF.Copy), inc=1)
            bank_release(c, b, "scalar")
    P.wait("tensor", "gpsimd", ldr2)
    P.wait("tensor", "sync", n2)
    for mt in range(2):
        for qc in range(NQ):
            b = 2 + qc % 2
            bank_acquire(c, b)
            for dc in range(NDC):
                P.do("tensor", lambda e, b=b, dc=dc, mt=mt, qc=qc: e.matmul(
                    c.ps[b][:, :], wtile(c, B, 2, dc, mt),
                    xln_ap(c, dc, qc * TQ, TQ),
                    start=(dc == 0), stop=(dc == NDC - 1)),
                    inc=(1 if dc == NDC - 1 else 0))
            bank_done(c, b)
            bank_consume(c, b, "scalar")
            P.do("scalar", lambda e, b=b, mt=mt, qc=qc: e.activation(
                h1b[:, mt * 2048 + qc * 512:mt * 2048 + qc * 512 + 512],
                c.ps[b][:, :], AF.Relu, bias=rb1[:, mt:mt + 1]), inc=1)
            bank_release(c, b, "scalar")
    sc_n = P.e["scalar"].n
    P.wait("tensor", "scalar", sc_n)
    for qc in range(NQ):
        b = 4 + qc % 2
        bank_acquire(c, b)
        for mt in range(2):
            P.do("tensor", lambda e, b=b, mt=mt, qc=qc: e.matmul(
                c.ps[b][0:16, :],
                c.wst[:, B + 2048 + mt * 16:B + 2048 + mt * 16 + 16],
                h1b[:, mt * 2048 + qc * 512:mt * 2048 + qc * 512 + 512],
                start=(mt == 0), stop=(mt == 1)), inc=(1 if mt == 1 else 0))
        bank_done(c, b)
        bank_consume(c, b, "scalar")
        P.do("scalar", lambda e, b=b, qc=qc: e.activation(
            rsb[:, qc * 512:qc * 512 + 512], c.ps[b][0:16, :], AF.Sigmoid,
            bias=rb2[:]), inc=1)
        P.do("scalar", lambda e, qc=qc: e.activation(
            rsb[:, qc * 512:qc * 512 + 512], rsb[:, qc * 512:qc * 512 + 512],
            AF.Sigmoid), inc=1)
        bank_release(c, b, "scalar")
    P.wait("vector", "scalar")
    cmpb = h1b[:].bitcast(F32)[0:16, 0:2048]
    P.do("vector", lambda e: e.memset(m3[:, 0:1], 0.5), inc=1)
    P.do("vector", lambda e: e.memset(m3[:, 2:3], 0.7310586), inc=1)
    KKEEP = float(int(T * 0.4))
    for it in range(25):
        P.do("vector", lambda e: e.tensor_tensor(
            out=m3[:, 1:2], in0=m3[:, 0:1], in1=m3[:, 2:3], op=ALU.add), inc=1)
        P.do("vector", lambda e: e.tensor_scalar(
            out=m3[:, 1:2], in0=m3[:, 1:2], scalar1=0.5, scalar2=None,
            op0=ALU.mult), inc=1)
        P.do("vector", lambda e: e.tensor_scalar(
            out=cmpb, in0=rsb[:], scalar1=m3[:, 1:2], scalar2=None,
            op0=ALU.is_ge, op1=ALU.add, accum_out=cnt[:]), inc=1)
        P.do("vector", lambda e: e.tensor_scalar(
            out=gts[:], in0=cnt[:], scalar1=KKEEP, scalar2=None,
            op0=ALU.is_ge), inc=1)
        P.do("vector", lambda e: e.tensor_tensor(
            out=df2[:], in0=m3[:, 1:3], in1=m3[:, 0:2], op=ALU.subtract), inc=1)
        P.do("vector", lambda e: e.tensor_scalar(
            out=df2[:], in0=df2[:], scalar1=gts[:], scalar2=None,
            op0=ALU.mult), inc=1)
        P.do("vector", lambda e: e.tensor_tensor(
            out=m3[:, 0:3:2], in0=m3[:, 0:2], in1=df2[:], op=ALU.add), inc=1)
    P.do("vector", lambda e: e.tensor_scalar(
        out=kfm[:], in0=rsb[:], scalar1=m3[:, 0:1], scalar2=None,
        op0=ALU.is_ge), inc=1)
    v_n = P.e["vector"].n
    P.wait("tensor", "vector", v_n)
    for tc in range(16):
        b = 6 + tc % 2
        bank_acquire(c, b)
        P.do("tensor", lambda e, b=b, tc=tc: e.transpose(
            c.ps[b].bitcast(BF16)[0:128, 0:16], kfm[:, tc * 128:tc * 128 + 128],
            c.ident_b[0:16, 0:16]), inc=1)
        bank_done(c, b)
        bank_consume(c, b, "vector")
        P.do("vector", lambda e, b=b, tc=tc: e.tensor_copy(
            ktok[:, tc * 16:tc * 16 + 16],
            c.ps[b].bitcast(BF16)[0:128, 0:16]), inc=1)
        bank_release(c, b, "vector")
    ph1.close()
    qh2 = [bes.enter_context(nc.sbuf_tensor(f"l1a_qh{i}", [128, 2048], BF16))
           for i in range(2)]
    kh2 = [bes.enter_context(nc.sbuf_tensor(f"l1a_kh{i}", [128, 2048], BF16))
           for i in range(2)]
    expt = bes.enter_context(nc.sbuf_tensor("l1a_exp", [128, 2048], BF16))
    bcolp2 = [bes.enter_context(nc.sbuf_tensor(f"l1a_bcp{i}", [128, 32], F32))
              for i in range(2)]
    onesc = bes.enter_context(nc.sbuf_tensor("l1a_on1", [128, 1], BF16))
    pe_n = P.e["tensor"].n
    P.wait("gpsimd", "tensor", pe_n)
    ldq = stage_weight_bf16(c, c.w["L1_attn_qkv_w"], 0, 0, 8, 8, A)
    ldk = stage_weight_bf16(c, c.w["L1_attn_qkv_w"], 0, 1024, 8, 8, B)
    P.do("vector", lambda e: e.memset(onesc[:], 1.0), inc=1)
    v_ones = P.e["vector"].n
    P.wait("tensor", "gpsimd", ldk)
    P.wait("tensor", "vector", v_ones)
    for b in range(6):
        bank_acquire(c, b)
    qk_rd_pe2 = [0, 0]
    bcp_rd_s2 = [0, 0]
    dnv_bc_pe = 0
    exp_done = [0, 0]
    ctx_done = [0, 0]
    ctx_tail_v = 0
    ctx_tail_s = 0
    for hp in range(8):
        h0 = 2 * hp
        qh = qh2[hp % 2]
        kh = kh2[hp % 2]
        bcolp = bcolp2[hp % 2]
        P.wait("scalar", "tensor", qk_rd_pe2[hp % 2])
        for reg, buf, bcoli in ((A, qh, hp), (B, kh, 8 + hp)):
            for qc in range(NQ):
                b = 6 + qc % 2
                bank_acquire(c, b)
                for dc in range(NDC):
                    P.do("tensor", lambda e, b=b, dc=dc, reg=reg, hp=hp, qc=qc:
                         e.matmul(c.ps[b][:, :], wtile(c, reg, 8, dc, hp),
                                  xln_ap(c, dc, qc * TQ, TQ),
                                  start=(dc == 0), stop=(dc == NDC - 1)),
                         inc=(1 if dc == NDC - 1 else 0))
                bank_done(c, b)
                bank_consume(c, b, "scalar")
                P.do("scalar", lambda e, b=b, buf=buf, bcoli=bcoli, qc=qc:
                     e.activation(buf[:, qc * 512:qc * 512 + 512],
                                  c.ps[b][:, :], AF.Identity,
                                  bias=qkvb[:, bcoli:bcoli + 1]), inc=1)
                bank_release(c, b, "scalar")
        qk_wr_s = P.e["scalar"].n
        P.wait("vector", "scalar", bcp_rd_s2[hp % 2])
        for hsel in range(2):
            P.do("vector", lambda e, hsel=hsel, h0=h0, bcolp=bcolp:
                 e.tensor_scalar(
                     out=bcolp[:, hsel:32:2], in0=ktok[:, h0 + hsel:256:16],
                     scalar1=-1.0, scalar2=1e30, op0=ALU.add, op1=ALU.mult),
                 inc=1)
        v_bcp = P.e["vector"].n
        P.wait("scalar", "vector", v_bcp)
        P.wait("tensor", "scalar", qk_wr_s)
        for qc in range(NQ):
            for tc in range(17):
                par = tc % 2
                if tc < 16:
                    P.wait("tensor", "scalar", exp_done[par])
                    for hsel in range(2):
                        sb = (0 if par == 0 else 4) + hsel
                        po = hsel * 64
                        P.do("tensor", lambda e, sb=sb, po=po, tc=tc, qc=qc,
                             kh=kh, qh=qh:
                             e.matmul(c.ps[sb][:, :],
                                      kh[po:po + 64, tc * 128:tc * 128 + 128],
                                      qh[po:po + 64, qc * 512:qc * 512 + 512],
                                      start=True, stop=True), inc=1)
                    sc_pe = P.e["tensor"].n
                    if qc == NQ - 1 and tc == 15:
                        qk_rd_pe2[hp % 2] = sc_pe
                    P.wait("scalar", "tensor", sc_pe)
                    P.wait("scalar", "tensor", ctx_done[par])
                    for hsel in range(2):
                        sb = (0 if par == 0 else 4) + hsel
                        slot = (par * 2 + hsel) * 512
                        P.do("scalar", lambda e, sb=sb, slot=slot, tc=tc,
                             hsel=hsel, bcolp=bcolp: e.activation(
                                 expt[:, slot:slot + 512], c.ps[sb][:, :],
                                 AF.Exp, scale=0.125,
                                 bias=bcolp[:, tc * 2 + hsel:tc * 2 + hsel + 1]),
                             inc=1)
                    exp_done[par] = P.e["scalar"].n
                if tc > 0:
                    tp_ = tc - 1
                    parp = tp_ % 2
                    P.wait("tensor", "scalar", exp_done[parp])
                    if tp_ == 0:
                        P.wait("tensor", "vector", ctx_tail_v)
                        P.wait("tensor", "scalar", ctx_tail_s)
                    for hsel in range(2):
                        slot = (parp * 2 + hsel) * 512
                        cb = 2 + hsel
                        vcol = tp_ * 1040 + (h0 + hsel) * 65
                        P.do("tensor", lambda e, cb=cb, slot=slot, tp_=tp_,
                             vcol=vcol: e.matmul(
                                 c.ps[cb][0:65, :],
                                 c.scr[:, vcol:vcol + 65],
                                 expt[:, slot:slot + 512],
                                 start=(tp_ == 0), stop=(tp_ == 15)), inc=1)
                    ctx_done[parp] = P.e["tensor"].n
            for hsel in range(2):
                cb = 2 + hsel
                po = hsel * 64
                P.wait("scalar", "tensor", ctx_done[1])
                P.do("scalar", lambda e, cb=cb: e.activation(
                    dnvf[:], c.ps[cb][64:65, :], AF.Ln), inc=1)
                P.do("scalar", lambda e: e.activation(
                    dnvb[:], dnvf[:], AF.Exp, scale=-1.0), inc=1)
                s_d = P.e["scalar"].n
                bank_acquire(c, 6)
                P.wait("tensor", "scalar", s_d)
                P.do("tensor", lambda e: e.matmul(
                    c.ps[6][:, :], c.ones_b[0:1, :], dnvb[:],
                    start=True, stop=True), inc=1)
                bank_done(c, 6)
                bank_consume(c, 6, "scalar")
                P.wait("scalar", "vector", dnv_bc_pe)
                P.do("scalar", lambda e: e.activation(
                    dbcs[:], c.ps[6][:, :], AF.Copy), inc=1)
                bank_release(c, 6, "scalar")
                P.wait("vector", "scalar")
                P.do("vector", lambda e, cb=cb, po=po: e.tensor_tensor(
                    out=tsb[po:po + 64, :], in0=c.ps[cb][0:64, :],
                    in1=dbcs[0:64, :], op=ALU.mult), inc=1)
                dnv_bc_pe = P.e["vector"].n
                P.do("vector", lambda e, po=po, hp=hp, qc=qc: e.tensor_tensor(
                    out=c.xr[po:po + 64, hp * T + qc * TQ:hp * T + qc * TQ + TQ],
                    in0=c.xr[po:po + 64, hp * T + qc * TQ:hp * T + qc * TQ + TQ],
                    in1=tsb[po:po + 64, :], op=ALU.add), inc=1)
            ctx_tail_v = P.e["vector"].n
            ctx_tail_s = P.e["scalar"].n
        bcp_rd_s2[hp % 2] = P.e["scalar"].n
    P.barrier()
    bes.close()


def b_l3_attn(c):
    P = c.P
    emit_ln(c)
    bes = contextlib.ExitStack()
    nc = c.nc
    A, B = 0, 8192
    vwst = bes.enter_context(nc.sbuf_tensor("l3a_vw", [128, 1024], BF16))
    vpair = bes.enter_context(nc.sbuf_tensor("l3a_vp", [128, 2080], BF16))
    qh = bes.enter_context(nc.sbuf_tensor("l3a_qh", [128, 2048], BF16))
    kh = bes.enter_context(nc.sbuf_tensor("l3a_kh", [128, 2048], BF16))
    onesc3 = bes.enter_context(nc.sbuf_tensor("l3a_on1", [128, 1], BF16))
    expt = bes.enter_context(nc.sbuf_tensor("l3a_exp", [128, 2048], BF16))
    dnvf = bes.enter_context(nc.sbuf_tensor("l3a_dnvf", [1, 512], F32))
    dnvb = bes.enter_context(nc.sbuf_tensor("l3a_dnvb", [1, 512], BF16))
    dbcs = bes.enter_context(nc.sbuf_tensor("l3a_dbc", [128, 512], BF16))
    bo_sb = bes.enter_context(nc.sbuf_tensor("l3a_bo", [128, 8], F32))
    trib = bes.enter_context(nc.sbuf_tensor("l3a_trib", [128, 128], BF16))

    nbo = P.dma("sync", bo_sb[:], c.w["L3_attn_bo"])
    P.do("vector", lambda e: e.tensor_copy(trib[:], c.tri_f[:]), inc=1)
    ldq = stage_weight_bf16(c, c.w["L3_attn_Wq"], 0, 0, 8, 8, A)
    ldk = stage_weight_bf16(c, c.w["L3_attn_Wk"], 0, 0, 8, 8, B)
    P.wait("tensor", "gpsimd", ldk)
    for b in range(6):
        bank_acquire(c, b)
    P.do("vector", lambda e: e.memset(onesc3[:], 1.0), inc=1)
    P.do("vector", lambda e: e.tensor_copy(
        vpair[:].rearrange("p (s w) -> p s w", w=65)[:, :, 64:65],
        c.ones_b[:, 0:1].rearrange(
            "p (s w) -> p s w", w=1).broadcast_to((128, 32, 1))), inc=1)
    v_on3 = P.e["vector"].n
    P.wait("tensor", "vector", v_on3)
    trib_bc = trib[:].rearrange("p (u n) -> p u n", u=1).broadcast_to((128, 2, 128))
    pe_prev_hp = 0
    dnv_bc_pe = 0
    exp_done = [0, 0]
    ctx_done = [0, 0]
    msk_done = [0, 0]
    ctx_tail_v = 0
    ctx_tail_s = 0
    for hp in range(8):
        h0 = 2 * hp
        P.wait("scalar", "tensor", pe_prev_hp)
        for reg, buf in ((A, qh), (B, kh)):
            for qc in range(NQ):
                b = 6 + qc % 2
                bank_acquire(c, b)
                for dc in range(NDC):
                    P.do("tensor", lambda e, b=b, dc=dc, reg=reg, hp=hp, qc=qc:
                         e.matmul(c.ps[b][:, :], wtile(c, reg, 8, dc, hp),
                                  xln_ap(c, dc, qc * TQ, TQ),
                                  start=(dc == 0), stop=(dc == NDC - 1)),
                         inc=(1 if dc == NDC - 1 else 0))
                bank_done(c, b)
                bank_consume(c, b, "scalar")
                P.do("scalar", lambda e, b=b, buf=buf, qc=qc: e.activation(
                    buf[:, qc * 512:qc * 512 + 512], c.ps[b][:, :],
                    AF.Copy), inc=1)
                bank_release(c, b, "scalar")
        P.wait("gpsimd", "tensor", pe_prev_hp)
        ldv = stage_weight_bf16(c, c.w["L3_attn_Wv"], 0, hp * 128, 8, 1,
                                0, mtile=128, dst=vwst)
        P.wait("tensor", "gpsimd", ldv)
        for tc in range(16):
            b = 6 + tc % 2
            bank_acquire(c, b)
            for dc in range(NDC):
                P.do("tensor", lambda e, b=b, dc=dc, tc=tc: e.matmul(
                    c.ps[b][:, 0:128],
                    c.xln[:, dc * T + tc * 128:dc * T + tc * 128 + 128],
                    vwst[:, dc * 128:dc * 128 + 128],
                    start=(dc == 0), stop=(dc == NDC - 1)),
                    inc=(1 if dc == NDC - 1 else 0))
            bank_done(c, b)
            bank_consume(c, b, "scalar")
            P.do("scalar", lambda e, b=b, tc=tc: e.activation(
                vpair[:, tc * 130:tc * 130 + 130].rearrange(
                    "p (h w) -> p h w", w=65)[:, :, 0:64],
                c.ps[b][:, 0:128].rearrange("p (h w) -> p h w", w=64),
                AF.Copy), inc=1)
            bank_release(c, b, "scalar")
        qkv_wr_s = P.e["scalar"].n
        P.wait("tensor", "scalar", qkv_wr_s)
        for qc in range(NQ):
            ntc = 4 * qc + 4
            for tc in range(ntc + 1):
                par = tc % 2
                if tc < ntc:
                    k = tc - 4 * qc
                    off = 128 * k if k > 0 else 0
                    w = 512 - off
                    P.wait("tensor", "scalar", exp_done[par])
                    for hsel in range(2):
                        sb = (0 if par == 0 else 4) + hsel
                        po = hsel * 64
                        P.do("tensor", lambda e, sb=sb, po=po, tc=tc, qc=qc,
                             off=off, w=w: e.matmul(
                                 c.ps[sb][:, 0:w],
                                 kh[po:po + 64, tc * 128:tc * 128 + 128],
                                 qh[po:po + 64,
                                    qc * 512 + off:qc * 512 + 512],
                                 start=True, stop=True), inc=1)
                    sc_pe = P.e["tensor"].n
                    P.wait("scalar", "tensor", sc_pe)
                    P.wait("scalar", "tensor", ctx_done[par])
                    for hsel in range(2):
                        sb = (0 if par == 0 else 4) + hsel
                        slot = (par * 2 + hsel) * 512
                        P.do("scalar", lambda e, sb=sb, slot=slot, off=off,
                             w=w: e.activation(
                                 expt[:, slot + off:slot + 512],
                                 c.ps[sb][:, 0:w], AF.Exp, scale=0.125),
                             inc=1)
                    exp_done[par] = P.e["scalar"].n
                    if k >= 0:
                        mb_ = par * 1024 + off
                        vslot = expt[:, mb_:mb_ + 640].rearrange(
                            "p (u n) -> p u n", n=128)[:, 0:5:4, :]
                        P.wait("vector", "scalar", exp_done[par])
                        P.do("vector", lambda e, vslot=vslot: e.tensor_tensor(
                            out=vslot, in0=vslot, in1=trib_bc,
                            op=ALU.mult), inc=1)
                        msk_done[par] = P.e["vector"].n
                if tc > 0:
                    tp_ = tc - 1
                    parp = tp_ % 2
                    kp = tp_ - 4 * qc
                    offp = 128 * kp if kp > 0 else 0
                    P.wait("tensor", "scalar", exp_done[parp])
                    if kp >= 0:
                        P.wait("tensor", "vector", msk_done[parp])
                    if tp_ == 0:
                        P.wait("tensor", "vector", ctx_tail_v)
                        P.wait("tensor", "scalar", ctx_tail_s)
                    for hsel in range(2):
                        slot = (parp * 2 + hsel) * 512
                        cb = 2 + hsel
                        vcol = tp_ * 130 + hsel * 65
                        P.do("tensor", lambda e, cb=cb, slot=slot, tp_=tp_,
                             vcol=vcol, offp=offp, ntc=ntc: e.matmul(
                                 c.ps[cb][0:65, offp:512],
                                 vpair[:, vcol:vcol + 65],
                                 expt[:, slot + offp:slot + 512],
                                 start=(tp_ == 0), stop=(tp_ == ntc - 1),
                                 skip_group_check=True), inc=1)
                    ctx_done[parp] = P.e["tensor"].n
            for hsel in range(2):
                cb = 2 + hsel
                po = hsel * 64
                P.wait("scalar", "tensor", ctx_done[1])
                P.do("scalar", lambda e, cb=cb: e.activation(
                    dnvf[:], c.ps[cb][64:65, :], AF.Ln), inc=1)
                P.do("scalar", lambda e: e.activation(
                    dnvb[:], dnvf[:], AF.Exp, scale=-1.0), inc=1)
                s_d = P.e["scalar"].n
                bank_acquire(c, 6)
                P.wait("tensor", "scalar", s_d)
                P.do("tensor", lambda e: e.matmul(
                    c.ps[6][:, :], c.ones_b[0:1, :], dnvb[:],
                    start=True, stop=True), inc=1)
                bank_done(c, 6)
                bank_consume(c, 6, "scalar")
                P.wait("scalar", "vector", dnv_bc_pe)
                P.do("scalar", lambda e: e.activation(
                    dbcs[:], c.ps[6][:, :], AF.Copy), inc=1)
                bank_release(c, 6, "scalar")
                P.wait("vector", "scalar")
                P.do("vector", lambda e, cb=cb, po=po, hp=hp, qc=qc:
                     e.tensor_tensor(
                         out=c.scr[po:po + 64,
                                   hp * 2048 + qc * 512:hp * 2048 + qc * 512 + 512],
                         in0=c.ps[cb][0:64, :], in1=dbcs[0:64, :],
                         op=ALU.mult), inc=1)
                dnv_bc_pe = P.e["vector"].n
            ctx_tail_v = P.e["vector"].n
            ctx_tail_s = P.e["scalar"].n
        pe_prev_hp = P.e["tensor"].n
    pe_n = P.e["tensor"].n
    P.wait("gpsimd", "tensor", pe_n)
    ldo = stage_weight_bf16(c, c.w["L3_attn_Wo"], 0, 0, 8, 8, A)
    P.wait("tensor", "gpsimd", ldo)
    P.wait("tensor", "vector")
    P.wait("tensor", "sync", nbo)
    for qc in range(NQ):
        for dct in range(8):
            b = dct % 2
            bank_acquire(c, b)
            for cc in range(8):
                P.do("tensor", lambda e, b=b, cc=cc, dct=dct, qc=qc: e.matmul(
                    c.ps[b][:, :], wtile(c, A, 8, cc, dct),
                    c.scr[:, cc * 2048 + qc * 512:cc * 2048 + qc * 512 + 512],
                    start=(cc == 0), stop=(cc == 7)),
                    inc=(1 if cc == 7 else 0))
            bank_done(c, b)
            bank_consume(c, b, "vector")
            P.do("vector", lambda e, b=b, dct=dct, qc=qc: e.scalar_tensor_tensor(
                out=xr_ap(c, dct, qc * TQ, TQ), in0=c.ps[b][:, :],
                scalar=bo_sb[:, dct:dct + 1], in1=xr_ap(c, dct, qc * TQ, TQ),
                op0=ALU.add, op1=ALU.add), inc=1)
            bank_release(c, b, "vector")
    P.barrier()
    bes.close()



def b_l1_ffn(c):
    P = c.P
    emit_ln(c)
    bes = contextlib.ExitStack()
    H = 4096
    b1_sb = bes.enter_context(c.nc.sbuf_tensor("l1f_b1", [128, 32], F32))
    b2_sb = bes.enter_context(c.nc.sbuf_tensor("l1f_b2", [128, NDC], F32))
    P.dma("sync", b1_sb[:], c.w["L1_ffn_b1"])
    P.dma("sync", b2_sb[:], c.w["L1_ffn_b2"])
    P.wait("scalar", "sync")
    P.wait("vector", "sync")

    region = [0, 8192]
    region_pe = [0, 0]
    stages_q = ([("L1_ffn_w1", 0, hs * 1024) for hs in range(4)] +
                [("L1_ffn_w2", hs * 1024, 0) for hs in range(4)])

    def l1f_stage(tag):
        w_name, k0, m0 = stages_q[tag]
        r = region[tag % 2]
        P.wait("gpsimd", "tensor", region_pe[tag % 2])
        ld = stage_weight_bf16(c, c.w[w_name], k0, m0, 8, 8, r)
        return r, ld

    for q in range(NQ):
        t0 = q * TQ
        tag = 0
        nxt = l1f_stage(0)
        for hs in range(4):
            r, ld = nxt
            P.wait("tensor", "gpsimd", ld)
            if tag + 1 < 8:
                nxt = l1f_stage(tag + 1)
            for hl in range(8):
                hc = hs * 8 + hl
                b = hc % 4
                bank_acquire(c, b)
                for dc in range(NDC):
                    P.do("tensor", lambda e, b=b, r=r, dc=dc, hl=hl, t0=t0:
                         e.matmul(c.ps[b][:, :], wtile(c, r, 8, dc, hl),
                                  xln_ap(c, dc, t0, TQ),
                                  start=(dc == 0), stop=(dc == NDC - 1)),
                         inc=(1 if dc == NDC - 1 else 0))
                bank_done(c, b)
                bank_consume(c, b, "scalar")
                P.do("scalar", lambda e, b=b, hc=hc: e.activation(
                    c.scr[:, hc * 512:hc * 512 + 512], c.ps[b][:, :], AF.Gelu,
                    bias=b1_sb[:, hc:hc + 1]), inc=1)
                bank_release(c, b, "scalar")
            region_pe[tag % 2] = P.e["tensor"].n
            tag += 1
        sc_n = P.e["scalar"].n
        for hs in range(4):
            r, ld = nxt
            P.wait("tensor", "gpsimd", ld)
            if tag + 1 < 8:
                nxt = l1f_stage(tag + 1)
            if hs == 0:
                for b in range(8):
                    bank_acquire(c, b)
                P.wait("tensor", "scalar", sc_n)
            for dc in range(NDC):
                for hl in range(8):
                    hc = hs * 8 + hl
                    P.do("tensor", lambda e, dc=dc, r=r, hl=hl, hc=hc, hs=hs:
                         e.matmul(c.ps[dc][:, :], wtile(c, r, 8, hl, dc),
                                  c.scr[:, hc * 512:hc * 512 + 512],
                                  start=(hs == 0 and hl == 0),
                                  stop=(hs == 3 and hl == 7)),
                         inc=(1 if hl == 7 else 0))
            region_pe[tag % 2] = P.e["tensor"].n
            tag += 1
        for dc in range(NDC):
            bank_done(c, dc)
            bank_consume(c, dc, "vector")
            P.do("vector", lambda e, dc=dc, t0=t0: e.scalar_tensor_tensor(
                out=xr_ap(c, dc, t0, TQ), in0=c.ps[dc][:, :],
                scalar=b2_sb[:, dc:dc + 1], in1=xr_ap(c, dc, t0, TQ),
                op0=ALU.add, op1=ALU.add), inc=1)
            bank_release(c, dc, "vector")
    P.barrier()
    bes.close()

def b_identity(c):
    pass


def finish(c):
    P = c.P
    P.barrier()
    n = P.dma("sync", c.x_out.rearrange("(ch p) t -> p ch t", p=128),
              c.xr[:].rearrange("p (ch t) -> p ch t", ch=NDC))
    P.wait("sync", "sync", n)


def _w(name, tr=None, srckey=None):
    return (name, tr, srckey)


def _tT(a):
    return a.T


def _tB(a):
    return a.reshape(-1, 128).T


BLOCKS = [
    ("L0_attn", b_l0_attn, [
        _w("L0_attn_in_proj_w"), _w("L0_attn_out_proj_w"),
        _w("L0_attn_W_re"), _w("L0_attn_W_im"),
        _w("L0THETA", lambda a: np.ascontiguousarray(
            a.reshape(2048).reshape(16, 128).T), srckey="L0_attn_theta"),
        _w("L0TAPS", lambda a: _l0taps(a), srckey="L0_attn_conv_k"),
        _w("L0CONVB", lambda a: np.ascontiguousarray(
            np.pad(a, (0, 96)).reshape(33, 128).T), srckey="L0_attn_conv_b"),
        _w("L0SSC", lambda a: a.reshape(32, 1), srckey="L0_attn_score_scale"),
        _w("L0NSC", lambda a: a.reshape(128, 1), srckey="L0_attn_norm_scale"),
        _w("L0DERIV", lambda a: a.reshape(1, 3), srckey="L0_attn_deriv_logits"),
        _w("L0SLOPES", None, srckey="__L0SLOPES__"),
        _w("L0DIST", None), _w("L0SEL", None), _w("L0HSEL", None),
        _w("L0HSEL2", None), _w("IDENT128", None)]),
    ("L0_ffn", b_l0_ffn, [_w("L0_ffn_w_gate", _tT), _w("L0_ffn_w_up", _tT),
                          _w("L0_ffn_w_down", _tT)]),
    ("L1_attn", b_l1_attn, [
        _w("L1_attn_qkv_w", _tT), _w("L1_attn_qkv_b", _tB),
        _w("L1A_VBROW", lambda a: a[2048:3072].reshape(1, 1024),
           srckey="L1_attn_qkv_b"),
        _w("L1_attn_r_w1", _tT), _w("L1_attn_r_b1", _tB),
        _w("L1_attn_r_w2", _tT), _w("L1_attn_r_b2", lambda a: a.reshape(16, 1)),
        _w("IDENT128", None)]),
    ("L1_ffn", b_l1_ffn, [_w("L1_ffn_w1"), _w("L1_ffn_b1", _tB),
                          _w("L1_ffn_w2"), _w("L1_ffn_b2", _tB)]),
    ("L2_attn", b_l2_attn, [_w("L2_attn_w_q"), _w("L2_attn_w_k"),
                            _w("L2_attn_w_v"), _w("L2_attn_w_o")]),
    ("L2_ffn", b_l2_ffn, [_w("L2_ffn_Wr"), _w("L2_ffn_br", lambda a: a.reshape(1, 8)),
                          _w("L2_ffn_W1"), _w("L2_ffn_b1", lambda a: np.ascontiguousarray(a.reshape(8, 8, 128).transpose(0, 2, 1))),
                          _w("L2_ffn_W2"),
                          _w("L2B2T", lambda a: np.ascontiguousarray(
                              a.reshape(8, 8, 128).transpose(0, 2, 1)),
                             srckey="L2_ffn_b2"),
                          _w("IDENT128", None), _w("TRI128", None)]),
    ("L3_attn", b_l3_attn, [_w("L3_attn_Wq", _tT), _w("L3_attn_Wk", _tT),
                          _w("L3_attn_Wv", _tT), _w("L3_attn_Wo", _tT),
                          _w("L3_attn_bo", _tB), _w("IDENT128", None),
                          _w("TRI128", None)]),
    ("L3_ffn", b_l3_ffn, [_w("L3_ffn_gate_w", _tT),
                          _w("L3_ffn_expert_w", lambda a: a.transpose(0, 2, 1)),
                          _w("IDENT128", None)]),
]

GROUPS = [[0, 1, 2, 3, 4, 5, 6, 7]]


def _l0taps(a):
    ck = np.pad(a[:, 0, :], ((0, 0), (0, 96)))
    out = np.zeros((128, 132), np.float32)
    for ct in range(33):
        for k in range(4):
            out[:, ct * 4 + k] = ck[k, ct * 128:ct * 128 + 128]
    return out


def _l0dist():
    d = np.zeros((32, 2048), np.float32)
    t = np.arange(2048, dtype=np.float32)
    d[:28] = 2047.0 - t
    d[28:] = t
    return d


def _l0sel():
    s = np.zeros((32, 2048), np.float32)
    for ch in range(2048):
        s[ch // 64, ch] = 1.0
    return s


_CONSTS = {"IDENT128": lambda: np.eye(128, dtype=np.float32),
           "TRI128": lambda: np.ascontiguousarray(
               np.tril(np.ones((128, 128), np.float32)).T),
           "L0DIST": _l0dist,
           "L0SEL": _l0sel,
           "L0HSEL": lambda: (np.arange(128)[:, None] // 64 ==
                              np.arange(2)[None, :]).astype(np.float32),
           "L0HSEL2": lambda: (np.arange(128)[None, :] // 64 ==
                               np.arange(2)[:, None]).astype(np.float32)}


def _prep_host(inputs):
    arrs = {}
    for name, builder, wl in BLOCKS:
        for key, tr, srck in wl:
            kk = srck or key
            if kk == "__L0SLOPES__":
                a = np.concatenate([np.asarray(inputs["L0_attn_decay_slopes"]),
                                    np.asarray(inputs["L0_attn_anchor_slopes"])]
                                   ).reshape(32, 1)
            else:
                a = np.asarray(inputs[kk]) if kk in inputs else _CONSTS[key]()
            if tr is not None:
                a = np.ascontiguousarray(tr(a))
            else:
                a = np.ascontiguousarray(a)
            arrs[key] = a
    return arrs


def _weight_decls(block_ids, arrs):
    decl = {}
    for bi in block_ids:
        for key, tr, srck in BLOCKS[bi][2]:
            a = arrs[key]
            dt = I32 if a.dtype == np.int32 else F32
            decl[key] = (a.shape, dt)
    return decl


def make_program(block_ids, arrs):
    nc = bass.Bass("TRN2", target_bir_lowering=False)
    es = contextlib.ExitStack()
    P = Prog(nc, es)
    c = make_ctx(nc, es, P, _weight_decls(block_ids, arrs))
    mark = es.enter_context(nc.sbuf_tensor("blkmark", [1, 8], F32))
    P.do("vector", lambda e: e.memset(mark[:], 0.0), inc=1)
    for bi in block_ids:
        P.do("vector", lambda e: e.pool_avg(mark[0:1, 0:1], mark[0:1, 0:8]),
             inc=1)
        BLOCKS[bi][1](c)
    P.do("vector", lambda e: e.pool_avg(mark[0:1, 0:1], mark[0:1, 0:8]),
         inc=1)
    finish(c)
    P.emit()
    es.close()
    return nc


def kernel(**inputs):
    x = np.asarray(inputs["L0_attn_x"], np.float32)
    arrs = _prep_host(inputs)
    cur = [np.ascontiguousarray(x[b].T) for b in range(8)]
    for g in GROUPS:
        nc = make_program(g, arrs)
        need = {k for bi in g for k, _, _s in BLOCKS[bi][2]}
        in_maps = [dict({k: arrs[k] for k in need}, x_in=cur[b]) for b in range(8)]
        res = run_bass_kernel_spmd(nc, in_maps, core_ids=list(range(8)))
        cur = [np.asarray(res.results[b]["x_out"]) for b in range(8)]
    out = np.stack([c.T for c in cur]).astype(np.float32)
    return out


# version 36 (final)
# speedup vs baseline: 1.0077x; 1.0077x over previous
import contextlib
import numpy as np
import ml_dtypes
import concourse.bass as bass
import concourse.mybir as mybir
from concourse.bass_utils import run_bass_kernel_spmd

F32 = mybir.dt.float32
BF16 = mybir.dt.bfloat16
I32 = mybir.dt.int32
AF = mybir.ActivationFunctionType
ALU = mybir.AluOpType
AX = mybir.AxisListType

T = 2048
D = 1024
NDC = D // 128
TQ = 512
NQ = T // TQ
ENGS = ("sync", "gpsimd", "tensor", "vector", "scalar")
BF = ml_dtypes.bfloat16


class _Eng:
    def __init__(self, sem):
        self.sem = sem
        self.n = 0
        self.q = []
        self.selfwait = 0


class Prog:
    def __init__(self, nc, es):
        self.nc = nc
        self.es = es
        self.e = {name: _Eng(es.enter_context(nc.semaphore(f"sem_{name}")))
                  for name in ENGS}

    def do(self, eng, fn, inc=0):
        E = self.e[eng]
        if eng != "tensor" and E.n > E.selfwait:
            E.q.append(lambda e, s=E.sem, v=E.n: e.wait_ge(s, v))
            E.selfwait = E.n
        if inc:
            E.n += inc

            def thunk(e, fn=fn, sem=E.sem, inc=inc):
                fn(e).then_inc(sem, inc)
            E.q.append(thunk)
        else:
            E.q.append(lambda e, fn=fn: fn(e))
        return E.n

    def dma(self, eng, out, in_, **kw):
        return self.do(eng, lambda e, o=out, i=in_, kw=kw:
                       e.dma_start(out=o, in_=i, **kw), inc=16)

    def wait(self, eng, other, val=None):
        O = self.e[other]
        v = O.n if val is None else val
        if v > 0:
            self.e[eng].q.append(lambda e, s=O.sem, v=v: e.wait_ge(s, v))

    def barrier(self):
        counts = {n: self.e[n].n for n in ENGS}
        for a in ENGS:
            for b in ENGS:
                if a != b and counts[b] > 0:
                    self.e[a].q.append(
                        lambda e, s=self.e[b].sem, v=counts[b]: e.wait_ge(s, v))

    def emit(self):
        with self.nc.Block() as blk:
            for name in ENGS:
                E = self.e[name]
                if E.q:
                    getattr(blk, name)(lambda e, q=E.q: [f(e) for f in q])


class Ctx:
    pass


def make_ctx(nc, es, P, weights):
    c = Ctx()
    c.nc, c.es, c.P = nc, es, P
    c.x_in = nc.dram_tensor("x_in", [D, T], F32, kind="ExternalInput").ap()
    c.x_out = nc.dram_tensor("x_out", [D, T], F32, kind="ExternalOutput").ap()
    c.w = {nm: nc.dram_tensor(nm, list(shape), dt, kind="ExternalInput").ap()
           for nm, (shape, dt) in weights.items()}

    c.xr = es.enter_context(nc.sbuf_tensor("xr", [128, NDC * T], F32))
    c.xln = es.enter_context(nc.sbuf_tensor("xln", [128, NDC * T], BF16))
    c.scr = es.enter_context(nc.sbuf_tensor("scr", [128, 16 * 1024 + 256], BF16))
    c.wst = es.enter_context(nc.sbuf_tensor("wst", [128, 16 * 1024], BF16))
    c.aux = es.enter_context(nc.sbuf_tensor("aux", [128, 512], F32))
    c.strow = es.enter_context(nc.sbuf_tensor("strow", [1, 1536], F32))
    c.ones_b = es.enter_context(nc.sbuf_tensor("ones_b", [128, 128], BF16))
    c.ones_f = es.enter_context(nc.sbuf_tensor("ones_f", [1, 128], F32))
    c.consts = es.enter_context(nc.sbuf_tensor("consts", [128, 12], F32))
    P.do("vector", lambda e: e.memset(c.ones_b[:], 1.0), inc=1)
    P.do("vector", lambda e: e.memset(c.ones_f[:], 1.0), inc=1)
    P.do("vector", lambda e: e.memset(c.consts[:, 0:1], 1e-5), inc=1)
    P.do("vector", lambda e: e.memset(c.consts[:, 1:2], 1e-5 * 2.0 ** -64), inc=1)
    P.do("vector", lambda e: e.memset(c.consts[:, 2:3], -32.0 * float(np.log(2.0))), inc=1)

    c.ps = [es.enter_context(nc.psum_tensor(f"ps{i}", [128, 512], F32))
            for i in range(8)]
    c.ps_n = [0] * 8
    c.ps_f = [None] * 8

    if "IDENT128" in c.w:
        c.ident_b = es.enter_context(nc.sbuf_tensor("ident_b", [128, 128], BF16))
        P.dma("gpsimd", c.ident_b[:], c.w["IDENT128"])
        c.ident_f = es.enter_context(nc.sbuf_tensor("ident_f", [128, 128], F32))
        P.dma("sync", c.ident_f[:], c.w["IDENT128"])
    if "TRI128" in c.w:
        c.tri_f = es.enter_context(nc.sbuf_tensor("tri_f", [128, 128], F32))
        P.dma("sync", c.tri_f[:], c.w["TRI128"])
    c.ones_cf = es.enter_context(nc.sbuf_tensor("ones_cf", [128, 1], F32))
    P.do("vector", lambda e: e.memset(c.ones_cf[:], 1.0), inc=1)
    P.do("vector", lambda e: e.memset(c.consts[:, 3:4], 0.0), inc=1)
    P.dma("sync", c.xr[:].rearrange("p (ch t) -> p ch t", ch=NDC),
          c.x_in.rearrange("(ch p) t -> p ch t", p=128))
    P.wait("tensor", "sync")
    P.wait("vector", "sync")
    P.wait("scalar", "sync")
    return c


def bank_acquire(c, b):
    if c.ps_f[b] is not None:
        eng, cnt = c.ps_f[b]
        c.P.wait("tensor", eng, cnt)


def bank_done(c, b):
    c.ps_n[b] = c.P.e["tensor"].n


def bank_consume(c, b, eng):
    c.P.wait(eng, "tensor", c.ps_n[b])


def bank_release(c, b, eng):
    c.ps_f[b] = (eng, c.P.e[eng].n)


def xr_ap(c, dc, t0=0, tn=T):
    return c.xr[:, dc * T + t0: dc * T + t0 + tn]


def xln_ap(c, dc, t0=0, tn=T):
    return c.xln[:, dc * T + t0: dc * T + t0 + tn]


def emit_ln(c):
    P = c.P
    slot_pe = [0, 0]

    def stage(q):
        t0 = q * TQ
        off = (q % 2) * 1024
        P.wait("vector", "tensor", slot_pe[q % 2])
        P.wait("scalar", "tensor", slot_pe[q % 2])
        for dc in range(NDC):
            P.do("vector", lambda e, dc=dc, t0=t0, off=off: e.tensor_copy(
                c.scr[:, dc * 2048 + off:dc * 2048 + off + 512],
                xr_ap(c, dc, t0, TQ)), inc=1)
            P.do("scalar", lambda e, dc=dc, t0=t0, off=off: e.activation(
                c.scr[:, dc * 2048 + off + 512:dc * 2048 + off + 1024],
                xr_ap(c, dc, t0, TQ), AF.Square), inc=1)
        return P.e["vector"].n, P.e["scalar"].n

    staged = stage(0)
    for q in range(NQ):
        t0 = q * TQ
        soff = (q % 2) * 1024
        r_mean = c.strow[0:1, 0:TQ]
        r_vb = c.strow[0:1, TQ:2 * TQ]
        r_a = c.strow[0:1, 2 * TQ:3 * TQ]
        v_n, s_n = staged
        for b, off in ((0, soff), (1, soff + 512)):
            bank_acquire(c, b)
            P.wait("tensor", "vector", v_n)
            P.wait("tensor", "scalar", s_n)
            for dc in range(NDC):
                P.do("tensor", lambda e, b=b, dc=dc, off=off: e.matmul(
                    c.ps[b][0:1, :], c.ones_b[:, 0:1],
                    c.scr[:, dc * 2048 + off:dc * 2048 + off + 512],
                    start=(dc == 0), stop=(dc == NDC - 1)),
                    inc=(1 if dc == NDC - 1 else 0))
            bank_done(c, b)
        slot_pe[q % 2] = P.e["tensor"].n
        if q + 1 < NQ:
            staged = stage(q + 1)
        bank_consume(c, 0, "vector")
        bank_consume(c, 1, "vector")
        P.do("vector", lambda e, r=r_mean: e.tensor_scalar(
            out=r, in0=c.ps[0][0:1, :], scalar1=1.0 / D,
            scalar2=None, op0=ALU.mult), inc=1)
        P.do("vector", lambda e, r=r_vb: e.tensor_scalar(
            out=r, in0=c.ps[1][0:1, :], scalar1=1.0 / D,
            scalar2=None, op0=ALU.mult), inc=1)
        bank_release(c, 0, "vector")
        bank_release(c, 1, "vector")
        P.do("vector", lambda e, r_a=r_a, r_mean=r_mean: e.tensor_tensor(
            out=r_a, in0=r_mean, in1=r_mean, op=ALU.mult), inc=1)
        P.do("vector", lambda e, r_vb=r_vb, r_a=r_a: e.tensor_tensor(
            out=r_vb, in0=r_vb, in1=r_a, op=ALU.subtract), inc=1)
        P.wait("scalar", "vector")
        P.do("scalar", lambda e, r_a=r_a, r_vb=r_vb: e.activation(
            r_a, r_vb, AF.Ln, bias=c.consts[0:1, 1:2], scale=2.0 ** -64), inc=1)
        P.do("scalar", lambda e, r_a=r_a: e.activation(
            r_a, r_a, AF.Exp, scale=-0.5, bias=c.consts[0:1, 2:3]), inc=1)
        P.wait("vector", "scalar")
        P.do("vector", lambda e, r_vb=r_vb, r_mean=r_mean, r_a=r_a:
             e.scalar_tensor_tensor(out=r_vb, in0=r_mean, scalar=-1.0,
                                    in1=r_a, op0=ALU.mult, op1=ALU.mult), inc=1)
        v_n2 = P.e["vector"].n
        for b, cb in ((2, 2 * TQ), (3, TQ)):
            bank_acquire(c, b)
            P.wait("tensor", "vector", v_n2)
            P.do("tensor", lambda e, b=b, cb=cb: e.matmul(
                c.ps[b][:, :], c.ones_f[:, :], c.strow[0:1, cb:cb + TQ],
                start=True, stop=True), inc=1)
            bank_done(c, b)
        bank_consume(c, 2, "vector")
        bank_consume(c, 3, "vector")
        for dc in range(NDC):
            P.do("vector", lambda e, dc=dc, t0=t0: e.tensor_tensor(
                out=c.aux[:, 0:TQ], in0=xr_ap(c, dc, t0, TQ), in1=c.ps[2][:, :],
                op=ALU.mult), inc=1)
            P.do("vector", lambda e, dc=dc, t0=t0: e.tensor_tensor(
                out=xln_ap(c, dc, t0, TQ), in0=c.aux[:, 0:TQ], in1=c.ps[3][:, :],
                op=ALU.add), inc=1)
        bank_release(c, 2, "vector")
        bank_release(c, 3, "vector")
    P.barrier()


def stage_weight_bf16(c, dram_ap, k0, m0, kc, mc, dst_col, mtile=128, dst=None,
                      q="gpsimd"):
    src = dram_ap[k0:k0 + kc * 128, m0:m0 + mc * mtile]
    src = src.rearrange("(ki p) mm -> p ki mm", p=128)
    base = c.wst if dst is None else dst
    d = base[:, dst_col:dst_col + kc * mc * mtile]
    d = d.rearrange("p (ki mm) -> p ki mm", ki=kc)
    return c.P.dma(q, d, src)


def wtile(c, dst_col, mc, ki, mi, mtile=128):
    col = dst_col + (ki * mc + mi) * mtile
    return c.wst[:, col:col + mtile]



def b_l0_attn(c):
    P = c.P
    emit_ln(c)
    bes = contextlib.ExitStack()
    ygq = None
    nc = c.nc
    A, B = 0, 8192
    TWO_PI = 6.283185307179586
    MAGIC = 12582912.0
    C1 = 6.28125
    C2 = 1.9353072e-03
    C2B = float(np.float32(np.float64(TWO_PI) - 6.28125))

    svst = None
    theta = bes.enter_context(nc.sbuf_tensor("l0a_th", [128, 16], F32))
    taps = bes.enter_context(nc.sbuf_tensor("l0a_tap", [128, 132], F32))
    convb = bes.enter_context(nc.sbuf_tensor("l0a_cb", [128, 33], F32))
    ssc = bes.enter_context(nc.sbuf_tensor("l0a_ssc", [32, 1], F32))
    slp = bes.enter_context(nc.sbuf_tensor("l0a_slp", [32, 1], F32))
    selk = bes.enter_context(nc.sbuf_tensor("l0a_selk", [32, 2048], BF16))
    hsel = bes.enter_context(nc.sbuf_tensor("l0a_hsel", [128, 2], BF16))
    hsel2 = bes.enter_context(nc.sbuf_tensor("l0a_hsel2", [2, 128], BF16))
    bdre = bes.enter_context(nc.sbuf_tensor("l0a_bdre", [128, 128], BF16))
    bdim = bes.enter_context(nc.sbuf_tensor("l0a_bdim", [128, 128], BF16))
    wres = bes.enter_context(nc.sbuf_tensor("l0a_wres", [64, 64], F32))
    nsc = bes.enter_context(nc.sbuf_tensor("l0a_nsc", [128, 1], F32))
    wpoly = bes.enter_context(nc.sbuf_tensor("l0a_wp", [128, 4], F32))
    wrow = bes.enter_context(nc.sbuf_tensor("l0a_wrow", [1, 3], F32))
    wsum = bes.enter_context(nc.sbuf_tensor("l0a_wsum", [1, 1], F32))
    ztail = bes.enter_context(nc.sbuf_tensor("l0a_zt", [128, 51], F32))
    _f32v = c.scr[:].bitcast(F32)
    zbuf2 = [bes.enter_context(nc.sbuf_tensor(f"l0a_zb{i}", [128, 515], F32))
             for i in range(2)]
    pwb = zbuf2[0][0:32, 0:512]
    svst = zbuf2[1][:].bitcast(BF16)[0:128, 0:256]
    xvb2 = [_f32v[:, 4096:4608], _f32v[:, 4608:5120]]
    kb2 = [_f32v[:, 5120:5632], _f32v[:, 5632:6144]]
    prd2 = [_f32v[:, 6144:6656], _f32v[:, 6656:7168]]
    phib2 = [_f32v[:, 7168:7680], _f32v[:, 7680:8192]]
    sinb2 = [bes.enter_context(nc.sbuf_tensor(f"l0a_sin{i}", [128, 512], BF16))
             for i in range(2)]
    cosb2 = [bes.enter_context(nc.sbuf_tensor(f"l0a_cos{i}", [128, 512], BF16))
             for i in range(2)]
    polyb2 = [bes.enter_context(nc.sbuf_tensor(f"l0a_poly{i}", [128, 512], BF16))
              for i in range(2)]
    rem2 = [bes.enter_context(nc.sbuf_tensor(f"l0a_rem{i}", [128, 512], BF16))
            for i in range(2)]
    imm2 = [bes.enter_context(nc.sbuf_tensor(f"l0a_imm{i}", [128, 512], BF16))
            for i in range(2)]
    npr1 = bes.enter_context(nc.sbuf_tensor("l0a_npr", [128, 1024], F32))
    npr2 = [npr1, npr1]
    ren2 = [bes.enter_context(nc.sbuf_tensor(f"l0a_ren{i}", [128, 512], BF16))
            for i in range(2)]
    imn2 = [bes.enter_context(nc.sbuf_tensor(f"l0a_imn{i}", [128, 512], BF16))
            for i in range(2)]
    rsbb = None
    carp = bes.enter_context(nc.sbuf_tensor("l0a_crp", [128, 32], F32))
    sqb2 = [kb2[1], kb2[0]]
    carden = bes.enter_context(nc.sbuf_tensor("l0a_cden", [32, 1], F32))
    srawb = bes.enter_context(nc.sbuf_tensor("l0a_sraw", [32, 515], F32))
    rsbb = srawb[:].bitcast(BF16)[0:2, 0:512]
    pwbf = bes.enter_context(nc.sbuf_tensor("l0a_pwbf", [32, 512], BF16))
    invb = bes.enter_context(nc.sbuf_tensor("l0a_inv", [32, 512], BF16))
    tmps = c.aux[0:32, 0:512]
    gzs2 = [bes.enter_context(nc.sbuf_tensor(f"l0a_gzs{i}", [128, 515], BF16))
            for i in range(2)]
    diagb2 = [bes.enter_context(nc.sbuf_tensor(f"l0a_diag{i}", [128, 512], BF16))
              for i in range(2)]
    gcv2 = [c.scr[:, 12288:12800], c.scr[:, 13312:13824]]
    gzt = bes.enter_context(nc.sbuf_tensor("l0a_gzt", [128, 48], BF16))

    ygq = c.scr
    n1 = P.dma("sync", theta[:], c.w["L0THETA"])
    n2 = P.dma("sync", taps[:], c.w["L0TAPS"])
    n3 = P.dma("sync", convb[:], c.w["L0CONVB"])
    n4 = P.dma("sync", ssc[:], c.w["L0SSC"])
    n5 = 0
    n6 = P.dma("gpsimd", selk[:], c.w["L0SEL"])
    n7 = P.dma("gpsimd", hsel[:], c.w["L0HSEL"])
    n8 = P.dma("gpsimd", hsel2[:], c.w["L0HSEL2"])
    n9 = P.dma("sync", wres[:], c.w["L0_attn_W_re"])
    n10 = P.dma("sync", nsc[:], c.w["L0NSC"])
    n11 = P.dma("sync", wrow[:], c.w["L0DERIV"])
    n12 = P.dma("sync", c.aux[0:32, 0:1], c.w["L0SLOPES"])
    P.wait("vector", "sync", n12)
    P.wait("vector", "gpsimd", n8)
    P.do("vector", lambda e: e.memset(c.consts[:, 4:5], 1.0), inc=1)
    P.do("vector", lambda e: e.memset(c.consts[:, 5:6], float(np.pi / 2)), inc=1)
    P.do("vector", lambda e: e.memset(c.consts[:, 6:7], MAGIC), inc=1)
    P.do("vector", lambda e: e.memset(c.consts[:, 7:8], -MAGIC), inc=1)
    P.do("vector", lambda e: e.memset(c.consts[:, 8:9], 0.25), inc=1)
    P.do("vector", lambda e: e.memset(ztail[:], 0.0), inc=1)
    P.do("vector", lambda e: e.memset(gzt[:], 0.0), inc=1)
    P.do("vector", lambda e: e.memset(carp[:], 0.0), inc=1)
    P.do("vector", lambda e: e.memset(carden[:], 0.0), inc=1)
    P.wait("scalar", "vector")
    P.do("scalar", lambda e: e.activation(slp[:], c.aux[0:32, 0:1], AF.Abs), inc=1)
    P.do("scalar", lambda e: e.activation(slp[:], slp[:], AF.Exp, scale=-1.0), inc=1)
    P.do("scalar", lambda e: e.activation(slp[:], slp[:], AF.Ln,
                                          bias=c.consts[0:32, 4:5]), inc=1)
    P.wait("vector", "scalar")
    P.do("vector", lambda e: e.tensor_scalar(
        out=c.aux[0:32, 1:2], in0=c.aux[0:32, 0:1], scalar1=0.0, scalar2=None,
        op0=ALU.max), inc=1)
    P.do("vector", lambda e: e.tensor_tensor(
        out=slp[:], in0=slp[:], in1=c.aux[0:32, 1:2], op=ALU.add), inc=1)
    P.do("vector", lambda e: e.tensor_scalar(
        out=slp[:], in0=slp[:], scalar1=-1.0, scalar2=None, op0=ALU.mult), inc=1)
    P.wait("scalar", "sync", n11)
    P.do("scalar", lambda e: e.activation(wrow[:], wrow[:], AF.Exp,
                                          accum_out=wsum[:]), inc=1)
    P.wait("vector", "scalar")
    P.do("vector", lambda e: e.reciprocal(wsum[:], wsum[:]), inc=1)
    P.do("vector", lambda e: e.tensor_tensor(
        out=wrow[:], in0=wrow[:], in1=wsum[:].broadcast_to((1, 3)),
        op=ALU.mult), inc=1)
    v_n = P.e["vector"].n
    bank_acquire(c, 7)
    P.wait("tensor", "vector", v_n)
    P.do("tensor", lambda e: e.matmul(c.ps[7][:, 0:3], c.ones_f[:, :], wrow[:],
                                      start=True, stop=True), inc=1)
    bank_done(c, 7)
    bank_consume(c, 7, "vector")
    P.do("vector", lambda e: e.tensor_copy(wpoly[:, 0:3], c.ps[7][:, 0:3]), inc=1)
    P.do("vector", lambda e: e.tensor_scalar(
        out=wpoly[:, 3:4], in0=wpoly[:, 2:3], scalar1=-1.0, scalar2=None,
        op0=ALU.mult), inc=1)
    bank_release(c, 7, "vector")
    P.wait("vector", "sync", n10)
    P.do("vector", lambda e: e.memset(bdre[:], 0.0), inc=1)
    P.do("vector", lambda e: e.memset(bdim[:], 0.0), inc=1)
    for nm, bd, ncol in (("L0_attn_W_re", bdre, 0), ("L0_attn_W_im", bdim, 64)):
        if ncol:
            P.wait("sync", "vector")
            nn = P.dma("sync", wres[:], c.w[nm])
            P.wait("vector", "sync", nn)
        else:
            P.wait("vector", "sync", n9)
        P.do("vector", lambda e, bd=bd, ncol=ncol: e.tensor_scalar(
            out=bd[0:64, 0:64], in0=wres[:], scalar1=nsc[ncol:ncol + 64, 0:1],
            scalar2=None, op0=ALU.mult), inc=1)
        P.do("vector", lambda e, bd=bd: e.tensor_copy(
            bd[64:128, 64:128], bd[0:64, 0:64]), inc=1)

    inproj_pe = [0, 0, 0, 0]
    sraw_pe = 0
    dist_rd_v = 0
    msq_pe2 = [0, 0]
    y_pe2 = [0, 0]
    zb_rd_v = [0, 0]
    gz_rd_pe = [0]
    gz_wr_s = [0, 0]
    s1dve = [0, 0]
    aff_s = [0, 0]
    diag_rd_pe2 = [0, 0]
    diag_v2 = [0, 0]
    sq_gp = [0]

    def emit_s1_head(ct, t0):
        p = ct % 2
        bx = ct % 2
        zbuf = zbuf2[p]
        xvb = xvb2[p]
        gct = 16 + ct
        P.wait("vector", "tensor", diag_rd_pe2[p])
        P.do("vector", lambda e, gct=gct, p=p: e.tensor_tensor(
            out=diagb2[p][:].rearrange("p (k n) -> p k n", k=4),
            in0=c.ident_b[:].rearrange("p (u n) -> p u n", u=1).broadcast_to(
                (128, 4, 128)),
            in1=taps[:, gct * 4:gct * 4 + 4].rearrange(
                "p (k u) -> p k u", u=1).broadcast_to((128, 4, 128)),
            op=ALU.mult), inc=1)
        diag_v2[p] = P.e["vector"].n
        bank_consume(c, bx, "scalar")
        P.wait("scalar", "vector", zb_rd_v[p])
        P.do("scalar", lambda e, bx=bx, zbuf=zbuf: e.activation(
            zbuf[:, 3:515], c.ps[bx][:, :], AF.Copy), inc=1)
        bank_release(c, bx, "scalar")
        P.do("scalar", lambda e, ct=ct, zbuf=zbuf: e.activation(
            zbuf[:, 0:3], ztail[:, ct * 3:ct * 3 + 3], AF.Copy), inc=1)
        P.do("scalar", lambda e, ct=ct, zbuf=zbuf: e.activation(
            ztail[:, ct * 3:ct * 3 + 3], zbuf[:, 512:515], AF.Copy), inc=1)
        s1_act = P.e["scalar"].n
        P.wait("vector", "scalar", s1_act)
        P.wait("vector", "scalar", aff_s[p])
        P.do("vector", lambda e, ct=ct, zbuf=zbuf, xvb=xvb: e.tensor_scalar(
            out=xvb, in0=zbuf[:, 0:512], scalar1=taps[:, ct * 4:ct * 4 + 1],
            scalar2=convb[:, ct:ct + 1], op0=ALU.mult, op1=ALU.add), inc=1)
        for kk in range(1, 4):
            P.do("vector", lambda e, kk=kk, ct=ct, zbuf=zbuf, xvb=xvb:
                 e.scalar_tensor_tensor(
                     out=xvb, in0=zbuf[:, kk:kk + 512],
                     scalar=taps[:, ct * 4 + kk:ct * 4 + kk + 1], in1=xvb,
                     op0=ALU.mult, op1=ALU.add), inc=1)
        zb_rd_v[p] = P.e["vector"].n
        return zb_rd_v[p]

    def emit_s1_trig(ct, conv_n):
        p = ct % 2
        xvb = xvb2[p]
        kb = kb2[p]
        sqb = sqb2[p]
        P.wait("scalar", "vector", s1dve[p])
        P.wait("scalar", "vector", conv_n)
        P.do("scalar", lambda e, ct=ct, p=p, xvb=xvb: e.activation(
            phib2[p][:], xvb, AF.Copy, scale=theta[:, ct:ct + 1]), inc=1)
        P.do("scalar", lambda e, p=p, kb=kb: e.activation(
            kb, phib2[p][:], AF.Identity, scale=1.0 / TWO_PI,
            bias=c.consts[:, 6:7]), inc=1)
        P.do("scalar", lambda e, kb=kb: e.activation(
            kb, kb, AF.Identity, bias=c.consts[:, 7:8]), inc=1)
        act_k1 = P.e["scalar"].n
        P.wait("vector", "scalar", act_k1)
        for i, cc_ in enumerate((C1, C2B)):
            P.do("vector", lambda e, cc_=cc_, i=i, p=p, kb=kb:
                 e.scalar_tensor_tensor(
                     out=prd2[p][:], in0=kb, scalar=-cc_,
                     in1=(phib2[p][:] if i == 0 else prd2[p][:]),
                     op0=ALU.mult, op1=ALU.add), inc=1)
        P.do("vector", lambda e, xvb=xvb, sqb=sqb: e.tensor_tensor(
            out=sqb, in0=xvb, in1=xvb, op=ALU.mult), inc=1)
        casc1_n = P.e["vector"].n
        P.wait("scalar", "vector", casc1_n)
        P.do("scalar", lambda e, p=p: e.activation(
            sinb2[p][:], prd2[p][:], AF.Sin), inc=1)
        P.do("scalar", lambda e, p=p: e.activation(
            cosb2[p][:], prd2[p][:], AF.Sin, scale=0.5), inc=1)
        P.do("scalar", lambda e, p=p, kb=kb: e.activation(
            kb, cosb2[p][:], AF.Square), inc=1)
        P.do("scalar", lambda e, xvb=xvb: e.activation(
            xvb, xvb, AF.Identity, scale=wpoly[:, 1:2],
            bias=wpoly[:, 0:1]), inc=1)
        act_k2 = P.e["scalar"].n
        aff_s[p] = act_k2
        P.wait("vector", "scalar", act_k2)
        P.do("vector", lambda e, p=p, kb=kb: e.tensor_scalar(
            out=cosb2[p][:], in0=kb, scalar1=-2.0, scalar2=1.0,
            op0=ALU.mult, op1=ALU.add), inc=1)
        P.do("vector", lambda e, p=p, xvb=xvb, sqb=sqb: e.scalar_tensor_tensor(
            out=polyb2[p][:], in0=sqb, scalar=wpoly[:, 3:4], in1=xvb,
            op0=ALU.mult, op1=ALU.add), inc=1)
        s1dve[p] = P.e["vector"].n
        return act_k2


    def emit_tail_A(ct, t0, q, s2_n, first):
        p = ct % 2
        remb = rem2[p]
        immb = imm2[p]
        numpair = npr2[p]
        bank_acquire(c, 4)
        if first:
            P.wait("tensor", "vector", v_sr)
            P.wait("tensor", "scalar", sr_s)
        P.do("tensor", lambda e, ct=ct: e.matmul(
            c.ps[4][:, :], selk[:, ct * 128:ct * 128 + 128], pwbf[:],
            start=True, stop=True), inc=1)
        bank_done(c, 4)
        bank_consume(c, 4, "vector")
        P.wait("vector", "scalar", s2_n)
        P.wait("vector", "tensor", msq_pe2[p])
        P.do("vector", lambda e, p=p: e.tensor_tensor(
            out=polyb2[p][:], in0=polyb2[p][:], in1=c.ps[4][:, :],
            op=ALU.mult), inc=1)
        P.do("vector", lambda e, p=p, remb=remb: e.tensor_tensor(
            out=remb[:], in0=polyb2[p][:], in1=cosb2[p][:], op=ALU.mult), inc=1)
        P.do("vector", lambda e, p=p, immb=immb: e.tensor_tensor(
            out=immb[:], in0=polyb2[p][:], in1=sinb2[p][:], op=ALU.mult), inc=1)
        bank_release(c, 4, "vector")
        P.do("vector", lambda e, q=q, ct=ct, numpair=numpair, remb=remb:
             e.tensor_tensor_scan(
                 out=numpair[:, 0:512],
                 data0=c.ones_b[:, 0:1].broadcast_to((128, 512)),
                 data1=remb[:],
                 initial=(0.0 if q == 0 else carp[:, 2 * ct:2 * ct + 1]),
                 op0=ALU.mult, op1=ALU.add), inc=1)
        P.do("vector", lambda e, q=q, ct=ct, numpair=numpair, immb=immb:
             e.tensor_tensor_scan(
                 out=numpair[:, 512:1024],
                 data0=c.ones_b[:, 0:1].broadcast_to((128, 512)),
                 data1=immb[:],
                 initial=(0.0 if q == 0 else carp[:, 2 * ct + 1:2 * ct + 2]),
                 op0=ALU.mult, op1=ALU.add), inc=1)
        P.do("vector", lambda e, ct=ct, numpair=numpair: e.tensor_copy(
            carp[:, 2 * ct:2 * ct + 2], numpair[:, 511:1024:512]), inc=1)
        bank_acquire(c, 5)
        P.do("tensor", lambda e, ct=ct: e.matmul(
            c.ps[5][:, :], selk[:, ct * 128:ct * 128 + 128], invb[:],
            start=True, stop=True), inc=1)
        bank_done(c, 5)
        bank_consume(c, 5, "vector")
        P.wait("vector", "tensor", y_pe2[p])
        P.do("vector", lambda e, p=p, numpair=numpair: e.tensor_tensor(
            out=ren2[p][:], in0=numpair[:, 0:512], in1=c.ps[5][:, :],
            op=ALU.mult), inc=1)
        P.do("vector", lambda e, p=p, numpair=numpair: e.tensor_tensor(
            out=imn2[p][:], in0=numpair[:, 512:1024], in1=c.ps[5][:, :],
            op=ALU.mult), inc=1)
        bank_release(c, 5, "vector")
        norm_v = P.e["vector"].n
        P.do("vector", lambda e, p=p, remb=remb: e.tensor_tensor(
            out=remb[:], in0=ren2[p][:], in1=ren2[p][:], op=ALU.mult), inc=1)
        P.do("vector", lambda e, p=p, immb=immb: e.tensor_tensor(
            out=immb[:], in0=imn2[p][:], in1=imn2[p][:], op=ALU.mult), inc=1)
        sq_gp[0] = P.e["vector"].n
        return norm_v

    def emit_tail_B1(ct, norm_n):
        p = ct % 2
        remb = rem2[p]
        immb = imm2[p]
        renb = ren2[p]
        imnb = imn2[p]
        bank_acquire(c, 6)
        P.wait("tensor", "vector", norm_n)
        P.do("tensor", lambda e, renb=renb: e.matmul(
            c.ps[6][:, :], bdre[:], renb[:], start=True, stop=False), inc=0)
        P.do("tensor", lambda e, imnb=imnb: e.matmul(
            c.ps[6][:, :], bdim[:], imnb[:], start=False, stop=True), inc=1)
        bank_done(c, 6)
        y_pe2[p] = P.e["tensor"].n
        bank_acquire(c, 7)
        P.wait("tensor", "vector", sq_gp[0])
        P.do("tensor", lambda e, remb=remb: e.matmul(
            c.ps[7][0:2, :], hsel[:], remb[:], start=True, stop=False), inc=0)
        P.do("tensor", lambda e, immb=immb: e.matmul(
            c.ps[7][0:2, :], hsel[:], immb[:], start=False, stop=True), inc=1)
        bank_done(c, 7)
        msq_pe2[p] = P.e["tensor"].n

    def emit_tail_B2(ct, t0):
        p = ct % 2
        gct = 16 + ct
        p_ = ct % 2
        bank_consume(c, 7, "scalar")
        P.do("scalar", lambda e: e.activation(
            tmps[0:2, :], c.ps[7][0:2, :], AF.Ln, scale=1.0 / 128.0,
            bias=c.consts[0:2, 0:1]), inc=1)
        P.do("scalar", lambda e: e.activation(
            rsbb, tmps[0:2, :], AF.Exp, scale=-0.5), inc=1)
        s_rs = P.e["scalar"].n
        bg = 2 + ct % 2
        bank_acquire(c, bg)
        P.wait("tensor", "vector", diag_v2[p_])
        P.wait("tensor", "scalar", gz_wr_s[p_])
        for kk in range(4):
            P.do("tensor", lambda e, kk=kk, p_=p_, bg=bg: e.matmul(
                c.ps[bg][:, :], diagb2[p_][:, kk * 128:kk * 128 + 128],
                gzs2[p_][:, kk:kk + 512],
                start=(kk == 0), stop=(kk == 3)), inc=(1 if kk == 3 else 0))
        bank_done(c, bg)
        diag_rd_pe2[p_] = P.e["tensor"].n
        gz_rd_pe[0] = P.e["tensor"].n
        P.wait("tensor", "scalar", s_rs)
        P.do("tensor", lambda e: e.matmul(
            c.ps[7][:, :], hsel2[:], rsbb, start=True, stop=True), inc=1)
        bank_done(c, 7)
        bank_consume(c, bg, "scalar")
        gcv = gcv2[p]
        P.do("scalar", lambda e, gct=gct, bg=bg, gcv=gcv: e.activation(
            gcv[:], c.ps[bg][:, :], AF.Silu, bias=convb[:, gct:gct + 1]), inc=1)
        bank_release(c, bg, "scalar")
        s_n = P.e["scalar"].n
        tjoin = kb2[1 - p]
        bank_consume(c, 7, "vector")
        bank_consume(c, 6, "vector")
        P.wait("vector", "scalar", s_n)
        P.do("vector", lambda e, gcv=gcv, tjoin=tjoin: e.tensor_tensor(
            out=tjoin, in0=c.ps[7][:, :], in1=gcv[:], op=ALU.mult), inc=1)
        bank_release(c, 7, "vector")
        P.do("vector", lambda e, ct=ct, tjoin=tjoin: e.tensor_tensor(
            out=ygq[:, ct * 512:ct * 512 + 512], in0=tjoin,
            in1=c.ps[6][:, :], op=ALU.mult), inc=1)
        bank_release(c, 6, "vector")

    q_first = [True]
    for q in range(NQ):
        q_first[0] = True
        t0 = q * TQ
        P.wait("gpsimd", "tensor", sraw_pe)
        P.wait("gpsimd", "vector", zb_rd_v[1])
        lds = stage_weight_bf16(c, c.w["L0_attn_in_proj_w"], 0, 4096, 8, 1,
                                0, mtile=32, dst=svst)
        b = 6
        bank_acquire(c, b)
        P.wait("tensor", "gpsimd", lds)
        for dc in range(NDC):
            P.do("tensor", lambda e, b=b, dc=dc, t0=t0: e.matmul(
                c.ps[b][0:32, :], svst[:, dc * 32:dc * 32 + 32],
                xln_ap(c, dc, t0, TQ),
                start=(dc == 0), stop=(dc == NDC - 1)),
                inc=(1 if dc == NDC - 1 else 0))
        bank_done(c, b)
        sraw_pe = P.e["tensor"].n
        bank_consume(c, b, "vector")
        P.wait("vector", "scalar")
        P.do("vector", lambda e: e.tensor_copy(
            srawb[:, 0:3], ztail[0:32, 48:51]), inc=1)
        P.do("vector", lambda e, b=b: e.tensor_copy(
            srawb[:, 3:515], c.ps[b][0:32, :]), inc=1)
        bank_release(c, b, "vector")
        P.do("vector", lambda e: e.tensor_copy(
            ztail[0:32, 48:51], srawb[:, 512:515]), inc=1)
        P.do("vector", lambda e: e.tensor_scalar(
            out=pwb, in0=srawb[:, 0:512], scalar1=taps[0:32, 128:129],
            scalar2=None, op0=ALU.mult), inc=1)
        for kk in range(1, 4):
            P.do("vector", lambda e, kk=kk: e.scalar_tensor_tensor(
                out=pwb, in0=srawb[:, kk:kk + 512],
                scalar=taps[0:32, 128 + kk:129 + kk], in1=pwb,
                op0=ALU.mult, op1=ALU.add), inc=1)
        P.do("vector", lambda e: e.tensor_scalar(
            out=pwb, in0=pwb, scalar1=convb[0:32, 32:33], scalar2=None,
            op0=ALU.add), inc=1)
        P.do("vector", lambda e: e.tensor_scalar(
            out=pwb, in0=pwb, scalar1=ssc[:], scalar2=20.0,
            op0=ALU.mult, op1=ALU.min), inc=1)
        P.do("vector", lambda e: e.tensor_scalar(
            out=pwb, in0=pwb, scalar1=-20.0, scalar2=None,
            op0=ALU.max), inc=1)
        distc = srawb[:].bitcast(mybir.dt.float16)[0:32, 0:512]
        P.wait("gpsimd", "vector")
        nd = P.dma("gpsimd", distc, c.w["L0DIST"][:, t0:t0 + TQ])
        P.wait("vector", "gpsimd", nd)
        P.do("vector", lambda e, distc=distc: e.tensor_scalar(
            out=tmps, in0=distc, scalar1=slp[:],
            scalar2=None, op0=ALU.mult), inc=1)
        dist_rd_v = P.e["vector"].n
        P.wait("scalar", "vector")
        P.do("scalar", lambda e: e.activation(pwb, pwb, AF.Exp), inc=1)
        P.do("scalar", lambda e: e.activation(tmps, tmps, AF.Exp), inc=1)
        P.wait("vector", "scalar")
        P.do("vector", lambda e: e.tensor_tensor(
            out=pwb, in0=pwb, in1=tmps, op=ALU.mult), inc=1)
        P.do("vector", lambda e: e.tensor_copy(pwbf[:], pwb), inc=1)
        P.do("vector", lambda e, q=q: e.tensor_tensor_scan(
            out=tmps, data0=c.ones_b[0:32, 0:1].broadcast_to((32, 512)), data1=pwb,
            initial=(0.0 if q == 0 else carden[:]),
            op0=ALU.mult, op1=ALU.add), inc=1)
        P.do("vector", lambda e: e.tensor_copy(carden[:], tmps[0:32, 511:512]), inc=1)
        P.do("vector", lambda e: e.tensor_scalar(
            out=tmps, in0=tmps, scalar1=1e-4, scalar2=None,
            op0=ALU.max), inc=1)
        P.wait("scalar", "vector")
        P.do("scalar", lambda e: e.activation(tmps, tmps, AF.Ln), inc=1)
        P.do("scalar", lambda e: e.activation(invb[:], tmps, AF.Exp,
                                              scale=-1.0), inc=1)
        v_sr = P.e["vector"].n
        sr_s = P.e["scalar"].n
        zb_rd_v[0] = max(zb_rd_v[0], v_sr)

        def stage_group(g):
            half, part = g // 2, g % 2
            rg = (g % 2) * 2
            ru = rg + 1
            P.wait("gpsimd", "tensor", inproj_pe[rg])
            stage_weight_bf16(c, c.w["L0_attn_in_proj_w"], 0,
                              half * 1024 + part * 512, 8, 4, rg * 4096)
            P.wait("gpsimd", "tensor", inproj_pe[ru])
            ld = stage_weight_bf16(c, c.w["L0_attn_in_proj_w"], 0,
                                   2048 + half * 1024 + part * 512, 8, 4,
                                   ru * 4096)
            return rg, ru, ld

        def emit_zmm(ct, rg, ru, ld, t0):
            P.wait("tensor", "gpsimd", ld)
            cl4 = ct % 4
            for reg, bb in ((rg, ct % 2), (ru, 2 + ct % 2)):
                bank_acquire(c, bb)
                for dc in range(NDC):
                    P.do("tensor", lambda e, bb=bb, dc=dc, reg=reg,
                         cl4=cl4, t0=t0:
                         e.matmul(c.ps[bb][:, :],
                                  wtile(c, reg * 4096, 4, dc, cl4),
                                  xln_ap(c, dc, t0, TQ),
                                  start=(dc == 0), stop=(dc == NDC - 1)),
                         inc=(1 if dc == NDC - 1 else 0))
                bank_done(c, bb)
            p_ = ct % 2
            bg_ = 2 + ct % 2
            bank_consume(c, bg_, "scalar")
            P.wait("scalar", "tensor", gz_rd_pe[0])
            P.do("scalar", lambda e, bg_=bg_, p_=p_: e.activation(
                gzs2[p_][:, 3:515], c.ps[bg_][:, :], AF.Copy), inc=1)
            bank_release(c, bg_, "scalar")
            P.do("scalar", lambda e, ct=ct, p_=p_: e.activation(
                gzs2[p_][:, 0:3], gzt[:, ct * 3:ct * 3 + 3], AF.Copy), inc=1)
            P.do("scalar", lambda e, ct=ct, p_=p_: e.activation(
                gzt[:, ct * 3:ct * 3 + 3], gzs2[p_][:, 512:515], AF.Copy), inc=1)
            gz_wr_s[p_] = P.e["scalar"].n

        rg, ru, ld = stage_group(0)
        grp = [(rg, ru, ld)]
        emit_zmm(0, rg, ru, ld, t0)
        pend = None
        for ct in range(16):
            if pend is not None:
                normA = emit_tail_A(pend[0], t0, q, pend[1], pend[2])
                emit_tail_B1(pend[0], normA)
            conv_n = emit_s1_head(ct, t0)
            s2_n = emit_s1_trig(ct, conv_n)
            if pend is not None:
                emit_tail_B2(pend[0], t0)
            pend = (ct, s2_n, q_first[0])
            q_first[0] = False
            if ct < 15:
                nct = ct + 1
                if nct % 4 == 0:
                    g = nct // 4
                    rgp, rup, _ = grp[-1]
                    inproj_pe[rgp] = P.e["tensor"].n
                    inproj_pe[rup] = P.e["tensor"].n
                    rg, ru, ld = stage_group(g)
                    grp.append((rg, ru, ld))
                emit_zmm(nct, rg, ru, ld, t0)
        normA = emit_tail_A(pend[0], t0, q, pend[1], pend[2])
        emit_tail_B1(pend[0], normA)
        emit_tail_B2(pend[0], t0)
        pend = None
        pe_n = P.e["tensor"].n
        P.wait("gpsimd", "tensor", pe_n)
        ldo = stage_weight_bf16(c, c.w["L0_attn_out_proj_w"], 0, 0, 16, 8, 0)
        P.wait("tensor", "gpsimd", ldo)
        P.wait("tensor", "vector")
        for dct in range(8):
            b = dct % 2
            bank_acquire(c, b)
            for ct in range(16):
                P.do("tensor", lambda e, b=b, ct=ct, dct=dct: e.matmul(
                    c.ps[b][:, :], wtile(c, 0, 8, ct, dct),
                    ygq[:, ct * 512:ct * 512 + 512],
                    start=(ct == 0), stop=(ct == 15)),
                    inc=(1 if ct == 15 else 0))
            bank_done(c, b)
            bank_consume(c, b, "vector")
            P.do("vector", lambda e, b=b, dct=dct, t0=t0: e.tensor_tensor(
                out=xr_ap(c, dct, t0, TQ), in0=xr_ap(c, dct, t0, TQ),
                in1=c.ps[b][:, :], op=ALU.add), inc=1)
            bank_release(c, b, "vector")
        inproj_pe = [P.e["tensor"].n] * 4
        sraw_pe = P.e["tensor"].n
    P.barrier()
    bes.close()


def b_l0_ffn(c):
    P = c.P
    emit_ln(c)
    bes = contextlib.ExitStack()
    S = 1.0 / 127.0
    tmpb2 = [bes.enter_context(c.nc.sbuf_tensor(f"l0f_tmp{i}", [128, 512], BF16))
             for i in range(2)]
    rpe = [0, 0, 0, 0]
    tmp_read_v2 = [0, 0]

    for q in range(NQ):
        t0 = q * TQ
        def l0f_stage(idx):
            hs, part = divmod(idx, 2)
            rg = (idx % 2) * 2
            ru = rg + 1
            P.wait("gpsimd", "tensor", rpe[rg])
            stage_weight_bf16(c, c.w["L0_ffn_w_gate"], 0,
                              hs * 1024 + part * 512, 8, 4, rg * 4096)
            P.wait("gpsimd", "tensor", rpe[ru])
            ldb = stage_weight_bf16(c, c.w["L0_ffn_w_up"], 0,
                                    hs * 1024 + part * 512, 8, 4, ru * 4096)
            return rg, ru, ldb

        nxt = l0f_stage(0)
        for hs in range(3):
            for part in range(2):
                idx = hs * 2 + part
                rg, ru, ldb = nxt
                P.wait("tensor", "gpsimd", ldb)
                if idx + 1 < 6:
                    nxt = l0f_stage(idx + 1)
                for hl4 in range(4):
                    hc = hs * 8 + part * 4 + hl4
                    bg = hc % 2
                    bu = 2 + hc % 2
                    bank_acquire(c, bg)
                    for dc in range(NDC):
                        P.do("tensor", lambda e, bg=bg, dc=dc, hl4=hl4, t0=t0, rg=rg:
                             e.matmul(c.ps[bg][:, :], wtile(c, rg * 4096, 4, dc, hl4),
                                      xln_ap(c, dc, t0, TQ),
                                      start=(dc == 0), stop=(dc == NDC - 1)),
                             inc=(1 if dc == NDC - 1 else 0))
                    bank_done(c, bg)
                    bank_acquire(c, bu)
                    for dc in range(NDC):
                        P.do("tensor", lambda e, bu=bu, dc=dc, hl4=hl4, t0=t0, ru=ru:
                             e.matmul(c.ps[bu][:, :], wtile(c, ru * 4096, 4, dc, hl4),
                                      xln_ap(c, dc, t0, TQ),
                                      start=(dc == 0), stop=(dc == NDC - 1)),
                             inc=(1 if dc == NDC - 1 else 0))
                    bank_done(c, bu)
                    tmpb = tmpb2[hc % 2]
                    bank_consume(c, bg, "scalar")
                    P.wait("scalar", "vector", tmp_read_v2[hc % 2])
                    P.do("scalar", lambda e, bg=bg, tmpb=tmpb: e.activation(
                        tmpb[:], c.ps[bg][:, :], AF.Silu, scale=S), inc=1)
                    bank_release(c, bg, "scalar")
                    P.wait("vector", "scalar")
                    bank_consume(c, bu, "vector")
                    P.do("vector", lambda e, bu=bu, hc=hc, tmpb=tmpb:
                         e.tensor_tensor(
                             out=c.scr[:, hc * 512:hc * 512 + 512], in0=tmpb[:],
                             in1=c.ps[bu][:, :], op=ALU.mult), inc=1)
                    tmp_read_v2[hc % 2] = P.e["vector"].n
                    bank_release(c, bu, "vector")
                rpe[rg] = rpe[ru] = P.e["tensor"].n
        sc_n = P.e["scalar"].n
        v_n = P.e["vector"].n

        def l0f_stage_down(hs):
            rp = (hs % 2) * 2
            P.wait("gpsimd", "tensor", max(rpe[rp], rpe[rp + 1]))
            ld = stage_weight_bf16(c, c.w["L0_ffn_w_down"], hs * 1024, 0, 8, 8,
                                   rp * 4096)
            return rp, ld

        nxtb = l0f_stage_down(0)
        for hs in range(3):
            rp, ld = nxtb
            P.wait("tensor", "gpsimd", ld)
            if hs + 1 < 3:
                nxtb = l0f_stage_down(hs + 1)
            if hs == 0:
                for b in range(8):
                    bank_acquire(c, b)
                P.wait("tensor", "scalar", sc_n)
                P.wait("tensor", "vector", v_n)
            for dc in range(NDC):
                for hl in range(8):
                    hc = hs * 8 + hl
                    P.do("tensor", lambda e, dc=dc, rp=rp, hl=hl, hc=hc, hs=hs:
                         e.matmul(c.ps[dc][:, :], wtile(c, rp * 4096, 8, hl, dc),
                                  c.scr[:, hc * 512:hc * 512 + 512],
                                  start=(hs == 0 and hl == 0),
                                  stop=(hs == 2 and hl == 7)),
                         inc=(1 if hl == 7 else 0))
            rpe[rp] = rpe[rp + 1] = P.e["tensor"].n
        for dc in range(NDC):
            bank_done(c, dc)
            bank_consume(c, dc, "vector")
            P.do("vector", lambda e, dc=dc, t0=t0: e.scalar_tensor_tensor(
                out=xr_ap(c, dc, t0, TQ), in0=c.ps[dc][:, :],
                scalar=S * S, in1=xr_ap(c, dc, t0, TQ),
                op0=ALU.mult, op1=ALU.add), inc=1)
            bank_release(c, dc, "vector")
    P.barrier()
    bes.close()



def b_l2_attn(c):
    P = c.P
    emit_ln(c)
    bes = contextlib.ExitStack()
    kv = bes.enter_context(c.nc.sbuf_tensor("l2_kv", [128, 8192], BF16))
    o1 = bes.enter_context(c.nc.sbuf_tensor("l2_o1", [128, 4096], BF16))
    A, B = 0, 8192

    lda = stage_weight_bf16(c, c.w["L2_attn_w_k"], 0, 0, 8, 8, A)
    ldb = stage_weight_bf16(c, c.w["L2_attn_w_v"], 0, 0, 8, 8, B)
    P.wait("tensor", "gpsimd", ldb)
    for th in range(2):
        for tc8 in range(8):
            t0 = th * 1024 + tc8 * 128
            for src_reg, dst0, b0 in ((A, 0, 0), (B, 8192, 2)):
                for et in range(2):
                    b = b0 + et % 2
                    bank_acquire(c, b)
                    for dc in range(NDC):
                        P.do("tensor", lambda e, b=b, dc=dc, t0=t0, src_reg=src_reg, et=et:
                             e.matmul(c.ps[b][:, :],
                                      c.xln[:, dc * T + t0:dc * T + t0 + 128],
                                      c.wst[:, src_reg + dc * 1024 + et * 512:
                                            src_reg + dc * 1024 + et * 512 + 512],
                                      start=(dc == 0), stop=(dc == NDC - 1)),
                             inc=(1 if dc == NDC - 1 else 0))
                    bank_done(c, b)
                    bank_consume(c, b, "scalar")
                    P.do("scalar", lambda e, b=b, dst0=dst0, tc8=tc8, et=et:
                         e.activation(c.scr[:, dst0 + tc8 * 1024 + et * 512:
                                            dst0 + tc8 * 1024 + et * 512 + 512],
                                      c.ps[b][:, :], AF.Copy), inc=1)
                    bank_release(c, b, "scalar")
        sc_n = P.e["scalar"].n
        for dvh in range(2):
            for dkt in range(8):
                bank_acquire(c, dkt)
            P.wait("tensor", "scalar", sc_n)
            for tc8 in range(8):
                for dkt in range(8):
                    P.do("tensor", lambda e, dkt=dkt, tc8=tc8, dvh=dvh:
                         e.matmul(c.ps[dkt][:, :],
                                  c.scr[:, tc8 * 1024 + dkt * 128:
                                        tc8 * 1024 + dkt * 128 + 128],
                                  c.scr[:, 8192 + tc8 * 1024 + dvh * 512:
                                        8192 + tc8 * 1024 + dvh * 512 + 512],
                                  start=(tc8 == 0), stop=(tc8 == 7)),
                         inc=(1 if tc8 == 7 else 0))
            for dkt in range(8):
                bank_done(c, dkt)
                col = dkt * 1024 + dvh * 512
                if th == 0:
                    bank_consume(c, dkt, "scalar")
                    P.do("scalar", lambda e, dkt=dkt, col=col: e.activation(
                        kv[:, col:col + 512], c.ps[dkt][:, :], AF.Copy), inc=1)
                    bank_release(c, dkt, "scalar")
                else:
                    bank_consume(c, dkt, "vector")
                    P.do("vector", lambda e, dkt=dkt, col=col: e.tensor_tensor(
                        out=kv[:, col:col + 512], in0=kv[:, col:col + 512],
                        in1=c.ps[dkt][:, :], op=ALU.add), inc=1)
                    bank_release(c, dkt, "vector")
    pe_n = P.e["tensor"].n
    P.wait("gpsimd", "tensor", pe_n)
    lda = stage_weight_bf16(c, c.w["L2_attn_w_q"], 0, 0, 8, 8, A)
    ldb = stage_weight_bf16(c, c.w["L2_attn_w_o"], 0, 0, 8, 8, B)
    P.wait("tensor", "gpsimd", lda)
    for mt in range(NDC):
        for qc in range(NQ):
            b = (mt * NQ + qc) % 4
            bank_acquire(c, b)
            for dc in range(NDC):
                P.do("tensor", lambda e, b=b, dc=dc, mt=mt, qc=qc:
                     e.matmul(c.ps[b][:, :], wtile(c, A, 8, dc, mt),
                              xln_ap(c, dc, qc * TQ, TQ),
                              start=(dc == 0), stop=(dc == NDC - 1)),
                     inc=(1 if dc == NDC - 1 else 0))
            bank_done(c, b)
            bank_consume(c, b, "scalar")
            P.do("scalar", lambda e, b=b, mt=mt, qc=qc: e.activation(
                c.scr[:, mt * 2048 + qc * 512:mt * 2048 + qc * 512 + 512],
                c.ps[b][:, :], AF.Copy), inc=1)
            bank_release(c, b, "scalar")
    P.wait("tensor", "gpsimd", ldb)
    P.wait("tensor", "scalar")
    P.wait("tensor", "vector")
    for qc in range(NQ):
        for dvt in range(8):
            b = 4 + dvt % 2
            bank_acquire(c, b)
            for dkt in range(8):
                P.do("tensor", lambda e, b=b, dkt=dkt, dvt=dvt, qc=qc:
                     e.matmul(c.ps[b][:, :],
                              kv[:, dkt * 1024 + dvt * 128:dkt * 1024 + dvt * 128 + 128],
                              c.scr[:, dkt * 2048 + qc * 512:dkt * 2048 + qc * 512 + 512],
                              start=(dkt == 0), stop=(dkt == 7)),
                     inc=(1 if dkt == 7 else 0))
            bank_done(c, b)
            bank_consume(c, b, "scalar")
            P.do("scalar", lambda e, b=b, dvt=dvt: e.activation(
                o1[:, dvt * 512:dvt * 512 + 512], c.ps[b][:, :], AF.Copy), inc=1)
            bank_release(c, b, "scalar")
        sc_n = P.e["scalar"].n
        for dct in range(8):
            b = 6 + dct % 2
            bank_acquire(c, b)
            if dct == 0:
                P.wait("tensor", "scalar", sc_n)
            for dvt in range(8):
                P.do("tensor", lambda e, b=b, dvt=dvt, dct=dct:
                     e.matmul(c.ps[b][:, :], wtile(c, B, 8, dvt, dct),
                              o1[:, dvt * 512:dvt * 512 + 512],
                              start=(dvt == 0), stop=(dvt == 7)),
                     inc=(1 if dvt == 7 else 0))
            bank_done(c, b)
            bank_consume(c, b, "vector")
            P.do("vector", lambda e, b=b, dct=dct, qc=qc: e.tensor_tensor(
                out=xr_ap(c, dct, qc * TQ, TQ), in0=xr_ap(c, dct, qc * TQ, TQ),
                in1=c.ps[b][:, :], op=ALU.add), inc=1)
            bank_release(c, b, "vector")
    P.barrier()
    bes.close()



def b_l3_ffn(c):
    P = c.P
    emit_ln(c)
    bes = contextlib.ExitStack()
    nc = c.nc
    lg = bes.enter_context(nc.sbuf_tensor("l3f_lg", [128, 128], F32))
    mx = bes.enter_context(nc.sbuf_tensor("l3f_mx", [128, 16], F32))
    mtok = bes.enter_context(nc.sbuf_tensor("l3f_mtok", [128, 128], BF16))
    mfm = bes.enter_context(nc.sbuf_tensor("l3f_mfm", [8, 2048], BF16))
    mrow = bes.enter_context(nc.sbuf_tensor("l3f_mrow", [1, 1024], BF16))
    xinb = [bes.enter_context(nc.sbuf_tensor(f"l3f_xin{i}", [128, 4096], BF16))
            for i in range(2)]
    gw = bes.enter_context(nc.sbuf_tensor("l3f_gw", [128, 64], BF16))
    ld = stage_weight_bf16(c, c.w["L3_ffn_gate_w"], 0, 0, 8, 1, 0, mtile=8, dst=gw)
    P.wait("tensor", "gpsimd", ld)
    for tc in range(16):
        b = tc % 2
        bank_acquire(c, b)
        for dc in range(NDC):
            P.do("tensor", lambda e, b=b, dc=dc, tc=tc: e.matmul(
                c.ps[b][:, 0:8], c.xln[:, dc * T + tc * 128:dc * T + tc * 128 + 128],
                gw[:, dc * 8:dc * 8 + 8],
                start=(dc == 0), stop=(dc == NDC - 1)),
                inc=(1 if dc == NDC - 1 else 0))
        bank_done(c, b)
        bank_consume(c, b, "vector")
        P.do("vector", lambda e, b=b, tc=tc: e.tensor_copy(
            lg[:, tc * 8:tc * 8 + 8], c.ps[b][:, 0:8]), inc=1)
        bank_release(c, b, "vector")
        P.do("vector", lambda e, tc=tc: e.tensor_reduce(
            out=mx[:, tc:tc + 1], in_=lg[:, tc * 8:tc * 8 + 8],
            axis=AX.X, op=ALU.max), inc=1)
        P.do("vector", lambda e, tc=tc: e.tensor_tensor(
            out=mtok[:, tc * 8:tc * 8 + 8], in0=lg[:, tc * 8:tc * 8 + 8],
            in1=mx[:, tc:tc + 1].broadcast_to((128, 8)), op=ALU.is_ge), inc=1)
    v_n = P.e["vector"].n
    P.wait("tensor", "vector", v_n)
    for tc in range(16):
        b = 2 + tc % 2
        bank_acquire(c, b)
        P.do("tensor", lambda e, b=b, tc=tc: e.transpose(
            c.ps[b].bitcast(BF16)[0:8, 0:128], mtok[:, tc * 8:tc * 8 + 8],
            c.ident_b[:]), inc=1)
        bank_done(c, b)
        bank_consume(c, b, "vector")
        P.do("vector", lambda e, b=b, tc=tc: e.tensor_copy(
            mfm[:, tc * 128:tc * 128 + 128], c.ps[b].bitcast(BF16)[0:8, 0:128]),
            inc=1)
        bank_release(c, b, "vector")
    v_n = P.e["vector"].n
    P.wait("sync", "vector", v_n)
    mrow_pe = [0, 0]
    for e_ in range(8):
        for qc in range(NQ):
            i = e_ * NQ + qc
            mr = mrow[0:1, (i % 2) * 512:(i % 2) * 512 + 512]
            P.wait("sync", "tensor", mrow_pe[i % 2])
            n = P.dma("sync", mr, mfm[e_:e_ + 1, qc * TQ:qc * TQ + TQ])
            b = 4 + i % 2
            bank_acquire(c, b)
            P.wait("tensor", "sync", n)
            if i == 0:
                P.wait("tensor", "vector", v_n)
            P.do("tensor", lambda e, b=b, mr=mr: e.matmul(
                c.ps[b][:, :], c.ones_b[0:1, :], mr,
                start=True, stop=True), inc=1)
            bank_done(c, b)
            mrow_pe[i % 2] = P.e["tensor"].n
            bank_consume(c, b, "scalar")
            P.do("scalar", lambda e, e_=e_, qc=qc, b=b: e.activation(
                c.scr[:, e_ * 2048 + qc * 512:e_ * 2048 + qc * 512 + 512],
                c.ps[b][:, :], AF.Copy), inc=1)
            bank_release(c, b, "scalar")
    for dc in range(NDC):
        P.do("vector", lambda e, dc=dc: e.tensor_tensor(
            out=xr_ap(c, dc), in0=xr_ap(c, dc), in1=xln_ap(c, dc),
            op=ALU.add), inc=1)
    regs = [0, 8192]
    reg_pe = [0, 0]
    xin_rd_pe = [0, 0]
    xin_wr_v = [0, 0]
    sc_n = P.e["scalar"].n
    P.wait("vector", "scalar", sc_n)

    def l3f_emit_xin(i):
        qc_, e2 = divmod(i, 8)
        t0_ = qc_ * TQ
        buf = xinb[i % 2]
        xin3 = buf[:].rearrange("p (dcx t) -> p dcx t", dcx=NDC)
        xln3 = c.xln[:].rearrange(
            "p (dcx t) -> p dcx t", dcx=NDC)[:, :, t0_:t0_ + TQ]
        mbc = c.scr[:, e2 * 2048 + qc_ * 512:e2 * 2048 + qc_ * 512 + 512]
        mbc3 = mbc.rearrange("p (u t) -> p u t", u=1).broadcast_to(
            (128, NDC, TQ))
        P.wait("vector", "tensor", xin_rd_pe[i % 2])
        P.do("vector", lambda e, xin3=xin3, xln3=xln3, mbc3=mbc3:
             e.tensor_tensor(out=xin3, in0=xln3, in1=mbc3, op=ALU.mult), inc=1)
        xin_wr_v[i % 2] = P.e["vector"].n

    l3f_emit_xin(0)
    for qc in range(NQ):
        t0 = qc * TQ
        for b in range(8):
            bank_acquire(c, b)
        for e_ in range(8):
            i = qc * 8 + e_
            cur = i % 2
            xin = xinb[cur]
            reg = regs[e_ % 2]
            P.wait("gpsimd", "tensor", reg_pe[e_ % 2])
            ld = stage_weight_bf16(c, c.w["L3_ffn_expert_w"][e_], 0, 0, 8, 8, reg)
            P.wait("tensor", "gpsimd", ld)
            P.wait("tensor", "vector", xin_wr_v[cur])
            for dct in range(8):
                for hc in range(NDC):
                    P.do("tensor", lambda e, dct=dct, hc=hc, reg=reg, e_=e_,
                         xin=xin:
                         e.matmul(c.ps[dct][:, :], wtile(c, reg, 8, hc, dct),
                                  xin[:, hc * 512:hc * 512 + 512],
                                  start=(e_ == 0 and hc == 0),
                                  stop=(e_ == 7 and hc == 7)),
                         inc=(1 if hc == 7 else 0))
            reg_pe[e_ % 2] = P.e["tensor"].n
            xin_rd_pe[cur] = P.e["tensor"].n
            if i + 1 < 32:
                l3f_emit_xin(i + 1)
        for dct in range(8):
            bank_done(c, dct)
            bank_consume(c, dct, "vector")
            P.do("vector", lambda e, dct=dct, t0=t0: e.tensor_tensor(
                out=xr_ap(c, dct, t0, TQ), in0=xr_ap(c, dct, t0, TQ),
                in1=c.ps[dct][:, :], op=ALU.add), inc=1)
            bank_release(c, dct, "vector")
    P.barrier()
    bes.close()



def b_l2_ffn(c):
    P = c.P
    emit_ln(c)
    bes = contextlib.ExitStack()
    nc = c.nc
    lg = bes.enter_context(nc.sbuf_tensor("l2f_lg", [128, 128], F32))
    mxn = bes.enter_context(nc.sbuf_tensor("l2f_mxn", [128, 16], F32))
    den = bes.enter_context(nc.sbuf_tensor("l2f_den", [128, 16], F32))
    gate = bes.enter_context(nc.sbuf_tensor("l2f_gate", [128, 16], F32))
    msk = bes.enter_context(nc.sbuf_tensor("l2f_msk", [128, 128], F32))
    sel = bes.enter_context(nc.sbuf_tensor("l2f_sel", [128, 128], F32))
    selfm = bes.enter_context(nc.sbuf_tensor("l2f_selfm", [8, 2048], BF16))
    mrow = bes.enter_context(nc.sbuf_tensor("l2f_mrow", [1, 1024], BF16))
    carry = bes.enter_context(nc.sbuf_tensor("l2f_carry", [1, 8], F32))
    wrst = bes.enter_context(nc.sbuf_tensor("l2f_wr", [128, 64], BF16))
    brbc = bes.enter_context(nc.sbuf_tensor("l2f_br", [128, 8], F32))
    b1sb = bes.enter_context(nc.sbuf_tensor("l2f_b1", [128, 8], F32))
    b2t = bes.enter_context(nc.sbuf_tensor("l2f_b2t", [128, 8], F32))
    xin0 = bes.enter_context(nc.sbuf_tensor("l2f_xin0", [128, 4096], BF16))
    xin1 = bes.enter_context(nc.sbuf_tensor("l2f_xin1", [128, 4096], BF16))
    tsel = bes.enter_context(nc.sbuf_tensor("l2f_ts", [128, 512], BF16))
    hbuf = bes.enter_context(nc.sbuf_tensor("l2f_h", [128, 4096], BF16))
    tmp = bes.enter_context(nc.sbuf_tensor("l2f_tmp", [128, 8], F32))

    ld = stage_weight_bf16(c, c.w["L2_ffn_Wr"], 0, 0, 8, 1, 0, mtile=8, dst=wrst)
    nbr = P.dma("sync", c.aux[0:1, 0:8], c.w["L2_ffn_br"])
    P.do("vector", lambda e: e.memset(carry[:], 0.0), inc=1)
    bank_acquire(c, 6)
    P.wait("tensor", "sync", nbr)
    P.do("tensor", lambda e: e.matmul(c.ps[6][:, 0:8], c.ones_f[:, :],
                                      c.aux[0:1, 0:8], start=True, stop=True), inc=1)
    bank_done(c, 6)
    bank_consume(c, 6, "vector")
    P.do("vector", lambda e: e.tensor_copy(brbc[:], c.ps[6][:, 0:8]), inc=1)
    bank_release(c, 6, "vector")
    P.wait("tensor", "gpsimd", ld)

    for tc in range(16):
        b = tc % 2
        bank_acquire(c, b)
        for dc in range(NDC):
            P.do("tensor", lambda e, b=b, dc=dc, tc=tc: e.matmul(
                c.ps[b][:, 0:8], c.xln[:, dc * T + tc * 128:dc * T + tc * 128 + 128],
                wrst[:, dc * 8:dc * 8 + 8],
                start=(dc == 0), stop=(dc == NDC - 1)),
                inc=(1 if dc == NDC - 1 else 0))
        bank_done(c, b)
        bank_consume(c, b, "vector")
        P.do("vector", lambda e, b=b, tc=tc: e.tensor_tensor(
            out=lg[:, tc * 8:tc * 8 + 8], in0=c.ps[b][:, 0:8], in1=brbc[:],
            op=ALU.add), inc=1)
        bank_release(c, b, "vector")
        P.do("vector", lambda e, tc=tc: e.tensor_reduce(
            out=mxn[:, tc:tc + 1], in_=lg[:, tc * 8:tc * 8 + 8],
            axis=AX.X, op=ALU.max, negate=True), inc=1)
        P.wait("scalar", "vector")
        P.do("scalar", lambda e, tc=tc: e.activation(
            tmp[:, 0:8], lg[:, tc * 8:tc * 8 + 8], AF.Exp,
            bias=mxn[:, tc:tc + 1], accum_out=den[:, tc:tc + 1]), inc=1)
        P.wait("vector", "scalar")
        P.do("vector", lambda e, tc=tc: e.reciprocal(
            gate[:, tc:tc + 1], den[:, tc:tc + 1]), inc=1)
        P.do("vector", lambda e, tc=tc: e.scalar_tensor_tensor(
            out=msk[:, tc * 8:tc * 8 + 8], in0=lg[:, tc * 8:tc * 8 + 8],
            scalar=mxn[:, tc:tc + 1], in1=c.consts[:, 3:4].broadcast_to((128, 8)),
            op0=ALU.add, op1=ALU.is_ge), inc=1)
        v_n = P.e["vector"].n
        b2_ = 2 + tc % 2
        bank_acquire(c, b2_)
        P.wait("tensor", "vector", v_n)
        P.do("tensor", lambda e, b2_=b2_, tc=tc: e.matmul(
            c.ps[b2_][:, 0:8], c.tri_f[:], msk[:, tc * 8:tc * 8 + 8],
            start=True, stop=False), inc=0)
        P.do("tensor", lambda e, b2_=b2_: e.matmul(
            c.ps[b2_][:, 0:8], c.ones_f[:, :], carry[:],
            start=False, stop=True), inc=1)
        bank_done(c, b2_)
        b3 = 4 + tc % 2
        bank_acquire(c, b3)
        P.do("tensor", lambda e, b3=b3, tc=tc: e.matmul(
            c.ps[b3][0:1, 0:8], c.ones_cf[:], msk[:, tc * 8:tc * 8 + 8],
            start=True, stop=True), inc=1)
        bank_done(c, b3)
        bank_consume(c, b2_, "vector")
        P.do("vector", lambda e, b2_=b2_, tc=tc: e.tensor_scalar(
            out=sel[:, tc * 8:tc * 8 + 8], in0=c.ps[b2_][:, 0:8],
            scalar1=2047.5, scalar2=None, op0=ALU.is_le), inc=1)
        bank_release(c, b2_, "vector")
        P.do("vector", lambda e, tc=tc: e.tensor_tensor(
            out=sel[:, tc * 8:tc * 8 + 8], in0=sel[:, tc * 8:tc * 8 + 8],
            in1=msk[:, tc * 8:tc * 8 + 8], op=ALU.mult), inc=1)
        P.do("vector", lambda e, tc=tc: e.tensor_tensor(
            out=sel[:, tc * 8:tc * 8 + 8], in0=sel[:, tc * 8:tc * 8 + 8],
            in1=gate[:, tc:tc + 1].broadcast_to((128, 8)), op=ALU.mult), inc=1)
        bank_consume(c, b3, "vector")
        P.do("vector", lambda e, b3=b3: e.tensor_tensor(
            out=carry[:], in0=carry[:], in1=c.ps[b3][0:1, 0:8], op=ALU.add), inc=1)
        bank_release(c, b3, "vector")
    v_n = P.e["vector"].n
    P.wait("tensor", "vector", v_n)
    for tc in range(16):
        b = 6 + tc % 2
        bank_acquire(c, b)
        P.do("tensor", lambda e, b=b, tc=tc: e.transpose(
            c.ps[b][0:8, 0:128], sel[:, tc * 8:tc * 8 + 8], c.ident_f[:]), inc=1)
        bank_done(c, b)
        bank_consume(c, b, "vector")
        P.do("vector", lambda e, b=b, tc=tc: e.tensor_copy(
            selfm[:, tc * 128:tc * 128 + 128], c.ps[b][0:8, 0:128]), inc=1)
        bank_release(c, b, "vector")
    v_n = P.e["vector"].n
    P.wait("sync", "vector", v_n)
    mrow_pe = [0, 0]
    for e_ in range(8):
        for qc in range(NQ):
            i = e_ * NQ + qc
            mr = mrow[0:1, (i % 2) * 512:(i % 2) * 512 + 512]
            P.wait("sync", "tensor", mrow_pe[i % 2])
            n = P.dma("sync", mr, selfm[e_:e_ + 1, qc * TQ:qc * TQ + TQ])
            b = i % 2
            bank_acquire(c, b)
            P.wait("tensor", "sync", n)
            P.do("tensor", lambda e, b=b, mr=mr: e.matmul(
                c.ps[b][:, :], c.ones_b[0:1, :], mr,
                start=True, stop=True), inc=1)
            bank_done(c, b)
            mrow_pe[i % 2] = P.e["tensor"].n
            bank_consume(c, b, "scalar")
            P.do("scalar", lambda e, e_=e_, qc=qc, b=b: e.activation(
                c.scr[:, e_ * 2048 + qc * 512:e_ * 2048 + qc * 512 + 512],
                c.ps[b][:, :], AF.Copy), inc=1)
            bank_release(c, b, "scalar")
    A2, B2 = 0, 8192
    w1_rd_pe = 0
    w2_rd_pe = 0
    b1_rd_s = 0
    b2r_rd_v = 0
    xinb = [xin0, xin1]
    xin_rd_pe = [0, 0]
    xin_wr_v = [0, 0]
    sc_pb = P.e["scalar"].n
    P.wait("vector", "scalar", sc_pb)

    def l2f_mbc3(i):
        e_, qc = divmod(i, NQ)
        mbc = c.scr[:, e_ * 2048 + qc * 512:e_ * 2048 + qc * 512 + 512]
        return mbc.rearrange("p (u t) -> p u t", u=1).broadcast_to(
            (128, NDC, TQ))

    def l2f_emit_xin(i):
        qc = i % NQ
        t0 = qc * TQ
        buf = xinb[i % 2]
        xin3 = buf[:].rearrange("p (dcx t) -> p dcx t", dcx=NDC)
        xln3 = c.xln[:].rearrange(
            "p (dcx t) -> p dcx t", dcx=NDC)[:, :, t0:t0 + TQ]
        P.wait("vector", "tensor", xin_rd_pe[i % 2])
        P.do("vector", lambda e, xin3=xin3, xln3=xln3, m3_=l2f_mbc3(i):
             e.tensor_tensor(out=xin3, in0=xln3, in1=m3_, op=ALU.mult), inc=1)
        xin_wr_v[i % 2] = P.e["vector"].n

    l2f_emit_xin(0)
    for e_ in range(8):
        P.wait("gpsimd", "tensor", w1_rd_pe)
        ld1 = stage_weight_bf16(c, c.w["L2_ffn_W1"][e_], 0, 0, 8, 8, A2)
        P.wait("gpsimd", "scalar", b1_rd_s)
        nb = P.dma("gpsimd", b1sb[:], c.w["L2_ffn_b1"][e_])
        P.wait("gpsimd", "tensor", w2_rd_pe)
        ld2 = stage_weight_bf16(c, c.w["L2_ffn_W2"][e_], 0, 0, 8, 8, B2)
        P.wait("gpsimd", "vector", b2r_rd_v)
        nb2 = P.dma("gpsimd", b2t[:], c.w["L2B2T"][e_])
        for qc in range(NQ):
            t0 = qc * TQ
            i = e_ * NQ + qc
            cur = i % 2
            xin = xinb[cur]
            if qc == 0:
                P.wait("tensor", "gpsimd", ld1)
                P.wait("scalar", "gpsimd", nb)
            P.wait("tensor", "vector", xin_wr_v[cur])
            for ft in range(8):
                b = ft % 2
                bank_acquire(c, b)
                for dc in range(NDC):
                    P.do("tensor", lambda e, b=b, dc=dc, ft=ft, xin=xin:
                         e.matmul(c.ps[b][:, :], wtile(c, A2, 8, dc, ft),
                                  xin[:, dc * 512:dc * 512 + 512],
                                  start=(dc == 0), stop=(dc == NDC - 1)),
                         inc=(1 if dc == NDC - 1 else 0))
                bank_done(c, b)
                bank_consume(c, b, "scalar")
                P.do("scalar", lambda e, b=b, ft=ft: e.activation(
                    hbuf[:, ft * 512:ft * 512 + 512], c.ps[b][:, :], AF.Relu,
                    bias=b1sb[:, ft:ft + 1]), inc=1)
                bank_release(c, b, "scalar")
            xin_rd_pe[cur] = P.e["tensor"].n
            if qc == NQ - 1:
                w1_rd_pe = P.e["tensor"].n
            sc_n = P.e["scalar"].n
            b1_rd_s = sc_n
            if i + 1 < 32:
                l2f_emit_xin(i + 1)
            if qc == 0:
                P.wait("vector", "gpsimd", nb2)
                P.wait("tensor", "gpsimd", ld2)
            P.wait("tensor", "scalar", sc_n)
            mbc = c.scr[:, e_ * 2048 + qc * 512:e_ * 2048 + qc * 512 + 512]
            for dct in range(8):
                b = 2 + dct % 2
                bank_acquire(c, b)
                for fc in range(8):
                    P.do("tensor", lambda e, b=b, fc=fc, dct=dct:
                         e.matmul(c.ps[b][:, :],
                                  wtile(c, B2, 8, fc, dct),
                                  hbuf[:, fc * 512:fc * 512 + 512],
                                  start=(fc == 0), stop=(fc == 7)),
                         inc=(1 if fc == 7 else 0))
                bank_done(c, b)
                bank_consume(c, b, "vector")
                P.do("vector", lambda e, b=b, mbc=mbc, dct=dct:
                     e.scalar_tensor_tensor(
                         out=tsel[:], in0=c.ps[b][:, :],
                         scalar=b2t[:, dct:dct + 1], in1=mbc,
                         op0=ALU.add, op1=ALU.mult), inc=1)
                P.do("vector", lambda e, dct=dct, t0=t0: e.tensor_tensor(
                    out=xr_ap(c, dct, t0, TQ), in0=xr_ap(c, dct, t0, TQ),
                    in1=tsel[:], op=ALU.add), inc=1)
                bank_release(c, b, "vector")
        w2_rd_pe = P.e["tensor"].n
        b2r_rd_v = P.e["vector"].n
    P.barrier()
    bes.close()



def b_l1_attn(c):
    P = c.P
    emit_ln(c)
    bes = contextlib.ExitStack()
    nc = c.nc
    A, B = 0, 8192
    m3 = bes.enter_context(nc.sbuf_tensor("l1a_m3", [16, 3], F32))
    cnt = bes.enter_context(nc.sbuf_tensor("l1a_cnt", [16, 1], F32))
    gts = bes.enter_context(nc.sbuf_tensor("l1a_gt", [16, 1], F32))
    df2 = bes.enter_context(nc.sbuf_tensor("l1a_df2", [16, 2], F32))
    ktok = bes.enter_context(nc.sbuf_tensor("l1a_ktok", [128, 256], BF16))
    dnvf = bes.enter_context(nc.sbuf_tensor("l1a_dnvf", [1, 512], F32))
    dnvb = bes.enter_context(nc.sbuf_tensor("l1a_dnvb", [1, 512], BF16))
    dbcs = bes.enter_context(nc.sbuf_tensor("l1a_dbc", [128, 512], BF16))
    tsb = bes.enter_context(nc.sbuf_tensor("l1a_tsb", [128, 512], F32))
    qkvb = bes.enter_context(nc.sbuf_tensor("l1a_qkvb", [128, 24], F32))
    rb1 = bes.enter_context(nc.sbuf_tensor("l1a_rb1", [128, 2], F32))
    rb2 = bes.enter_context(nc.sbuf_tensor("l1a_rb2", [16, 1], F32))
    vbias = bes.enter_context(nc.sbuf_tensor("l1a_vbias", [1, 1024], BF16))

    n1 = P.dma("sync", qkvb[:], c.w["L1_attn_qkv_b"])
    n2 = P.dma("sync", rb1[:], c.w["L1_attn_r_b1"])
    n3 = P.dma("sync", rb2[:], c.w["L1_attn_r_b2"])
    n4 = P.dma("gpsimd", vbias[:], c.w["L1A_VBROW"])
    ldv = stage_weight_bf16(c, c.w["L1_attn_qkv_w"], 0, 2048, 8, 8, A)
    ldr1 = stage_weight_bf16(c, c.w["L1_attn_r_w1"], 0, 0, 8, 2, B)
    ldr2 = stage_weight_bf16(c, c.w["L1_attn_r_w2"], 0, 0, 2, 1, B + 2048, mtile=16)
    P.wait("tensor", "gpsimd", n4)
    P.wait("tensor", "gpsimd", ldv)
    P.wait("tensor", "sync", n3)
    ph1 = contextlib.ExitStack()
    rsb = ph1.enter_context(nc.sbuf_tensor("l1a_rs", [16, 2048], F32))
    h1b = ph1.enter_context(nc.sbuf_tensor("l1a_h1", [128, 4096], BF16))
    kfm = ph1.enter_context(nc.sbuf_tensor("l1a_kfm", [16, 2048], BF16))
    ones65 = c.scr[:, 0:16 * 1040].rearrange(
        "p (s w) -> p s w", w=65)[:, :, 64:65]
    P.do("vector", lambda e: e.tensor_copy(
        ones65, c.ones_b[:, 0:1].rearrange(
            "p (s w) -> p s w", w=1).broadcast_to((128, 256, 1))), inc=1)
    v_ones65 = P.e["vector"].n
    P.wait("tensor", "vector", v_ones65)
    for tc in range(16):
        for et in range(2):
            b = et % 2
            bank_acquire(c, b)
            for dc in range(NDC):
                P.do("tensor", lambda e, b=b, dc=dc, tc=tc, et=et: e.matmul(
                    c.ps[b][:, :],
                    c.xln[:, dc * T + tc * 128:dc * T + tc * 128 + 128],
                    c.wst[:, A + dc * 1024 + et * 512:A + dc * 1024 + et * 512 + 512],
                    start=(dc == 0), stop=False), inc=0)
            P.do("tensor", lambda e, b=b, et=et: e.matmul(
                c.ps[b][:, :], c.ones_b[0:1, :],
                vbias[:, et * 512:et * 512 + 512],
                start=False, stop=True), inc=1)
            bank_done(c, b)
            bank_consume(c, b, "scalar")
            P.do("scalar", lambda e, b=b, tc=tc, et=et: e.activation(
                c.scr[:, tc * 1040 + et * 520:tc * 1040 + et * 520 + 520]
                .rearrange("p (h w) -> p h w", w=65)[:, :, 0:64],
                c.ps[b][:, :].rearrange("p (h w) -> p h w", w=64),
                AF.Copy), inc=1)
            bank_release(c, b, "scalar")
    P.wait("tensor", "gpsimd", ldr2)
    P.wait("tensor", "sync", n2)
    for mt in range(2):
        for qc in range(NQ):
            b = 2 + qc % 2
            bank_acquire(c, b)
            for dc in range(NDC):
                P.do("tensor", lambda e, b=b, dc=dc, mt=mt, qc=qc: e.matmul(
                    c.ps[b][:, :], wtile(c, B, 2, dc, mt),
                    xln_ap(c, dc, qc * TQ, TQ),
                    start=(dc == 0), stop=(dc == NDC - 1)),
                    inc=(1 if dc == NDC - 1 else 0))
            bank_done(c, b)
            bank_consume(c, b, "scalar")
            P.do("scalar", lambda e, b=b, mt=mt, qc=qc: e.activation(
                h1b[:, mt * 2048 + qc * 512:mt * 2048 + qc * 512 + 512],
                c.ps[b][:, :], AF.Relu, bias=rb1[:, mt:mt + 1]), inc=1)
            bank_release(c, b, "scalar")
    sc_n = P.e["scalar"].n
    P.wait("tensor", "scalar", sc_n)
    for qc in range(NQ):
        b = 4 + qc % 2
        bank_acquire(c, b)
        for mt in range(2):
            P.do("tensor", lambda e, b=b, mt=mt, qc=qc: e.matmul(
                c.ps[b][0:16, :],
                c.wst[:, B + 2048 + mt * 16:B + 2048 + mt * 16 + 16],
                h1b[:, mt * 2048 + qc * 512:mt * 2048 + qc * 512 + 512],
                start=(mt == 0), stop=(mt == 1)), inc=(1 if mt == 1 else 0))
        bank_done(c, b)
        bank_consume(c, b, "scalar")
        P.do("scalar", lambda e, b=b, qc=qc: e.activation(
            rsb[:, qc * 512:qc * 512 + 512], c.ps[b][0:16, :], AF.Sigmoid,
            bias=rb2[:]), inc=1)
        P.do("scalar", lambda e, qc=qc: e.activation(
            rsb[:, qc * 512:qc * 512 + 512], rsb[:, qc * 512:qc * 512 + 512],
            AF.Sigmoid), inc=1)
        bank_release(c, b, "scalar")
    P.wait("vector", "scalar")
    cmpb = h1b[:].bitcast(F32)[0:16, 0:2048]
    P.do("vector", lambda e: e.memset(m3[:, 0:1], 0.5), inc=1)
    P.do("vector", lambda e: e.memset(m3[:, 2:3], 0.7310586), inc=1)
    KKEEP = float(int(T * 0.4))
    for it in range(25):
        P.do("vector", lambda e: e.tensor_tensor(
            out=m3[:, 1:2], in0=m3[:, 0:1], in1=m3[:, 2:3], op=ALU.add), inc=1)
        P.do("vector", lambda e: e.tensor_scalar(
            out=m3[:, 1:2], in0=m3[:, 1:2], scalar1=0.5, scalar2=None,
            op0=ALU.mult), inc=1)
        P.do("vector", lambda e: e.tensor_scalar(
            out=cmpb, in0=rsb[:], scalar1=m3[:, 1:2], scalar2=None,
            op0=ALU.is_ge, op1=ALU.add, accum_out=cnt[:]), inc=1)
        P.do("vector", lambda e: e.tensor_scalar(
            out=gts[:], in0=cnt[:], scalar1=KKEEP, scalar2=None,
            op0=ALU.is_ge), inc=1)
        P.do("vector", lambda e: e.tensor_tensor(
            out=df2[:], in0=m3[:, 1:3], in1=m3[:, 0:2], op=ALU.subtract), inc=1)
        P.do("vector", lambda e: e.tensor_scalar(
            out=df2[:], in0=df2[:], scalar1=gts[:], scalar2=None,
            op0=ALU.mult), inc=1)
        P.do("vector", lambda e: e.tensor_tensor(
            out=m3[:, 0:3:2], in0=m3[:, 0:2], in1=df2[:], op=ALU.add), inc=1)
    P.do("vector", lambda e: e.tensor_scalar(
        out=kfm[:], in0=rsb[:], scalar1=m3[:, 0:1], scalar2=None,
        op0=ALU.is_ge), inc=1)
    v_n = P.e["vector"].n
    P.wait("tensor", "vector", v_n)
    for tc in range(16):
        b = 6 + tc % 2
        bank_acquire(c, b)
        P.do("tensor", lambda e, b=b, tc=tc: e.transpose(
            c.ps[b].bitcast(BF16)[0:128, 0:16], kfm[:, tc * 128:tc * 128 + 128],
            c.ident_b[0:16, 0:16]), inc=1)
        bank_done(c, b)
        bank_consume(c, b, "vector")
        P.do("vector", lambda e, b=b, tc=tc: e.tensor_copy(
            ktok[:, tc * 16:tc * 16 + 16],
            c.ps[b].bitcast(BF16)[0:128, 0:16]), inc=1)
        bank_release(c, b, "vector")
    ph1.close()
    qh2 = [bes.enter_context(nc.sbuf_tensor(f"l1a_qh{i}", [128, 2048], BF16))
           for i in range(2)]
    kh2 = [bes.enter_context(nc.sbuf_tensor(f"l1a_kh{i}", [128, 2048], BF16))
           for i in range(2)]
    expt = bes.enter_context(nc.sbuf_tensor("l1a_exp", [128, 2048], BF16))
    bcolp2 = [bes.enter_context(nc.sbuf_tensor(f"l1a_bcp{i}", [128, 32], F32))
              for i in range(2)]
    onesc = bes.enter_context(nc.sbuf_tensor("l1a_on1", [128, 1], BF16))
    pe_n = P.e["tensor"].n
    P.wait("gpsimd", "tensor", pe_n)
    ldq = stage_weight_bf16(c, c.w["L1_attn_qkv_w"], 0, 0, 8, 8, A)
    ldk = stage_weight_bf16(c, c.w["L1_attn_qkv_w"], 0, 1024, 8, 8, B)
    P.do("vector", lambda e: e.memset(onesc[:], 1.0), inc=1)
    v_ones = P.e["vector"].n
    P.wait("tensor", "gpsimd", ldk)
    P.wait("tensor", "vector", v_ones)
    for b in range(6):
        bank_acquire(c, b)
    qk_rd_pe2 = [0, 0]
    bcp_rd_s2 = [0, 0]
    dnv_bc_pe = 0
    exp_done = [0, 0]
    ctx_done = [0, 0]
    ctx_tail_v = 0
    ctx_tail_s = 0
    for hp in range(8):
        h0 = 2 * hp
        qh = qh2[hp % 2]
        kh = kh2[hp % 2]
        bcolp = bcolp2[hp % 2]
        P.wait("scalar", "tensor", qk_rd_pe2[hp % 2])
        for reg, buf, bcoli in ((A, qh, hp), (B, kh, 8 + hp)):
            for qc in range(NQ):
                b = 6 + qc % 2
                bank_acquire(c, b)
                for dc in range(NDC):
                    P.do("tensor", lambda e, b=b, dc=dc, reg=reg, hp=hp, qc=qc:
                         e.matmul(c.ps[b][:, :], wtile(c, reg, 8, dc, hp),
                                  xln_ap(c, dc, qc * TQ, TQ),
                                  start=(dc == 0), stop=(dc == NDC - 1)),
                         inc=(1 if dc == NDC - 1 else 0))
                bank_done(c, b)
                bank_consume(c, b, "scalar")
                P.do("scalar", lambda e, b=b, buf=buf, bcoli=bcoli, qc=qc:
                     e.activation(buf[:, qc * 512:qc * 512 + 512],
                                  c.ps[b][:, :], AF.Identity,
                                  bias=qkvb[:, bcoli:bcoli + 1]), inc=1)
                bank_release(c, b, "scalar")
        qk_wr_s = P.e["scalar"].n
        P.wait("vector", "scalar", bcp_rd_s2[hp % 2])
        for hsel in range(2):
            P.do("vector", lambda e, hsel=hsel, h0=h0, bcolp=bcolp:
                 e.tensor_scalar(
                     out=bcolp[:, hsel:32:2], in0=ktok[:, h0 + hsel:256:16],
                     scalar1=-1.0, scalar2=1e30, op0=ALU.add, op1=ALU.mult),
                 inc=1)
        v_bcp = P.e["vector"].n
        P.wait("scalar", "vector", v_bcp)
        P.wait("tensor", "scalar", qk_wr_s)
        for qc in range(NQ):
            for tc in range(17):
                par = tc % 2
                if tc < 16:
                    P.wait("tensor", "scalar", exp_done[par])
                    for hsel in range(2):
                        sb = (0 if par == 0 else 4) + hsel
                        po = hsel * 64
                        P.do("tensor", lambda e, sb=sb, po=po, tc=tc, qc=qc,
                             kh=kh, qh=qh:
                             e.matmul(c.ps[sb][:, :],
                                      kh[po:po + 64, tc * 128:tc * 128 + 128],
                                      qh[po:po + 64, qc * 512:qc * 512 + 512],
                                      start=True, stop=True), inc=1)
                    sc_pe = P.e["tensor"].n
                    if qc == NQ - 1 and tc == 15:
                        qk_rd_pe2[hp % 2] = sc_pe
                    P.wait("scalar", "tensor", sc_pe)
                    P.wait("scalar", "tensor", ctx_done[par])
                    for hsel in range(2):
                        sb = (0 if par == 0 else 4) + hsel
                        slot = (par * 2 + hsel) * 512
                        P.do("scalar", lambda e, sb=sb, slot=slot, tc=tc,
                             hsel=hsel, bcolp=bcolp: e.activation(
                                 expt[:, slot:slot + 512], c.ps[sb][:, :],
                                 AF.Exp, scale=0.125,
                                 bias=bcolp[:, tc * 2 + hsel:tc * 2 + hsel + 1]),
                             inc=1)
                    exp_done[par] = P.e["scalar"].n
                if tc > 0:
                    tp_ = tc - 1
                    parp = tp_ % 2
                    P.wait("tensor", "scalar", exp_done[parp])
                    if tp_ == 0:
                        P.wait("tensor", "vector", ctx_tail_v)
                        P.wait("tensor", "scalar", ctx_tail_s)
                    for hsel in range(2):
                        slot = (parp * 2 + hsel) * 512
                        cb = 2 + hsel
                        vcol = tp_ * 1040 + (h0 + hsel) * 65
                        P.do("tensor", lambda e, cb=cb, slot=slot, tp_=tp_,
                             vcol=vcol: e.matmul(
                                 c.ps[cb][0:65, :],
                                 c.scr[:, vcol:vcol + 65],
                                 expt[:, slot:slot + 512],
                                 start=(tp_ == 0), stop=(tp_ == 15)), inc=1)
                    ctx_done[parp] = P.e["tensor"].n
            for hsel in range(2):
                cb = 2 + hsel
                po = hsel * 64
                P.wait("scalar", "tensor", ctx_done[1])
                P.do("scalar", lambda e, cb=cb: e.activation(
                    dnvf[:], c.ps[cb][64:65, :], AF.Ln), inc=1)
                P.do("scalar", lambda e: e.activation(
                    dnvb[:], dnvf[:], AF.Exp, scale=-1.0), inc=1)
                s_d = P.e["scalar"].n
                bank_acquire(c, 6)
                P.wait("tensor", "scalar", s_d)
                P.do("tensor", lambda e: e.matmul(
                    c.ps[6][:, :], c.ones_b[0:1, :], dnvb[:],
                    start=True, stop=True), inc=1)
                bank_done(c, 6)
                bank_consume(c, 6, "scalar")
                P.wait("scalar", "vector", dnv_bc_pe)
                P.do("scalar", lambda e: e.activation(
                    dbcs[:], c.ps[6][:, :], AF.Copy), inc=1)
                bank_release(c, 6, "scalar")
                P.wait("vector", "scalar")
                P.do("vector", lambda e, cb=cb, po=po: e.tensor_tensor(
                    out=tsb[po:po + 64, :], in0=c.ps[cb][0:64, :],
                    in1=dbcs[0:64, :], op=ALU.mult), inc=1)
                dnv_bc_pe = P.e["vector"].n
                P.do("vector", lambda e, po=po, hp=hp, qc=qc: e.tensor_tensor(
                    out=c.xr[po:po + 64, hp * T + qc * TQ:hp * T + qc * TQ + TQ],
                    in0=c.xr[po:po + 64, hp * T + qc * TQ:hp * T + qc * TQ + TQ],
                    in1=tsb[po:po + 64, :], op=ALU.add), inc=1)
            ctx_tail_v = P.e["vector"].n
            ctx_tail_s = P.e["scalar"].n
        bcp_rd_s2[hp % 2] = P.e["scalar"].n
    P.barrier()
    bes.close()


def b_l3_attn(c):
    P = c.P
    emit_ln(c)
    bes = contextlib.ExitStack()
    nc = c.nc
    A, B = 0, 8192
    vwst = bes.enter_context(nc.sbuf_tensor("l3a_vw", [128, 1024], BF16))
    vpair = bes.enter_context(nc.sbuf_tensor("l3a_vp", [128, 2080], BF16))
    qh = bes.enter_context(nc.sbuf_tensor("l3a_qh", [128, 2048], BF16))
    kh = bes.enter_context(nc.sbuf_tensor("l3a_kh", [128, 2048], BF16))
    onesc3 = bes.enter_context(nc.sbuf_tensor("l3a_on1", [128, 1], BF16))
    expt = bes.enter_context(nc.sbuf_tensor("l3a_exp", [128, 2048], BF16))
    dnvf = bes.enter_context(nc.sbuf_tensor("l3a_dnvf", [1, 512], F32))
    dnvb = bes.enter_context(nc.sbuf_tensor("l3a_dnvb", [1, 512], BF16))
    dbcs = bes.enter_context(nc.sbuf_tensor("l3a_dbc", [128, 512], BF16))
    bo_sb = bes.enter_context(nc.sbuf_tensor("l3a_bo", [128, 8], F32))
    trib = bes.enter_context(nc.sbuf_tensor("l3a_trib", [128, 128], BF16))

    nbo = P.dma("sync", bo_sb[:], c.w["L3_attn_bo"])
    P.do("vector", lambda e: e.tensor_copy(trib[:], c.tri_f[:]), inc=1)
    ldq = stage_weight_bf16(c, c.w["L3_attn_Wq"], 0, 0, 8, 8, A)
    ldk = stage_weight_bf16(c, c.w["L3_attn_Wk"], 0, 0, 8, 8, B)
    P.wait("tensor", "gpsimd", ldk)
    for b in range(6):
        bank_acquire(c, b)
    P.do("vector", lambda e: e.memset(onesc3[:], 1.0), inc=1)
    P.do("vector", lambda e: e.tensor_copy(
        vpair[:].rearrange("p (s w) -> p s w", w=65)[:, :, 64:65],
        c.ones_b[:, 0:1].rearrange(
            "p (s w) -> p s w", w=1).broadcast_to((128, 32, 1))), inc=1)
    v_on3 = P.e["vector"].n
    P.wait("tensor", "vector", v_on3)
    trib_bc = trib[:].rearrange("p (u n) -> p u n", u=1).broadcast_to((128, 2, 128))
    pe_prev_hp = 0
    dnv_bc_pe = 0
    exp_done = [0, 0]
    ctx_done = [0, 0]
    msk_done = [0, 0]
    ctx_tail_v = 0
    ctx_tail_s = 0
    for hp in range(8):
        h0 = 2 * hp
        P.wait("vector", "tensor", pe_prev_hp)
        for reg, buf in ((A, qh), (B, kh)):
            for qc in range(NQ):
                b = 6 + qc % 2
                bank_acquire(c, b)
                for dc in range(NDC):
                    P.do("tensor", lambda e, b=b, dc=dc, reg=reg, hp=hp, qc=qc:
                         e.matmul(c.ps[b][:, :], wtile(c, reg, 8, dc, hp),
                                  xln_ap(c, dc, qc * TQ, TQ),
                                  start=(dc == 0), stop=(dc == NDC - 1)),
                         inc=(1 if dc == NDC - 1 else 0))
                bank_done(c, b)
                bank_consume(c, b, "vector")
                P.do("vector", lambda e, b=b, buf=buf, qc=qc: e.tensor_copy(
                    buf[:, qc * 512:qc * 512 + 512], c.ps[b][:, :]), inc=1)
                bank_release(c, b, "vector")
        P.wait("gpsimd", "tensor", pe_prev_hp)
        ldv = stage_weight_bf16(c, c.w["L3_attn_Wv"], 0, hp * 128, 8, 1,
                                0, mtile=128, dst=vwst)
        P.wait("tensor", "gpsimd", ldv)
        for tc in range(16):
            b = 6 + tc % 2
            bank_acquire(c, b)
            for dc in range(NDC):
                P.do("tensor", lambda e, b=b, dc=dc, tc=tc: e.matmul(
                    c.ps[b][:, 0:128],
                    c.xln[:, dc * T + tc * 128:dc * T + tc * 128 + 128],
                    vwst[:, dc * 128:dc * 128 + 128],
                    start=(dc == 0), stop=(dc == NDC - 1)),
                    inc=(1 if dc == NDC - 1 else 0))
            bank_done(c, b)
            bank_consume(c, b, "vector")
            P.do("vector", lambda e, b=b, tc=tc: e.tensor_copy(
                vpair[:, tc * 130:tc * 130 + 130].rearrange(
                    "p (h w) -> p h w", w=65)[:, :, 0:64],
                c.ps[b][:, 0:128].rearrange("p (h w) -> p h w", w=64)), inc=1)
            bank_release(c, b, "vector")
        qkv_wr_v = P.e["vector"].n
        P.wait("tensor", "vector", qkv_wr_v)
        for qc in range(NQ):
            ntc = 4 * qc + 4
            for tc in range(ntc + 1):
                par = tc % 2
                if tc < ntc:
                    k = tc - 4 * qc
                    off = 128 * k if k > 0 else 0
                    w = 512 - off
                    P.wait("tensor", "scalar", exp_done[par])
                    for hsel in range(2):
                        sb = (0 if par == 0 else 4) + hsel
                        po = hsel * 64
                        P.do("tensor", lambda e, sb=sb, po=po, tc=tc, qc=qc,
                             off=off, w=w: e.matmul(
                                 c.ps[sb][:, 0:w],
                                 kh[po:po + 64, tc * 128:tc * 128 + 128],
                                 qh[po:po + 64,
                                    qc * 512 + off:qc * 512 + 512],
                                 start=True, stop=True), inc=1)
                    sc_pe = P.e["tensor"].n
                    P.wait("scalar", "tensor", sc_pe)
                    P.wait("scalar", "tensor", ctx_done[par])
                    for hsel in range(2):
                        sb = (0 if par == 0 else 4) + hsel
                        slot = (par * 2 + hsel) * 512
                        P.do("scalar", lambda e, sb=sb, slot=slot, off=off,
                             w=w: e.activation(
                                 expt[:, slot + off:slot + 512],
                                 c.ps[sb][:, 0:w], AF.Exp, scale=0.125),
                             inc=1)
                    exp_done[par] = P.e["scalar"].n
                    if k >= 0:
                        mb_ = par * 1024 + off
                        vslot = expt[:, mb_:mb_ + 640].rearrange(
                            "p (u n) -> p u n", n=128)[:, 0:5:4, :]
                        P.wait("vector", "scalar", exp_done[par])
                        P.do("vector", lambda e, vslot=vslot: e.tensor_tensor(
                            out=vslot, in0=vslot, in1=trib_bc,
                            op=ALU.mult), inc=1)
                        msk_done[par] = P.e["vector"].n
                if tc > 0:
                    tp_ = tc - 1
                    parp = tp_ % 2
                    kp = tp_ - 4 * qc
                    offp = 128 * kp if kp > 0 else 0
                    P.wait("tensor", "scalar", exp_done[parp])
                    if kp >= 0:
                        P.wait("tensor", "vector", msk_done[parp])
                    if tp_ == 0:
                        P.wait("tensor", "vector", ctx_tail_v)
                        P.wait("tensor", "scalar", ctx_tail_s)
                    for hsel in range(2):
                        slot = (parp * 2 + hsel) * 512
                        cb = 2 + hsel
                        vcol = tp_ * 130 + hsel * 65
                        P.do("tensor", lambda e, cb=cb, slot=slot, tp_=tp_,
                             vcol=vcol, offp=offp, ntc=ntc: e.matmul(
                                 c.ps[cb][0:65, offp:512],
                                 vpair[:, vcol:vcol + 65],
                                 expt[:, slot + offp:slot + 512],
                                 start=(tp_ == 0), stop=(tp_ == ntc - 1),
                                 skip_group_check=True), inc=1)
                    ctx_done[parp] = P.e["tensor"].n
            for hsel in range(2):
                cb = 2 + hsel
                po = hsel * 64
                P.wait("scalar", "tensor", ctx_done[1])
                P.do("scalar", lambda e, cb=cb: e.activation(
                    dnvf[:], c.ps[cb][64:65, :], AF.Ln), inc=1)
                P.do("scalar", lambda e: e.activation(
                    dnvb[:], dnvf[:], AF.Exp, scale=-1.0), inc=1)
                s_d = P.e["scalar"].n
                bank_acquire(c, 6)
                P.wait("tensor", "scalar", s_d)
                P.do("tensor", lambda e: e.matmul(
                    c.ps[6][:, :], c.ones_b[0:1, :], dnvb[:],
                    start=True, stop=True), inc=1)
                bank_done(c, 6)
                bank_consume(c, 6, "vector")
                P.do("vector", lambda e: e.tensor_copy(
                    dbcs[:], c.ps[6][:, :]), inc=1)
                bank_release(c, 6, "vector")
                P.do("vector", lambda e, cb=cb, po=po, hp=hp, qc=qc:
                     e.tensor_tensor(
                         out=c.scr[po:po + 64,
                                   hp * 2048 + qc * 512:hp * 2048 + qc * 512 + 512],
                         in0=c.ps[cb][0:64, :], in1=dbcs[0:64, :],
                         op=ALU.mult), inc=1)
                dnv_bc_pe = P.e["vector"].n
            ctx_tail_v = P.e["vector"].n
            ctx_tail_s = P.e["scalar"].n
        pe_prev_hp = P.e["tensor"].n
    pe_n = P.e["tensor"].n
    P.wait("gpsimd", "tensor", pe_n)
    ldo = stage_weight_bf16(c, c.w["L3_attn_Wo"], 0, 0, 8, 8, A)
    P.wait("tensor", "gpsimd", ldo)
    P.wait("tensor", "vector")
    P.wait("tensor", "sync", nbo)
    for qc in range(NQ):
        for dct in range(8):
            b = dct % 2
            bank_acquire(c, b)
            for cc in range(8):
                P.do("tensor", lambda e, b=b, cc=cc, dct=dct, qc=qc: e.matmul(
                    c.ps[b][:, :], wtile(c, A, 8, cc, dct),
                    c.scr[:, cc * 2048 + qc * 512:cc * 2048 + qc * 512 + 512],
                    start=(cc == 0), stop=(cc == 7)),
                    inc=(1 if cc == 7 else 0))
            bank_done(c, b)
            bank_consume(c, b, "vector")
            P.do("vector", lambda e, b=b, dct=dct, qc=qc: e.scalar_tensor_tensor(
                out=xr_ap(c, dct, qc * TQ, TQ), in0=c.ps[b][:, :],
                scalar=bo_sb[:, dct:dct + 1], in1=xr_ap(c, dct, qc * TQ, TQ),
                op0=ALU.add, op1=ALU.add), inc=1)
            bank_release(c, b, "vector")
    P.barrier()
    bes.close()



def b_l1_ffn(c):
    P = c.P
    emit_ln(c)
    bes = contextlib.ExitStack()
    H = 4096
    b1_sb = bes.enter_context(c.nc.sbuf_tensor("l1f_b1", [128, 32], F32))
    b2_sb = bes.enter_context(c.nc.sbuf_tensor("l1f_b2", [128, NDC], F32))
    P.dma("sync", b1_sb[:], c.w["L1_ffn_b1"])
    P.dma("sync", b2_sb[:], c.w["L1_ffn_b2"])
    P.wait("scalar", "sync")
    P.wait("vector", "sync")

    region = [0, 8192]
    region_pe = [0, 0]
    stages_q = ([("L1_ffn_w1", 0, hs * 1024) for hs in range(4)] +
                [("L1_ffn_w2", hs * 1024, 0) for hs in range(4)])

    def l1f_stage(tag):
        w_name, k0, m0 = stages_q[tag]
        r = region[tag % 2]
        P.wait("gpsimd", "tensor", region_pe[tag % 2])
        ld = stage_weight_bf16(c, c.w[w_name], k0, m0, 8, 8, r)
        return r, ld

    for q in range(NQ):
        t0 = q * TQ
        tag = 0
        nxt = l1f_stage(0)
        for hs in range(4):
            r, ld = nxt
            P.wait("tensor", "gpsimd", ld)
            if tag + 1 < 8:
                nxt = l1f_stage(tag + 1)
            for hl in range(8):
                hc = hs * 8 + hl
                b = hc % 4
                bank_acquire(c, b)
                for dc in range(NDC):
                    P.do("tensor", lambda e, b=b, r=r, dc=dc, hl=hl, t0=t0:
                         e.matmul(c.ps[b][:, :], wtile(c, r, 8, dc, hl),
                                  xln_ap(c, dc, t0, TQ),
                                  start=(dc == 0), stop=(dc == NDC - 1)),
                         inc=(1 if dc == NDC - 1 else 0))
                bank_done(c, b)
                bank_consume(c, b, "scalar")
                P.do("scalar", lambda e, b=b, hc=hc: e.activation(
                    c.scr[:, hc * 512:hc * 512 + 512], c.ps[b][:, :], AF.Gelu,
                    bias=b1_sb[:, hc:hc + 1]), inc=1)
                bank_release(c, b, "scalar")
            region_pe[tag % 2] = P.e["tensor"].n
            tag += 1
        sc_n = P.e["scalar"].n
        for hs in range(4):
            r, ld = nxt
            P.wait("tensor", "gpsimd", ld)
            if tag + 1 < 8:
                nxt = l1f_stage(tag + 1)
            if hs == 0:
                for b in range(8):
                    bank_acquire(c, b)
                P.wait("tensor", "scalar", sc_n)
            for dc in range(NDC):
                for hl in range(8):
                    hc = hs * 8 + hl
                    P.do("tensor", lambda e, dc=dc, r=r, hl=hl, hc=hc, hs=hs:
                         e.matmul(c.ps[dc][:, :], wtile(c, r, 8, hl, dc),
                                  c.scr[:, hc * 512:hc * 512 + 512],
                                  start=(hs == 0 and hl == 0),
                                  stop=(hs == 3 and hl == 7)),
                         inc=(1 if hl == 7 else 0))
            region_pe[tag % 2] = P.e["tensor"].n
            tag += 1
        for dc in range(NDC):
            bank_done(c, dc)
            bank_consume(c, dc, "vector")
            P.do("vector", lambda e, dc=dc, t0=t0: e.scalar_tensor_tensor(
                out=xr_ap(c, dc, t0, TQ), in0=c.ps[dc][:, :],
                scalar=b2_sb[:, dc:dc + 1], in1=xr_ap(c, dc, t0, TQ),
                op0=ALU.add, op1=ALU.add), inc=1)
            bank_release(c, dc, "vector")
    P.barrier()
    bes.close()

def b_identity(c):
    pass


def finish(c):
    P = c.P
    P.barrier()
    n = P.dma("sync", c.x_out.rearrange("(ch p) t -> p ch t", p=128),
              c.xr[:].rearrange("p (ch t) -> p ch t", ch=NDC))
    P.wait("sync", "sync", n)


def _w(name, tr=None, srckey=None):
    return (name, tr, srckey)


def _tT(a):
    return a.T


def _tB(a):
    return a.reshape(-1, 128).T


BLOCKS = [
    ("L0_attn", b_l0_attn, [
        _w("L0_attn_in_proj_w"), _w("L0_attn_out_proj_w"),
        _w("L0_attn_W_re"), _w("L0_attn_W_im"),
        _w("L0THETA", lambda a: np.ascontiguousarray(
            a.reshape(2048).reshape(16, 128).T), srckey="L0_attn_theta"),
        _w("L0TAPS", lambda a: _l0taps(a), srckey="L0_attn_conv_k"),
        _w("L0CONVB", lambda a: np.ascontiguousarray(
            np.pad(a, (0, 96)).reshape(33, 128).T), srckey="L0_attn_conv_b"),
        _w("L0SSC", lambda a: a.reshape(32, 1), srckey="L0_attn_score_scale"),
        _w("L0NSC", lambda a: a.reshape(128, 1), srckey="L0_attn_norm_scale"),
        _w("L0DERIV", lambda a: a.reshape(1, 3), srckey="L0_attn_deriv_logits"),
        _w("L0SLOPES", None, srckey="__L0SLOPES__"),
        _w("L0DIST", None), _w("L0SEL", None), _w("L0HSEL", None),
        _w("L0HSEL2", None), _w("IDENT128", None)]),
    ("L0_ffn", b_l0_ffn, [_w("L0_ffn_w_gate", _tT), _w("L0_ffn_w_up", _tT),
                          _w("L0_ffn_w_down", _tT)]),
    ("L1_attn", b_l1_attn, [
        _w("L1_attn_qkv_w", _tT), _w("L1_attn_qkv_b", _tB),
        _w("L1A_VBROW", lambda a: a[2048:3072].reshape(1, 1024),
           srckey="L1_attn_qkv_b"),
        _w("L1_attn_r_w1", _tT), _w("L1_attn_r_b1", _tB),
        _w("L1_attn_r_w2", _tT), _w("L1_attn_r_b2", lambda a: a.reshape(16, 1)),
        _w("IDENT128", None)]),
    ("L1_ffn", b_l1_ffn, [_w("L1_ffn_w1"), _w("L1_ffn_b1", _tB),
                          _w("L1_ffn_w2"), _w("L1_ffn_b2", _tB)]),
    ("L2_attn", b_l2_attn, [_w("L2_attn_w_q"), _w("L2_attn_w_k"),
                            _w("L2_attn_w_v"), _w("L2_attn_w_o")]),
    ("L2_ffn", b_l2_ffn, [_w("L2_ffn_Wr"), _w("L2_ffn_br", lambda a: a.reshape(1, 8)),
                          _w("L2_ffn_W1"), _w("L2_ffn_b1", lambda a: np.ascontiguousarray(a.reshape(8, 8, 128).transpose(0, 2, 1))),
                          _w("L2_ffn_W2"),
                          _w("L2B2T", lambda a: np.ascontiguousarray(
                              a.reshape(8, 8, 128).transpose(0, 2, 1)),
                             srckey="L2_ffn_b2"),
                          _w("IDENT128", None), _w("TRI128", None)]),
    ("L3_attn", b_l3_attn, [_w("L3_attn_Wq", _tT), _w("L3_attn_Wk", _tT),
                          _w("L3_attn_Wv", _tT), _w("L3_attn_Wo", _tT),
                          _w("L3_attn_bo", _tB), _w("IDENT128", None),
                          _w("TRI128", None)]),
    ("L3_ffn", b_l3_ffn, [_w("L3_ffn_gate_w", _tT),
                          _w("L3_ffn_expert_w", lambda a: a.transpose(0, 2, 1)),
                          _w("IDENT128", None)]),
]

GROUPS = [[0, 1, 2, 3, 4, 5, 6, 7]]


def _l0taps(a):
    ck = np.pad(a[:, 0, :], ((0, 0), (0, 96)))
    out = np.zeros((128, 132), np.float32)
    for ct in range(33):
        for k in range(4):
            out[:, ct * 4 + k] = ck[k, ct * 128:ct * 128 + 128]
    return out


def _l0dist():
    d = np.zeros((32, 2048), np.float32)
    t = np.arange(2048, dtype=np.float32)
    d[:28] = 2047.0 - t
    d[28:] = t
    return d


def _l0sel():
    s = np.zeros((32, 2048), np.float32)
    for ch in range(2048):
        s[ch // 64, ch] = 1.0
    return s


_CONSTS = {"IDENT128": lambda: np.eye(128, dtype=np.float32),
           "TRI128": lambda: np.ascontiguousarray(
               np.tril(np.ones((128, 128), np.float32)).T),
           "L0DIST": _l0dist,
           "L0SEL": _l0sel,
           "L0HSEL": lambda: (np.arange(128)[:, None] // 64 ==
                              np.arange(2)[None, :]).astype(np.float32),
           "L0HSEL2": lambda: (np.arange(128)[None, :] // 64 ==
                               np.arange(2)[:, None]).astype(np.float32)}


def _prep_host(inputs):
    arrs = {}
    for name, builder, wl in BLOCKS:
        for key, tr, srck in wl:
            kk = srck or key
            if kk == "__L0SLOPES__":
                a = np.concatenate([np.asarray(inputs["L0_attn_decay_slopes"]),
                                    np.asarray(inputs["L0_attn_anchor_slopes"])]
                                   ).reshape(32, 1)
            else:
                a = np.asarray(inputs[kk]) if kk in inputs else _CONSTS[key]()
            if tr is not None:
                a = np.ascontiguousarray(tr(a))
            else:
                a = np.ascontiguousarray(a)
            arrs[key] = a
    return arrs


def _weight_decls(block_ids, arrs):
    decl = {}
    for bi in block_ids:
        for key, tr, srck in BLOCKS[bi][2]:
            a = arrs[key]
            dt = I32 if a.dtype == np.int32 else F32
            decl[key] = (a.shape, dt)
    return decl


def make_program(block_ids, arrs):
    nc = bass.Bass("TRN2", target_bir_lowering=False)
    es = contextlib.ExitStack()
    P = Prog(nc, es)
    c = make_ctx(nc, es, P, _weight_decls(block_ids, arrs))
    mark = es.enter_context(nc.sbuf_tensor("blkmark", [1, 8], F32))
    P.do("vector", lambda e: e.memset(mark[:], 0.0), inc=1)
    for bi in block_ids:
        P.do("vector", lambda e: e.pool_avg(mark[0:1, 0:1], mark[0:1, 0:8]),
             inc=1)
        BLOCKS[bi][1](c)
    P.do("vector", lambda e: e.pool_avg(mark[0:1, 0:1], mark[0:1, 0:8]),
         inc=1)
    finish(c)
    P.emit()
    es.close()
    return nc


def kernel(**inputs):
    x = np.asarray(inputs["L0_attn_x"], np.float32)
    arrs = _prep_host(inputs)
    cur = [np.ascontiguousarray(x[b].T) for b in range(8)]
    for g in GROUPS:
        nc = make_program(g, arrs)
        need = {k for bi in g for k, _, _s in BLOCKS[bi][2]}
        in_maps = [dict({k: arrs[k] for k in need}, x_in=cur[b]) for b in range(8)]
        res = run_bass_kernel_spmd(nc, in_maps, core_ids=list(range(8)))
        cur = [np.asarray(res.results[b]["x_out"]) for b in range(8)]
    out = np.stack([c.T for c in cur]).astype(np.float32)
    return out


# version 37
# speedup vs baseline: 1.0183x; 1.0106x over previous
import contextlib
import numpy as np
import ml_dtypes
import concourse.bass as bass
import concourse.mybir as mybir
from concourse.bass_utils import run_bass_kernel_spmd

F32 = mybir.dt.float32
BF16 = mybir.dt.bfloat16
I32 = mybir.dt.int32
AF = mybir.ActivationFunctionType
ALU = mybir.AluOpType
AX = mybir.AxisListType

T = 2048
D = 1024
NDC = D // 128
TQ = 512
NQ = T // TQ
ENGS = ("sync", "gpsimd", "tensor", "vector", "scalar")
BF = ml_dtypes.bfloat16


class _Eng:
    def __init__(self, sem):
        self.sem = sem
        self.n = 0
        self.q = []
        self.selfwait = 0


class Prog:
    def __init__(self, nc, es):
        self.nc = nc
        self.es = es
        self.e = {name: _Eng(es.enter_context(nc.semaphore(f"sem_{name}")))
                  for name in ENGS}

    def do(self, eng, fn, inc=0):
        E = self.e[eng]
        if eng != "tensor" and E.n > E.selfwait:
            E.q.append(lambda e, s=E.sem, v=E.n: e.wait_ge(s, v))
            E.selfwait = E.n
        if inc:
            E.n += inc

            def thunk(e, fn=fn, sem=E.sem, inc=inc):
                fn(e).then_inc(sem, inc)
            E.q.append(thunk)
        else:
            E.q.append(lambda e, fn=fn: fn(e))
        return E.n

    def dma(self, eng, out, in_, **kw):
        return self.do(eng, lambda e, o=out, i=in_, kw=kw:
                       e.dma_start(out=o, in_=i, **kw), inc=16)

    def wait(self, eng, other, val=None):
        O = self.e[other]
        v = O.n if val is None else val
        if v > 0:
            self.e[eng].q.append(lambda e, s=O.sem, v=v: e.wait_ge(s, v))

    def barrier(self):
        counts = {n: self.e[n].n for n in ENGS}
        for a in ENGS:
            for b in ENGS:
                if a != b and counts[b] > 0:
                    self.e[a].q.append(
                        lambda e, s=self.e[b].sem, v=counts[b]: e.wait_ge(s, v))

    def emit(self):
        with self.nc.Block() as blk:
            for name in ENGS:
                E = self.e[name]
                if E.q:
                    getattr(blk, name)(lambda e, q=E.q: [f(e) for f in q])


class Ctx:
    pass


def make_ctx(nc, es, P, weights):
    c = Ctx()
    c.nc, c.es, c.P = nc, es, P
    c.x_in = nc.dram_tensor("x_in", [D, T], F32, kind="ExternalInput").ap()
    c.x_out = nc.dram_tensor("x_out", [D, T], F32, kind="ExternalOutput").ap()
    c.w = {nm: nc.dram_tensor(nm, list(shape), dt, kind="ExternalInput").ap()
           for nm, (shape, dt) in weights.items()}

    c.xr = es.enter_context(nc.sbuf_tensor("xr", [128, NDC * T], F32))
    c.xln = es.enter_context(nc.sbuf_tensor("xln", [128, NDC * T], BF16))
    c.scr = es.enter_context(nc.sbuf_tensor("scr", [128, 16 * 1024 + 256], BF16))
    c.wst = es.enter_context(nc.sbuf_tensor("wst", [128, 16 * 1024], BF16))
    c.aux = es.enter_context(nc.sbuf_tensor("aux", [128, 512], F32))
    c.strow = es.enter_context(nc.sbuf_tensor("strow", [1, 1536], F32))
    c.ones_b = es.enter_context(nc.sbuf_tensor("ones_b", [128, 128], BF16))
    c.ones_f = es.enter_context(nc.sbuf_tensor("ones_f", [1, 128], F32))
    c.consts = es.enter_context(nc.sbuf_tensor("consts", [128, 12], F32))
    P.do("vector", lambda e: e.memset(c.ones_b[:], 1.0), inc=1)
    P.do("vector", lambda e: e.memset(c.ones_f[:], 1.0), inc=1)
    P.do("vector", lambda e: e.memset(c.consts[:, 0:1], 1e-5), inc=1)
    P.do("vector", lambda e: e.memset(c.consts[:, 1:2], 1e-5 * 2.0 ** -64), inc=1)
    P.do("vector", lambda e: e.memset(c.consts[:, 2:3], -32.0 * float(np.log(2.0))), inc=1)

    c.ps = [es.enter_context(nc.psum_tensor(f"ps{i}", [128, 512], F32))
            for i in range(8)]
    c.ps_n = [0] * 8
    c.ps_f = [None] * 8

    if "IDENT128" in c.w:
        c.ident_b = es.enter_context(nc.sbuf_tensor("ident_b", [128, 128], BF16))
        P.dma("gpsimd", c.ident_b[:], c.w["IDENT128"])
        c.ident_f = es.enter_context(nc.sbuf_tensor("ident_f", [128, 128], F32))
        P.dma("sync", c.ident_f[:], c.w["IDENT128"])
    if "TRI128" in c.w:
        c.tri_f = es.enter_context(nc.sbuf_tensor("tri_f", [128, 128], F32))
        P.dma("sync", c.tri_f[:], c.w["TRI128"])
    c.ones_cf = es.enter_context(nc.sbuf_tensor("ones_cf", [128, 1], F32))
    P.do("vector", lambda e: e.memset(c.ones_cf[:], 1.0), inc=1)
    P.do("vector", lambda e: e.memset(c.consts[:, 3:4], 0.0), inc=1)
    P.dma("sync", c.xr[:].rearrange("p (ch t) -> p ch t", ch=NDC),
          c.x_in.rearrange("(ch p) t -> p ch t", p=128))
    P.wait("tensor", "sync")
    P.wait("vector", "sync")
    P.wait("scalar", "sync")
    return c


def bank_acquire(c, b):
    if c.ps_f[b] is not None:
        eng, cnt = c.ps_f[b]
        c.P.wait("tensor", eng, cnt)


def bank_done(c, b):
    c.ps_n[b] = c.P.e["tensor"].n


def bank_consume(c, b, eng):
    c.P.wait(eng, "tensor", c.ps_n[b])


def bank_release(c, b, eng):
    c.ps_f[b] = (eng, c.P.e[eng].n)


def xr_ap(c, dc, t0=0, tn=T):
    return c.xr[:, dc * T + t0: dc * T + t0 + tn]


def xln_ap(c, dc, t0=0, tn=T):
    return c.xln[:, dc * T + t0: dc * T + t0 + tn]


def emit_ln(c):
    P = c.P
    slot_pe = [0, 0]

    def stage(q):
        t0 = q * TQ
        off = (q % 2) * 1024
        P.wait("vector", "tensor", slot_pe[q % 2])
        P.wait("scalar", "tensor", slot_pe[q % 2])
        for dc in range(NDC):
            P.do("vector", lambda e, dc=dc, t0=t0, off=off: e.tensor_copy(
                c.scr[:, dc * 2048 + off:dc * 2048 + off + 512],
                xr_ap(c, dc, t0, TQ)), inc=1)
            P.do("scalar", lambda e, dc=dc, t0=t0, off=off: e.activation(
                c.scr[:, dc * 2048 + off + 512:dc * 2048 + off + 1024],
                xr_ap(c, dc, t0, TQ), AF.Square), inc=1)
        return P.e["vector"].n, P.e["scalar"].n

    staged = stage(0)
    for q in range(NQ):
        t0 = q * TQ
        soff = (q % 2) * 1024
        r_mean = c.strow[0:1, 0:TQ]
        r_vb = c.strow[0:1, TQ:2 * TQ]
        r_a = c.strow[0:1, 2 * TQ:3 * TQ]
        v_n, s_n = staged
        for b, off in ((0, soff), (1, soff + 512)):
            bank_acquire(c, b)
            P.wait("tensor", "vector", v_n)
            P.wait("tensor", "scalar", s_n)
            for dc in range(NDC):
                P.do("tensor", lambda e, b=b, dc=dc, off=off: e.matmul(
                    c.ps[b][0:1, :], c.ones_b[:, 0:1],
                    c.scr[:, dc * 2048 + off:dc * 2048 + off + 512],
                    start=(dc == 0), stop=(dc == NDC - 1)),
                    inc=(1 if dc == NDC - 1 else 0))
            bank_done(c, b)
        slot_pe[q % 2] = P.e["tensor"].n
        if q + 1 < NQ:
            staged = stage(q + 1)
        bank_consume(c, 0, "vector")
        bank_consume(c, 1, "vector")
        P.do("vector", lambda e, r=r_mean: e.tensor_scalar(
            out=r, in0=c.ps[0][0:1, :], scalar1=1.0 / D,
            scalar2=None, op0=ALU.mult), inc=1)
        P.do("vector", lambda e, r=r_vb: e.tensor_scalar(
            out=r, in0=c.ps[1][0:1, :], scalar1=1.0 / D,
            scalar2=None, op0=ALU.mult), inc=1)
        bank_release(c, 0, "vector")
        bank_release(c, 1, "vector")
        P.do("vector", lambda e, r_a=r_a, r_mean=r_mean: e.tensor_tensor(
            out=r_a, in0=r_mean, in1=r_mean, op=ALU.mult), inc=1)
        P.do("vector", lambda e, r_vb=r_vb, r_a=r_a: e.tensor_tensor(
            out=r_vb, in0=r_vb, in1=r_a, op=ALU.subtract), inc=1)
        P.wait("scalar", "vector")
        P.do("scalar", lambda e, r_a=r_a, r_vb=r_vb: e.activation(
            r_a, r_vb, AF.Ln, bias=c.consts[0:1, 1:2], scale=2.0 ** -64), inc=1)
        P.do("scalar", lambda e, r_a=r_a: e.activation(
            r_a, r_a, AF.Exp, scale=-0.5, bias=c.consts[0:1, 2:3]), inc=1)
        P.wait("vector", "scalar")
        P.do("vector", lambda e, r_vb=r_vb, r_mean=r_mean, r_a=r_a:
             e.scalar_tensor_tensor(out=r_vb, in0=r_mean, scalar=-1.0,
                                    in1=r_a, op0=ALU.mult, op1=ALU.mult), inc=1)
        v_n2 = P.e["vector"].n
        for b, cb in ((2, 2 * TQ), (3, TQ)):
            bank_acquire(c, b)
            P.wait("tensor", "vector", v_n2)
            P.do("tensor", lambda e, b=b, cb=cb: e.matmul(
                c.ps[b][:, :], c.ones_f[:, :], c.strow[0:1, cb:cb + TQ],
                start=True, stop=True), inc=1)
            bank_done(c, b)
        bank_consume(c, 2, "vector")
        bank_consume(c, 3, "vector")
        for dc in range(NDC):
            P.do("vector", lambda e, dc=dc, t0=t0: e.tensor_tensor(
                out=c.aux[:, 0:TQ], in0=xr_ap(c, dc, t0, TQ), in1=c.ps[2][:, :],
                op=ALU.mult), inc=1)
            P.do("vector", lambda e, dc=dc, t0=t0: e.tensor_tensor(
                out=xln_ap(c, dc, t0, TQ), in0=c.aux[:, 0:TQ], in1=c.ps[3][:, :],
                op=ALU.add), inc=1)
        bank_release(c, 2, "vector")
        bank_release(c, 3, "vector")
    P.barrier()


def stage_weight_bf16(c, dram_ap, k0, m0, kc, mc, dst_col, mtile=128, dst=None,
                      q="gpsimd"):
    src = dram_ap[k0:k0 + kc * 128, m0:m0 + mc * mtile]
    src = src.rearrange("(ki p) mm -> p ki mm", p=128)
    base = c.wst if dst is None else dst
    d = base[:, dst_col:dst_col + kc * mc * mtile]
    d = d.rearrange("p (ki mm) -> p ki mm", ki=kc)
    return c.P.dma(q, d, src)


def wtile(c, dst_col, mc, ki, mi, mtile=128):
    col = dst_col + (ki * mc + mi) * mtile
    return c.wst[:, col:col + mtile]



def b_l0_attn(c):
    P = c.P
    emit_ln(c)
    bes = contextlib.ExitStack()
    ygq = None
    nc = c.nc
    A, B = 0, 8192
    TWO_PI = 6.283185307179586
    MAGIC = 12582912.0
    C1 = 6.28125
    C2 = 1.9353072e-03
    C2B = float(np.float32(np.float64(TWO_PI) - 6.28125))

    svst = None
    theta = bes.enter_context(nc.sbuf_tensor("l0a_th", [128, 16], F32))
    taps = bes.enter_context(nc.sbuf_tensor("l0a_tap", [128, 132], F32))
    convb = bes.enter_context(nc.sbuf_tensor("l0a_cb", [128, 33], F32))
    ssc = bes.enter_context(nc.sbuf_tensor("l0a_ssc", [32, 1], F32))
    slp = bes.enter_context(nc.sbuf_tensor("l0a_slp", [32, 1], F32))
    selk = bes.enter_context(nc.sbuf_tensor("l0a_selk", [32, 2048], BF16))
    hsel = bes.enter_context(nc.sbuf_tensor("l0a_hsel", [128, 2], BF16))
    hsel2 = bes.enter_context(nc.sbuf_tensor("l0a_hsel2", [2, 128], BF16))
    bdre = bes.enter_context(nc.sbuf_tensor("l0a_bdre", [128, 128], BF16))
    bdim = bes.enter_context(nc.sbuf_tensor("l0a_bdim", [128, 128], BF16))
    wres = bes.enter_context(nc.sbuf_tensor("l0a_wres", [64, 64], F32))
    nsc = bes.enter_context(nc.sbuf_tensor("l0a_nsc", [128, 1], F32))
    wpoly = bes.enter_context(nc.sbuf_tensor("l0a_wp", [128, 4], F32))
    wrow = bes.enter_context(nc.sbuf_tensor("l0a_wrow", [1, 3], F32))
    wsum = bes.enter_context(nc.sbuf_tensor("l0a_wsum", [1, 1], F32))
    ztail = bes.enter_context(nc.sbuf_tensor("l0a_zt", [128, 51], F32))
    _f32v = c.scr[:].bitcast(F32)
    zbuf2 = [bes.enter_context(nc.sbuf_tensor(f"l0a_zb{i}", [128, 515], F32))
             for i in range(2)]
    pwb = zbuf2[0][0:32, 0:512]
    svst = zbuf2[1][:].bitcast(BF16)[0:128, 0:256]
    xvb2 = [_f32v[:, 4096:4608], _f32v[:, 4608:5120]]
    kb2 = [_f32v[:, 5120:5632], _f32v[:, 5632:6144]]
    prd2 = [_f32v[:, 6144:6656], _f32v[:, 6656:7168]]
    phib2 = [_f32v[:, 7168:7680], _f32v[:, 7680:8192]]
    sinb2 = [bes.enter_context(nc.sbuf_tensor(f"l0a_sin{i}", [128, 512], BF16))
             for i in range(2)]
    cosb2 = [bes.enter_context(nc.sbuf_tensor(f"l0a_cos{i}", [128, 512], BF16))
             for i in range(2)]
    polyb2 = [bes.enter_context(nc.sbuf_tensor(f"l0a_poly{i}", [128, 512], BF16))
              for i in range(2)]
    rem2 = [bes.enter_context(nc.sbuf_tensor(f"l0a_rem{i}", [128, 512], BF16))
            for i in range(2)]
    imm2 = [bes.enter_context(nc.sbuf_tensor(f"l0a_imm{i}", [128, 512], BF16))
            for i in range(2)]
    npr1 = bes.enter_context(nc.sbuf_tensor("l0a_npr", [128, 1024], F32))
    npr2 = [npr1, npr1]
    ren2 = [bes.enter_context(nc.sbuf_tensor(f"l0a_ren{i}", [128, 512], BF16))
            for i in range(2)]
    imn2 = [bes.enter_context(nc.sbuf_tensor(f"l0a_imn{i}", [128, 512], BF16))
            for i in range(2)]
    rsbb = None
    carp = bes.enter_context(nc.sbuf_tensor("l0a_crp", [128, 32], F32))
    sqb2 = [kb2[1], kb2[0]]
    carden = bes.enter_context(nc.sbuf_tensor("l0a_cden", [32, 1], F32))
    srawb = bes.enter_context(nc.sbuf_tensor("l0a_sraw", [32, 515], F32))
    rsbb = srawb[:].bitcast(BF16)[0:2, 0:512]
    pwbf = bes.enter_context(nc.sbuf_tensor("l0a_pwbf", [32, 512], BF16))
    invb = bes.enter_context(nc.sbuf_tensor("l0a_inv", [32, 512], BF16))
    tmps = c.aux[0:32, 0:512]
    gzs2 = [bes.enter_context(nc.sbuf_tensor(f"l0a_gzs{i}", [128, 515], BF16))
            for i in range(2)]
    diagb2 = [bes.enter_context(nc.sbuf_tensor(f"l0a_diag{i}", [128, 512], BF16))
              for i in range(2)]
    gcv2 = [c.scr[:, 12288:12800], c.scr[:, 13312:13824]]
    gzt = bes.enter_context(nc.sbuf_tensor("l0a_gzt", [128, 48], BF16))

    ygq = c.scr
    n1 = P.dma("sync", theta[:], c.w["L0THETA"])
    n2 = P.dma("sync", taps[:], c.w["L0TAPS"])
    n3 = P.dma("sync", convb[:], c.w["L0CONVB"])
    n4 = P.dma("sync", ssc[:], c.w["L0SSC"])
    n5 = 0
    n6 = P.dma("gpsimd", selk[:], c.w["L0SEL"])
    n7 = P.dma("gpsimd", hsel[:], c.w["L0HSEL"])
    n8 = P.dma("gpsimd", hsel2[:], c.w["L0HSEL2"])
    n9 = P.dma("sync", wres[:], c.w["L0_attn_W_re"])
    n10 = P.dma("sync", nsc[:], c.w["L0NSC"])
    n11 = P.dma("sync", wrow[:], c.w["L0DERIV"])
    n12 = P.dma("sync", c.aux[0:32, 0:1], c.w["L0SLOPES"])
    P.wait("vector", "sync", n12)
    P.wait("vector", "gpsimd", n8)
    P.do("vector", lambda e: e.memset(c.consts[:, 4:5], 1.0), inc=1)
    P.do("vector", lambda e: e.memset(c.consts[:, 5:6], float(np.pi / 2)), inc=1)
    P.do("vector", lambda e: e.memset(c.consts[:, 6:7], MAGIC), inc=1)
    P.do("vector", lambda e: e.memset(c.consts[:, 7:8], -MAGIC), inc=1)
    P.do("vector", lambda e: e.memset(c.consts[:, 8:9], 0.25), inc=1)
    P.do("vector", lambda e: e.memset(ztail[:], 0.0), inc=1)
    P.do("vector", lambda e: e.memset(gzt[:], 0.0), inc=1)
    P.do("vector", lambda e: e.memset(carp[:], 0.0), inc=1)
    P.do("vector", lambda e: e.memset(carden[:], 0.0), inc=1)
    P.wait("scalar", "vector")
    P.do("scalar", lambda e: e.activation(slp[:], c.aux[0:32, 0:1], AF.Abs), inc=1)
    P.do("scalar", lambda e: e.activation(slp[:], slp[:], AF.Exp, scale=-1.0), inc=1)
    P.do("scalar", lambda e: e.activation(slp[:], slp[:], AF.Ln,
                                          bias=c.consts[0:32, 4:5]), inc=1)
    P.wait("vector", "scalar")
    P.do("vector", lambda e: e.tensor_scalar(
        out=c.aux[0:32, 1:2], in0=c.aux[0:32, 0:1], scalar1=0.0, scalar2=None,
        op0=ALU.max), inc=1)
    P.do("vector", lambda e: e.tensor_tensor(
        out=slp[:], in0=slp[:], in1=c.aux[0:32, 1:2], op=ALU.add), inc=1)
    P.do("vector", lambda e: e.tensor_scalar(
        out=slp[:], in0=slp[:], scalar1=-1.0, scalar2=None, op0=ALU.mult), inc=1)
    P.wait("scalar", "sync", n11)
    P.do("scalar", lambda e: e.activation(wrow[:], wrow[:], AF.Exp,
                                          accum_out=wsum[:]), inc=1)
    P.wait("vector", "scalar")
    P.do("vector", lambda e: e.reciprocal(wsum[:], wsum[:]), inc=1)
    P.do("vector", lambda e: e.tensor_tensor(
        out=wrow[:], in0=wrow[:], in1=wsum[:].broadcast_to((1, 3)),
        op=ALU.mult), inc=1)
    v_n = P.e["vector"].n
    bank_acquire(c, 7)
    P.wait("tensor", "vector", v_n)
    P.do("tensor", lambda e: e.matmul(c.ps[7][:, 0:3], c.ones_f[:, :], wrow[:],
                                      start=True, stop=True), inc=1)
    bank_done(c, 7)
    bank_consume(c, 7, "vector")
    P.do("vector", lambda e: e.tensor_copy(wpoly[:, 0:3], c.ps[7][:, 0:3]), inc=1)
    P.do("vector", lambda e: e.tensor_scalar(
        out=wpoly[:, 3:4], in0=wpoly[:, 2:3], scalar1=-1.0, scalar2=None,
        op0=ALU.mult), inc=1)
    bank_release(c, 7, "vector")
    P.wait("vector", "sync", n10)
    P.do("vector", lambda e: e.memset(bdre[:], 0.0), inc=1)
    P.do("vector", lambda e: e.memset(bdim[:], 0.0), inc=1)
    for nm, bd, ncol in (("L0_attn_W_re", bdre, 0), ("L0_attn_W_im", bdim, 64)):
        if ncol:
            P.wait("sync", "vector")
            nn = P.dma("sync", wres[:], c.w[nm])
            P.wait("vector", "sync", nn)
        else:
            P.wait("vector", "sync", n9)
        P.do("vector", lambda e, bd=bd, ncol=ncol: e.tensor_scalar(
            out=bd[0:64, 0:64], in0=wres[:], scalar1=nsc[ncol:ncol + 64, 0:1],
            scalar2=None, op0=ALU.mult), inc=1)
        P.do("vector", lambda e, bd=bd: e.tensor_copy(
            bd[64:128, 64:128], bd[0:64, 0:64]), inc=1)

    inproj_pe = [0, 0, 0, 0]
    sraw_pe = 0
    dist_rd_v = 0
    msq_pe2 = [0, 0]
    y_pe2 = [0, 0]
    zb_rd_v = [0, 0]
    gz_rd_pe = [0]
    gz_wr_s = [0, 0]
    s1dve = [0, 0]
    aff_s = [0, 0]
    diag_rd_pe2 = [0, 0]
    diag_v2 = [0, 0]
    sq_gp = [0]

    def emit_s1_head(ct, t0):
        p = ct % 2
        bx = ct % 2
        zbuf = zbuf2[p]
        xvb = xvb2[p]
        gct = 16 + ct
        P.wait("vector", "tensor", diag_rd_pe2[p])
        P.do("vector", lambda e, gct=gct, p=p: e.tensor_tensor(
            out=diagb2[p][:].rearrange("p (k n) -> p k n", k=4),
            in0=c.ident_b[:].rearrange("p (u n) -> p u n", u=1).broadcast_to(
                (128, 4, 128)),
            in1=taps[:, gct * 4:gct * 4 + 4].rearrange(
                "p (k u) -> p k u", u=1).broadcast_to((128, 4, 128)),
            op=ALU.mult), inc=1)
        diag_v2[p] = P.e["vector"].n
        bank_consume(c, bx, "scalar")
        P.wait("scalar", "vector", zb_rd_v[p])
        P.do("scalar", lambda e, bx=bx, zbuf=zbuf: e.activation(
            zbuf[:, 3:515], c.ps[bx][:, :], AF.Copy), inc=1)
        bank_release(c, bx, "scalar")
        P.do("scalar", lambda e, ct=ct, zbuf=zbuf: e.activation(
            zbuf[:, 0:3], ztail[:, ct * 3:ct * 3 + 3], AF.Copy), inc=1)
        P.do("scalar", lambda e, ct=ct, zbuf=zbuf: e.activation(
            ztail[:, ct * 3:ct * 3 + 3], zbuf[:, 512:515], AF.Copy), inc=1)
        s1_act = P.e["scalar"].n
        P.wait("vector", "scalar", s1_act)
        P.wait("vector", "scalar", aff_s[p])
        P.do("vector", lambda e, ct=ct, zbuf=zbuf, xvb=xvb: e.tensor_scalar(
            out=xvb, in0=zbuf[:, 0:512], scalar1=taps[:, ct * 4:ct * 4 + 1],
            scalar2=convb[:, ct:ct + 1], op0=ALU.mult, op1=ALU.add), inc=1)
        for kk in range(1, 4):
            P.do("vector", lambda e, kk=kk, ct=ct, zbuf=zbuf, xvb=xvb:
                 e.scalar_tensor_tensor(
                     out=xvb, in0=zbuf[:, kk:kk + 512],
                     scalar=taps[:, ct * 4 + kk:ct * 4 + kk + 1], in1=xvb,
                     op0=ALU.mult, op1=ALU.add), inc=1)
        zb_rd_v[p] = P.e["vector"].n
        return zb_rd_v[p]

    def emit_s1_trig(ct, conv_n):
        p = ct % 2
        xvb = xvb2[p]
        kb = kb2[p]
        sqb = sqb2[p]
        P.wait("scalar", "vector", s1dve[p])
        P.wait("scalar", "vector", conv_n)
        P.do("scalar", lambda e, ct=ct, p=p, xvb=xvb: e.activation(
            phib2[p][:], xvb, AF.Copy, scale=theta[:, ct:ct + 1]), inc=1)
        P.do("scalar", lambda e, p=p, kb=kb: e.activation(
            kb, phib2[p][:], AF.Identity, scale=1.0 / TWO_PI,
            bias=c.consts[:, 6:7]), inc=1)
        P.do("scalar", lambda e, kb=kb: e.activation(
            kb, kb, AF.Identity, bias=c.consts[:, 7:8]), inc=1)
        act_k1 = P.e["scalar"].n
        P.wait("vector", "scalar", act_k1)
        for i, cc_ in enumerate((C1, C2B)):
            P.do("vector", lambda e, cc_=cc_, i=i, p=p, kb=kb:
                 e.scalar_tensor_tensor(
                     out=prd2[p][:], in0=kb, scalar=-cc_,
                     in1=(phib2[p][:] if i == 0 else prd2[p][:]),
                     op0=ALU.mult, op1=ALU.add), inc=1)
        P.do("vector", lambda e, xvb=xvb, sqb=sqb: e.tensor_tensor(
            out=sqb, in0=xvb, in1=xvb, op=ALU.mult), inc=1)
        casc1_n = P.e["vector"].n
        P.wait("scalar", "vector", casc1_n)
        P.do("scalar", lambda e, p=p: e.activation(
            sinb2[p][:], prd2[p][:], AF.Sin), inc=1)
        P.do("scalar", lambda e, p=p: e.activation(
            cosb2[p][:], prd2[p][:], AF.Sin, scale=0.5), inc=1)
        P.do("scalar", lambda e, p=p, kb=kb: e.activation(
            kb, cosb2[p][:], AF.Square), inc=1)
        P.do("scalar", lambda e, xvb=xvb: e.activation(
            xvb, xvb, AF.Identity, scale=wpoly[:, 1:2],
            bias=wpoly[:, 0:1]), inc=1)
        act_k2 = P.e["scalar"].n
        aff_s[p] = act_k2
        P.wait("vector", "scalar", act_k2)
        P.do("vector", lambda e, p=p, kb=kb: e.tensor_scalar(
            out=cosb2[p][:], in0=kb, scalar1=-2.0, scalar2=1.0,
            op0=ALU.mult, op1=ALU.add), inc=1)
        P.do("vector", lambda e, p=p, xvb=xvb, sqb=sqb: e.scalar_tensor_tensor(
            out=polyb2[p][:], in0=sqb, scalar=wpoly[:, 3:4], in1=xvb,
            op0=ALU.mult, op1=ALU.add), inc=1)
        s1dve[p] = P.e["vector"].n
        return act_k2


    def emit_tail_A(ct, t0, q, s2_n, first):
        p = ct % 2
        remb = rem2[p]
        immb = imm2[p]
        numpair = npr2[p]
        bank_acquire(c, 4)
        if first:
            P.wait("tensor", "vector", v_sr)
            P.wait("tensor", "scalar", sr_s)
        P.do("tensor", lambda e, ct=ct: e.matmul(
            c.ps[4][:, :], selk[:, ct * 128:ct * 128 + 128], pwbf[:],
            start=True, stop=True), inc=1)
        bank_done(c, 4)
        bank_consume(c, 4, "vector")
        P.wait("vector", "scalar", s2_n)
        P.wait("vector", "tensor", msq_pe2[p])
        P.do("vector", lambda e, p=p: e.tensor_tensor(
            out=polyb2[p][:], in0=polyb2[p][:], in1=c.ps[4][:, :],
            op=ALU.mult), inc=1)
        P.do("vector", lambda e, p=p, remb=remb: e.tensor_tensor(
            out=remb[:], in0=polyb2[p][:], in1=cosb2[p][:], op=ALU.mult), inc=1)
        P.do("vector", lambda e, p=p, immb=immb: e.tensor_tensor(
            out=immb[:], in0=polyb2[p][:], in1=sinb2[p][:], op=ALU.mult), inc=1)
        bank_release(c, 4, "vector")
        P.do("vector", lambda e, q=q, ct=ct, numpair=numpair, remb=remb:
             e.tensor_tensor_scan(
                 out=numpair[:, 0:512],
                 data0=c.ones_b[:, 0:1].broadcast_to((128, 512)),
                 data1=remb[:],
                 initial=(0.0 if q == 0 else carp[:, 2 * ct:2 * ct + 1]),
                 op0=ALU.mult, op1=ALU.add), inc=1)
        P.do("vector", lambda e, q=q, ct=ct, numpair=numpair, immb=immb:
             e.tensor_tensor_scan(
                 out=numpair[:, 512:1024],
                 data0=c.ones_b[:, 0:1].broadcast_to((128, 512)),
                 data1=immb[:],
                 initial=(0.0 if q == 0 else carp[:, 2 * ct + 1:2 * ct + 2]),
                 op0=ALU.mult, op1=ALU.add), inc=1)
        P.do("vector", lambda e, ct=ct, numpair=numpair: e.tensor_copy(
            carp[:, 2 * ct:2 * ct + 2], numpair[:, 511:1024:512]), inc=1)
        bank_acquire(c, 5)
        P.do("tensor", lambda e, ct=ct: e.matmul(
            c.ps[5][:, :], selk[:, ct * 128:ct * 128 + 128], invb[:],
            start=True, stop=True), inc=1)
        bank_done(c, 5)
        bank_consume(c, 5, "vector")
        P.wait("vector", "tensor", y_pe2[p])
        P.do("vector", lambda e, p=p, numpair=numpair: e.tensor_tensor(
            out=ren2[p][:], in0=numpair[:, 0:512], in1=c.ps[5][:, :],
            op=ALU.mult), inc=1)
        P.do("vector", lambda e, p=p, numpair=numpair: e.tensor_tensor(
            out=imn2[p][:], in0=numpair[:, 512:1024], in1=c.ps[5][:, :],
            op=ALU.mult), inc=1)
        bank_release(c, 5, "vector")
        norm_v = P.e["vector"].n
        P.do("vector", lambda e, p=p, remb=remb: e.tensor_tensor(
            out=remb[:], in0=ren2[p][:], in1=ren2[p][:], op=ALU.mult), inc=1)
        P.do("vector", lambda e, p=p, immb=immb: e.tensor_tensor(
            out=immb[:], in0=imn2[p][:], in1=imn2[p][:], op=ALU.mult), inc=1)
        sq_gp[0] = P.e["vector"].n
        return norm_v

    def emit_tail_B1(ct, norm_n):
        p = ct % 2
        remb = rem2[p]
        immb = imm2[p]
        renb = ren2[p]
        imnb = imn2[p]
        bank_acquire(c, 6)
        P.wait("tensor", "vector", norm_n)
        P.do("tensor", lambda e, renb=renb: e.matmul(
            c.ps[6][:, :], bdre[:], renb[:], start=True, stop=False), inc=0)
        P.do("tensor", lambda e, imnb=imnb: e.matmul(
            c.ps[6][:, :], bdim[:], imnb[:], start=False, stop=True), inc=1)
        bank_done(c, 6)
        y_pe2[p] = P.e["tensor"].n
        bank_acquire(c, 7)
        P.wait("tensor", "vector", sq_gp[0])
        P.do("tensor", lambda e, remb=remb: e.matmul(
            c.ps[7][0:2, :], hsel[:], remb[:], start=True, stop=False), inc=0)
        P.do("tensor", lambda e, immb=immb: e.matmul(
            c.ps[7][0:2, :], hsel[:], immb[:], start=False, stop=True), inc=1)
        bank_done(c, 7)
        msq_pe2[p] = P.e["tensor"].n

    def emit_tail_B2(ct, t0):
        p = ct % 2
        gct = 16 + ct
        p_ = ct % 2
        bank_consume(c, 7, "scalar")
        P.do("scalar", lambda e: e.activation(
            tmps[0:2, :], c.ps[7][0:2, :], AF.Ln, scale=1.0 / 128.0,
            bias=c.consts[0:2, 0:1]), inc=1)
        P.do("scalar", lambda e: e.activation(
            rsbb, tmps[0:2, :], AF.Exp, scale=-0.5), inc=1)
        s_rs = P.e["scalar"].n
        bg = 2 + ct % 2
        bank_acquire(c, bg)
        P.wait("tensor", "vector", diag_v2[p_])
        P.wait("tensor", "scalar", gz_wr_s[p_])
        for kk in range(4):
            P.do("tensor", lambda e, kk=kk, p_=p_, bg=bg: e.matmul(
                c.ps[bg][:, :], diagb2[p_][:, kk * 128:kk * 128 + 128],
                gzs2[p_][:, kk:kk + 512],
                start=(kk == 0), stop=(kk == 3)), inc=(1 if kk == 3 else 0))
        bank_done(c, bg)
        diag_rd_pe2[p_] = P.e["tensor"].n
        gz_rd_pe[0] = P.e["tensor"].n
        P.wait("tensor", "scalar", s_rs)
        P.do("tensor", lambda e: e.matmul(
            c.ps[7][:, :], hsel2[:], rsbb, start=True, stop=True), inc=1)
        bank_done(c, 7)
        bank_consume(c, bg, "scalar")
        gcv = gcv2[p]
        P.do("scalar", lambda e, gct=gct, bg=bg, gcv=gcv: e.activation(
            gcv[:], c.ps[bg][:, :], AF.Silu, bias=convb[:, gct:gct + 1]), inc=1)
        bank_release(c, bg, "scalar")
        s_n = P.e["scalar"].n
        tjoin = kb2[1 - p]
        bank_consume(c, 7, "vector")
        bank_consume(c, 6, "vector")
        P.wait("vector", "scalar", s_n)
        P.do("vector", lambda e, gcv=gcv, tjoin=tjoin: e.tensor_tensor(
            out=tjoin, in0=c.ps[7][:, :], in1=gcv[:], op=ALU.mult), inc=1)
        bank_release(c, 7, "vector")
        P.do("vector", lambda e, ct=ct, tjoin=tjoin: e.tensor_tensor(
            out=ygq[:, ct * 512:ct * 512 + 512], in0=tjoin,
            in1=c.ps[6][:, :], op=ALU.mult), inc=1)
        bank_release(c, 6, "vector")

    q_first = [True]
    for q in range(NQ):
        q_first[0] = True
        t0 = q * TQ
        P.wait("gpsimd", "tensor", sraw_pe)
        P.wait("gpsimd", "vector", zb_rd_v[1])
        lds = stage_weight_bf16(c, c.w["L0_attn_in_proj_w"], 0, 4096, 8, 1,
                                0, mtile=32, dst=svst)
        b = 6
        bank_acquire(c, b)
        P.wait("tensor", "gpsimd", lds)
        for dc in range(NDC):
            P.do("tensor", lambda e, b=b, dc=dc, t0=t0: e.matmul(
                c.ps[b][0:32, :], svst[:, dc * 32:dc * 32 + 32],
                xln_ap(c, dc, t0, TQ),
                start=(dc == 0), stop=(dc == NDC - 1)),
                inc=(1 if dc == NDC - 1 else 0))
        bank_done(c, b)
        sraw_pe = P.e["tensor"].n
        bank_consume(c, b, "vector")
        P.wait("vector", "scalar")
        P.do("vector", lambda e: e.tensor_copy(
            srawb[:, 0:3], ztail[0:32, 48:51]), inc=1)
        P.do("vector", lambda e, b=b: e.tensor_copy(
            srawb[:, 3:515], c.ps[b][0:32, :]), inc=1)
        bank_release(c, b, "vector")
        P.do("vector", lambda e: e.tensor_copy(
            ztail[0:32, 48:51], srawb[:, 512:515]), inc=1)
        P.do("vector", lambda e: e.tensor_scalar(
            out=pwb, in0=srawb[:, 0:512], scalar1=taps[0:32, 128:129],
            scalar2=None, op0=ALU.mult), inc=1)
        for kk in range(1, 4):
            P.do("vector", lambda e, kk=kk: e.scalar_tensor_tensor(
                out=pwb, in0=srawb[:, kk:kk + 512],
                scalar=taps[0:32, 128 + kk:129 + kk], in1=pwb,
                op0=ALU.mult, op1=ALU.add), inc=1)
        P.do("vector", lambda e: e.tensor_scalar(
            out=pwb, in0=pwb, scalar1=convb[0:32, 32:33], scalar2=None,
            op0=ALU.add), inc=1)
        P.do("vector", lambda e: e.tensor_scalar(
            out=pwb, in0=pwb, scalar1=ssc[:], scalar2=20.0,
            op0=ALU.mult, op1=ALU.min), inc=1)
        P.do("vector", lambda e: e.tensor_scalar(
            out=pwb, in0=pwb, scalar1=-20.0, scalar2=None,
            op0=ALU.max), inc=1)
        distc = srawb[:].bitcast(mybir.dt.float16)[0:32, 0:512]
        P.wait("gpsimd", "vector")
        nd = P.dma("gpsimd", distc, c.w["L0DIST"][:, t0:t0 + TQ])
        P.wait("vector", "gpsimd", nd)
        P.do("vector", lambda e, distc=distc: e.tensor_scalar(
            out=tmps, in0=distc, scalar1=slp[:],
            scalar2=None, op0=ALU.mult), inc=1)
        dist_rd_v = P.e["vector"].n
        P.wait("scalar", "vector")
        P.do("scalar", lambda e: e.activation(pwb, pwb, AF.Exp), inc=1)
        P.do("scalar", lambda e: e.activation(tmps, tmps, AF.Exp), inc=1)
        P.wait("vector", "scalar")
        P.do("vector", lambda e: e.tensor_tensor(
            out=pwb, in0=pwb, in1=tmps, op=ALU.mult), inc=1)
        P.do("vector", lambda e: e.tensor_copy(pwbf[:], pwb), inc=1)
        P.do("vector", lambda e, q=q: e.tensor_tensor_scan(
            out=tmps, data0=c.ones_b[0:32, 0:1].broadcast_to((32, 512)), data1=pwb,
            initial=(0.0 if q == 0 else carden[:]),
            op0=ALU.mult, op1=ALU.add), inc=1)
        P.do("vector", lambda e: e.tensor_copy(carden[:], tmps[0:32, 511:512]), inc=1)
        P.do("vector", lambda e: e.tensor_scalar(
            out=tmps, in0=tmps, scalar1=1e-4, scalar2=None,
            op0=ALU.max), inc=1)
        P.wait("scalar", "vector")
        P.do("scalar", lambda e: e.activation(tmps, tmps, AF.Ln), inc=1)
        P.do("scalar", lambda e: e.activation(invb[:], tmps, AF.Exp,
                                              scale=-1.0), inc=1)
        v_sr = P.e["vector"].n
        sr_s = P.e["scalar"].n
        zb_rd_v[0] = max(zb_rd_v[0], v_sr)

        def stage_group(g):
            half, part = g // 2, g % 2
            rg = (g % 2) * 2
            ru = rg + 1
            P.wait("gpsimd", "tensor", inproj_pe[rg])
            stage_weight_bf16(c, c.w["L0_attn_in_proj_w"], 0,
                              half * 1024 + part * 512, 8, 4, rg * 4096)
            P.wait("gpsimd", "tensor", inproj_pe[ru])
            ld = stage_weight_bf16(c, c.w["L0_attn_in_proj_w"], 0,
                                   2048 + half * 1024 + part * 512, 8, 4,
                                   ru * 4096)
            return rg, ru, ld

        def emit_zmm(ct, rg, ru, ld, t0):
            P.wait("tensor", "gpsimd", ld)
            cl4 = ct % 4
            for reg, bb in ((rg, ct % 2), (ru, 2 + ct % 2)):
                bank_acquire(c, bb)
                for dc in range(NDC):
                    P.do("tensor", lambda e, bb=bb, dc=dc, reg=reg,
                         cl4=cl4, t0=t0:
                         e.matmul(c.ps[bb][:, :],
                                  wtile(c, reg * 4096, 4, dc, cl4),
                                  xln_ap(c, dc, t0, TQ),
                                  start=(dc == 0), stop=(dc == NDC - 1)),
                         inc=(1 if dc == NDC - 1 else 0))
                bank_done(c, bb)
            p_ = ct % 2
            bg_ = 2 + ct % 2
            bank_consume(c, bg_, "scalar")
            P.wait("scalar", "tensor", gz_rd_pe[0])
            P.do("scalar", lambda e, bg_=bg_, p_=p_: e.activation(
                gzs2[p_][:, 3:515], c.ps[bg_][:, :], AF.Copy), inc=1)
            bank_release(c, bg_, "scalar")
            P.do("scalar", lambda e, ct=ct, p_=p_: e.activation(
                gzs2[p_][:, 0:3], gzt[:, ct * 3:ct * 3 + 3], AF.Copy), inc=1)
            P.do("scalar", lambda e, ct=ct, p_=p_: e.activation(
                gzt[:, ct * 3:ct * 3 + 3], gzs2[p_][:, 512:515], AF.Copy), inc=1)
            gz_wr_s[p_] = P.e["scalar"].n

        rg, ru, ld = stage_group(0)
        grp = [(rg, ru, ld)]
        emit_zmm(0, rg, ru, ld, t0)
        pend = None
        for ct in range(16):
            if pend is not None:
                normA = emit_tail_A(pend[0], t0, q, pend[1], pend[2])
                emit_tail_B1(pend[0], normA)
            conv_n = emit_s1_head(ct, t0)
            s2_n = emit_s1_trig(ct, conv_n)
            if pend is not None:
                emit_tail_B2(pend[0], t0)
            pend = (ct, s2_n, q_first[0])
            q_first[0] = False
            if ct < 15:
                nct = ct + 1
                if nct % 4 == 0:
                    g = nct // 4
                    rgp, rup, _ = grp[-1]
                    inproj_pe[rgp] = P.e["tensor"].n
                    inproj_pe[rup] = P.e["tensor"].n
                    rg, ru, ld = stage_group(g)
                    grp.append((rg, ru, ld))
                emit_zmm(nct, rg, ru, ld, t0)
        normA = emit_tail_A(pend[0], t0, q, pend[1], pend[2])
        emit_tail_B1(pend[0], normA)
        emit_tail_B2(pend[0], t0)
        pend = None
        pe_n = P.e["tensor"].n
        P.wait("gpsimd", "tensor", pe_n)
        ldo = stage_weight_bf16(c, c.w["L0_attn_out_proj_w"], 0, 0, 16, 8, 0)
        P.wait("tensor", "gpsimd", ldo)
        P.wait("tensor", "vector")
        for dct in range(8):
            b = dct % 2
            bank_acquire(c, b)
            for ct in range(16):
                P.do("tensor", lambda e, b=b, ct=ct, dct=dct: e.matmul(
                    c.ps[b][:, :], wtile(c, 0, 8, ct, dct),
                    ygq[:, ct * 512:ct * 512 + 512],
                    start=(ct == 0), stop=(ct == 15)),
                    inc=(1 if ct == 15 else 0))
            bank_done(c, b)
            bank_consume(c, b, "vector")
            P.do("vector", lambda e, b=b, dct=dct, t0=t0: e.tensor_tensor(
                out=xr_ap(c, dct, t0, TQ), in0=xr_ap(c, dct, t0, TQ),
                in1=c.ps[b][:, :], op=ALU.add), inc=1)
            bank_release(c, b, "vector")
        inproj_pe = [P.e["tensor"].n] * 4
        sraw_pe = P.e["tensor"].n
    P.barrier()
    bes.close()


def b_l0_ffn(c):
    P = c.P
    emit_ln(c)
    bes = contextlib.ExitStack()
    S = 1.0 / 127.0
    tmpb2 = [bes.enter_context(c.nc.sbuf_tensor(f"l0f_tmp{i}", [128, 512], BF16))
             for i in range(2)]
    rpe = [0, 0, 0, 0]
    tmp_read_v2 = [0, 0]

    for q in range(NQ):
        t0 = q * TQ
        def l0f_stage(idx):
            hs, part = divmod(idx, 2)
            rg = (idx % 2) * 2
            ru = rg + 1
            P.wait("gpsimd", "tensor", rpe[rg])
            stage_weight_bf16(c, c.w["L0_ffn_w_gate"], 0,
                              hs * 1024 + part * 512, 8, 4, rg * 4096)
            P.wait("gpsimd", "tensor", rpe[ru])
            ldb = stage_weight_bf16(c, c.w["L0_ffn_w_up"], 0,
                                    hs * 1024 + part * 512, 8, 4, ru * 4096)
            return rg, ru, ldb

        nxt = l0f_stage(0)
        for hs in range(3):
            for part in range(2):
                idx = hs * 2 + part
                rg, ru, ldb = nxt
                P.wait("tensor", "gpsimd", ldb)
                if idx + 1 < 6:
                    nxt = l0f_stage(idx + 1)
                for hl4 in range(4):
                    hc = hs * 8 + part * 4 + hl4
                    bg = hc % 2
                    bu = 2 + hc % 2
                    bank_acquire(c, bg)
                    for dc in range(NDC):
                        P.do("tensor", lambda e, bg=bg, dc=dc, hl4=hl4, t0=t0, rg=rg:
                             e.matmul(c.ps[bg][:, :], wtile(c, rg * 4096, 4, dc, hl4),
                                      xln_ap(c, dc, t0, TQ),
                                      start=(dc == 0), stop=(dc == NDC - 1)),
                             inc=(1 if dc == NDC - 1 else 0))
                    bank_done(c, bg)
                    bank_acquire(c, bu)
                    for dc in range(NDC):
                        P.do("tensor", lambda e, bu=bu, dc=dc, hl4=hl4, t0=t0, ru=ru:
                             e.matmul(c.ps[bu][:, :], wtile(c, ru * 4096, 4, dc, hl4),
                                      xln_ap(c, dc, t0, TQ),
                                      start=(dc == 0), stop=(dc == NDC - 1)),
                             inc=(1 if dc == NDC - 1 else 0))
                    bank_done(c, bu)
                    tmpb = tmpb2[hc % 2]
                    bank_consume(c, bg, "scalar")
                    P.wait("scalar", "vector", tmp_read_v2[hc % 2])
                    P.do("scalar", lambda e, bg=bg, tmpb=tmpb: e.activation(
                        tmpb[:], c.ps[bg][:, :], AF.Silu, scale=S), inc=1)
                    bank_release(c, bg, "scalar")
                    P.wait("vector", "scalar")
                    bank_consume(c, bu, "vector")
                    P.do("vector", lambda e, bu=bu, hc=hc, tmpb=tmpb:
                         e.tensor_tensor(
                             out=c.scr[:, hc * 512:hc * 512 + 512], in0=tmpb[:],
                             in1=c.ps[bu][:, :], op=ALU.mult), inc=1)
                    tmp_read_v2[hc % 2] = P.e["vector"].n
                    bank_release(c, bu, "vector")
                rpe[rg] = rpe[ru] = P.e["tensor"].n
        sc_n = P.e["scalar"].n
        v_n = P.e["vector"].n

        def l0f_stage_down(hs):
            rp = (hs % 2) * 2
            P.wait("gpsimd", "tensor", max(rpe[rp], rpe[rp + 1]))
            ld = stage_weight_bf16(c, c.w["L0_ffn_w_down"], hs * 1024, 0, 8, 8,
                                   rp * 4096)
            return rp, ld

        nxtb = l0f_stage_down(0)
        for hs in range(3):
            rp, ld = nxtb
            P.wait("tensor", "gpsimd", ld)
            if hs + 1 < 3:
                nxtb = l0f_stage_down(hs + 1)
            if hs == 0:
                for b in range(8):
                    bank_acquire(c, b)
                P.wait("tensor", "scalar", sc_n)
                P.wait("tensor", "vector", v_n)
            for dc in range(NDC):
                for hl in range(8):
                    hc = hs * 8 + hl
                    P.do("tensor", lambda e, dc=dc, rp=rp, hl=hl, hc=hc, hs=hs:
                         e.matmul(c.ps[dc][:, :], wtile(c, rp * 4096, 8, hl, dc),
                                  c.scr[:, hc * 512:hc * 512 + 512],
                                  start=(hs == 0 and hl == 0),
                                  stop=(hs == 2 and hl == 7)),
                         inc=(1 if hl == 7 else 0))
            rpe[rp] = rpe[rp + 1] = P.e["tensor"].n
        for dc in range(NDC):
            bank_done(c, dc)
            bank_consume(c, dc, "vector")
            P.do("vector", lambda e, dc=dc, t0=t0: e.scalar_tensor_tensor(
                out=xr_ap(c, dc, t0, TQ), in0=c.ps[dc][:, :],
                scalar=S * S, in1=xr_ap(c, dc, t0, TQ),
                op0=ALU.mult, op1=ALU.add), inc=1)
            bank_release(c, dc, "vector")
    P.barrier()
    bes.close()



def b_l2_attn(c):
    P = c.P
    emit_ln(c)
    bes = contextlib.ExitStack()
    kv = bes.enter_context(c.nc.sbuf_tensor("l2_kv", [128, 8192], BF16))
    o1 = bes.enter_context(c.nc.sbuf_tensor("l2_o1", [128, 4096], BF16))
    A, B = 0, 8192

    lda = stage_weight_bf16(c, c.w["L2_attn_w_k"], 0, 0, 8, 8, A)
    ldb = stage_weight_bf16(c, c.w["L2_attn_w_v"], 0, 0, 8, 8, B)
    P.wait("tensor", "gpsimd", ldb)
    for th in range(2):
        for tc8 in range(8):
            t0 = th * 1024 + tc8 * 128
            for src_reg, dst0, b0 in ((A, 0, 0), (B, 8192, 2)):
                for et in range(2):
                    b = b0 + et % 2
                    bank_acquire(c, b)
                    for dc in range(NDC):
                        P.do("tensor", lambda e, b=b, dc=dc, t0=t0, src_reg=src_reg, et=et:
                             e.matmul(c.ps[b][:, :],
                                      c.xln[:, dc * T + t0:dc * T + t0 + 128],
                                      c.wst[:, src_reg + dc * 1024 + et * 512:
                                            src_reg + dc * 1024 + et * 512 + 512],
                                      start=(dc == 0), stop=(dc == NDC - 1)),
                             inc=(1 if dc == NDC - 1 else 0))
                    bank_done(c, b)
                    bank_consume(c, b, "scalar")
                    P.do("scalar", lambda e, b=b, dst0=dst0, tc8=tc8, et=et:
                         e.activation(c.scr[:, dst0 + tc8 * 1024 + et * 512:
                                            dst0 + tc8 * 1024 + et * 512 + 512],
                                      c.ps[b][:, :], AF.Copy), inc=1)
                    bank_release(c, b, "scalar")
        sc_n = P.e["scalar"].n
        for dvh in range(2):
            for dkt in range(8):
                bank_acquire(c, dkt)
            P.wait("tensor", "scalar", sc_n)
            for tc8 in range(8):
                for dkt in range(8):
                    P.do("tensor", lambda e, dkt=dkt, tc8=tc8, dvh=dvh:
                         e.matmul(c.ps[dkt][:, :],
                                  c.scr[:, tc8 * 1024 + dkt * 128:
                                        tc8 * 1024 + dkt * 128 + 128],
                                  c.scr[:, 8192 + tc8 * 1024 + dvh * 512:
                                        8192 + tc8 * 1024 + dvh * 512 + 512],
                                  start=(tc8 == 0), stop=(tc8 == 7)),
                         inc=(1 if tc8 == 7 else 0))
            for dkt in range(8):
                bank_done(c, dkt)
                col = dkt * 1024 + dvh * 512
                if th == 0:
                    bank_consume(c, dkt, "scalar")
                    P.do("scalar", lambda e, dkt=dkt, col=col: e.activation(
                        kv[:, col:col + 512], c.ps[dkt][:, :], AF.Copy), inc=1)
                    bank_release(c, dkt, "scalar")
                else:
                    bank_consume(c, dkt, "vector")
                    P.do("vector", lambda e, dkt=dkt, col=col: e.tensor_tensor(
                        out=kv[:, col:col + 512], in0=kv[:, col:col + 512],
                        in1=c.ps[dkt][:, :], op=ALU.add), inc=1)
                    bank_release(c, dkt, "vector")
    pe_n = P.e["tensor"].n
    P.wait("gpsimd", "tensor", pe_n)
    lda = stage_weight_bf16(c, c.w["L2_attn_w_q"], 0, 0, 8, 8, A)
    ldb = stage_weight_bf16(c, c.w["L2_attn_w_o"], 0, 0, 8, 8, B)
    P.wait("tensor", "gpsimd", lda)
    for mt in range(NDC):
        for qc in range(NQ):
            b = (mt * NQ + qc) % 4
            bank_acquire(c, b)
            for dc in range(NDC):
                P.do("tensor", lambda e, b=b, dc=dc, mt=mt, qc=qc:
                     e.matmul(c.ps[b][:, :], wtile(c, A, 8, dc, mt),
                              xln_ap(c, dc, qc * TQ, TQ),
                              start=(dc == 0), stop=(dc == NDC - 1)),
                     inc=(1 if dc == NDC - 1 else 0))
            bank_done(c, b)
            bank_consume(c, b, "scalar")
            P.do("scalar", lambda e, b=b, mt=mt, qc=qc: e.activation(
                c.scr[:, mt * 2048 + qc * 512:mt * 2048 + qc * 512 + 512],
                c.ps[b][:, :], AF.Copy), inc=1)
            bank_release(c, b, "scalar")
    P.wait("tensor", "gpsimd", ldb)
    P.wait("tensor", "scalar")
    P.wait("tensor", "vector")
    for qc in range(NQ):
        for dvt in range(8):
            b = 4 + dvt % 2
            bank_acquire(c, b)
            for dkt in range(8):
                P.do("tensor", lambda e, b=b, dkt=dkt, dvt=dvt, qc=qc:
                     e.matmul(c.ps[b][:, :],
                              kv[:, dkt * 1024 + dvt * 128:dkt * 1024 + dvt * 128 + 128],
                              c.scr[:, dkt * 2048 + qc * 512:dkt * 2048 + qc * 512 + 512],
                              start=(dkt == 0), stop=(dkt == 7)),
                     inc=(1 if dkt == 7 else 0))
            bank_done(c, b)
            bank_consume(c, b, "scalar")
            P.do("scalar", lambda e, b=b, dvt=dvt: e.activation(
                o1[:, dvt * 512:dvt * 512 + 512], c.ps[b][:, :], AF.Copy), inc=1)
            bank_release(c, b, "scalar")
        sc_n = P.e["scalar"].n
        for dct in range(8):
            b = 6 + dct % 2
            bank_acquire(c, b)
            if dct == 0:
                P.wait("tensor", "scalar", sc_n)
            for dvt in range(8):
                P.do("tensor", lambda e, b=b, dvt=dvt, dct=dct:
                     e.matmul(c.ps[b][:, :], wtile(c, B, 8, dvt, dct),
                              o1[:, dvt * 512:dvt * 512 + 512],
                              start=(dvt == 0), stop=(dvt == 7)),
                     inc=(1 if dvt == 7 else 0))
            bank_done(c, b)
            bank_consume(c, b, "vector")
            P.do("vector", lambda e, b=b, dct=dct, qc=qc: e.tensor_tensor(
                out=xr_ap(c, dct, qc * TQ, TQ), in0=xr_ap(c, dct, qc * TQ, TQ),
                in1=c.ps[b][:, :], op=ALU.add), inc=1)
            bank_release(c, b, "vector")
    P.barrier()
    bes.close()



def b_l3_ffn(c):
    P = c.P
    emit_ln(c)
    bes = contextlib.ExitStack()
    nc = c.nc
    lg = bes.enter_context(nc.sbuf_tensor("l3f_lg", [128, 128], F32))
    mx = bes.enter_context(nc.sbuf_tensor("l3f_mx", [128, 16], F32))
    mtok = bes.enter_context(nc.sbuf_tensor("l3f_mtok", [128, 128], BF16))
    mfm = bes.enter_context(nc.sbuf_tensor("l3f_mfm", [8, 2048], BF16))
    mrow = bes.enter_context(nc.sbuf_tensor("l3f_mrow", [1, 1024], BF16))
    xinb = [bes.enter_context(nc.sbuf_tensor(f"l3f_xin{i}", [128, 4096], BF16))
            for i in range(2)]
    gw = bes.enter_context(nc.sbuf_tensor("l3f_gw", [128, 64], BF16))
    ld = stage_weight_bf16(c, c.w["L3_ffn_gate_w"], 0, 0, 8, 1, 0, mtile=8, dst=gw)
    P.wait("tensor", "gpsimd", ld)
    for tc in range(16):
        b = tc % 2
        bank_acquire(c, b)
        for dc in range(NDC):
            P.do("tensor", lambda e, b=b, dc=dc, tc=tc: e.matmul(
                c.ps[b][:, 0:8], c.xln[:, dc * T + tc * 128:dc * T + tc * 128 + 128],
                gw[:, dc * 8:dc * 8 + 8],
                start=(dc == 0), stop=(dc == NDC - 1)),
                inc=(1 if dc == NDC - 1 else 0))
        bank_done(c, b)
        bank_consume(c, b, "vector")
        P.do("vector", lambda e, b=b, tc=tc: e.tensor_copy(
            lg[:, tc * 8:tc * 8 + 8], c.ps[b][:, 0:8]), inc=1)
        bank_release(c, b, "vector")
        P.do("vector", lambda e, tc=tc: e.tensor_reduce(
            out=mx[:, tc:tc + 1], in_=lg[:, tc * 8:tc * 8 + 8],
            axis=AX.X, op=ALU.max), inc=1)
        P.do("vector", lambda e, tc=tc: e.tensor_tensor(
            out=mtok[:, tc * 8:tc * 8 + 8], in0=lg[:, tc * 8:tc * 8 + 8],
            in1=mx[:, tc:tc + 1].broadcast_to((128, 8)), op=ALU.is_ge), inc=1)
    v_n = P.e["vector"].n
    P.wait("tensor", "vector", v_n)
    for tc in range(16):
        b = 2 + tc % 2
        bank_acquire(c, b)
        P.do("tensor", lambda e, b=b, tc=tc: e.transpose(
            c.ps[b].bitcast(BF16)[0:8, 0:128], mtok[:, tc * 8:tc * 8 + 8],
            c.ident_b[:]), inc=1)
        bank_done(c, b)
        bank_consume(c, b, "vector")
        P.do("vector", lambda e, b=b, tc=tc: e.tensor_copy(
            mfm[:, tc * 128:tc * 128 + 128], c.ps[b].bitcast(BF16)[0:8, 0:128]),
            inc=1)
        bank_release(c, b, "vector")
    v_n = P.e["vector"].n
    P.wait("sync", "vector", v_n)
    mrow_pe = [0, 0]
    for e_ in range(8):
        for qc in range(NQ):
            i = e_ * NQ + qc
            mr = mrow[0:1, (i % 2) * 512:(i % 2) * 512 + 512]
            P.wait("sync", "tensor", mrow_pe[i % 2])
            n = P.dma("sync", mr, mfm[e_:e_ + 1, qc * TQ:qc * TQ + TQ])
            b = 4 + i % 2
            bank_acquire(c, b)
            P.wait("tensor", "sync", n)
            if i == 0:
                P.wait("tensor", "vector", v_n)
            P.do("tensor", lambda e, b=b, mr=mr: e.matmul(
                c.ps[b][:, :], c.ones_b[0:1, :], mr,
                start=True, stop=True), inc=1)
            bank_done(c, b)
            mrow_pe[i % 2] = P.e["tensor"].n
            bank_consume(c, b, "scalar")
            P.do("scalar", lambda e, e_=e_, qc=qc, b=b: e.activation(
                c.scr[:, e_ * 2048 + qc * 512:e_ * 2048 + qc * 512 + 512],
                c.ps[b][:, :], AF.Copy), inc=1)
            bank_release(c, b, "scalar")
    for dc in range(NDC):
        P.do("vector", lambda e, dc=dc: e.tensor_tensor(
            out=xr_ap(c, dc), in0=xr_ap(c, dc), in1=xln_ap(c, dc),
            op=ALU.add), inc=1)
    regs = [0, 8192]
    reg_pe = [0, 0]
    xin_rd_pe = [0, 0]
    xin_wr_v = [0, 0]
    sc_n = P.e["scalar"].n
    P.wait("vector", "scalar", sc_n)

    def l3f_emit_xin(i):
        qc_, e2 = divmod(i, 8)
        t0_ = qc_ * TQ
        buf = xinb[i % 2]
        xin3 = buf[:].rearrange("p (dcx t) -> p dcx t", dcx=NDC)
        xln3 = c.xln[:].rearrange(
            "p (dcx t) -> p dcx t", dcx=NDC)[:, :, t0_:t0_ + TQ]
        mbc = c.scr[:, e2 * 2048 + qc_ * 512:e2 * 2048 + qc_ * 512 + 512]
        mbc3 = mbc.rearrange("p (u t) -> p u t", u=1).broadcast_to(
            (128, NDC, TQ))
        P.wait("vector", "tensor", xin_rd_pe[i % 2])
        P.do("vector", lambda e, xin3=xin3, xln3=xln3, mbc3=mbc3:
             e.tensor_tensor(out=xin3, in0=xln3, in1=mbc3, op=ALU.mult), inc=1)
        xin_wr_v[i % 2] = P.e["vector"].n

    l3f_emit_xin(0)
    for qc in range(NQ):
        t0 = qc * TQ
        for b in range(8):
            bank_acquire(c, b)
        for e_ in range(8):
            i = qc * 8 + e_
            cur = i % 2
            xin = xinb[cur]
            reg = regs[e_ % 2]
            P.wait("gpsimd", "tensor", reg_pe[e_ % 2])
            ld = stage_weight_bf16(c, c.w["L3_ffn_expert_w"][e_], 0, 0, 8, 8, reg)
            P.wait("tensor", "gpsimd", ld)
            P.wait("tensor", "vector", xin_wr_v[cur])
            for dct in range(8):
                for hc in range(NDC):
                    P.do("tensor", lambda e, dct=dct, hc=hc, reg=reg, e_=e_,
                         xin=xin:
                         e.matmul(c.ps[dct][:, :], wtile(c, reg, 8, hc, dct),
                                  xin[:, hc * 512:hc * 512 + 512],
                                  start=(e_ == 0 and hc == 0),
                                  stop=(e_ == 7 and hc == 7)),
                         inc=(1 if hc == 7 else 0))
            reg_pe[e_ % 2] = P.e["tensor"].n
            xin_rd_pe[cur] = P.e["tensor"].n
            if i + 1 < 32:
                l3f_emit_xin(i + 1)
        for dct in range(8):
            bank_done(c, dct)
            bank_consume(c, dct, "vector")
            P.do("vector", lambda e, dct=dct, t0=t0: e.tensor_tensor(
                out=xr_ap(c, dct, t0, TQ), in0=xr_ap(c, dct, t0, TQ),
                in1=c.ps[dct][:, :], op=ALU.add), inc=1)
            bank_release(c, dct, "vector")
    P.barrier()
    bes.close()



def b_l2_ffn(c):
    P = c.P
    emit_ln(c)
    bes = contextlib.ExitStack()
    nc = c.nc
    lg = bes.enter_context(nc.sbuf_tensor("l2f_lg", [128, 128], F32))
    mxn = bes.enter_context(nc.sbuf_tensor("l2f_mxn", [128, 16], F32))
    den = bes.enter_context(nc.sbuf_tensor("l2f_den", [128, 16], F32))
    gate = bes.enter_context(nc.sbuf_tensor("l2f_gate", [128, 16], F32))
    msk = bes.enter_context(nc.sbuf_tensor("l2f_msk", [128, 128], F32))
    sel = bes.enter_context(nc.sbuf_tensor("l2f_sel", [128, 128], F32))
    selfm = bes.enter_context(nc.sbuf_tensor("l2f_selfm", [8, 2048], BF16))
    mrow = bes.enter_context(nc.sbuf_tensor("l2f_mrow", [1, 1024], BF16))
    carry = bes.enter_context(nc.sbuf_tensor("l2f_carry", [1, 8], F32))
    wrst = bes.enter_context(nc.sbuf_tensor("l2f_wr", [128, 64], BF16))
    brbc = bes.enter_context(nc.sbuf_tensor("l2f_br", [128, 8], F32))
    b1sb = bes.enter_context(nc.sbuf_tensor("l2f_b1", [128, 8], F32))
    b2t = bes.enter_context(nc.sbuf_tensor("l2f_b2t", [128, 8], F32))
    xin0 = bes.enter_context(nc.sbuf_tensor("l2f_xin0", [128, 4096], BF16))
    xin1 = bes.enter_context(nc.sbuf_tensor("l2f_xin1", [128, 4096], BF16))
    tsel = bes.enter_context(nc.sbuf_tensor("l2f_ts", [128, 512], BF16))
    hbuf = bes.enter_context(nc.sbuf_tensor("l2f_h", [128, 4096], BF16))
    tmp = bes.enter_context(nc.sbuf_tensor("l2f_tmp", [128, 8], F32))

    ld = stage_weight_bf16(c, c.w["L2_ffn_Wr"], 0, 0, 8, 1, 0, mtile=8, dst=wrst)
    nbr = P.dma("sync", c.aux[0:1, 0:8], c.w["L2_ffn_br"])
    P.do("vector", lambda e: e.memset(carry[:], 0.0), inc=1)
    bank_acquire(c, 6)
    P.wait("tensor", "sync", nbr)
    P.do("tensor", lambda e: e.matmul(c.ps[6][:, 0:8], c.ones_f[:, :],
                                      c.aux[0:1, 0:8], start=True, stop=True), inc=1)
    bank_done(c, 6)
    bank_consume(c, 6, "vector")
    P.do("vector", lambda e: e.tensor_copy(brbc[:], c.ps[6][:, 0:8]), inc=1)
    bank_release(c, 6, "vector")
    P.wait("tensor", "gpsimd", ld)

    for tc in range(16):
        b = tc % 2
        bank_acquire(c, b)
        for dc in range(NDC):
            P.do("tensor", lambda e, b=b, dc=dc, tc=tc: e.matmul(
                c.ps[b][:, 0:8], c.xln[:, dc * T + tc * 128:dc * T + tc * 128 + 128],
                wrst[:, dc * 8:dc * 8 + 8],
                start=(dc == 0), stop=(dc == NDC - 1)),
                inc=(1 if dc == NDC - 1 else 0))
        bank_done(c, b)
        bank_consume(c, b, "vector")
        P.do("vector", lambda e, b=b, tc=tc: e.tensor_tensor(
            out=lg[:, tc * 8:tc * 8 + 8], in0=c.ps[b][:, 0:8], in1=brbc[:],
            op=ALU.add), inc=1)
        bank_release(c, b, "vector")
        P.do("vector", lambda e, tc=tc: e.tensor_reduce(
            out=mxn[:, tc:tc + 1], in_=lg[:, tc * 8:tc * 8 + 8],
            axis=AX.X, op=ALU.max, negate=True), inc=1)
        P.wait("scalar", "vector")
        P.do("scalar", lambda e, tc=tc: e.activation(
            tmp[:, 0:8], lg[:, tc * 8:tc * 8 + 8], AF.Exp,
            bias=mxn[:, tc:tc + 1], accum_out=den[:, tc:tc + 1]), inc=1)
        P.wait("vector", "scalar")
        P.do("vector", lambda e, tc=tc: e.reciprocal(
            gate[:, tc:tc + 1], den[:, tc:tc + 1]), inc=1)
        P.do("vector", lambda e, tc=tc: e.scalar_tensor_tensor(
            out=msk[:, tc * 8:tc * 8 + 8], in0=lg[:, tc * 8:tc * 8 + 8],
            scalar=mxn[:, tc:tc + 1], in1=c.consts[:, 3:4].broadcast_to((128, 8)),
            op0=ALU.add, op1=ALU.is_ge), inc=1)
        v_n = P.e["vector"].n
        b2_ = 2 + tc % 2
        bank_acquire(c, b2_)
        P.wait("tensor", "vector", v_n)
        P.do("tensor", lambda e, b2_=b2_, tc=tc: e.matmul(
            c.ps[b2_][:, 0:8], c.tri_f[:], msk[:, tc * 8:tc * 8 + 8],
            start=True, stop=False), inc=0)
        P.do("tensor", lambda e, b2_=b2_: e.matmul(
            c.ps[b2_][:, 0:8], c.ones_f[:, :], carry[:],
            start=False, stop=True), inc=1)
        bank_done(c, b2_)
        b3 = 4 + tc % 2
        bank_acquire(c, b3)
        P.do("tensor", lambda e, b3=b3, tc=tc: e.matmul(
            c.ps[b3][0:1, 0:8], c.ones_cf[:], msk[:, tc * 8:tc * 8 + 8],
            start=True, stop=True), inc=1)
        bank_done(c, b3)
        bank_consume(c, b2_, "vector")
        P.do("vector", lambda e, b2_=b2_, tc=tc: e.tensor_scalar(
            out=sel[:, tc * 8:tc * 8 + 8], in0=c.ps[b2_][:, 0:8],
            scalar1=2047.5, scalar2=None, op0=ALU.is_le), inc=1)
        bank_release(c, b2_, "vector")
        P.do("vector", lambda e, tc=tc: e.tensor_tensor(
            out=sel[:, tc * 8:tc * 8 + 8], in0=sel[:, tc * 8:tc * 8 + 8],
            in1=msk[:, tc * 8:tc * 8 + 8], op=ALU.mult), inc=1)
        P.do("vector", lambda e, tc=tc: e.tensor_tensor(
            out=sel[:, tc * 8:tc * 8 + 8], in0=sel[:, tc * 8:tc * 8 + 8],
            in1=gate[:, tc:tc + 1].broadcast_to((128, 8)), op=ALU.mult), inc=1)
        bank_consume(c, b3, "vector")
        P.do("vector", lambda e, b3=b3: e.tensor_tensor(
            out=carry[:], in0=carry[:], in1=c.ps[b3][0:1, 0:8], op=ALU.add), inc=1)
        bank_release(c, b3, "vector")
    v_n = P.e["vector"].n
    P.wait("tensor", "vector", v_n)
    for tc in range(16):
        b = 6 + tc % 2
        bank_acquire(c, b)
        P.do("tensor", lambda e, b=b, tc=tc: e.transpose(
            c.ps[b][0:8, 0:128], sel[:, tc * 8:tc * 8 + 8], c.ident_f[:]), inc=1)
        bank_done(c, b)
        bank_consume(c, b, "vector")
        P.do("vector", lambda e, b=b, tc=tc: e.tensor_copy(
            selfm[:, tc * 128:tc * 128 + 128], c.ps[b][0:8, 0:128]), inc=1)
        bank_release(c, b, "vector")
    v_n = P.e["vector"].n
    P.wait("sync", "vector", v_n)
    mrow_pe = [0, 0]
    for e_ in range(8):
        for qc in range(NQ):
            i = e_ * NQ + qc
            mr = mrow[0:1, (i % 2) * 512:(i % 2) * 512 + 512]
            P.wait("sync", "tensor", mrow_pe[i % 2])
            n = P.dma("sync", mr, selfm[e_:e_ + 1, qc * TQ:qc * TQ + TQ])
            b = i % 2
            bank_acquire(c, b)
            P.wait("tensor", "sync", n)
            P.do("tensor", lambda e, b=b, mr=mr: e.matmul(
                c.ps[b][:, :], c.ones_b[0:1, :], mr,
                start=True, stop=True), inc=1)
            bank_done(c, b)
            mrow_pe[i % 2] = P.e["tensor"].n
            bank_consume(c, b, "scalar")
            P.do("scalar", lambda e, e_=e_, qc=qc, b=b: e.activation(
                c.scr[:, e_ * 2048 + qc * 512:e_ * 2048 + qc * 512 + 512],
                c.ps[b][:, :], AF.Copy), inc=1)
            bank_release(c, b, "scalar")
    A2, B2 = 0, 8192
    w1_rd_pe = 0
    w2_rd_pe = 0
    b1_rd_s = 0
    b2r_rd_v = 0
    xinb = [xin0, xin1]
    xin_rd_pe = [0, 0]
    xin_wr_v = [0, 0]
    sc_pb = P.e["scalar"].n
    P.wait("vector", "scalar", sc_pb)

    def l2f_mbc3(i):
        e_, qc = divmod(i, NQ)
        mbc = c.scr[:, e_ * 2048 + qc * 512:e_ * 2048 + qc * 512 + 512]
        return mbc.rearrange("p (u t) -> p u t", u=1).broadcast_to(
            (128, NDC, TQ))

    def l2f_emit_xin(i):
        qc = i % NQ
        t0 = qc * TQ
        buf = xinb[i % 2]
        xin3 = buf[:].rearrange("p (dcx t) -> p dcx t", dcx=NDC)
        xln3 = c.xln[:].rearrange(
            "p (dcx t) -> p dcx t", dcx=NDC)[:, :, t0:t0 + TQ]
        P.wait("vector", "tensor", xin_rd_pe[i % 2])
        P.do("vector", lambda e, xin3=xin3, xln3=xln3, m3_=l2f_mbc3(i):
             e.tensor_tensor(out=xin3, in0=xln3, in1=m3_, op=ALU.mult), inc=1)
        xin_wr_v[i % 2] = P.e["vector"].n

    l2f_emit_xin(0)
    for e_ in range(8):
        P.wait("gpsimd", "tensor", w1_rd_pe)
        ld1 = stage_weight_bf16(c, c.w["L2_ffn_W1"][e_], 0, 0, 8, 8, A2)
        P.wait("gpsimd", "scalar", b1_rd_s)
        nb = P.dma("gpsimd", b1sb[:], c.w["L2_ffn_b1"][e_])
        P.wait("gpsimd", "tensor", w2_rd_pe)
        ld2 = stage_weight_bf16(c, c.w["L2_ffn_W2"][e_], 0, 0, 8, 8, B2)
        P.wait("gpsimd", "vector", b2r_rd_v)
        nb2 = P.dma("gpsimd", b2t[:], c.w["L2B2T"][e_])
        for qc in range(NQ):
            t0 = qc * TQ
            i = e_ * NQ + qc
            cur = i % 2
            xin = xinb[cur]
            if qc == 0:
                P.wait("tensor", "gpsimd", ld1)
                P.wait("scalar", "gpsimd", nb)
            P.wait("tensor", "vector", xin_wr_v[cur])
            for ft in range(8):
                b = ft % 2
                bank_acquire(c, b)
                for dc in range(NDC):
                    P.do("tensor", lambda e, b=b, dc=dc, ft=ft, xin=xin:
                         e.matmul(c.ps[b][:, :], wtile(c, A2, 8, dc, ft),
                                  xin[:, dc * 512:dc * 512 + 512],
                                  start=(dc == 0), stop=(dc == NDC - 1)),
                         inc=(1 if dc == NDC - 1 else 0))
                bank_done(c, b)
                bank_consume(c, b, "scalar")
                P.do("scalar", lambda e, b=b, ft=ft: e.activation(
                    hbuf[:, ft * 512:ft * 512 + 512], c.ps[b][:, :], AF.Relu,
                    bias=b1sb[:, ft:ft + 1]), inc=1)
                bank_release(c, b, "scalar")
            xin_rd_pe[cur] = P.e["tensor"].n
            if qc == NQ - 1:
                w1_rd_pe = P.e["tensor"].n
            sc_n = P.e["scalar"].n
            b1_rd_s = sc_n
            if i + 1 < 32:
                l2f_emit_xin(i + 1)
            if qc == 0:
                P.wait("vector", "gpsimd", nb2)
                P.wait("tensor", "gpsimd", ld2)
            P.wait("tensor", "scalar", sc_n)
            mbc = c.scr[:, e_ * 2048 + qc * 512:e_ * 2048 + qc * 512 + 512]
            for dct in range(8):
                b = 2 + dct % 2
                bank_acquire(c, b)
                for fc in range(8):
                    P.do("tensor", lambda e, b=b, fc=fc, dct=dct:
                         e.matmul(c.ps[b][:, :],
                                  wtile(c, B2, 8, fc, dct),
                                  hbuf[:, fc * 512:fc * 512 + 512],
                                  start=(fc == 0), stop=(fc == 7)),
                         inc=(1 if fc == 7 else 0))
                bank_done(c, b)
                bank_consume(c, b, "vector")
                P.do("vector", lambda e, b=b, mbc=mbc, dct=dct:
                     e.scalar_tensor_tensor(
                         out=tsel[:], in0=c.ps[b][:, :],
                         scalar=b2t[:, dct:dct + 1], in1=mbc,
                         op0=ALU.add, op1=ALU.mult), inc=1)
                P.do("vector", lambda e, dct=dct, t0=t0: e.tensor_tensor(
                    out=xr_ap(c, dct, t0, TQ), in0=xr_ap(c, dct, t0, TQ),
                    in1=tsel[:], op=ALU.add), inc=1)
                bank_release(c, b, "vector")
        w2_rd_pe = P.e["tensor"].n
        b2r_rd_v = P.e["vector"].n
    P.barrier()
    bes.close()



def b_l1_attn(c):
    P = c.P
    emit_ln(c)
    bes = contextlib.ExitStack()
    nc = c.nc
    A, B = 0, 8192
    m3 = bes.enter_context(nc.sbuf_tensor("l1a_m3", [16, 3], F32))
    cnt = bes.enter_context(nc.sbuf_tensor("l1a_cnt", [16, 1], F32))
    gts = bes.enter_context(nc.sbuf_tensor("l1a_gt", [16, 1], F32))
    df2 = bes.enter_context(nc.sbuf_tensor("l1a_df2", [16, 2], F32))
    ktok = bes.enter_context(nc.sbuf_tensor("l1a_ktok", [128, 256], BF16))
    dnvf = bes.enter_context(nc.sbuf_tensor("l1a_dnvf", [1, 512], F32))
    dnvb = bes.enter_context(nc.sbuf_tensor("l1a_dnvb", [1, 512], BF16))
    dbcs = bes.enter_context(nc.sbuf_tensor("l1a_dbc", [128, 512], BF16))
    tsb = bes.enter_context(nc.sbuf_tensor("l1a_tsb", [128, 512], F32))
    qkvb = bes.enter_context(nc.sbuf_tensor("l1a_qkvb", [128, 24], F32))
    rb1 = bes.enter_context(nc.sbuf_tensor("l1a_rb1", [128, 2], F32))
    rb2 = bes.enter_context(nc.sbuf_tensor("l1a_rb2", [16, 1], F32))
    vbias = bes.enter_context(nc.sbuf_tensor("l1a_vbias", [1, 1024], BF16))

    n1 = P.dma("sync", qkvb[:], c.w["L1_attn_qkv_b"])
    n2 = P.dma("sync", rb1[:], c.w["L1_attn_r_b1"])
    n3 = P.dma("sync", rb2[:], c.w["L1_attn_r_b2"])
    n4 = P.dma("gpsimd", vbias[:], c.w["L1A_VBROW"])
    ldv = stage_weight_bf16(c, c.w["L1_attn_qkv_w"], 0, 2048, 8, 8, A)
    ldr1 = stage_weight_bf16(c, c.w["L1_attn_r_w1"], 0, 0, 8, 2, B)
    ldr2 = stage_weight_bf16(c, c.w["L1_attn_r_w2"], 0, 0, 2, 1, B + 2048, mtile=16)
    P.wait("tensor", "gpsimd", n4)
    P.wait("tensor", "gpsimd", ldv)
    P.wait("tensor", "sync", n3)
    ph1 = contextlib.ExitStack()
    rsb = ph1.enter_context(nc.sbuf_tensor("l1a_rs", [16, 2048], F32))
    h1b = ph1.enter_context(nc.sbuf_tensor("l1a_h1", [128, 4096], BF16))
    kfm = ph1.enter_context(nc.sbuf_tensor("l1a_kfm", [16, 2048], BF16))
    ones65 = c.scr[:, 0:16 * 1040].rearrange(
        "p (s w) -> p s w", w=65)[:, :, 64:65]
    P.do("vector", lambda e: e.tensor_copy(
        ones65, c.ones_b[:, 0:1].rearrange(
            "p (s w) -> p s w", w=1).broadcast_to((128, 256, 1))), inc=1)
    v_ones65 = P.e["vector"].n
    P.wait("tensor", "vector", v_ones65)
    for tc in range(16):
        for et in range(2):
            b = et % 2
            bank_acquire(c, b)
            for dc in range(NDC):
                P.do("tensor", lambda e, b=b, dc=dc, tc=tc, et=et: e.matmul(
                    c.ps[b][:, :],
                    c.xln[:, dc * T + tc * 128:dc * T + tc * 128 + 128],
                    c.wst[:, A + dc * 1024 + et * 512:A + dc * 1024 + et * 512 + 512],
                    start=(dc == 0), stop=False), inc=0)
            P.do("tensor", lambda e, b=b, et=et: e.matmul(
                c.ps[b][:, :], c.ones_b[0:1, :],
                vbias[:, et * 512:et * 512 + 512],
                start=False, stop=True), inc=1)
            bank_done(c, b)
            bank_consume(c, b, "vector")
            P.do("vector", lambda e, b=b, tc=tc, et=et: e.tensor_copy(
                c.scr[:, tc * 1040 + et * 520:tc * 1040 + et * 520 + 520]
                .rearrange("p (h w) -> p h w", w=65)[:, :, 0:64],
                c.ps[b][:, :].rearrange("p (h w) -> p h w", w=64)), inc=1)
            bank_release(c, b, "vector")
    P.wait("tensor", "vector")
    P.wait("tensor", "gpsimd", ldr2)
    P.wait("tensor", "sync", n2)
    for mt in range(2):
        for qc in range(NQ):
            b = 2 + qc % 2
            bank_acquire(c, b)
            for dc in range(NDC):
                P.do("tensor", lambda e, b=b, dc=dc, mt=mt, qc=qc: e.matmul(
                    c.ps[b][:, :], wtile(c, B, 2, dc, mt),
                    xln_ap(c, dc, qc * TQ, TQ),
                    start=(dc == 0), stop=(dc == NDC - 1)),
                    inc=(1 if dc == NDC - 1 else 0))
            bank_done(c, b)
            bank_consume(c, b, "vector")
            P.do("vector", lambda e, b=b, mt=mt, qc=qc: e.tensor_scalar(
                out=h1b[:, mt * 2048 + qc * 512:mt * 2048 + qc * 512 + 512],
                in0=c.ps[b][:, :], scalar1=rb1[:, mt:mt + 1], scalar2=0.0,
                op0=ALU.add, op1=ALU.max), inc=1)
            bank_release(c, b, "vector")
    v_h1 = P.e["vector"].n
    P.wait("tensor", "vector", v_h1)
    for qc in range(NQ):
        b = 4 + qc % 2
        bank_acquire(c, b)
        for mt in range(2):
            P.do("tensor", lambda e, b=b, mt=mt, qc=qc: e.matmul(
                c.ps[b][0:16, :],
                c.wst[:, B + 2048 + mt * 16:B + 2048 + mt * 16 + 16],
                h1b[:, mt * 2048 + qc * 512:mt * 2048 + qc * 512 + 512],
                start=(mt == 0), stop=(mt == 1)), inc=(1 if mt == 1 else 0))
        bank_done(c, b)
        bank_consume(c, b, "scalar")
        P.do("scalar", lambda e, b=b, qc=qc: e.activation(
            rsb[:, qc * 512:qc * 512 + 512], c.ps[b][0:16, :], AF.Sigmoid,
            bias=rb2[:]), inc=1)
        P.do("scalar", lambda e, qc=qc: e.activation(
            rsb[:, qc * 512:qc * 512 + 512], rsb[:, qc * 512:qc * 512 + 512],
            AF.Sigmoid), inc=1)
        bank_release(c, b, "scalar")
    P.wait("vector", "scalar")
    cmpb = h1b[:].bitcast(F32)[0:16, 0:2048]
    P.do("vector", lambda e: e.memset(m3[:, 0:1], 0.5), inc=1)
    P.do("vector", lambda e: e.memset(m3[:, 2:3], 0.7310586), inc=1)
    KKEEP = float(int(T * 0.4))
    for it in range(25):
        P.do("vector", lambda e: e.tensor_tensor(
            out=m3[:, 1:2], in0=m3[:, 0:1], in1=m3[:, 2:3], op=ALU.add), inc=1)
        P.do("vector", lambda e: e.tensor_scalar(
            out=m3[:, 1:2], in0=m3[:, 1:2], scalar1=0.5, scalar2=None,
            op0=ALU.mult), inc=1)
        P.do("vector", lambda e: e.tensor_scalar(
            out=cmpb, in0=rsb[:], scalar1=m3[:, 1:2], scalar2=None,
            op0=ALU.is_ge, op1=ALU.add, accum_out=cnt[:]), inc=1)
        P.do("vector", lambda e: e.tensor_scalar(
            out=gts[:], in0=cnt[:], scalar1=KKEEP, scalar2=None,
            op0=ALU.is_ge), inc=1)
        P.do("vector", lambda e: e.tensor_tensor(
            out=df2[:], in0=m3[:, 1:3], in1=m3[:, 0:2], op=ALU.subtract), inc=1)
        P.do("vector", lambda e: e.tensor_scalar(
            out=df2[:], in0=df2[:], scalar1=gts[:], scalar2=None,
            op0=ALU.mult), inc=1)
        P.do("vector", lambda e: e.tensor_tensor(
            out=m3[:, 0:3:2], in0=m3[:, 0:2], in1=df2[:], op=ALU.add), inc=1)
    P.do("vector", lambda e: e.tensor_scalar(
        out=kfm[:], in0=rsb[:], scalar1=m3[:, 0:1], scalar2=None,
        op0=ALU.is_ge), inc=1)
    v_n = P.e["vector"].n
    P.wait("tensor", "vector", v_n)
    for tc in range(16):
        b = 6 + tc % 2
        bank_acquire(c, b)
        P.do("tensor", lambda e, b=b, tc=tc: e.transpose(
            c.ps[b].bitcast(BF16)[0:128, 0:16], kfm[:, tc * 128:tc * 128 + 128],
            c.ident_b[0:16, 0:16]), inc=1)
        bank_done(c, b)
        bank_consume(c, b, "vector")
        P.do("vector", lambda e, b=b, tc=tc: e.tensor_copy(
            ktok[:, tc * 16:tc * 16 + 16],
            c.ps[b].bitcast(BF16)[0:128, 0:16]), inc=1)
        bank_release(c, b, "vector")
    ph1.close()
    qh2 = [bes.enter_context(nc.sbuf_tensor(f"l1a_qh{i}", [128, 2048], BF16))
           for i in range(2)]
    kh2 = [bes.enter_context(nc.sbuf_tensor(f"l1a_kh{i}", [128, 2048], BF16))
           for i in range(2)]
    expt = bes.enter_context(nc.sbuf_tensor("l1a_exp", [128, 2048], BF16))
    bcolp2 = [bes.enter_context(nc.sbuf_tensor(f"l1a_bcp{i}", [128, 32], F32))
              for i in range(2)]
    onesc = bes.enter_context(nc.sbuf_tensor("l1a_on1", [128, 1], BF16))
    pe_n = P.e["tensor"].n
    P.wait("gpsimd", "tensor", pe_n)
    ldq = stage_weight_bf16(c, c.w["L1_attn_qkv_w"], 0, 0, 8, 8, A)
    ldk = stage_weight_bf16(c, c.w["L1_attn_qkv_w"], 0, 1024, 8, 8, B)
    P.do("vector", lambda e: e.memset(onesc[:], 1.0), inc=1)
    v_ones = P.e["vector"].n
    P.wait("tensor", "gpsimd", ldk)
    P.wait("tensor", "vector", v_ones)
    for b in range(6):
        bank_acquire(c, b)
    qk_rd_pe2 = [0, 0]
    bcp_rd_s2 = [0, 0]
    dnv_bc_pe = 0
    exp_done = [0, 0]
    ctx_done = [0, 0]
    ctx_tail_v = 0
    ctx_tail_s = 0
    for hp in range(8):
        h0 = 2 * hp
        qh = qh2[hp % 2]
        kh = kh2[hp % 2]
        bcolp = bcolp2[hp % 2]
        P.wait("vector", "tensor", qk_rd_pe2[hp % 2])
        for reg, buf, bcoli in ((A, qh, hp), (B, kh, 8 + hp)):
            for qc in range(NQ):
                b = 6 + qc % 2
                bank_acquire(c, b)
                for dc in range(NDC):
                    P.do("tensor", lambda e, b=b, dc=dc, reg=reg, hp=hp, qc=qc:
                         e.matmul(c.ps[b][:, :], wtile(c, reg, 8, dc, hp),
                                  xln_ap(c, dc, qc * TQ, TQ),
                                  start=(dc == 0), stop=(dc == NDC - 1)),
                         inc=(1 if dc == NDC - 1 else 0))
                bank_done(c, b)
                bank_consume(c, b, "vector")
                P.do("vector", lambda e, b=b, buf=buf, bcoli=bcoli, qc=qc:
                     e.tensor_scalar(out=buf[:, qc * 512:qc * 512 + 512],
                                     in0=c.ps[b][:, :], scalar2=None,
                                     scalar1=qkvb[:, bcoli:bcoli + 1],
                                     op0=ALU.add), inc=1)
                bank_release(c, b, "vector")
        qk_wr_v = P.e["vector"].n
        P.wait("vector", "scalar", bcp_rd_s2[hp % 2])
        for hsel in range(2):
            P.do("vector", lambda e, hsel=hsel, h0=h0, bcolp=bcolp:
                 e.tensor_scalar(
                     out=bcolp[:, hsel:32:2], in0=ktok[:, h0 + hsel:256:16],
                     scalar1=-1.0, scalar2=1e30, op0=ALU.add, op1=ALU.mult),
                 inc=1)
        v_bcp = P.e["vector"].n
        P.wait("scalar", "vector", v_bcp)
        P.wait("tensor", "vector", qk_wr_v)
        for qc in range(NQ):
            for tc in range(17):
                par = tc % 2
                if tc < 16:
                    P.wait("tensor", "scalar", exp_done[par])
                    for hsel in range(2):
                        sb = (0 if par == 0 else 4) + hsel
                        po = hsel * 64
                        P.do("tensor", lambda e, sb=sb, po=po, tc=tc, qc=qc,
                             kh=kh, qh=qh:
                             e.matmul(c.ps[sb][:, :],
                                      kh[po:po + 64, tc * 128:tc * 128 + 128],
                                      qh[po:po + 64, qc * 512:qc * 512 + 512],
                                      start=True, stop=True), inc=1)
                    sc_pe = P.e["tensor"].n
                    if qc == NQ - 1 and tc == 15:
                        qk_rd_pe2[hp % 2] = sc_pe
                    P.wait("scalar", "tensor", sc_pe)
                    P.wait("scalar", "tensor", ctx_done[par])
                    for hsel in range(2):
                        sb = (0 if par == 0 else 4) + hsel
                        slot = (par * 2 + hsel) * 512
                        P.do("scalar", lambda e, sb=sb, slot=slot, tc=tc,
                             hsel=hsel, bcolp=bcolp: e.activation(
                                 expt[:, slot:slot + 512], c.ps[sb][:, :],
                                 AF.Exp, scale=0.125,
                                 bias=bcolp[:, tc * 2 + hsel:tc * 2 + hsel + 1]),
                             inc=1)
                    exp_done[par] = P.e["scalar"].n
                if tc > 0:
                    tp_ = tc - 1
                    parp = tp_ % 2
                    P.wait("tensor", "scalar", exp_done[parp])
                    if tp_ == 0:
                        P.wait("tensor", "vector", ctx_tail_v)
                        P.wait("tensor", "scalar", ctx_tail_s)
                    for hsel in range(2):
                        slot = (parp * 2 + hsel) * 512
                        cb = 2 + hsel
                        vcol = tp_ * 1040 + (h0 + hsel) * 65
                        P.do("tensor", lambda e, cb=cb, slot=slot, tp_=tp_,
                             vcol=vcol: e.matmul(
                                 c.ps[cb][0:65, :],
                                 c.scr[:, vcol:vcol + 65],
                                 expt[:, slot:slot + 512],
                                 start=(tp_ == 0), stop=(tp_ == 15)), inc=1)
                    ctx_done[parp] = P.e["tensor"].n
            for hsel in range(2):
                cb = 2 + hsel
                po = hsel * 64
                P.wait("scalar", "tensor", ctx_done[1])
                P.do("scalar", lambda e, cb=cb: e.activation(
                    dnvf[:], c.ps[cb][64:65, :], AF.Ln), inc=1)
                P.do("scalar", lambda e: e.activation(
                    dnvb[:], dnvf[:], AF.Exp, scale=-1.0), inc=1)
                s_d = P.e["scalar"].n
                bank_acquire(c, 6)
                P.wait("tensor", "scalar", s_d)
                P.do("tensor", lambda e: e.matmul(
                    c.ps[6][:, :], c.ones_b[0:1, :], dnvb[:],
                    start=True, stop=True), inc=1)
                bank_done(c, 6)
                bank_consume(c, 6, "vector")
                P.do("vector", lambda e: e.tensor_copy(
                    dbcs[:], c.ps[6][:, :]), inc=1)
                bank_release(c, 6, "vector")
                P.do("vector", lambda e, cb=cb, po=po: e.tensor_tensor(
                    out=tsb[po:po + 64, :], in0=c.ps[cb][0:64, :],
                    in1=dbcs[0:64, :], op=ALU.mult), inc=1)
                dnv_bc_pe = P.e["vector"].n
                P.do("vector", lambda e, po=po, hp=hp, qc=qc: e.tensor_tensor(
                    out=c.xr[po:po + 64, hp * T + qc * TQ:hp * T + qc * TQ + TQ],
                    in0=c.xr[po:po + 64, hp * T + qc * TQ:hp * T + qc * TQ + TQ],
                    in1=tsb[po:po + 64, :], op=ALU.add), inc=1)
            ctx_tail_v = P.e["vector"].n
            ctx_tail_s = P.e["scalar"].n
        bcp_rd_s2[hp % 2] = P.e["scalar"].n
    P.barrier()
    bes.close()


def b_l3_attn(c):
    P = c.P
    emit_ln(c)
    bes = contextlib.ExitStack()
    nc = c.nc
    A, B = 0, 8192
    vwst = bes.enter_context(nc.sbuf_tensor("l3a_vw", [128, 1024], BF16))
    vpair = bes.enter_context(nc.sbuf_tensor("l3a_vp", [128, 2080], BF16))
    qh = bes.enter_context(nc.sbuf_tensor("l3a_qh", [128, 2048], BF16))
    kh = bes.enter_context(nc.sbuf_tensor("l3a_kh", [128, 2048], BF16))
    onesc3 = bes.enter_context(nc.sbuf_tensor("l3a_on1", [128, 1], BF16))
    expt = bes.enter_context(nc.sbuf_tensor("l3a_exp", [128, 2048], BF16))
    dnvf = bes.enter_context(nc.sbuf_tensor("l3a_dnvf", [1, 512], F32))
    dnvb = bes.enter_context(nc.sbuf_tensor("l3a_dnvb", [1, 512], BF16))
    dbcs = bes.enter_context(nc.sbuf_tensor("l3a_dbc", [128, 512], BF16))
    bo_sb = bes.enter_context(nc.sbuf_tensor("l3a_bo", [128, 8], F32))
    trib = bes.enter_context(nc.sbuf_tensor("l3a_trib", [128, 128], BF16))

    nbo = P.dma("sync", bo_sb[:], c.w["L3_attn_bo"])
    P.do("vector", lambda e: e.tensor_copy(trib[:], c.tri_f[:]), inc=1)
    ldq = stage_weight_bf16(c, c.w["L3_attn_Wq"], 0, 0, 8, 8, A)
    ldk = stage_weight_bf16(c, c.w["L3_attn_Wk"], 0, 0, 8, 8, B)
    P.wait("tensor", "gpsimd", ldk)
    for b in range(6):
        bank_acquire(c, b)
    P.do("vector", lambda e: e.memset(onesc3[:], 1.0), inc=1)
    P.do("vector", lambda e: e.tensor_copy(
        vpair[:].rearrange("p (s w) -> p s w", w=65)[:, :, 64:65],
        c.ones_b[:, 0:1].rearrange(
            "p (s w) -> p s w", w=1).broadcast_to((128, 32, 1))), inc=1)
    v_on3 = P.e["vector"].n
    P.wait("tensor", "vector", v_on3)
    trib_bc = trib[:].rearrange("p (u n) -> p u n", u=1).broadcast_to((128, 2, 128))
    pe_prev_hp = 0
    dnv_bc_pe = 0
    exp_done = [0, 0]
    ctx_done = [0, 0]
    msk_done = [0, 0]
    ctx_tail_v = 0
    ctx_tail_s = 0
    for hp in range(8):
        h0 = 2 * hp
        P.wait("vector", "tensor", pe_prev_hp)
        for reg, buf in ((A, qh), (B, kh)):
            for qc in range(NQ):
                b = 6 + qc % 2
                bank_acquire(c, b)
                for dc in range(NDC):
                    P.do("tensor", lambda e, b=b, dc=dc, reg=reg, hp=hp, qc=qc:
                         e.matmul(c.ps[b][:, :], wtile(c, reg, 8, dc, hp),
                                  xln_ap(c, dc, qc * TQ, TQ),
                                  start=(dc == 0), stop=(dc == NDC - 1)),
                         inc=(1 if dc == NDC - 1 else 0))
                bank_done(c, b)
                bank_consume(c, b, "vector")
                P.do("vector", lambda e, b=b, buf=buf, qc=qc: e.tensor_copy(
                    buf[:, qc * 512:qc * 512 + 512], c.ps[b][:, :]), inc=1)
                bank_release(c, b, "vector")
        P.wait("gpsimd", "tensor", pe_prev_hp)
        ldv = stage_weight_bf16(c, c.w["L3_attn_Wv"], 0, hp * 128, 8, 1,
                                0, mtile=128, dst=vwst)
        P.wait("tensor", "gpsimd", ldv)
        for tc in range(16):
            b = 6 + tc % 2
            bank_acquire(c, b)
            for dc in range(NDC):
                P.do("tensor", lambda e, b=b, dc=dc, tc=tc: e.matmul(
                    c.ps[b][:, 0:128],
                    c.xln[:, dc * T + tc * 128:dc * T + tc * 128 + 128],
                    vwst[:, dc * 128:dc * 128 + 128],
                    start=(dc == 0), stop=(dc == NDC - 1)),
                    inc=(1 if dc == NDC - 1 else 0))
            bank_done(c, b)
            bank_consume(c, b, "vector")
            P.do("vector", lambda e, b=b, tc=tc: e.tensor_copy(
                vpair[:, tc * 130:tc * 130 + 130].rearrange(
                    "p (h w) -> p h w", w=65)[:, :, 0:64],
                c.ps[b][:, 0:128].rearrange("p (h w) -> p h w", w=64)), inc=1)
            bank_release(c, b, "vector")
        qkv_wr_v = P.e["vector"].n
        P.wait("tensor", "vector", qkv_wr_v)
        for qc in range(NQ):
            ntc = 4 * qc + 4
            for tc in range(ntc + 1):
                par = tc % 2
                if tc < ntc:
                    k = tc - 4 * qc
                    off = 128 * k if k > 0 else 0
                    w = 512 - off
                    P.wait("tensor", "scalar", exp_done[par])
                    for hsel in range(2):
                        sb = (0 if par == 0 else 4) + hsel
                        po = hsel * 64
                        P.do("tensor", lambda e, sb=sb, po=po, tc=tc, qc=qc,
                             off=off, w=w: e.matmul(
                                 c.ps[sb][:, 0:w],
                                 kh[po:po + 64, tc * 128:tc * 128 + 128],
                                 qh[po:po + 64,
                                    qc * 512 + off:qc * 512 + 512],
                                 start=True, stop=True), inc=1)
                    sc_pe = P.e["tensor"].n
                    P.wait("scalar", "tensor", sc_pe)
                    P.wait("scalar", "tensor", ctx_done[par])
                    for hsel in range(2):
                        sb = (0 if par == 0 else 4) + hsel
                        slot = (par * 2 + hsel) * 512
                        P.do("scalar", lambda e, sb=sb, slot=slot, off=off,
                             w=w: e.activation(
                                 expt[:, slot + off:slot + 512],
                                 c.ps[sb][:, 0:w], AF.Exp, scale=0.125),
                             inc=1)
                    exp_done[par] = P.e["scalar"].n
                    if k >= 0:
                        mb_ = par * 1024 + off
                        vslot = expt[:, mb_:mb_ + 640].rearrange(
                            "p (u n) -> p u n", n=128)[:, 0:5:4, :]
                        P.wait("vector", "scalar", exp_done[par])
                        P.do("vector", lambda e, vslot=vslot: e.tensor_tensor(
                            out=vslot, in0=vslot, in1=trib_bc,
                            op=ALU.mult), inc=1)
                        msk_done[par] = P.e["vector"].n
                if tc > 0:
                    tp_ = tc - 1
                    parp = tp_ % 2
                    kp = tp_ - 4 * qc
                    offp = 128 * kp if kp > 0 else 0
                    P.wait("tensor", "scalar", exp_done[parp])
                    if kp >= 0:
                        P.wait("tensor", "vector", msk_done[parp])
                    if tp_ == 0:
                        P.wait("tensor", "vector", ctx_tail_v)
                        P.wait("tensor", "scalar", ctx_tail_s)
                    for hsel in range(2):
                        slot = (parp * 2 + hsel) * 512
                        cb = 2 + hsel
                        vcol = tp_ * 130 + hsel * 65
                        P.do("tensor", lambda e, cb=cb, slot=slot, tp_=tp_,
                             vcol=vcol, offp=offp, ntc=ntc: e.matmul(
                                 c.ps[cb][0:65, offp:512],
                                 vpair[:, vcol:vcol + 65],
                                 expt[:, slot + offp:slot + 512],
                                 start=(tp_ == 0), stop=(tp_ == ntc - 1),
                                 skip_group_check=True), inc=1)
                    ctx_done[parp] = P.e["tensor"].n
            for hsel in range(2):
                cb = 2 + hsel
                po = hsel * 64
                P.wait("scalar", "tensor", ctx_done[1])
                P.do("scalar", lambda e, cb=cb: e.activation(
                    dnvf[:], c.ps[cb][64:65, :], AF.Ln), inc=1)
                P.do("scalar", lambda e: e.activation(
                    dnvb[:], dnvf[:], AF.Exp, scale=-1.0), inc=1)
                s_d = P.e["scalar"].n
                bank_acquire(c, 6)
                P.wait("tensor", "scalar", s_d)
                P.do("tensor", lambda e: e.matmul(
                    c.ps[6][:, :], c.ones_b[0:1, :], dnvb[:],
                    start=True, stop=True), inc=1)
                bank_done(c, 6)
                bank_consume(c, 6, "vector")
                P.do("vector", lambda e: e.tensor_copy(
                    dbcs[:], c.ps[6][:, :]), inc=1)
                bank_release(c, 6, "vector")
                P.do("vector", lambda e, cb=cb, po=po, hp=hp, qc=qc:
                     e.tensor_tensor(
                         out=c.scr[po:po + 64,
                                   hp * 2048 + qc * 512:hp * 2048 + qc * 512 + 512],
                         in0=c.ps[cb][0:64, :], in1=dbcs[0:64, :],
                         op=ALU.mult), inc=1)
                dnv_bc_pe = P.e["vector"].n
            ctx_tail_v = P.e["vector"].n
            ctx_tail_s = P.e["scalar"].n
        pe_prev_hp = P.e["tensor"].n
    pe_n = P.e["tensor"].n
    P.wait("gpsimd", "tensor", pe_n)
    ldo = stage_weight_bf16(c, c.w["L3_attn_Wo"], 0, 0, 8, 8, A)
    P.wait("tensor", "gpsimd", ldo)
    P.wait("tensor", "vector")
    P.wait("tensor", "sync", nbo)
    for qc in range(NQ):
        for dct in range(8):
            b = dct % 2
            bank_acquire(c, b)
            for cc in range(8):
                P.do("tensor", lambda e, b=b, cc=cc, dct=dct, qc=qc: e.matmul(
                    c.ps[b][:, :], wtile(c, A, 8, cc, dct),
                    c.scr[:, cc * 2048 + qc * 512:cc * 2048 + qc * 512 + 512],
                    start=(cc == 0), stop=(cc == 7)),
                    inc=(1 if cc == 7 else 0))
            bank_done(c, b)
            bank_consume(c, b, "vector")
            P.do("vector", lambda e, b=b, dct=dct, qc=qc: e.scalar_tensor_tensor(
                out=xr_ap(c, dct, qc * TQ, TQ), in0=c.ps[b][:, :],
                scalar=bo_sb[:, dct:dct + 1], in1=xr_ap(c, dct, qc * TQ, TQ),
                op0=ALU.add, op1=ALU.add), inc=1)
            bank_release(c, b, "vector")
    P.barrier()
    bes.close()



def b_l1_ffn(c):
    P = c.P
    emit_ln(c)
    bes = contextlib.ExitStack()
    H = 4096
    b1_sb = bes.enter_context(c.nc.sbuf_tensor("l1f_b1", [128, 32], F32))
    b2_sb = bes.enter_context(c.nc.sbuf_tensor("l1f_b2", [128, NDC], F32))
    P.dma("sync", b1_sb[:], c.w["L1_ffn_b1"])
    P.dma("sync", b2_sb[:], c.w["L1_ffn_b2"])
    P.wait("scalar", "sync")
    P.wait("vector", "sync")

    region = [0, 8192]
    region_pe = [0, 0]
    stages_q = ([("L1_ffn_w1", 0, hs * 1024) for hs in range(4)] +
                [("L1_ffn_w2", hs * 1024, 0) for hs in range(4)])

    def l1f_stage(tag):
        w_name, k0, m0 = stages_q[tag]
        r = region[tag % 2]
        P.wait("gpsimd", "tensor", region_pe[tag % 2])
        ld = stage_weight_bf16(c, c.w[w_name], k0, m0, 8, 8, r)
        return r, ld

    for q in range(NQ):
        t0 = q * TQ
        tag = 0
        nxt = l1f_stage(0)
        for hs in range(4):
            r, ld = nxt
            P.wait("tensor", "gpsimd", ld)
            if tag + 1 < 8:
                nxt = l1f_stage(tag + 1)
            for hl in range(8):
                hc = hs * 8 + hl
                b = hc % 4
                bank_acquire(c, b)
                for dc in range(NDC):
                    P.do("tensor", lambda e, b=b, r=r, dc=dc, hl=hl, t0=t0:
                         e.matmul(c.ps[b][:, :], wtile(c, r, 8, dc, hl),
                                  xln_ap(c, dc, t0, TQ),
                                  start=(dc == 0), stop=(dc == NDC - 1)),
                         inc=(1 if dc == NDC - 1 else 0))
                bank_done(c, b)
                bank_consume(c, b, "scalar")
                P.do("scalar", lambda e, b=b, hc=hc: e.activation(
                    c.scr[:, hc * 512:hc * 512 + 512], c.ps[b][:, :], AF.Gelu,
                    bias=b1_sb[:, hc:hc + 1]), inc=1)
                bank_release(c, b, "scalar")
            region_pe[tag % 2] = P.e["tensor"].n
            tag += 1
        sc_n = P.e["scalar"].n
        for hs in range(4):
            r, ld = nxt
            P.wait("tensor", "gpsimd", ld)
            if tag + 1 < 8:
                nxt = l1f_stage(tag + 1)
            if hs == 0:
                for b in range(8):
                    bank_acquire(c, b)
                P.wait("tensor", "scalar", sc_n)
            for dc in range(NDC):
                for hl in range(8):
                    hc = hs * 8 + hl
                    P.do("tensor", lambda e, dc=dc, r=r, hl=hl, hc=hc, hs=hs:
                         e.matmul(c.ps[dc][:, :], wtile(c, r, 8, hl, dc),
                                  c.scr[:, hc * 512:hc * 512 + 512],
                                  start=(hs == 0 and hl == 0),
                                  stop=(hs == 3 and hl == 7)),
                         inc=(1 if hl == 7 else 0))
            region_pe[tag % 2] = P.e["tensor"].n
            tag += 1
        for dc in range(NDC):
            bank_done(c, dc)
            bank_consume(c, dc, "vector")
            P.do("vector", lambda e, dc=dc, t0=t0: e.scalar_tensor_tensor(
                out=xr_ap(c, dc, t0, TQ), in0=c.ps[dc][:, :],
                scalar=b2_sb[:, dc:dc + 1], in1=xr_ap(c, dc, t0, TQ),
                op0=ALU.add, op1=ALU.add), inc=1)
            bank_release(c, dc, "vector")
    P.barrier()
    bes.close()

def b_identity(c):
    pass


def finish(c):
    P = c.P
    P.barrier()
    n = P.dma("sync", c.x_out.rearrange("(ch p) t -> p ch t", p=128),
              c.xr[:].rearrange("p (ch t) -> p ch t", ch=NDC))
    P.wait("sync", "sync", n)


def _w(name, tr=None, srckey=None):
    return (name, tr, srckey)


def _tT(a):
    return a.T


def _tB(a):
    return a.reshape(-1, 128).T


BLOCKS = [
    ("L0_attn", b_l0_attn, [
        _w("L0_attn_in_proj_w"), _w("L0_attn_out_proj_w"),
        _w("L0_attn_W_re"), _w("L0_attn_W_im"),
        _w("L0THETA", lambda a: np.ascontiguousarray(
            a.reshape(2048).reshape(16, 128).T), srckey="L0_attn_theta"),
        _w("L0TAPS", lambda a: _l0taps(a), srckey="L0_attn_conv_k"),
        _w("L0CONVB", lambda a: np.ascontiguousarray(
            np.pad(a, (0, 96)).reshape(33, 128).T), srckey="L0_attn_conv_b"),
        _w("L0SSC", lambda a: a.reshape(32, 1), srckey="L0_attn_score_scale"),
        _w("L0NSC", lambda a: a.reshape(128, 1), srckey="L0_attn_norm_scale"),
        _w("L0DERIV", lambda a: a.reshape(1, 3), srckey="L0_attn_deriv_logits"),
        _w("L0SLOPES", None, srckey="__L0SLOPES__"),
        _w("L0DIST", None), _w("L0SEL", None), _w("L0HSEL", None),
        _w("L0HSEL2", None), _w("IDENT128", None)]),
    ("L0_ffn", b_l0_ffn, [_w("L0_ffn_w_gate", _tT), _w("L0_ffn_w_up", _tT),
                          _w("L0_ffn_w_down", _tT)]),
    ("L1_attn", b_l1_attn, [
        _w("L1_attn_qkv_w", _tT), _w("L1_attn_qkv_b", _tB),
        _w("L1A_VBROW", lambda a: a[2048:3072].reshape(1, 1024),
           srckey="L1_attn_qkv_b"),
        _w("L1_attn_r_w1", _tT), _w("L1_attn_r_b1", _tB),
        _w("L1_attn_r_w2", _tT), _w("L1_attn_r_b2", lambda a: a.reshape(16, 1)),
        _w("IDENT128", None)]),
    ("L1_ffn", b_l1_ffn, [_w("L1_ffn_w1"), _w("L1_ffn_b1", _tB),
                          _w("L1_ffn_w2"), _w("L1_ffn_b2", _tB)]),
    ("L2_attn", b_l2_attn, [_w("L2_attn_w_q"), _w("L2_attn_w_k"),
                            _w("L2_attn_w_v"), _w("L2_attn_w_o")]),
    ("L2_ffn", b_l2_ffn, [_w("L2_ffn_Wr"), _w("L2_ffn_br", lambda a: a.reshape(1, 8)),
                          _w("L2_ffn_W1"), _w("L2_ffn_b1", lambda a: np.ascontiguousarray(a.reshape(8, 8, 128).transpose(0, 2, 1))),
                          _w("L2_ffn_W2"),
                          _w("L2B2T", lambda a: np.ascontiguousarray(
                              a.reshape(8, 8, 128).transpose(0, 2, 1)),
                             srckey="L2_ffn_b2"),
                          _w("IDENT128", None), _w("TRI128", None)]),
    ("L3_attn", b_l3_attn, [_w("L3_attn_Wq", _tT), _w("L3_attn_Wk", _tT),
                          _w("L3_attn_Wv", _tT), _w("L3_attn_Wo", _tT),
                          _w("L3_attn_bo", _tB), _w("IDENT128", None),
                          _w("TRI128", None)]),
    ("L3_ffn", b_l3_ffn, [_w("L3_ffn_gate_w", _tT),
                          _w("L3_ffn_expert_w", lambda a: a.transpose(0, 2, 1)),
                          _w("IDENT128", None)]),
]

GROUPS = [[0, 1, 2, 3, 4, 5, 6, 7]]


def _l0taps(a):
    ck = np.pad(a[:, 0, :], ((0, 0), (0, 96)))
    out = np.zeros((128, 132), np.float32)
    for ct in range(33):
        for k in range(4):
            out[:, ct * 4 + k] = ck[k, ct * 128:ct * 128 + 128]
    return out


def _l0dist():
    d = np.zeros((32, 2048), np.float32)
    t = np.arange(2048, dtype=np.float32)
    d[:28] = 2047.0 - t
    d[28:] = t
    return d


def _l0sel():
    s = np.zeros((32, 2048), np.float32)
    for ch in range(2048):
        s[ch // 64, ch] = 1.0
    return s


_CONSTS = {"IDENT128": lambda: np.eye(128, dtype=np.float32),
           "TRI128": lambda: np.ascontiguousarray(
               np.tril(np.ones((128, 128), np.float32)).T),
           "L0DIST": _l0dist,
           "L0SEL": _l0sel,
           "L0HSEL": lambda: (np.arange(128)[:, None] // 64 ==
                              np.arange(2)[None, :]).astype(np.float32),
           "L0HSEL2": lambda: (np.arange(128)[None, :] // 64 ==
                               np.arange(2)[:, None]).astype(np.float32)}


def _prep_host(inputs):
    arrs = {}
    for name, builder, wl in BLOCKS:
        for key, tr, srck in wl:
            kk = srck or key
            if kk == "__L0SLOPES__":
                a = np.concatenate([np.asarray(inputs["L0_attn_decay_slopes"]),
                                    np.asarray(inputs["L0_attn_anchor_slopes"])]
                                   ).reshape(32, 1)
            else:
                a = np.asarray(inputs[kk]) if kk in inputs else _CONSTS[key]()
            if tr is not None:
                a = np.ascontiguousarray(tr(a))
            else:
                a = np.ascontiguousarray(a)
            arrs[key] = a
    return arrs


def _weight_decls(block_ids, arrs):
    decl = {}
    for bi in block_ids:
        for key, tr, srck in BLOCKS[bi][2]:
            a = arrs[key]
            dt = I32 if a.dtype == np.int32 else F32
            decl[key] = (a.shape, dt)
    return decl


def make_program(block_ids, arrs):
    nc = bass.Bass("TRN2", target_bir_lowering=False)
    es = contextlib.ExitStack()
    P = Prog(nc, es)
    c = make_ctx(nc, es, P, _weight_decls(block_ids, arrs))
    mark = es.enter_context(nc.sbuf_tensor("blkmark", [1, 8], F32))
    P.do("vector", lambda e: e.memset(mark[:], 0.0), inc=1)
    for bi in block_ids:
        P.do("vector", lambda e: e.pool_avg(mark[0:1, 0:1], mark[0:1, 0:8]),
             inc=1)
        BLOCKS[bi][1](c)
    P.do("vector", lambda e: e.pool_avg(mark[0:1, 0:1], mark[0:1, 0:8]),
         inc=1)
    finish(c)
    P.emit()
    es.close()
    return nc


def kernel(**inputs):
    x = np.asarray(inputs["L0_attn_x"], np.float32)
    arrs = _prep_host(inputs)
    cur = [np.ascontiguousarray(x[b].T) for b in range(8)]
    for g in GROUPS:
        nc = make_program(g, arrs)
        need = {k for bi in g for k, _, _s in BLOCKS[bi][2]}
        in_maps = [dict({k: arrs[k] for k in need}, x_in=cur[b]) for b in range(8)]
        res = run_bass_kernel_spmd(nc, in_maps, core_ids=list(range(8)))
        cur = [np.asarray(res.results[b]["x_out"]) for b in range(8)]
    out = np.stack([c.T for c in cur]).astype(np.float32)
    return out


# version 38
# speedup vs baseline: 1.0199x; 1.0015x over previous
import contextlib
import numpy as np
import ml_dtypes
import concourse.bass as bass
import concourse.mybir as mybir
from concourse.bass_utils import run_bass_kernel_spmd

F32 = mybir.dt.float32
BF16 = mybir.dt.bfloat16
I32 = mybir.dt.int32
AF = mybir.ActivationFunctionType
ALU = mybir.AluOpType
AX = mybir.AxisListType

T = 2048
D = 1024
NDC = D // 128
TQ = 512
NQ = T // TQ
ENGS = ("sync", "gpsimd", "tensor", "vector", "scalar")
BF = ml_dtypes.bfloat16


class _Eng:
    def __init__(self, sem):
        self.sem = sem
        self.n = 0
        self.q = []
        self.selfwait = 0


class Prog:
    def __init__(self, nc, es):
        self.nc = nc
        self.es = es
        self.e = {name: _Eng(es.enter_context(nc.semaphore(f"sem_{name}")))
                  for name in ENGS}

    def do(self, eng, fn, inc=0):
        E = self.e[eng]
        if eng != "tensor" and E.n > E.selfwait:
            E.q.append(lambda e, s=E.sem, v=E.n: e.wait_ge(s, v))
            E.selfwait = E.n
        if inc:
            E.n += inc

            def thunk(e, fn=fn, sem=E.sem, inc=inc):
                fn(e).then_inc(sem, inc)
            E.q.append(thunk)
        else:
            E.q.append(lambda e, fn=fn: fn(e))
        return E.n

    def dma(self, eng, out, in_, **kw):
        return self.do(eng, lambda e, o=out, i=in_, kw=kw:
                       e.dma_start(out=o, in_=i, **kw), inc=16)

    def wait(self, eng, other, val=None):
        O = self.e[other]
        v = O.n if val is None else val
        if v > 0:
            self.e[eng].q.append(lambda e, s=O.sem, v=v: e.wait_ge(s, v))

    def barrier(self):
        counts = {n: self.e[n].n for n in ENGS}
        for a in ENGS:
            for b in ENGS:
                if a != b and counts[b] > 0:
                    self.e[a].q.append(
                        lambda e, s=self.e[b].sem, v=counts[b]: e.wait_ge(s, v))

    def emit(self):
        with self.nc.Block() as blk:
            for name in ENGS:
                E = self.e[name]
                if E.q:
                    getattr(blk, name)(lambda e, q=E.q: [f(e) for f in q])


class Ctx:
    pass


def make_ctx(nc, es, P, weights):
    c = Ctx()
    c.nc, c.es, c.P = nc, es, P
    c.x_in = nc.dram_tensor("x_in", [D, T], F32, kind="ExternalInput").ap()
    c.x_out = nc.dram_tensor("x_out", [D, T], F32, kind="ExternalOutput").ap()
    c.w = {nm: nc.dram_tensor(nm, list(shape), dt, kind="ExternalInput").ap()
           for nm, (shape, dt) in weights.items()}

    c.xr = es.enter_context(nc.sbuf_tensor("xr", [128, NDC * T], F32))
    c.xln = es.enter_context(nc.sbuf_tensor("xln", [128, NDC * T], BF16))
    c.scr = es.enter_context(nc.sbuf_tensor("scr", [128, 16 * 1024 + 256], BF16))
    c.wst = es.enter_context(nc.sbuf_tensor("wst", [128, 16 * 1024], BF16))
    c.aux = es.enter_context(nc.sbuf_tensor("aux", [128, 512], F32))
    c.strow = es.enter_context(nc.sbuf_tensor("strow", [1, 1536], F32))
    c.ones_b = es.enter_context(nc.sbuf_tensor("ones_b", [128, 128], BF16))
    c.ones_f = es.enter_context(nc.sbuf_tensor("ones_f", [1, 128], F32))
    c.consts = es.enter_context(nc.sbuf_tensor("consts", [128, 12], F32))
    P.do("vector", lambda e: e.memset(c.ones_b[:], 1.0), inc=1)
    P.do("vector", lambda e: e.memset(c.ones_f[:], 1.0), inc=1)
    P.do("vector", lambda e: e.memset(c.consts[:, 0:1], 1e-5), inc=1)
    P.do("vector", lambda e: e.memset(c.consts[:, 1:2], 1e-5 * 2.0 ** -64), inc=1)
    P.do("vector", lambda e: e.memset(c.consts[:, 2:3], -32.0 * float(np.log(2.0))), inc=1)

    c.ps = [es.enter_context(nc.psum_tensor(f"ps{i}", [128, 512], F32))
            for i in range(8)]
    c.ps_n = [0] * 8
    c.ps_f = [None] * 8

    if "IDENT128" in c.w:
        c.ident_b = es.enter_context(nc.sbuf_tensor("ident_b", [128, 128], BF16))
        P.dma("gpsimd", c.ident_b[:], c.w["IDENT128"])
        c.ident_f = es.enter_context(nc.sbuf_tensor("ident_f", [128, 128], F32))
        P.dma("sync", c.ident_f[:], c.w["IDENT128"])
    if "TRI128" in c.w:
        c.tri_f = es.enter_context(nc.sbuf_tensor("tri_f", [128, 128], F32))
        P.dma("sync", c.tri_f[:], c.w["TRI128"])
    c.ones_cf = es.enter_context(nc.sbuf_tensor("ones_cf", [128, 1], F32))
    P.do("vector", lambda e: e.memset(c.ones_cf[:], 1.0), inc=1)
    P.do("vector", lambda e: e.memset(c.consts[:, 3:4], 0.0), inc=1)
    P.dma("sync", c.xr[:].rearrange("p (ch t) -> p ch t", ch=NDC),
          c.x_in.rearrange("(ch p) t -> p ch t", p=128))
    P.wait("tensor", "sync")
    P.wait("vector", "sync")
    P.wait("scalar", "sync")
    return c


def bank_acquire(c, b):
    if c.ps_f[b] is not None:
        eng, cnt = c.ps_f[b]
        c.P.wait("tensor", eng, cnt)


def bank_done(c, b):
    c.ps_n[b] = c.P.e["tensor"].n


def bank_consume(c, b, eng):
    c.P.wait(eng, "tensor", c.ps_n[b])


def bank_release(c, b, eng):
    c.ps_f[b] = (eng, c.P.e[eng].n)


def xr_ap(c, dc, t0=0, tn=T):
    return c.xr[:, dc * T + t0: dc * T + t0 + tn]


def xln_ap(c, dc, t0=0, tn=T):
    return c.xln[:, dc * T + t0: dc * T + t0 + tn]


def emit_ln(c):
    P = c.P
    slot_pe = [0, 0]

    def stage(q):
        t0 = q * TQ
        off = (q % 2) * 1024
        P.wait("vector", "tensor", slot_pe[q % 2])
        P.wait("scalar", "tensor", slot_pe[q % 2])
        for dc in range(NDC):
            P.do("vector", lambda e, dc=dc, t0=t0, off=off: e.tensor_copy(
                c.scr[:, dc * 2048 + off:dc * 2048 + off + 512],
                xr_ap(c, dc, t0, TQ)), inc=1)
            P.do("scalar", lambda e, dc=dc, t0=t0, off=off: e.activation(
                c.scr[:, dc * 2048 + off + 512:dc * 2048 + off + 1024],
                xr_ap(c, dc, t0, TQ), AF.Square), inc=1)
        return P.e["vector"].n, P.e["scalar"].n

    staged = stage(0)
    for q in range(NQ):
        t0 = q * TQ
        soff = (q % 2) * 1024
        r_mean = c.strow[0:1, 0:TQ]
        r_vb = c.strow[0:1, TQ:2 * TQ]
        r_a = c.strow[0:1, 2 * TQ:3 * TQ]
        v_n, s_n = staged
        for b, off in ((0, soff), (1, soff + 512)):
            bank_acquire(c, b)
            P.wait("tensor", "vector", v_n)
            P.wait("tensor", "scalar", s_n)
            for dc in range(NDC):
                P.do("tensor", lambda e, b=b, dc=dc, off=off: e.matmul(
                    c.ps[b][0:1, :], c.ones_b[:, 0:1],
                    c.scr[:, dc * 2048 + off:dc * 2048 + off + 512],
                    start=(dc == 0), stop=(dc == NDC - 1)),
                    inc=(1 if dc == NDC - 1 else 0))
            bank_done(c, b)
        slot_pe[q % 2] = P.e["tensor"].n
        if q + 1 < NQ:
            staged = stage(q + 1)
        bank_consume(c, 0, "vector")
        bank_consume(c, 1, "vector")
        P.do("vector", lambda e, r=r_mean: e.tensor_scalar(
            out=r, in0=c.ps[0][0:1, :], scalar1=1.0 / D,
            scalar2=None, op0=ALU.mult), inc=1)
        P.do("vector", lambda e, r=r_vb: e.tensor_scalar(
            out=r, in0=c.ps[1][0:1, :], scalar1=1.0 / D,
            scalar2=None, op0=ALU.mult), inc=1)
        bank_release(c, 0, "vector")
        bank_release(c, 1, "vector")
        P.do("vector", lambda e, r_a=r_a, r_mean=r_mean: e.tensor_tensor(
            out=r_a, in0=r_mean, in1=r_mean, op=ALU.mult), inc=1)
        P.do("vector", lambda e, r_vb=r_vb, r_a=r_a: e.tensor_tensor(
            out=r_vb, in0=r_vb, in1=r_a, op=ALU.subtract), inc=1)
        P.wait("scalar", "vector")
        P.do("scalar", lambda e, r_a=r_a, r_vb=r_vb: e.activation(
            r_a, r_vb, AF.Ln, bias=c.consts[0:1, 1:2], scale=2.0 ** -64), inc=1)
        P.do("scalar", lambda e, r_a=r_a: e.activation(
            r_a, r_a, AF.Exp, scale=-0.5, bias=c.consts[0:1, 2:3]), inc=1)
        P.wait("vector", "scalar")
        P.do("vector", lambda e, r_vb=r_vb, r_mean=r_mean, r_a=r_a:
             e.scalar_tensor_tensor(out=r_vb, in0=r_mean, scalar=-1.0,
                                    in1=r_a, op0=ALU.mult, op1=ALU.mult), inc=1)
        v_n2 = P.e["vector"].n
        for b, cb in ((2, 2 * TQ), (3, TQ)):
            bank_acquire(c, b)
            P.wait("tensor", "vector", v_n2)
            P.do("tensor", lambda e, b=b, cb=cb: e.matmul(
                c.ps[b][:, :], c.ones_f[:, :], c.strow[0:1, cb:cb + TQ],
                start=True, stop=True), inc=1)
            bank_done(c, b)
        bank_consume(c, 2, "vector")
        bank_consume(c, 3, "vector")
        for dc in range(NDC):
            P.do("vector", lambda e, dc=dc, t0=t0: e.tensor_tensor(
                out=c.aux[:, 0:TQ], in0=xr_ap(c, dc, t0, TQ), in1=c.ps[2][:, :],
                op=ALU.mult), inc=1)
            P.do("vector", lambda e, dc=dc, t0=t0: e.tensor_tensor(
                out=xln_ap(c, dc, t0, TQ), in0=c.aux[:, 0:TQ], in1=c.ps[3][:, :],
                op=ALU.add), inc=1)
        bank_release(c, 2, "vector")
        bank_release(c, 3, "vector")
    P.barrier()


def stage_weight_bf16(c, dram_ap, k0, m0, kc, mc, dst_col, mtile=128, dst=None,
                      q="gpsimd"):
    src = dram_ap[k0:k0 + kc * 128, m0:m0 + mc * mtile]
    src = src.rearrange("(ki p) mm -> p ki mm", p=128)
    base = c.wst if dst is None else dst
    d = base[:, dst_col:dst_col + kc * mc * mtile]
    d = d.rearrange("p (ki mm) -> p ki mm", ki=kc)
    return c.P.dma(q, d, src)


def wtile(c, dst_col, mc, ki, mi, mtile=128):
    col = dst_col + (ki * mc + mi) * mtile
    return c.wst[:, col:col + mtile]



def b_l0_attn(c):
    P = c.P
    emit_ln(c)
    bes = contextlib.ExitStack()
    ygq = None
    nc = c.nc
    A, B = 0, 8192
    TWO_PI = 6.283185307179586
    MAGIC = 12582912.0
    C1 = 6.28125
    C2 = 1.9353072e-03
    C2B = float(np.float32(np.float64(TWO_PI) - 6.28125))

    svst = None
    theta = bes.enter_context(nc.sbuf_tensor("l0a_th", [128, 16], F32))
    taps = bes.enter_context(nc.sbuf_tensor("l0a_tap", [128, 132], F32))
    convb = bes.enter_context(nc.sbuf_tensor("l0a_cb", [128, 33], F32))
    ssc = bes.enter_context(nc.sbuf_tensor("l0a_ssc", [32, 1], F32))
    slp = bes.enter_context(nc.sbuf_tensor("l0a_slp", [32, 1], F32))
    selk = bes.enter_context(nc.sbuf_tensor("l0a_selk", [32, 2048], BF16))
    hsel = bes.enter_context(nc.sbuf_tensor("l0a_hsel", [128, 2], BF16))
    hsel2 = bes.enter_context(nc.sbuf_tensor("l0a_hsel2", [2, 128], BF16))
    bdre = bes.enter_context(nc.sbuf_tensor("l0a_bdre", [128, 128], BF16))
    bdim = bes.enter_context(nc.sbuf_tensor("l0a_bdim", [128, 128], BF16))
    wres = bes.enter_context(nc.sbuf_tensor("l0a_wres", [64, 64], F32))
    nsc = bes.enter_context(nc.sbuf_tensor("l0a_nsc", [128, 1], F32))
    wpoly = bes.enter_context(nc.sbuf_tensor("l0a_wp", [128, 4], F32))
    wrow = bes.enter_context(nc.sbuf_tensor("l0a_wrow", [1, 3], F32))
    wsum = bes.enter_context(nc.sbuf_tensor("l0a_wsum", [1, 1], F32))
    ztail = bes.enter_context(nc.sbuf_tensor("l0a_zt", [128, 51], F32))
    _f32v = c.scr[:].bitcast(F32)
    zbuf2 = [bes.enter_context(nc.sbuf_tensor(f"l0a_zb{i}", [128, 515], F32))
             for i in range(2)]
    pwb = zbuf2[0][0:32, 0:512]
    svst = zbuf2[1][:].bitcast(BF16)[0:128, 0:256]
    xvb2 = [_f32v[:, 4096:4608], _f32v[:, 4608:5120]]
    kb2 = [_f32v[:, 5120:5632], _f32v[:, 5632:6144]]
    prd2 = [_f32v[:, 6144:6656], _f32v[:, 6656:7168]]
    phib2 = [_f32v[:, 7168:7680], _f32v[:, 7680:8192]]
    sinb2 = [bes.enter_context(nc.sbuf_tensor(f"l0a_sin{i}", [128, 512], BF16))
             for i in range(2)]
    cosb2 = [bes.enter_context(nc.sbuf_tensor(f"l0a_cos{i}", [128, 512], BF16))
             for i in range(2)]
    polyb2 = [bes.enter_context(nc.sbuf_tensor(f"l0a_poly{i}", [128, 512], BF16))
              for i in range(2)]
    rem2 = [bes.enter_context(nc.sbuf_tensor(f"l0a_rem{i}", [128, 512], BF16))
            for i in range(2)]
    imm2 = [bes.enter_context(nc.sbuf_tensor(f"l0a_imm{i}", [128, 512], BF16))
            for i in range(2)]
    npr1 = bes.enter_context(nc.sbuf_tensor("l0a_npr", [128, 1024], F32))
    npr2 = [npr1, npr1]
    ren2 = [bes.enter_context(nc.sbuf_tensor(f"l0a_ren{i}", [128, 512], BF16))
            for i in range(2)]
    imn2 = [bes.enter_context(nc.sbuf_tensor(f"l0a_imn{i}", [128, 512], BF16))
            for i in range(2)]
    rsbb = None
    carp = bes.enter_context(nc.sbuf_tensor("l0a_crp", [128, 32], F32))
    sqb2 = [kb2[1], kb2[0]]
    carden = bes.enter_context(nc.sbuf_tensor("l0a_cden", [32, 1], F32))
    srawb = bes.enter_context(nc.sbuf_tensor("l0a_sraw", [32, 515], F32))
    rsbb = srawb[:].bitcast(BF16)[0:2, 0:512]
    pwbf = bes.enter_context(nc.sbuf_tensor("l0a_pwbf", [32, 512], BF16))
    invb = bes.enter_context(nc.sbuf_tensor("l0a_inv", [32, 512], BF16))
    tmps = c.aux[0:32, 0:512]
    gzs2 = [bes.enter_context(nc.sbuf_tensor(f"l0a_gzs{i}", [128, 515], BF16))
            for i in range(2)]
    diagb2 = [bes.enter_context(nc.sbuf_tensor(f"l0a_diag{i}", [128, 512], BF16))
              for i in range(2)]
    gcv2 = [c.scr[:, 12288:12800], c.scr[:, 13312:13824]]
    gzt = bes.enter_context(nc.sbuf_tensor("l0a_gzt", [128, 48], BF16))

    ygq = c.scr
    n1 = P.dma("sync", theta[:], c.w["L0THETA"])
    n2 = P.dma("sync", taps[:], c.w["L0TAPS"])
    n3 = P.dma("sync", convb[:], c.w["L0CONVB"])
    n4 = P.dma("sync", ssc[:], c.w["L0SSC"])
    n5 = 0
    n6 = P.dma("gpsimd", selk[:], c.w["L0SEL"])
    n7 = P.dma("gpsimd", hsel[:], c.w["L0HSEL"])
    n8 = P.dma("gpsimd", hsel2[:], c.w["L0HSEL2"])
    n9 = P.dma("sync", wres[:], c.w["L0_attn_W_re"])
    n10 = P.dma("sync", nsc[:], c.w["L0NSC"])
    n11 = P.dma("sync", wrow[:], c.w["L0DERIV"])
    n12 = P.dma("sync", c.aux[0:32, 0:1], c.w["L0SLOPES"])
    P.wait("vector", "sync", n12)
    P.wait("vector", "gpsimd", n8)
    P.do("vector", lambda e: e.memset(c.consts[:, 4:5], 1.0), inc=1)
    P.do("vector", lambda e: e.memset(c.consts[:, 5:6], float(np.pi / 2)), inc=1)
    P.do("vector", lambda e: e.memset(c.consts[:, 6:7], MAGIC), inc=1)
    P.do("vector", lambda e: e.memset(c.consts[:, 7:8], -MAGIC), inc=1)
    P.do("vector", lambda e: e.memset(c.consts[:, 8:9], 0.25), inc=1)
    P.do("vector", lambda e: e.memset(ztail[:], 0.0), inc=1)
    P.do("vector", lambda e: e.memset(gzt[:], 0.0), inc=1)
    P.do("vector", lambda e: e.memset(carp[:], 0.0), inc=1)
    P.do("vector", lambda e: e.memset(carden[:], 0.0), inc=1)
    P.wait("scalar", "vector")
    P.do("scalar", lambda e: e.activation(slp[:], c.aux[0:32, 0:1], AF.Abs), inc=1)
    P.do("scalar", lambda e: e.activation(slp[:], slp[:], AF.Exp, scale=-1.0), inc=1)
    P.do("scalar", lambda e: e.activation(slp[:], slp[:], AF.Ln,
                                          bias=c.consts[0:32, 4:5]), inc=1)
    P.wait("vector", "scalar")
    P.do("vector", lambda e: e.tensor_scalar(
        out=c.aux[0:32, 1:2], in0=c.aux[0:32, 0:1], scalar1=0.0, scalar2=None,
        op0=ALU.max), inc=1)
    P.do("vector", lambda e: e.tensor_tensor(
        out=slp[:], in0=slp[:], in1=c.aux[0:32, 1:2], op=ALU.add), inc=1)
    P.do("vector", lambda e: e.tensor_scalar(
        out=slp[:], in0=slp[:], scalar1=-1.0, scalar2=None, op0=ALU.mult), inc=1)
    P.wait("scalar", "sync", n11)
    P.do("scalar", lambda e: e.activation(wrow[:], wrow[:], AF.Exp,
                                          accum_out=wsum[:]), inc=1)
    P.wait("vector", "scalar")
    P.do("vector", lambda e: e.reciprocal(wsum[:], wsum[:]), inc=1)
    P.do("vector", lambda e: e.tensor_tensor(
        out=wrow[:], in0=wrow[:], in1=wsum[:].broadcast_to((1, 3)),
        op=ALU.mult), inc=1)
    v_n = P.e["vector"].n
    bank_acquire(c, 7)
    P.wait("tensor", "vector", v_n)
    P.do("tensor", lambda e: e.matmul(c.ps[7][:, 0:3], c.ones_f[:, :], wrow[:],
                                      start=True, stop=True), inc=1)
    bank_done(c, 7)
    bank_consume(c, 7, "vector")
    P.do("vector", lambda e: e.tensor_copy(wpoly[:, 0:3], c.ps[7][:, 0:3]), inc=1)
    P.do("vector", lambda e: e.tensor_scalar(
        out=wpoly[:, 3:4], in0=wpoly[:, 2:3], scalar1=-1.0, scalar2=None,
        op0=ALU.mult), inc=1)
    bank_release(c, 7, "vector")
    P.wait("vector", "sync", n10)
    P.do("vector", lambda e: e.memset(bdre[:], 0.0), inc=1)
    P.do("vector", lambda e: e.memset(bdim[:], 0.0), inc=1)
    for nm, bd, ncol in (("L0_attn_W_re", bdre, 0), ("L0_attn_W_im", bdim, 64)):
        if ncol:
            P.wait("sync", "vector")
            nn = P.dma("sync", wres[:], c.w[nm])
            P.wait("vector", "sync", nn)
        else:
            P.wait("vector", "sync", n9)
        P.do("vector", lambda e, bd=bd, ncol=ncol: e.tensor_scalar(
            out=bd[0:64, 0:64], in0=wres[:], scalar1=nsc[ncol:ncol + 64, 0:1],
            scalar2=None, op0=ALU.mult), inc=1)
        P.do("vector", lambda e, bd=bd: e.tensor_copy(
            bd[64:128, 64:128], bd[0:64, 0:64]), inc=1)

    inproj_pe = [0, 0, 0, 0]
    sraw_pe = 0
    dist_rd_v = 0
    msq_pe2 = [0, 0]
    y_pe2 = [0, 0]
    zb_rd_v = [0, 0]
    gz_rd_pe = [0]
    gz_wr_s = [0, 0]
    s1dve = [0, 0]
    aff_s = [0, 0]
    diag_rd_pe2 = [0, 0]
    diag_v2 = [0, 0]
    sq_gp = [0]

    def emit_s1_head(ct, t0):
        p = ct % 2
        bx = ct % 2
        zbuf = zbuf2[p]
        xvb = xvb2[p]
        gct = 16 + ct
        P.wait("vector", "tensor", diag_rd_pe2[p])
        P.do("vector", lambda e, gct=gct, p=p: e.tensor_tensor(
            out=diagb2[p][:].rearrange("p (k n) -> p k n", k=4),
            in0=c.ident_b[:].rearrange("p (u n) -> p u n", u=1).broadcast_to(
                (128, 4, 128)),
            in1=taps[:, gct * 4:gct * 4 + 4].rearrange(
                "p (k u) -> p k u", u=1).broadcast_to((128, 4, 128)),
            op=ALU.mult), inc=1)
        diag_v2[p] = P.e["vector"].n
        bank_consume(c, bx, "scalar")
        P.wait("scalar", "vector", zb_rd_v[p])
        P.do("scalar", lambda e, bx=bx, zbuf=zbuf: e.activation(
            zbuf[:, 3:515], c.ps[bx][:, :], AF.Copy), inc=1)
        bank_release(c, bx, "scalar")
        P.do("scalar", lambda e, ct=ct, zbuf=zbuf: e.activation(
            zbuf[:, 0:3], ztail[:, ct * 3:ct * 3 + 3], AF.Copy), inc=1)
        P.do("scalar", lambda e, ct=ct, zbuf=zbuf: e.activation(
            ztail[:, ct * 3:ct * 3 + 3], zbuf[:, 512:515], AF.Copy), inc=1)
        s1_act = P.e["scalar"].n
        P.wait("vector", "scalar", s1_act)
        P.wait("vector", "scalar", aff_s[p])
        P.do("vector", lambda e, ct=ct, zbuf=zbuf, xvb=xvb: e.tensor_scalar(
            out=xvb, in0=zbuf[:, 0:512], scalar1=taps[:, ct * 4:ct * 4 + 1],
            scalar2=convb[:, ct:ct + 1], op0=ALU.mult, op1=ALU.add), inc=1)
        for kk in range(1, 4):
            P.do("vector", lambda e, kk=kk, ct=ct, zbuf=zbuf, xvb=xvb:
                 e.scalar_tensor_tensor(
                     out=xvb, in0=zbuf[:, kk:kk + 512],
                     scalar=taps[:, ct * 4 + kk:ct * 4 + kk + 1], in1=xvb,
                     op0=ALU.mult, op1=ALU.add), inc=1)
        zb_rd_v[p] = P.e["vector"].n
        return zb_rd_v[p]

    def emit_s1_trig(ct, conv_n):
        p = ct % 2
        xvb = xvb2[p]
        kb = kb2[p]
        sqb = sqb2[p]
        P.wait("scalar", "vector", s1dve[p])
        P.wait("scalar", "vector", conv_n)
        P.do("scalar", lambda e, ct=ct, p=p, xvb=xvb: e.activation(
            phib2[p][:], xvb, AF.Copy, scale=theta[:, ct:ct + 1]), inc=1)
        P.do("scalar", lambda e, p=p, kb=kb: e.activation(
            kb, phib2[p][:], AF.Identity, scale=1.0 / TWO_PI,
            bias=c.consts[:, 6:7]), inc=1)
        P.do("scalar", lambda e, kb=kb: e.activation(
            kb, kb, AF.Identity, bias=c.consts[:, 7:8]), inc=1)
        act_k1 = P.e["scalar"].n
        P.wait("vector", "scalar", act_k1)
        for i, cc_ in enumerate((C1, C2B)):
            P.do("vector", lambda e, cc_=cc_, i=i, p=p, kb=kb:
                 e.scalar_tensor_tensor(
                     out=prd2[p][:], in0=kb, scalar=-cc_,
                     in1=(phib2[p][:] if i == 0 else prd2[p][:]),
                     op0=ALU.mult, op1=ALU.add), inc=1)
        P.do("vector", lambda e, xvb=xvb, sqb=sqb: e.tensor_tensor(
            out=sqb, in0=xvb, in1=xvb, op=ALU.mult), inc=1)
        casc1_n = P.e["vector"].n
        P.wait("scalar", "vector", casc1_n)
        P.do("scalar", lambda e, p=p: e.activation(
            sinb2[p][:], prd2[p][:], AF.Sin), inc=1)
        P.do("scalar", lambda e, p=p: e.activation(
            cosb2[p][:], prd2[p][:], AF.Sin, scale=0.5), inc=1)
        P.do("scalar", lambda e, p=p, kb=kb: e.activation(
            kb, cosb2[p][:], AF.Square), inc=1)
        P.do("scalar", lambda e, xvb=xvb: e.activation(
            xvb, xvb, AF.Identity, scale=wpoly[:, 1:2],
            bias=wpoly[:, 0:1]), inc=1)
        act_k2 = P.e["scalar"].n
        aff_s[p] = act_k2
        P.wait("vector", "scalar", act_k2)
        P.do("vector", lambda e, p=p, kb=kb: e.tensor_scalar(
            out=cosb2[p][:], in0=kb, scalar1=-2.0, scalar2=1.0,
            op0=ALU.mult, op1=ALU.add), inc=1)
        P.do("vector", lambda e, p=p, xvb=xvb, sqb=sqb: e.scalar_tensor_tensor(
            out=polyb2[p][:], in0=sqb, scalar=wpoly[:, 3:4], in1=xvb,
            op0=ALU.mult, op1=ALU.add), inc=1)
        s1dve[p] = P.e["vector"].n
        return act_k2


    def emit_tail_A(ct, t0, q, s2_n, first):
        p = ct % 2
        remb = rem2[p]
        immb = imm2[p]
        numpair = npr2[p]
        bank_acquire(c, 4)
        if first:
            P.wait("tensor", "vector", v_sr)
            P.wait("tensor", "scalar", sr_s)
        P.do("tensor", lambda e, ct=ct: e.matmul(
            c.ps[4][:, :], selk[:, ct * 128:ct * 128 + 128], pwbf[:],
            start=True, stop=True), inc=1)
        bank_done(c, 4)
        bank_consume(c, 4, "vector")
        P.wait("vector", "scalar", s2_n)
        P.wait("vector", "tensor", msq_pe2[p])
        P.do("vector", lambda e, p=p: e.tensor_tensor(
            out=polyb2[p][:], in0=polyb2[p][:], in1=c.ps[4][:, :],
            op=ALU.mult), inc=1)
        P.do("vector", lambda e, p=p, remb=remb: e.tensor_tensor(
            out=remb[:], in0=polyb2[p][:], in1=cosb2[p][:], op=ALU.mult), inc=1)
        P.do("vector", lambda e, p=p, immb=immb: e.tensor_tensor(
            out=immb[:], in0=polyb2[p][:], in1=sinb2[p][:], op=ALU.mult), inc=1)
        bank_release(c, 4, "vector")
        P.do("vector", lambda e, q=q, ct=ct, numpair=numpair, remb=remb:
             e.tensor_tensor_scan(
                 out=numpair[:, 0:512],
                 data0=c.ones_b[:, 0:1].broadcast_to((128, 512)),
                 data1=remb[:],
                 initial=(0.0 if q == 0 else carp[:, 2 * ct:2 * ct + 1]),
                 op0=ALU.mult, op1=ALU.add), inc=1)
        P.do("vector", lambda e, q=q, ct=ct, numpair=numpair, immb=immb:
             e.tensor_tensor_scan(
                 out=numpair[:, 512:1024],
                 data0=c.ones_b[:, 0:1].broadcast_to((128, 512)),
                 data1=immb[:],
                 initial=(0.0 if q == 0 else carp[:, 2 * ct + 1:2 * ct + 2]),
                 op0=ALU.mult, op1=ALU.add), inc=1)
        P.do("vector", lambda e, ct=ct, numpair=numpair: e.tensor_copy(
            carp[:, 2 * ct:2 * ct + 2], numpair[:, 511:1024:512]), inc=1)
        bank_acquire(c, 5)
        P.do("tensor", lambda e, ct=ct: e.matmul(
            c.ps[5][:, :], selk[:, ct * 128:ct * 128 + 128], invb[:],
            start=True, stop=True), inc=1)
        bank_done(c, 5)
        bank_consume(c, 5, "vector")
        P.wait("vector", "tensor", y_pe2[p])
        P.do("vector", lambda e, p=p, numpair=numpair: e.tensor_tensor(
            out=ren2[p][:], in0=numpair[:, 0:512], in1=c.ps[5][:, :],
            op=ALU.mult), inc=1)
        P.do("vector", lambda e, p=p, numpair=numpair: e.tensor_tensor(
            out=imn2[p][:], in0=numpair[:, 512:1024], in1=c.ps[5][:, :],
            op=ALU.mult), inc=1)
        bank_release(c, 5, "vector")
        norm_v = P.e["vector"].n
        P.do("vector", lambda e, p=p, remb=remb: e.tensor_tensor(
            out=remb[:], in0=ren2[p][:], in1=ren2[p][:], op=ALU.mult), inc=1)
        P.do("vector", lambda e, p=p, immb=immb: e.tensor_tensor(
            out=immb[:], in0=imn2[p][:], in1=imn2[p][:], op=ALU.mult), inc=1)
        sq_gp[0] = P.e["vector"].n
        return norm_v

    def emit_tail_B1(ct, norm_n):
        p = ct % 2
        remb = rem2[p]
        immb = imm2[p]
        renb = ren2[p]
        imnb = imn2[p]
        bank_acquire(c, 6)
        P.wait("tensor", "vector", norm_n)
        P.do("tensor", lambda e, renb=renb: e.matmul(
            c.ps[6][:, :], bdre[:], renb[:], start=True, stop=False), inc=0)
        P.do("tensor", lambda e, imnb=imnb: e.matmul(
            c.ps[6][:, :], bdim[:], imnb[:], start=False, stop=True), inc=1)
        bank_done(c, 6)
        y_pe2[p] = P.e["tensor"].n
        bank_acquire(c, 7)
        P.wait("tensor", "vector", sq_gp[0])
        P.do("tensor", lambda e, remb=remb: e.matmul(
            c.ps[7][0:2, :], hsel[:], remb[:], start=True, stop=False), inc=0)
        P.do("tensor", lambda e, immb=immb: e.matmul(
            c.ps[7][0:2, :], hsel[:], immb[:], start=False, stop=True), inc=1)
        bank_done(c, 7)
        msq_pe2[p] = P.e["tensor"].n

    def emit_tail_B2(ct, t0):
        p = ct % 2
        gct = 16 + ct
        p_ = ct % 2
        bank_consume(c, 7, "scalar")
        P.do("scalar", lambda e: e.activation(
            tmps[0:2, :], c.ps[7][0:2, :], AF.Ln, scale=1.0 / 128.0,
            bias=c.consts[0:2, 0:1]), inc=1)
        P.do("scalar", lambda e: e.activation(
            rsbb, tmps[0:2, :], AF.Exp, scale=-0.5), inc=1)
        s_rs = P.e["scalar"].n
        bg = 2 + ct % 2
        bank_acquire(c, bg)
        P.wait("tensor", "vector", diag_v2[p_])
        P.wait("tensor", "scalar", gz_wr_s[p_])
        for kk in range(4):
            P.do("tensor", lambda e, kk=kk, p_=p_, bg=bg: e.matmul(
                c.ps[bg][:, :], diagb2[p_][:, kk * 128:kk * 128 + 128],
                gzs2[p_][:, kk:kk + 512],
                start=(kk == 0), stop=(kk == 3)), inc=(1 if kk == 3 else 0))
        bank_done(c, bg)
        diag_rd_pe2[p_] = P.e["tensor"].n
        gz_rd_pe[0] = P.e["tensor"].n
        P.wait("tensor", "scalar", s_rs)
        P.do("tensor", lambda e: e.matmul(
            c.ps[7][:, :], hsel2[:], rsbb, start=True, stop=True), inc=1)
        bank_done(c, 7)
        bank_consume(c, bg, "scalar")
        gcv = gcv2[p]
        P.do("scalar", lambda e, gct=gct, bg=bg, gcv=gcv: e.activation(
            gcv[:], c.ps[bg][:, :], AF.Silu, bias=convb[:, gct:gct + 1]), inc=1)
        bank_release(c, bg, "scalar")
        s_n = P.e["scalar"].n
        tjoin = kb2[1 - p]
        bank_consume(c, 7, "vector")
        bank_consume(c, 6, "vector")
        P.wait("vector", "scalar", s_n)
        P.do("vector", lambda e, gcv=gcv, tjoin=tjoin: e.tensor_tensor(
            out=tjoin, in0=c.ps[7][:, :], in1=gcv[:], op=ALU.mult), inc=1)
        bank_release(c, 7, "vector")
        P.do("vector", lambda e, ct=ct, tjoin=tjoin: e.tensor_tensor(
            out=ygq[:, ct * 512:ct * 512 + 512], in0=tjoin,
            in1=c.ps[6][:, :], op=ALU.mult), inc=1)
        bank_release(c, 6, "vector")

    q_first = [True]
    for q in range(NQ):
        q_first[0] = True
        t0 = q * TQ
        P.wait("gpsimd", "tensor", sraw_pe)
        P.wait("gpsimd", "vector", zb_rd_v[1])
        lds = stage_weight_bf16(c, c.w["L0_attn_in_proj_w"], 0, 4096, 8, 1,
                                0, mtile=32, dst=svst)
        b = 6
        bank_acquire(c, b)
        P.wait("tensor", "gpsimd", lds)
        for dc in range(NDC):
            P.do("tensor", lambda e, b=b, dc=dc, t0=t0: e.matmul(
                c.ps[b][0:32, :], svst[:, dc * 32:dc * 32 + 32],
                xln_ap(c, dc, t0, TQ),
                start=(dc == 0), stop=(dc == NDC - 1)),
                inc=(1 if dc == NDC - 1 else 0))
        bank_done(c, b)
        sraw_pe = P.e["tensor"].n
        bank_consume(c, b, "vector")
        P.wait("vector", "scalar")
        P.do("vector", lambda e: e.tensor_copy(
            srawb[:, 0:3], ztail[0:32, 48:51]), inc=1)
        P.do("vector", lambda e, b=b: e.tensor_copy(
            srawb[:, 3:515], c.ps[b][0:32, :]), inc=1)
        bank_release(c, b, "vector")
        P.do("vector", lambda e: e.tensor_copy(
            ztail[0:32, 48:51], srawb[:, 512:515]), inc=1)
        P.do("vector", lambda e: e.tensor_scalar(
            out=pwb, in0=srawb[:, 0:512], scalar1=taps[0:32, 128:129],
            scalar2=None, op0=ALU.mult), inc=1)
        for kk in range(1, 4):
            P.do("vector", lambda e, kk=kk: e.scalar_tensor_tensor(
                out=pwb, in0=srawb[:, kk:kk + 512],
                scalar=taps[0:32, 128 + kk:129 + kk], in1=pwb,
                op0=ALU.mult, op1=ALU.add), inc=1)
        P.do("vector", lambda e: e.tensor_scalar(
            out=pwb, in0=pwb, scalar1=convb[0:32, 32:33], scalar2=None,
            op0=ALU.add), inc=1)
        P.do("vector", lambda e: e.tensor_scalar(
            out=pwb, in0=pwb, scalar1=ssc[:], scalar2=20.0,
            op0=ALU.mult, op1=ALU.min), inc=1)
        P.do("vector", lambda e: e.tensor_scalar(
            out=pwb, in0=pwb, scalar1=-20.0, scalar2=None,
            op0=ALU.max), inc=1)
        distc = srawb[:].bitcast(mybir.dt.float16)[0:32, 0:512]
        P.wait("gpsimd", "vector")
        nd = P.dma("gpsimd", distc, c.w["L0DIST"][:, t0:t0 + TQ])
        P.wait("vector", "gpsimd", nd)
        P.do("vector", lambda e, distc=distc: e.tensor_scalar(
            out=tmps, in0=distc, scalar1=slp[:],
            scalar2=None, op0=ALU.mult), inc=1)
        dist_rd_v = P.e["vector"].n
        P.wait("scalar", "vector")
        P.do("scalar", lambda e: e.activation(pwb, pwb, AF.Exp), inc=1)
        P.do("scalar", lambda e: e.activation(tmps, tmps, AF.Exp), inc=1)
        P.wait("vector", "scalar")
        P.do("vector", lambda e: e.tensor_tensor(
            out=pwb, in0=pwb, in1=tmps, op=ALU.mult), inc=1)
        P.do("vector", lambda e: e.tensor_copy(pwbf[:], pwb), inc=1)
        P.do("vector", lambda e, q=q: e.tensor_tensor_scan(
            out=tmps, data0=c.ones_b[0:32, 0:1].broadcast_to((32, 512)), data1=pwb,
            initial=(0.0 if q == 0 else carden[:]),
            op0=ALU.mult, op1=ALU.add), inc=1)
        P.do("vector", lambda e: e.tensor_copy(carden[:], tmps[0:32, 511:512]), inc=1)
        P.do("vector", lambda e: e.tensor_scalar(
            out=tmps, in0=tmps, scalar1=1e-4, scalar2=None,
            op0=ALU.max), inc=1)
        P.wait("scalar", "vector")
        P.do("scalar", lambda e: e.activation(tmps, tmps, AF.Ln), inc=1)
        P.do("scalar", lambda e: e.activation(invb[:], tmps, AF.Exp,
                                              scale=-1.0), inc=1)
        v_sr = P.e["vector"].n
        sr_s = P.e["scalar"].n
        zb_rd_v[0] = max(zb_rd_v[0], v_sr)

        def stage_group(g):
            half, part = g // 2, g % 2
            rg = (g % 2) * 2
            ru = rg + 1
            P.wait("gpsimd", "tensor", inproj_pe[rg])
            stage_weight_bf16(c, c.w["L0_attn_in_proj_w"], 0,
                              half * 1024 + part * 512, 8, 4, rg * 4096)
            P.wait("gpsimd", "tensor", inproj_pe[ru])
            ld = stage_weight_bf16(c, c.w["L0_attn_in_proj_w"], 0,
                                   2048 + half * 1024 + part * 512, 8, 4,
                                   ru * 4096)
            return rg, ru, ld

        def emit_zmm(ct, rg, ru, ld, t0):
            P.wait("tensor", "gpsimd", ld)
            cl4 = ct % 4
            for reg, bb in ((rg, ct % 2), (ru, 2 + ct % 2)):
                bank_acquire(c, bb)
                for dc in range(NDC):
                    P.do("tensor", lambda e, bb=bb, dc=dc, reg=reg,
                         cl4=cl4, t0=t0:
                         e.matmul(c.ps[bb][:, :],
                                  wtile(c, reg * 4096, 4, dc, cl4),
                                  xln_ap(c, dc, t0, TQ),
                                  start=(dc == 0), stop=(dc == NDC - 1)),
                         inc=(1 if dc == NDC - 1 else 0))
                bank_done(c, bb)
            p_ = ct % 2
            bg_ = 2 + ct % 2
            bank_consume(c, bg_, "scalar")
            P.wait("scalar", "tensor", gz_rd_pe[0])
            P.do("scalar", lambda e, bg_=bg_, p_=p_: e.activation(
                gzs2[p_][:, 3:515], c.ps[bg_][:, :], AF.Copy), inc=1)
            bank_release(c, bg_, "scalar")
            P.do("scalar", lambda e, ct=ct, p_=p_: e.activation(
                gzs2[p_][:, 0:3], gzt[:, ct * 3:ct * 3 + 3], AF.Copy), inc=1)
            P.do("scalar", lambda e, ct=ct, p_=p_: e.activation(
                gzt[:, ct * 3:ct * 3 + 3], gzs2[p_][:, 512:515], AF.Copy), inc=1)
            gz_wr_s[p_] = P.e["scalar"].n

        rg, ru, ld = stage_group(0)
        grp = [(rg, ru, ld)]
        emit_zmm(0, rg, ru, ld, t0)
        pend = None
        for ct in range(16):
            if pend is not None:
                normA = emit_tail_A(pend[0], t0, q, pend[1], pend[2])
                emit_tail_B1(pend[0], normA)
            conv_n = emit_s1_head(ct, t0)
            s2_n = emit_s1_trig(ct, conv_n)
            if pend is not None:
                emit_tail_B2(pend[0], t0)
            pend = (ct, s2_n, q_first[0])
            q_first[0] = False
            if ct < 15:
                nct = ct + 1
                if nct % 4 == 0:
                    g = nct // 4
                    rgp, rup, _ = grp[-1]
                    inproj_pe[rgp] = P.e["tensor"].n
                    inproj_pe[rup] = P.e["tensor"].n
                    rg, ru, ld = stage_group(g)
                    grp.append((rg, ru, ld))
                emit_zmm(nct, rg, ru, ld, t0)
        normA = emit_tail_A(pend[0], t0, q, pend[1], pend[2])
        emit_tail_B1(pend[0], normA)
        emit_tail_B2(pend[0], t0)
        pend = None
        pe_n = P.e["tensor"].n
        P.wait("gpsimd", "tensor", pe_n)
        ldo = stage_weight_bf16(c, c.w["L0_attn_out_proj_w"], 0, 0, 16, 8, 0)
        P.wait("tensor", "gpsimd", ldo)
        P.wait("tensor", "vector")
        for dct in range(8):
            b = dct % 2
            bank_acquire(c, b)
            for ct in range(16):
                P.do("tensor", lambda e, b=b, ct=ct, dct=dct: e.matmul(
                    c.ps[b][:, :], wtile(c, 0, 8, ct, dct),
                    ygq[:, ct * 512:ct * 512 + 512],
                    start=(ct == 0), stop=(ct == 15)),
                    inc=(1 if ct == 15 else 0))
            bank_done(c, b)
            bank_consume(c, b, "vector")
            P.do("vector", lambda e, b=b, dct=dct, t0=t0: e.tensor_tensor(
                out=xr_ap(c, dct, t0, TQ), in0=xr_ap(c, dct, t0, TQ),
                in1=c.ps[b][:, :], op=ALU.add), inc=1)
            bank_release(c, b, "vector")
        inproj_pe = [P.e["tensor"].n] * 4
        sraw_pe = P.e["tensor"].n
    P.barrier()
    bes.close()


def b_l0_ffn(c):
    P = c.P
    emit_ln(c)
    bes = contextlib.ExitStack()
    S = 1.0 / 127.0
    tmpb2 = [bes.enter_context(c.nc.sbuf_tensor(f"l0f_tmp{i}", [128, 512], BF16))
             for i in range(2)]
    rpe = [0, 0, 0, 0]
    tmp_read_v2 = [0, 0]

    for q in range(NQ):
        t0 = q * TQ
        def l0f_stage(idx):
            hs, part = divmod(idx, 2)
            rg = (idx % 2) * 2
            ru = rg + 1
            P.wait("gpsimd", "tensor", rpe[rg])
            stage_weight_bf16(c, c.w["L0_ffn_w_gate"], 0,
                              hs * 1024 + part * 512, 8, 4, rg * 4096)
            P.wait("gpsimd", "tensor", rpe[ru])
            ldb = stage_weight_bf16(c, c.w["L0_ffn_w_up"], 0,
                                    hs * 1024 + part * 512, 8, 4, ru * 4096)
            return rg, ru, ldb

        nxt = l0f_stage(0)
        for hs in range(3):
            for part in range(2):
                idx = hs * 2 + part
                rg, ru, ldb = nxt
                P.wait("tensor", "gpsimd", ldb)
                if idx + 1 < 6:
                    nxt = l0f_stage(idx + 1)
                for hl4 in range(4):
                    hc = hs * 8 + part * 4 + hl4
                    bg = hc % 2
                    bu = 2 + hc % 2
                    bank_acquire(c, bg)
                    for dc in range(NDC):
                        P.do("tensor", lambda e, bg=bg, dc=dc, hl4=hl4, t0=t0, rg=rg:
                             e.matmul(c.ps[bg][:, :], wtile(c, rg * 4096, 4, dc, hl4),
                                      xln_ap(c, dc, t0, TQ),
                                      start=(dc == 0), stop=(dc == NDC - 1)),
                             inc=(1 if dc == NDC - 1 else 0))
                    bank_done(c, bg)
                    bank_acquire(c, bu)
                    for dc in range(NDC):
                        P.do("tensor", lambda e, bu=bu, dc=dc, hl4=hl4, t0=t0, ru=ru:
                             e.matmul(c.ps[bu][:, :], wtile(c, ru * 4096, 4, dc, hl4),
                                      xln_ap(c, dc, t0, TQ),
                                      start=(dc == 0), stop=(dc == NDC - 1)),
                             inc=(1 if dc == NDC - 1 else 0))
                    bank_done(c, bu)
                    tmpb = tmpb2[hc % 2]
                    bank_consume(c, bg, "scalar")
                    P.wait("scalar", "vector", tmp_read_v2[hc % 2])
                    P.do("scalar", lambda e, bg=bg, tmpb=tmpb: e.activation(
                        tmpb[:], c.ps[bg][:, :], AF.Silu, scale=S), inc=1)
                    bank_release(c, bg, "scalar")
                    P.wait("vector", "scalar")
                    bank_consume(c, bu, "vector")
                    P.do("vector", lambda e, bu=bu, hc=hc, tmpb=tmpb:
                         e.tensor_tensor(
                             out=c.scr[:, hc * 512:hc * 512 + 512], in0=tmpb[:],
                             in1=c.ps[bu][:, :], op=ALU.mult), inc=1)
                    tmp_read_v2[hc % 2] = P.e["vector"].n
                    bank_release(c, bu, "vector")
                rpe[rg] = rpe[ru] = P.e["tensor"].n
        sc_n = P.e["scalar"].n
        v_n = P.e["vector"].n

        def l0f_stage_down(hs):
            rp = (hs % 2) * 2
            P.wait("gpsimd", "tensor", max(rpe[rp], rpe[rp + 1]))
            ld = stage_weight_bf16(c, c.w["L0_ffn_w_down"], hs * 1024, 0, 8, 8,
                                   rp * 4096)
            return rp, ld

        nxtb = l0f_stage_down(0)
        for hs in range(3):
            rp, ld = nxtb
            P.wait("tensor", "gpsimd", ld)
            if hs + 1 < 3:
                nxtb = l0f_stage_down(hs + 1)
            if hs == 0:
                for b in range(8):
                    bank_acquire(c, b)
                P.wait("tensor", "scalar", sc_n)
                P.wait("tensor", "vector", v_n)
            for dc in range(NDC):
                for hl in range(8):
                    hc = hs * 8 + hl
                    P.do("tensor", lambda e, dc=dc, rp=rp, hl=hl, hc=hc, hs=hs:
                         e.matmul(c.ps[dc][:, :], wtile(c, rp * 4096, 8, hl, dc),
                                  c.scr[:, hc * 512:hc * 512 + 512],
                                  start=(hs == 0 and hl == 0),
                                  stop=(hs == 2 and hl == 7)),
                         inc=(1 if hl == 7 else 0))
            rpe[rp] = rpe[rp + 1] = P.e["tensor"].n
        for dc in range(NDC):
            bank_done(c, dc)
            bank_consume(c, dc, "vector")
            P.do("vector", lambda e, dc=dc, t0=t0: e.scalar_tensor_tensor(
                out=xr_ap(c, dc, t0, TQ), in0=c.ps[dc][:, :],
                scalar=S * S, in1=xr_ap(c, dc, t0, TQ),
                op0=ALU.mult, op1=ALU.add), inc=1)
            bank_release(c, dc, "vector")
    P.barrier()
    bes.close()



def b_l2_attn(c):
    P = c.P
    emit_ln(c)
    bes = contextlib.ExitStack()
    kv = bes.enter_context(c.nc.sbuf_tensor("l2_kv", [128, 8192], BF16))
    o1 = bes.enter_context(c.nc.sbuf_tensor("l2_o1", [128, 4096], BF16))
    A, B = 0, 8192

    lda = stage_weight_bf16(c, c.w["L2_attn_w_k"], 0, 0, 8, 8, A)
    ldb = stage_weight_bf16(c, c.w["L2_attn_w_v"], 0, 0, 8, 8, B)
    P.wait("tensor", "gpsimd", ldb)
    for th in range(2):
        for tc8 in range(8):
            t0 = th * 1024 + tc8 * 128
            for src_reg, dst0, b0 in ((A, 0, 0), (B, 8192, 2)):
                for et in range(2):
                    b = b0 + et % 2
                    bank_acquire(c, b)
                    for dc in range(NDC):
                        P.do("tensor", lambda e, b=b, dc=dc, t0=t0, src_reg=src_reg, et=et:
                             e.matmul(c.ps[b][:, :],
                                      c.xln[:, dc * T + t0:dc * T + t0 + 128],
                                      c.wst[:, src_reg + dc * 1024 + et * 512:
                                            src_reg + dc * 1024 + et * 512 + 512],
                                      start=(dc == 0), stop=(dc == NDC - 1)),
                             inc=(1 if dc == NDC - 1 else 0))
                    bank_done(c, b)
                    bank_consume(c, b, "scalar")
                    P.do("scalar", lambda e, b=b, dst0=dst0, tc8=tc8, et=et:
                         e.activation(c.scr[:, dst0 + tc8 * 1024 + et * 512:
                                            dst0 + tc8 * 1024 + et * 512 + 512],
                                      c.ps[b][:, :], AF.Copy), inc=1)
                    bank_release(c, b, "scalar")
        sc_n = P.e["scalar"].n
        for dvh in range(2):
            for dkt in range(8):
                bank_acquire(c, dkt)
            P.wait("tensor", "scalar", sc_n)
            for tc8 in range(8):
                for dkt in range(8):
                    P.do("tensor", lambda e, dkt=dkt, tc8=tc8, dvh=dvh:
                         e.matmul(c.ps[dkt][:, :],
                                  c.scr[:, tc8 * 1024 + dkt * 128:
                                        tc8 * 1024 + dkt * 128 + 128],
                                  c.scr[:, 8192 + tc8 * 1024 + dvh * 512:
                                        8192 + tc8 * 1024 + dvh * 512 + 512],
                                  start=(tc8 == 0), stop=(tc8 == 7)),
                         inc=(1 if tc8 == 7 else 0))
            for dkt in range(8):
                bank_done(c, dkt)
                col = dkt * 1024 + dvh * 512
                if th == 0:
                    bank_consume(c, dkt, "scalar")
                    P.do("scalar", lambda e, dkt=dkt, col=col: e.activation(
                        kv[:, col:col + 512], c.ps[dkt][:, :], AF.Copy), inc=1)
                    bank_release(c, dkt, "scalar")
                else:
                    bank_consume(c, dkt, "vector")
                    P.do("vector", lambda e, dkt=dkt, col=col: e.tensor_tensor(
                        out=kv[:, col:col + 512], in0=kv[:, col:col + 512],
                        in1=c.ps[dkt][:, :], op=ALU.add), inc=1)
                    bank_release(c, dkt, "vector")
    pe_n = P.e["tensor"].n
    P.wait("gpsimd", "tensor", pe_n)
    lda = stage_weight_bf16(c, c.w["L2_attn_w_q"], 0, 0, 8, 8, A)
    ldb = stage_weight_bf16(c, c.w["L2_attn_w_o"], 0, 0, 8, 8, B)
    P.wait("tensor", "gpsimd", lda)
    for mt in range(NDC):
        for qc in range(NQ):
            b = (mt * NQ + qc) % 4
            bank_acquire(c, b)
            for dc in range(NDC):
                P.do("tensor", lambda e, b=b, dc=dc, mt=mt, qc=qc:
                     e.matmul(c.ps[b][:, :], wtile(c, A, 8, dc, mt),
                              xln_ap(c, dc, qc * TQ, TQ),
                              start=(dc == 0), stop=(dc == NDC - 1)),
                     inc=(1 if dc == NDC - 1 else 0))
            bank_done(c, b)
            bank_consume(c, b, "scalar")
            P.do("scalar", lambda e, b=b, mt=mt, qc=qc: e.activation(
                c.scr[:, mt * 2048 + qc * 512:mt * 2048 + qc * 512 + 512],
                c.ps[b][:, :], AF.Copy), inc=1)
            bank_release(c, b, "scalar")
    P.wait("tensor", "gpsimd", ldb)
    P.wait("tensor", "scalar")
    P.wait("tensor", "vector")
    for qc in range(NQ):
        for dvt in range(8):
            b = 4 + dvt % 2
            bank_acquire(c, b)
            for dkt in range(8):
                P.do("tensor", lambda e, b=b, dkt=dkt, dvt=dvt, qc=qc:
                     e.matmul(c.ps[b][:, :],
                              kv[:, dkt * 1024 + dvt * 128:dkt * 1024 + dvt * 128 + 128],
                              c.scr[:, dkt * 2048 + qc * 512:dkt * 2048 + qc * 512 + 512],
                              start=(dkt == 0), stop=(dkt == 7)),
                     inc=(1 if dkt == 7 else 0))
            bank_done(c, b)
            bank_consume(c, b, "scalar")
            P.do("scalar", lambda e, b=b, dvt=dvt: e.activation(
                o1[:, dvt * 512:dvt * 512 + 512], c.ps[b][:, :], AF.Copy), inc=1)
            bank_release(c, b, "scalar")
        sc_n = P.e["scalar"].n
        for dct in range(8):
            b = 6 + dct % 2
            bank_acquire(c, b)
            if dct == 0:
                P.wait("tensor", "scalar", sc_n)
            for dvt in range(8):
                P.do("tensor", lambda e, b=b, dvt=dvt, dct=dct:
                     e.matmul(c.ps[b][:, :], wtile(c, B, 8, dvt, dct),
                              o1[:, dvt * 512:dvt * 512 + 512],
                              start=(dvt == 0), stop=(dvt == 7)),
                     inc=(1 if dvt == 7 else 0))
            bank_done(c, b)
            bank_consume(c, b, "vector")
            P.do("vector", lambda e, b=b, dct=dct, qc=qc: e.tensor_tensor(
                out=xr_ap(c, dct, qc * TQ, TQ), in0=xr_ap(c, dct, qc * TQ, TQ),
                in1=c.ps[b][:, :], op=ALU.add), inc=1)
            bank_release(c, b, "vector")
    P.barrier()
    bes.close()



def b_l3_ffn(c):
    P = c.P
    emit_ln(c)
    bes = contextlib.ExitStack()
    nc = c.nc
    lg = bes.enter_context(nc.sbuf_tensor("l3f_lg", [128, 128], F32))
    mx = bes.enter_context(nc.sbuf_tensor("l3f_mx", [128, 16], F32))
    mtok = bes.enter_context(nc.sbuf_tensor("l3f_mtok", [128, 128], BF16))
    mfm = bes.enter_context(nc.sbuf_tensor("l3f_mfm", [8, 2048], BF16))
    mrow = bes.enter_context(nc.sbuf_tensor("l3f_mrow", [1, 1024], BF16))
    xinb = [bes.enter_context(nc.sbuf_tensor(f"l3f_xin{i}", [128, 4096], BF16))
            for i in range(2)]
    gw = bes.enter_context(nc.sbuf_tensor("l3f_gw", [128, 64], BF16))
    ld = stage_weight_bf16(c, c.w["L3_ffn_gate_w"], 0, 0, 8, 1, 0, mtile=8, dst=gw)
    P.wait("tensor", "gpsimd", ld)
    for tc in range(16):
        b = tc % 2
        bank_acquire(c, b)
        for dc in range(NDC):
            P.do("tensor", lambda e, b=b, dc=dc, tc=tc: e.matmul(
                c.ps[b][:, 0:8], c.xln[:, dc * T + tc * 128:dc * T + tc * 128 + 128],
                gw[:, dc * 8:dc * 8 + 8],
                start=(dc == 0), stop=(dc == NDC - 1)),
                inc=(1 if dc == NDC - 1 else 0))
        bank_done(c, b)
        bank_consume(c, b, "vector")
        P.do("vector", lambda e, b=b, tc=tc: e.tensor_copy(
            lg[:, tc * 8:tc * 8 + 8], c.ps[b][:, 0:8]), inc=1)
        bank_release(c, b, "vector")
        P.do("vector", lambda e, tc=tc: e.tensor_reduce(
            out=mx[:, tc:tc + 1], in_=lg[:, tc * 8:tc * 8 + 8],
            axis=AX.X, op=ALU.max), inc=1)
        P.do("vector", lambda e, tc=tc: e.tensor_tensor(
            out=mtok[:, tc * 8:tc * 8 + 8], in0=lg[:, tc * 8:tc * 8 + 8],
            in1=mx[:, tc:tc + 1].broadcast_to((128, 8)), op=ALU.is_ge), inc=1)
    v_n = P.e["vector"].n
    P.wait("tensor", "vector", v_n)
    for tc in range(16):
        b = 2 + tc % 2
        bank_acquire(c, b)
        P.do("tensor", lambda e, b=b, tc=tc: e.transpose(
            c.ps[b].bitcast(BF16)[0:8, 0:128], mtok[:, tc * 8:tc * 8 + 8],
            c.ident_b[:]), inc=1)
        bank_done(c, b)
        bank_consume(c, b, "vector")
        P.do("vector", lambda e, b=b, tc=tc: e.tensor_copy(
            mfm[:, tc * 128:tc * 128 + 128], c.ps[b].bitcast(BF16)[0:8, 0:128]),
            inc=1)
        bank_release(c, b, "vector")
    v_n = P.e["vector"].n
    P.wait("sync", "vector", v_n)
    mrow_pe = [0, 0]
    for e_ in range(8):
        for qc in range(NQ):
            i = e_ * NQ + qc
            mr = mrow[0:1, (i % 2) * 512:(i % 2) * 512 + 512]
            P.wait("sync", "tensor", mrow_pe[i % 2])
            n = P.dma("sync", mr, mfm[e_:e_ + 1, qc * TQ:qc * TQ + TQ])
            b = 4 + i % 2
            bank_acquire(c, b)
            P.wait("tensor", "sync", n)
            if i == 0:
                P.wait("tensor", "vector", v_n)
            P.do("tensor", lambda e, b=b, mr=mr: e.matmul(
                c.ps[b][:, :], c.ones_b[0:1, :], mr,
                start=True, stop=True), inc=1)
            bank_done(c, b)
            mrow_pe[i % 2] = P.e["tensor"].n
            bank_consume(c, b, "scalar")
            P.do("scalar", lambda e, e_=e_, qc=qc, b=b: e.activation(
                c.scr[:, e_ * 2048 + qc * 512:e_ * 2048 + qc * 512 + 512],
                c.ps[b][:, :], AF.Copy), inc=1)
            bank_release(c, b, "scalar")
    for dc in range(NDC):
        P.do("vector", lambda e, dc=dc: e.tensor_tensor(
            out=xr_ap(c, dc), in0=xr_ap(c, dc), in1=xln_ap(c, dc),
            op=ALU.add), inc=1)
    regs = [0, 8192]
    reg_pe = [0, 0]
    xin_rd_pe = [0, 0]
    xin_wr_v = [0, 0]
    sc_n = P.e["scalar"].n
    P.wait("vector", "scalar", sc_n)

    def l3f_emit_xin(i):
        qc_, e2 = divmod(i, 8)
        t0_ = qc_ * TQ
        buf = xinb[i % 2]
        xin3 = buf[:].rearrange("p (dcx t) -> p dcx t", dcx=NDC)
        xln3 = c.xln[:].rearrange(
            "p (dcx t) -> p dcx t", dcx=NDC)[:, :, t0_:t0_ + TQ]
        mbc = c.scr[:, e2 * 2048 + qc_ * 512:e2 * 2048 + qc_ * 512 + 512]
        mbc3 = mbc.rearrange("p (u t) -> p u t", u=1).broadcast_to(
            (128, NDC, TQ))
        P.wait("vector", "tensor", xin_rd_pe[i % 2])
        P.do("vector", lambda e, xin3=xin3, xln3=xln3, mbc3=mbc3:
             e.tensor_tensor(out=xin3, in0=xln3, in1=mbc3, op=ALU.mult), inc=1)
        xin_wr_v[i % 2] = P.e["vector"].n

    l3f_emit_xin(0)
    for qc in range(NQ):
        t0 = qc * TQ
        for b in range(8):
            bank_acquire(c, b)
        for e_ in range(8):
            i = qc * 8 + e_
            cur = i % 2
            xin = xinb[cur]
            reg = regs[e_ % 2]
            P.wait("gpsimd", "tensor", reg_pe[e_ % 2])
            ld = stage_weight_bf16(c, c.w["L3_ffn_expert_w"][e_], 0, 0, 8, 8, reg)
            P.wait("tensor", "gpsimd", ld)
            P.wait("tensor", "vector", xin_wr_v[cur])
            for dct in range(8):
                for hc in range(NDC):
                    P.do("tensor", lambda e, dct=dct, hc=hc, reg=reg, e_=e_,
                         xin=xin:
                         e.matmul(c.ps[dct][:, :], wtile(c, reg, 8, hc, dct),
                                  xin[:, hc * 512:hc * 512 + 512],
                                  start=(e_ == 0 and hc == 0),
                                  stop=(e_ == 7 and hc == 7)),
                         inc=(1 if hc == 7 else 0))
            reg_pe[e_ % 2] = P.e["tensor"].n
            xin_rd_pe[cur] = P.e["tensor"].n
            if i + 1 < 32:
                l3f_emit_xin(i + 1)
        for dct in range(8):
            bank_done(c, dct)
            bank_consume(c, dct, "vector")
            P.do("vector", lambda e, dct=dct, t0=t0: e.tensor_tensor(
                out=xr_ap(c, dct, t0, TQ), in0=xr_ap(c, dct, t0, TQ),
                in1=c.ps[dct][:, :], op=ALU.add), inc=1)
            bank_release(c, dct, "vector")
    P.barrier()
    bes.close()



def b_l2_ffn(c):
    P = c.P
    emit_ln(c)
    bes = contextlib.ExitStack()
    nc = c.nc
    lg = bes.enter_context(nc.sbuf_tensor("l2f_lg", [128, 128], F32))
    mxn = bes.enter_context(nc.sbuf_tensor("l2f_mxn", [128, 16], F32))
    den = bes.enter_context(nc.sbuf_tensor("l2f_den", [128, 16], F32))
    gate = bes.enter_context(nc.sbuf_tensor("l2f_gate", [128, 16], F32))
    msk = bes.enter_context(nc.sbuf_tensor("l2f_msk", [128, 128], F32))
    sel = bes.enter_context(nc.sbuf_tensor("l2f_sel", [128, 128], F32))
    selfm = bes.enter_context(nc.sbuf_tensor("l2f_selfm", [8, 2048], BF16))
    mrow = bes.enter_context(nc.sbuf_tensor("l2f_mrow", [1, 1024], BF16))
    carry = bes.enter_context(nc.sbuf_tensor("l2f_carry", [1, 8], F32))
    wrst = bes.enter_context(nc.sbuf_tensor("l2f_wr", [128, 64], BF16))
    brbc = bes.enter_context(nc.sbuf_tensor("l2f_br", [128, 8], F32))
    b1sb = bes.enter_context(nc.sbuf_tensor("l2f_b1", [128, 8], F32))
    b2t = bes.enter_context(nc.sbuf_tensor("l2f_b2t", [128, 8], F32))
    xin0 = bes.enter_context(nc.sbuf_tensor("l2f_xin0", [128, 4096], BF16))
    xin1 = bes.enter_context(nc.sbuf_tensor("l2f_xin1", [128, 4096], BF16))
    tsel = bes.enter_context(nc.sbuf_tensor("l2f_ts", [128, 512], BF16))
    hbuf = bes.enter_context(nc.sbuf_tensor("l2f_h", [128, 4096], BF16))
    tmp = bes.enter_context(nc.sbuf_tensor("l2f_tmp", [128, 8], F32))

    ld = stage_weight_bf16(c, c.w["L2_ffn_Wr"], 0, 0, 8, 1, 0, mtile=8, dst=wrst)
    nbr = P.dma("sync", c.aux[0:1, 0:8], c.w["L2_ffn_br"])
    P.do("vector", lambda e: e.memset(carry[:], 0.0), inc=1)
    bank_acquire(c, 6)
    P.wait("tensor", "sync", nbr)
    P.do("tensor", lambda e: e.matmul(c.ps[6][:, 0:8], c.ones_f[:, :],
                                      c.aux[0:1, 0:8], start=True, stop=True), inc=1)
    bank_done(c, 6)
    bank_consume(c, 6, "vector")
    P.do("vector", lambda e: e.tensor_copy(brbc[:], c.ps[6][:, 0:8]), inc=1)
    bank_release(c, 6, "vector")
    P.wait("tensor", "gpsimd", ld)

    for tc in range(16):
        b = tc % 2
        bank_acquire(c, b)
        for dc in range(NDC):
            P.do("tensor", lambda e, b=b, dc=dc, tc=tc: e.matmul(
                c.ps[b][:, 0:8], c.xln[:, dc * T + tc * 128:dc * T + tc * 128 + 128],
                wrst[:, dc * 8:dc * 8 + 8],
                start=(dc == 0), stop=(dc == NDC - 1)),
                inc=(1 if dc == NDC - 1 else 0))
        bank_done(c, b)
        bank_consume(c, b, "vector")
        P.do("vector", lambda e, b=b, tc=tc: e.tensor_tensor(
            out=lg[:, tc * 8:tc * 8 + 8], in0=c.ps[b][:, 0:8], in1=brbc[:],
            op=ALU.add), inc=1)
        bank_release(c, b, "vector")
        P.do("vector", lambda e, tc=tc: e.tensor_reduce(
            out=mxn[:, tc:tc + 1], in_=lg[:, tc * 8:tc * 8 + 8],
            axis=AX.X, op=ALU.max, negate=True), inc=1)
        P.wait("scalar", "vector")
        P.do("scalar", lambda e, tc=tc: e.activation(
            tmp[:, 0:8], lg[:, tc * 8:tc * 8 + 8], AF.Exp,
            bias=mxn[:, tc:tc + 1], accum_out=den[:, tc:tc + 1]), inc=1)
        P.wait("vector", "scalar")
        P.do("vector", lambda e, tc=tc: e.reciprocal(
            gate[:, tc:tc + 1], den[:, tc:tc + 1]), inc=1)
        P.do("vector", lambda e, tc=tc: e.scalar_tensor_tensor(
            out=msk[:, tc * 8:tc * 8 + 8], in0=lg[:, tc * 8:tc * 8 + 8],
            scalar=mxn[:, tc:tc + 1], in1=c.consts[:, 3:4].broadcast_to((128, 8)),
            op0=ALU.add, op1=ALU.is_ge), inc=1)
        v_n = P.e["vector"].n
        b2_ = 2 + tc % 2
        bank_acquire(c, b2_)
        P.wait("tensor", "vector", v_n)
        P.do("tensor", lambda e, b2_=b2_, tc=tc: e.matmul(
            c.ps[b2_][:, 0:8], c.tri_f[:], msk[:, tc * 8:tc * 8 + 8],
            start=True, stop=False), inc=0)
        P.do("tensor", lambda e, b2_=b2_: e.matmul(
            c.ps[b2_][:, 0:8], c.ones_f[:, :], carry[:],
            start=False, stop=True), inc=1)
        bank_done(c, b2_)
        b3 = 4 + tc % 2
        bank_acquire(c, b3)
        P.do("tensor", lambda e, b3=b3, tc=tc: e.matmul(
            c.ps[b3][0:1, 0:8], c.ones_cf[:], msk[:, tc * 8:tc * 8 + 8],
            start=True, stop=True), inc=1)
        bank_done(c, b3)
        bank_consume(c, b2_, "vector")
        P.do("vector", lambda e, b2_=b2_, tc=tc: e.tensor_scalar(
            out=sel[:, tc * 8:tc * 8 + 8], in0=c.ps[b2_][:, 0:8],
            scalar1=2047.5, scalar2=None, op0=ALU.is_le), inc=1)
        bank_release(c, b2_, "vector")
        P.do("vector", lambda e, tc=tc: e.tensor_tensor(
            out=sel[:, tc * 8:tc * 8 + 8], in0=sel[:, tc * 8:tc * 8 + 8],
            in1=msk[:, tc * 8:tc * 8 + 8], op=ALU.mult), inc=1)
        P.do("vector", lambda e, tc=tc: e.tensor_tensor(
            out=sel[:, tc * 8:tc * 8 + 8], in0=sel[:, tc * 8:tc * 8 + 8],
            in1=gate[:, tc:tc + 1].broadcast_to((128, 8)), op=ALU.mult), inc=1)
        bank_consume(c, b3, "vector")
        P.do("vector", lambda e, b3=b3: e.tensor_tensor(
            out=carry[:], in0=carry[:], in1=c.ps[b3][0:1, 0:8], op=ALU.add), inc=1)
        bank_release(c, b3, "vector")
    v_n = P.e["vector"].n
    P.wait("tensor", "vector", v_n)
    for tc in range(16):
        b = 6 + tc % 2
        bank_acquire(c, b)
        P.do("tensor", lambda e, b=b, tc=tc: e.transpose(
            c.ps[b][0:8, 0:128], sel[:, tc * 8:tc * 8 + 8], c.ident_f[:]), inc=1)
        bank_done(c, b)
        bank_consume(c, b, "vector")
        P.do("vector", lambda e, b=b, tc=tc: e.tensor_copy(
            selfm[:, tc * 128:tc * 128 + 128], c.ps[b][0:8, 0:128]), inc=1)
        bank_release(c, b, "vector")
    v_n = P.e["vector"].n
    P.wait("sync", "vector", v_n)
    mrow_pe = [0, 0]
    for e_ in range(8):
        for qc in range(NQ):
            i = e_ * NQ + qc
            mr = mrow[0:1, (i % 2) * 512:(i % 2) * 512 + 512]
            P.wait("sync", "tensor", mrow_pe[i % 2])
            n = P.dma("sync", mr, selfm[e_:e_ + 1, qc * TQ:qc * TQ + TQ])
            b = i % 2
            bank_acquire(c, b)
            P.wait("tensor", "sync", n)
            P.do("tensor", lambda e, b=b, mr=mr: e.matmul(
                c.ps[b][:, :], c.ones_b[0:1, :], mr,
                start=True, stop=True), inc=1)
            bank_done(c, b)
            mrow_pe[i % 2] = P.e["tensor"].n
            bank_consume(c, b, "scalar")
            P.do("scalar", lambda e, e_=e_, qc=qc, b=b: e.activation(
                c.scr[:, e_ * 2048 + qc * 512:e_ * 2048 + qc * 512 + 512],
                c.ps[b][:, :], AF.Copy), inc=1)
            bank_release(c, b, "scalar")
    A2, B2 = 0, 8192
    w1_rd_pe = 0
    w2_rd_pe = 0
    b1_rd_s = 0
    b2r_rd_v = 0
    xinb = [xin0, xin1]
    xin_rd_pe = [0, 0]
    xin_wr_v = [0, 0]
    sc_pb = P.e["scalar"].n
    P.wait("vector", "scalar", sc_pb)

    def l2f_mbc3(i):
        e_, qc = divmod(i, NQ)
        mbc = c.scr[:, e_ * 2048 + qc * 512:e_ * 2048 + qc * 512 + 512]
        return mbc.rearrange("p (u t) -> p u t", u=1).broadcast_to(
            (128, NDC, TQ))

    def l2f_emit_xin(i):
        qc = i % NQ
        t0 = qc * TQ
        buf = xinb[i % 2]
        xin3 = buf[:].rearrange("p (dcx t) -> p dcx t", dcx=NDC)
        xln3 = c.xln[:].rearrange(
            "p (dcx t) -> p dcx t", dcx=NDC)[:, :, t0:t0 + TQ]
        P.wait("vector", "tensor", xin_rd_pe[i % 2])
        P.do("vector", lambda e, xin3=xin3, xln3=xln3, m3_=l2f_mbc3(i):
             e.tensor_tensor(out=xin3, in0=xln3, in1=m3_, op=ALU.mult), inc=1)
        xin_wr_v[i % 2] = P.e["vector"].n

    l2f_emit_xin(0)
    for e_ in range(8):
        P.wait("gpsimd", "tensor", w1_rd_pe)
        ld1 = stage_weight_bf16(c, c.w["L2_ffn_W1"][e_], 0, 0, 8, 8, A2)
        P.wait("gpsimd", "scalar", b1_rd_s)
        nb = P.dma("gpsimd", b1sb[:], c.w["L2_ffn_b1"][e_])
        P.wait("gpsimd", "tensor", w2_rd_pe)
        ld2 = stage_weight_bf16(c, c.w["L2_ffn_W2"][e_], 0, 0, 8, 8, B2)
        P.wait("gpsimd", "vector", b2r_rd_v)
        nb2 = P.dma("gpsimd", b2t[:], c.w["L2B2T"][e_])
        for qc in range(NQ):
            t0 = qc * TQ
            i = e_ * NQ + qc
            cur = i % 2
            xin = xinb[cur]
            if qc == 0:
                P.wait("tensor", "gpsimd", ld1)
                P.wait("scalar", "gpsimd", nb)
            P.wait("tensor", "vector", xin_wr_v[cur])
            for ft in range(8):
                b = ft % 2
                bank_acquire(c, b)
                for dc in range(NDC):
                    P.do("tensor", lambda e, b=b, dc=dc, ft=ft, xin=xin:
                         e.matmul(c.ps[b][:, :], wtile(c, A2, 8, dc, ft),
                                  xin[:, dc * 512:dc * 512 + 512],
                                  start=(dc == 0), stop=(dc == NDC - 1)),
                         inc=(1 if dc == NDC - 1 else 0))
                bank_done(c, b)
                bank_consume(c, b, "scalar")
                P.do("scalar", lambda e, b=b, ft=ft: e.activation(
                    hbuf[:, ft * 512:ft * 512 + 512], c.ps[b][:, :], AF.Relu,
                    bias=b1sb[:, ft:ft + 1]), inc=1)
                bank_release(c, b, "scalar")
            xin_rd_pe[cur] = P.e["tensor"].n
            if qc == NQ - 1:
                w1_rd_pe = P.e["tensor"].n
            sc_n = P.e["scalar"].n
            b1_rd_s = sc_n
            if i + 1 < 32:
                l2f_emit_xin(i + 1)
            if qc == 0:
                P.wait("vector", "gpsimd", nb2)
                P.wait("tensor", "gpsimd", ld2)
            P.wait("tensor", "scalar", sc_n)
            mbc = c.scr[:, e_ * 2048 + qc * 512:e_ * 2048 + qc * 512 + 512]
            for dct in range(8):
                b = 2 + dct % 2
                bank_acquire(c, b)
                for fc in range(8):
                    P.do("tensor", lambda e, b=b, fc=fc, dct=dct:
                         e.matmul(c.ps[b][:, :],
                                  wtile(c, B2, 8, fc, dct),
                                  hbuf[:, fc * 512:fc * 512 + 512],
                                  start=(fc == 0), stop=(fc == 7)),
                         inc=(1 if fc == 7 else 0))
                bank_done(c, b)
                bank_consume(c, b, "vector")
                P.do("vector", lambda e, b=b, mbc=mbc, dct=dct:
                     e.scalar_tensor_tensor(
                         out=tsel[:], in0=c.ps[b][:, :],
                         scalar=b2t[:, dct:dct + 1], in1=mbc,
                         op0=ALU.add, op1=ALU.mult), inc=1)
                P.do("vector", lambda e, dct=dct, t0=t0: e.tensor_tensor(
                    out=xr_ap(c, dct, t0, TQ), in0=xr_ap(c, dct, t0, TQ),
                    in1=tsel[:], op=ALU.add), inc=1)
                bank_release(c, b, "vector")
        w2_rd_pe = P.e["tensor"].n
        b2r_rd_v = P.e["vector"].n
    P.barrier()
    bes.close()



def b_l1_attn(c):
    P = c.P
    emit_ln(c)
    bes = contextlib.ExitStack()
    nc = c.nc
    A, B = 0, 8192
    m3 = bes.enter_context(nc.sbuf_tensor("l1a_m3", [16, 3], F32))
    cnt = bes.enter_context(nc.sbuf_tensor("l1a_cnt", [16, 1], F32))
    gts = bes.enter_context(nc.sbuf_tensor("l1a_gt", [16, 1], F32))
    df2 = bes.enter_context(nc.sbuf_tensor("l1a_df2", [16, 2], F32))
    ktok = bes.enter_context(nc.sbuf_tensor("l1a_ktok", [128, 256], BF16))
    dnvf = bes.enter_context(nc.sbuf_tensor("l1a_dnvf", [1, 512], F32))
    dnvb = bes.enter_context(nc.sbuf_tensor("l1a_dnvb", [1, 512], BF16))
    dbcs = bes.enter_context(nc.sbuf_tensor("l1a_dbc", [128, 512], BF16))
    tsb = bes.enter_context(nc.sbuf_tensor("l1a_tsb", [128, 512], F32))
    qkvb = bes.enter_context(nc.sbuf_tensor("l1a_qkvb", [128, 24], F32))
    rb1 = bes.enter_context(nc.sbuf_tensor("l1a_rb1", [128, 2], F32))
    rb2 = bes.enter_context(nc.sbuf_tensor("l1a_rb2", [16, 1], F32))
    vbias = bes.enter_context(nc.sbuf_tensor("l1a_vbias", [1, 1024], BF16))

    n1 = P.dma("sync", qkvb[:], c.w["L1_attn_qkv_b"])
    n2 = P.dma("sync", rb1[:], c.w["L1_attn_r_b1"])
    n3 = P.dma("sync", rb2[:], c.w["L1_attn_r_b2"])
    n4 = P.dma("gpsimd", vbias[:], c.w["L1A_VBROW"])
    ldv = stage_weight_bf16(c, c.w["L1_attn_qkv_w"], 0, 2048, 8, 8, A)
    ldr1 = stage_weight_bf16(c, c.w["L1_attn_r_w1"], 0, 0, 8, 2, B)
    ldr2 = stage_weight_bf16(c, c.w["L1_attn_r_w2"], 0, 0, 2, 1, B + 2048, mtile=16)
    P.wait("tensor", "gpsimd", n4)
    P.wait("tensor", "gpsimd", ldv)
    P.wait("tensor", "sync", n3)
    ph1 = contextlib.ExitStack()
    rsb = ph1.enter_context(nc.sbuf_tensor("l1a_rs", [16, 2048], F32))
    h1b = ph1.enter_context(nc.sbuf_tensor("l1a_h1", [128, 4096], BF16))
    kfm = ph1.enter_context(nc.sbuf_tensor("l1a_kfm", [16, 2048], BF16))
    ones65 = c.scr[:, 0:16 * 1040].rearrange(
        "p (s w) -> p s w", w=65)[:, :, 64:65]
    P.do("vector", lambda e: e.tensor_copy(
        ones65, c.ones_b[:, 0:1].rearrange(
            "p (s w) -> p s w", w=1).broadcast_to((128, 256, 1))), inc=1)
    v_ones65 = P.e["vector"].n
    P.wait("tensor", "vector", v_ones65)
    for tc in range(16):
        for et in range(2):
            b = et % 2
            bank_acquire(c, b)
            for dc in range(NDC):
                P.do("tensor", lambda e, b=b, dc=dc, tc=tc, et=et: e.matmul(
                    c.ps[b][:, :],
                    c.xln[:, dc * T + tc * 128:dc * T + tc * 128 + 128],
                    c.wst[:, A + dc * 1024 + et * 512:A + dc * 1024 + et * 512 + 512],
                    start=(dc == 0), stop=False), inc=0)
            P.do("tensor", lambda e, b=b, et=et: e.matmul(
                c.ps[b][:, :], c.ones_b[0:1, :],
                vbias[:, et * 512:et * 512 + 512],
                start=False, stop=True), inc=1)
            bank_done(c, b)
            bank_consume(c, b, "scalar")
            P.do("scalar", lambda e, b=b, tc=tc, et=et: e.activation(
                c.scr[:, tc * 1040 + et * 520:tc * 1040 + et * 520 + 520]
                .rearrange("p (h w) -> p h w", w=65)[:, :, 0:64],
                c.ps[b][:, :].rearrange("p (h w) -> p h w", w=64),
                AF.Copy), inc=1)
            bank_release(c, b, "scalar")
    P.wait("tensor", "gpsimd", ldr2)
    P.wait("tensor", "sync", n2)
    for mt in range(2):
        for qc in range(NQ):
            b = 2 + qc % 2
            bank_acquire(c, b)
            for dc in range(NDC):
                P.do("tensor", lambda e, b=b, dc=dc, mt=mt, qc=qc: e.matmul(
                    c.ps[b][:, :], wtile(c, B, 2, dc, mt),
                    xln_ap(c, dc, qc * TQ, TQ),
                    start=(dc == 0), stop=(dc == NDC - 1)),
                    inc=(1 if dc == NDC - 1 else 0))
            bank_done(c, b)
            bank_consume(c, b, "scalar")
            P.do("scalar", lambda e, b=b, mt=mt, qc=qc: e.activation(
                h1b[:, mt * 2048 + qc * 512:mt * 2048 + qc * 512 + 512],
                c.ps[b][:, :], AF.Relu, bias=rb1[:, mt:mt + 1]), inc=1)
            bank_release(c, b, "scalar")
    sc_n = P.e["scalar"].n
    P.wait("tensor", "scalar", sc_n)
    for qc in range(NQ):
        b = 4 + qc % 2
        bank_acquire(c, b)
        for mt in range(2):
            P.do("tensor", lambda e, b=b, mt=mt, qc=qc: e.matmul(
                c.ps[b][0:16, :],
                c.wst[:, B + 2048 + mt * 16:B + 2048 + mt * 16 + 16],
                h1b[:, mt * 2048 + qc * 512:mt * 2048 + qc * 512 + 512],
                start=(mt == 0), stop=(mt == 1)), inc=(1 if mt == 1 else 0))
        bank_done(c, b)
        bank_consume(c, b, "scalar")
        P.do("scalar", lambda e, b=b, qc=qc: e.activation(
            rsb[:, qc * 512:qc * 512 + 512], c.ps[b][0:16, :], AF.Sigmoid,
            bias=rb2[:]), inc=1)
        P.do("scalar", lambda e, qc=qc: e.activation(
            rsb[:, qc * 512:qc * 512 + 512], rsb[:, qc * 512:qc * 512 + 512],
            AF.Sigmoid), inc=1)
        bank_release(c, b, "scalar")
    P.wait("vector", "scalar")
    cmpb = h1b[:].bitcast(F32)[0:16, 0:2048]
    P.do("vector", lambda e: e.memset(m3[:, 0:1], 0.5), inc=1)
    P.do("vector", lambda e: e.memset(m3[:, 2:3], 0.7310586), inc=1)
    KKEEP = float(int(T * 0.4))
    for it in range(25):
        P.do("vector", lambda e: e.tensor_tensor(
            out=m3[:, 1:2], in0=m3[:, 0:1], in1=m3[:, 2:3], op=ALU.add), inc=1)
        P.do("vector", lambda e: e.tensor_scalar(
            out=m3[:, 1:2], in0=m3[:, 1:2], scalar1=0.5, scalar2=None,
            op0=ALU.mult), inc=1)
        P.do("vector", lambda e: e.tensor_scalar(
            out=cmpb, in0=rsb[:], scalar1=m3[:, 1:2], scalar2=None,
            op0=ALU.is_ge, op1=ALU.add, accum_out=cnt[:]), inc=1)
        P.do("vector", lambda e: e.tensor_scalar(
            out=gts[:], in0=cnt[:], scalar1=KKEEP, scalar2=None,
            op0=ALU.is_ge), inc=1)
        P.do("vector", lambda e: e.tensor_tensor(
            out=df2[:], in0=m3[:, 1:3], in1=m3[:, 0:2], op=ALU.subtract), inc=1)
        P.do("vector", lambda e: e.tensor_scalar(
            out=df2[:], in0=df2[:], scalar1=gts[:], scalar2=None,
            op0=ALU.mult), inc=1)
        P.do("vector", lambda e: e.tensor_tensor(
            out=m3[:, 0:3:2], in0=m3[:, 0:2], in1=df2[:], op=ALU.add), inc=1)
    P.do("vector", lambda e: e.tensor_scalar(
        out=kfm[:], in0=rsb[:], scalar1=m3[:, 0:1], scalar2=None,
        op0=ALU.is_ge), inc=1)
    v_n = P.e["vector"].n
    P.wait("tensor", "vector", v_n)
    for tc in range(16):
        b = 6 + tc % 2
        bank_acquire(c, b)
        P.do("tensor", lambda e, b=b, tc=tc: e.transpose(
            c.ps[b].bitcast(BF16)[0:128, 0:16], kfm[:, tc * 128:tc * 128 + 128],
            c.ident_b[0:16, 0:16]), inc=1)
        bank_done(c, b)
        bank_consume(c, b, "vector")
        P.do("vector", lambda e, b=b, tc=tc: e.tensor_copy(
            ktok[:, tc * 16:tc * 16 + 16],
            c.ps[b].bitcast(BF16)[0:128, 0:16]), inc=1)
        bank_release(c, b, "vector")
    ph1.close()
    qh2 = [bes.enter_context(nc.sbuf_tensor(f"l1a_qh{i}", [128, 2048], BF16))
           for i in range(2)]
    kh2 = [bes.enter_context(nc.sbuf_tensor(f"l1a_kh{i}", [128, 2048], BF16))
           for i in range(2)]
    expt = bes.enter_context(nc.sbuf_tensor("l1a_exp", [128, 2048], BF16))
    bcolp2 = [bes.enter_context(nc.sbuf_tensor(f"l1a_bcp{i}", [128, 32], F32))
              for i in range(2)]
    onesc = bes.enter_context(nc.sbuf_tensor("l1a_on1", [128, 1], BF16))
    pe_n = P.e["tensor"].n
    P.wait("gpsimd", "tensor", pe_n)
    ldq = stage_weight_bf16(c, c.w["L1_attn_qkv_w"], 0, 0, 8, 8, A)
    ldk = stage_weight_bf16(c, c.w["L1_attn_qkv_w"], 0, 1024, 8, 8, B)
    P.do("vector", lambda e: e.memset(onesc[:], 1.0), inc=1)
    v_ones = P.e["vector"].n
    P.wait("tensor", "gpsimd", ldk)
    P.wait("tensor", "vector", v_ones)
    for b in range(6):
        bank_acquire(c, b)
    qk_rd_pe2 = [0, 0]
    bcp_rd_s2 = [0, 0]
    dnv_bc_pe = 0
    exp_done = [0, 0]
    ctx_done = [0, 0]
    ctx_tail_v = 0
    ctx_tail_s = 0
    for hp in range(8):
        h0 = 2 * hp
        qh = qh2[hp % 2]
        kh = kh2[hp % 2]
        bcolp = bcolp2[hp % 2]
        P.wait("scalar", "tensor", qk_rd_pe2[hp % 2])
        for reg, buf, bcoli in ((A, qh, hp), (B, kh, 8 + hp)):
            for qc in range(NQ):
                b = 6 + qc % 2
                bank_acquire(c, b)
                for dc in range(NDC):
                    P.do("tensor", lambda e, b=b, dc=dc, reg=reg, hp=hp, qc=qc:
                         e.matmul(c.ps[b][:, :], wtile(c, reg, 8, dc, hp),
                                  xln_ap(c, dc, qc * TQ, TQ),
                                  start=(dc == 0), stop=(dc == NDC - 1)),
                         inc=(1 if dc == NDC - 1 else 0))
                bank_done(c, b)
                bank_consume(c, b, "scalar")
                P.do("scalar", lambda e, b=b, buf=buf, bcoli=bcoli, qc=qc:
                     e.activation(buf[:, qc * 512:qc * 512 + 512],
                                  c.ps[b][:, :], AF.Identity,
                                  bias=qkvb[:, bcoli:bcoli + 1]), inc=1)
                bank_release(c, b, "scalar")
        qk_wr_s = P.e["scalar"].n
        P.wait("vector", "scalar", bcp_rd_s2[hp % 2])
        for hsel in range(2):
            P.do("vector", lambda e, hsel=hsel, h0=h0, bcolp=bcolp:
                 e.tensor_scalar(
                     out=bcolp[:, hsel:32:2], in0=ktok[:, h0 + hsel:256:16],
                     scalar1=-1.0, scalar2=1e30, op0=ALU.add, op1=ALU.mult),
                 inc=1)
        v_bcp = P.e["vector"].n
        P.wait("scalar", "vector", v_bcp)
        P.wait("tensor", "scalar", qk_wr_s)
        for qc in range(NQ):
            for tc in range(17):
                par = tc % 2
                if tc < 16:
                    P.wait("tensor", "scalar", exp_done[par])
                    for hsel in range(2):
                        sb = (0 if par == 0 else 4) + hsel
                        po = hsel * 64
                        P.do("tensor", lambda e, sb=sb, po=po, tc=tc, qc=qc,
                             kh=kh, qh=qh:
                             e.matmul(c.ps[sb][:, :],
                                      kh[po:po + 64, tc * 128:tc * 128 + 128],
                                      qh[po:po + 64, qc * 512:qc * 512 + 512],
                                      start=True, stop=True), inc=1)
                    sc_pe = P.e["tensor"].n
                    if qc == NQ - 1 and tc == 15:
                        qk_rd_pe2[hp % 2] = sc_pe
                    P.wait("scalar", "tensor", sc_pe)
                    P.wait("scalar", "tensor", ctx_done[par])
                    for hsel in range(2):
                        sb = (0 if par == 0 else 4) + hsel
                        slot = (par * 2 + hsel) * 512
                        P.do("scalar", lambda e, sb=sb, slot=slot, tc=tc,
                             hsel=hsel, bcolp=bcolp: e.activation(
                                 expt[:, slot:slot + 512], c.ps[sb][:, :],
                                 AF.Exp, scale=0.125,
                                 bias=bcolp[:, tc * 2 + hsel:tc * 2 + hsel + 1]),
                             inc=1)
                    exp_done[par] = P.e["scalar"].n
                if tc > 0:
                    tp_ = tc - 1
                    parp = tp_ % 2
                    P.wait("tensor", "scalar", exp_done[parp])
                    if tp_ == 0:
                        P.wait("tensor", "vector", ctx_tail_v)
                        P.wait("tensor", "scalar", ctx_tail_s)
                    for hsel in range(2):
                        slot = (parp * 2 + hsel) * 512
                        cb = 2 + hsel
                        vcol = tp_ * 1040 + (h0 + hsel) * 65
                        P.do("tensor", lambda e, cb=cb, slot=slot, tp_=tp_,
                             vcol=vcol: e.matmul(
                                 c.ps[cb][0:65, :],
                                 c.scr[:, vcol:vcol + 65],
                                 expt[:, slot:slot + 512],
                                 start=(tp_ == 0), stop=(tp_ == 15)), inc=1)
                    ctx_done[parp] = P.e["tensor"].n
            for hsel in range(2):
                cb = 2 + hsel
                po = hsel * 64
                P.wait("scalar", "tensor", ctx_done[1])
                P.do("scalar", lambda e, cb=cb: e.activation(
                    dnvf[:], c.ps[cb][64:65, :], AF.Ln), inc=1)
                P.do("scalar", lambda e: e.activation(
                    dnvb[:], dnvf[:], AF.Exp, scale=-1.0), inc=1)
                s_d = P.e["scalar"].n
                bank_acquire(c, 6)
                P.wait("tensor", "scalar", s_d)
                P.do("tensor", lambda e: e.matmul(
                    c.ps[6][:, :], c.ones_b[0:1, :], dnvb[:],
                    start=True, stop=True), inc=1)
                bank_done(c, 6)
                bank_consume(c, 6, "vector")
                P.do("vector", lambda e: e.tensor_copy(
                    dbcs[:], c.ps[6][:, :]), inc=1)
                bank_release(c, 6, "vector")
                P.do("vector", lambda e, cb=cb, po=po: e.tensor_tensor(
                    out=tsb[po:po + 64, :], in0=c.ps[cb][0:64, :],
                    in1=dbcs[0:64, :], op=ALU.mult), inc=1)
                dnv_bc_pe = P.e["vector"].n
                P.do("vector", lambda e, po=po, hp=hp, qc=qc: e.tensor_tensor(
                    out=c.xr[po:po + 64, hp * T + qc * TQ:hp * T + qc * TQ + TQ],
                    in0=c.xr[po:po + 64, hp * T + qc * TQ:hp * T + qc * TQ + TQ],
                    in1=tsb[po:po + 64, :], op=ALU.add), inc=1)
            ctx_tail_v = P.e["vector"].n
            ctx_tail_s = P.e["scalar"].n
        bcp_rd_s2[hp % 2] = P.e["scalar"].n
    P.barrier()
    bes.close()


def b_l3_attn(c):
    P = c.P
    emit_ln(c)
    bes = contextlib.ExitStack()
    nc = c.nc
    A, B = 0, 8192
    vwst = bes.enter_context(nc.sbuf_tensor("l3a_vw", [128, 1024], BF16))
    vpair = bes.enter_context(nc.sbuf_tensor("l3a_vp", [128, 2080], BF16))
    qh = bes.enter_context(nc.sbuf_tensor("l3a_qh", [128, 2048], BF16))
    kh = bes.enter_context(nc.sbuf_tensor("l3a_kh", [128, 2048], BF16))
    onesc3 = bes.enter_context(nc.sbuf_tensor("l3a_on1", [128, 1], BF16))
    expt = bes.enter_context(nc.sbuf_tensor("l3a_exp", [128, 2048], BF16))
    dnvf = bes.enter_context(nc.sbuf_tensor("l3a_dnvf", [1, 512], F32))
    dnvb = bes.enter_context(nc.sbuf_tensor("l3a_dnvb", [1, 512], BF16))
    dbcs = bes.enter_context(nc.sbuf_tensor("l3a_dbc", [128, 512], BF16))
    bo_sb = bes.enter_context(nc.sbuf_tensor("l3a_bo", [128, 8], F32))
    trib = bes.enter_context(nc.sbuf_tensor("l3a_trib", [128, 128], BF16))

    nbo = P.dma("sync", bo_sb[:], c.w["L3_attn_bo"])
    P.do("vector", lambda e: e.tensor_copy(trib[:], c.tri_f[:]), inc=1)
    ldq = stage_weight_bf16(c, c.w["L3_attn_Wq"], 0, 0, 8, 8, A)
    ldk = stage_weight_bf16(c, c.w["L3_attn_Wk"], 0, 0, 8, 8, B)
    P.wait("tensor", "gpsimd", ldk)
    for b in range(6):
        bank_acquire(c, b)
    P.do("vector", lambda e: e.memset(onesc3[:], 1.0), inc=1)
    P.do("vector", lambda e: e.tensor_copy(
        vpair[:].rearrange("p (s w) -> p s w", w=65)[:, :, 64:65],
        c.ones_b[:, 0:1].rearrange(
            "p (s w) -> p s w", w=1).broadcast_to((128, 32, 1))), inc=1)
    v_on3 = P.e["vector"].n
    P.wait("tensor", "vector", v_on3)
    trib_bc = trib[:].rearrange("p (u n) -> p u n", u=1).broadcast_to((128, 2, 128))
    pe_prev_hp = 0
    dnv_bc_pe = 0
    exp_done = [0, 0]
    ctx_done = [0, 0]
    msk_done = [0, 0]
    ctx_tail_v = 0
    ctx_tail_s = 0
    for hp in range(8):
        h0 = 2 * hp
        P.wait("vector", "tensor", pe_prev_hp)
        for reg, buf in ((A, qh), (B, kh)):
            for qc in range(NQ):
                b = 6 + qc % 2
                bank_acquire(c, b)
                for dc in range(NDC):
                    P.do("tensor", lambda e, b=b, dc=dc, reg=reg, hp=hp, qc=qc:
                         e.matmul(c.ps[b][:, :], wtile(c, reg, 8, dc, hp),
                                  xln_ap(c, dc, qc * TQ, TQ),
                                  start=(dc == 0), stop=(dc == NDC - 1)),
                         inc=(1 if dc == NDC - 1 else 0))
                bank_done(c, b)
                bank_consume(c, b, "vector")
                P.do("vector", lambda e, b=b, buf=buf, qc=qc: e.tensor_copy(
                    buf[:, qc * 512:qc * 512 + 512], c.ps[b][:, :]), inc=1)
                bank_release(c, b, "vector")
        P.wait("gpsimd", "tensor", pe_prev_hp)
        ldv = stage_weight_bf16(c, c.w["L3_attn_Wv"], 0, hp * 128, 8, 1,
                                0, mtile=128, dst=vwst)
        P.wait("tensor", "gpsimd", ldv)
        for tc in range(16):
            b = 6 + tc % 2
            bank_acquire(c, b)
            for dc in range(NDC):
                P.do("tensor", lambda e, b=b, dc=dc, tc=tc: e.matmul(
                    c.ps[b][:, 0:128],
                    c.xln[:, dc * T + tc * 128:dc * T + tc * 128 + 128],
                    vwst[:, dc * 128:dc * 128 + 128],
                    start=(dc == 0), stop=(dc == NDC - 1)),
                    inc=(1 if dc == NDC - 1 else 0))
            bank_done(c, b)
            bank_consume(c, b, "vector")
            P.do("vector", lambda e, b=b, tc=tc: e.tensor_copy(
                vpair[:, tc * 130:tc * 130 + 130].rearrange(
                    "p (h w) -> p h w", w=65)[:, :, 0:64],
                c.ps[b][:, 0:128].rearrange("p (h w) -> p h w", w=64)), inc=1)
            bank_release(c, b, "vector")
        qkv_wr_v = P.e["vector"].n
        P.wait("tensor", "vector", qkv_wr_v)
        for qc in range(NQ):
            ntc = 4 * qc + 4
            for tc in range(ntc + 1):
                par = tc % 2
                if tc < ntc:
                    k = tc - 4 * qc
                    off = 128 * k if k > 0 else 0
                    w = 512 - off
                    P.wait("tensor", "scalar", exp_done[par])
                    for hsel in range(2):
                        sb = (0 if par == 0 else 4) + hsel
                        po = hsel * 64
                        P.do("tensor", lambda e, sb=sb, po=po, tc=tc, qc=qc,
                             off=off, w=w: e.matmul(
                                 c.ps[sb][:, 0:w],
                                 kh[po:po + 64, tc * 128:tc * 128 + 128],
                                 qh[po:po + 64,
                                    qc * 512 + off:qc * 512 + 512],
                                 start=True, stop=True), inc=1)
                    sc_pe = P.e["tensor"].n
                    P.wait("scalar", "tensor", sc_pe)
                    P.wait("scalar", "tensor", ctx_done[par])
                    for hsel in range(2):
                        sb = (0 if par == 0 else 4) + hsel
                        slot = (par * 2 + hsel) * 512
                        P.do("scalar", lambda e, sb=sb, slot=slot, off=off,
                             w=w: e.activation(
                                 expt[:, slot + off:slot + 512],
                                 c.ps[sb][:, 0:w], AF.Exp, scale=0.125),
                             inc=1)
                    exp_done[par] = P.e["scalar"].n
                    if k >= 0:
                        mb_ = par * 1024 + off
                        vslot = expt[:, mb_:mb_ + 640].rearrange(
                            "p (u n) -> p u n", n=128)[:, 0:5:4, :]
                        P.wait("vector", "scalar", exp_done[par])
                        P.do("vector", lambda e, vslot=vslot: e.tensor_tensor(
                            out=vslot, in0=vslot, in1=trib_bc,
                            op=ALU.mult), inc=1)
                        msk_done[par] = P.e["vector"].n
                if tc > 0:
                    tp_ = tc - 1
                    parp = tp_ % 2
                    kp = tp_ - 4 * qc
                    offp = 128 * kp if kp > 0 else 0
                    P.wait("tensor", "scalar", exp_done[parp])
                    if kp >= 0:
                        P.wait("tensor", "vector", msk_done[parp])
                    if tp_ == 0:
                        P.wait("tensor", "vector", ctx_tail_v)
                        P.wait("tensor", "scalar", ctx_tail_s)
                    for hsel in range(2):
                        slot = (parp * 2 + hsel) * 512
                        cb = 2 + hsel
                        vcol = tp_ * 130 + hsel * 65
                        P.do("tensor", lambda e, cb=cb, slot=slot, tp_=tp_,
                             vcol=vcol, offp=offp, ntc=ntc: e.matmul(
                                 c.ps[cb][0:65, offp:512],
                                 vpair[:, vcol:vcol + 65],
                                 expt[:, slot + offp:slot + 512],
                                 start=(tp_ == 0), stop=(tp_ == ntc - 1),
                                 skip_group_check=True), inc=1)
                    ctx_done[parp] = P.e["tensor"].n
            for hsel in range(2):
                cb = 2 + hsel
                po = hsel * 64
                P.wait("scalar", "tensor", ctx_done[1])
                P.do("scalar", lambda e, cb=cb: e.activation(
                    dnvf[:], c.ps[cb][64:65, :], AF.Ln), inc=1)
                P.do("scalar", lambda e: e.activation(
                    dnvb[:], dnvf[:], AF.Exp, scale=-1.0), inc=1)
                s_d = P.e["scalar"].n
                bank_acquire(c, 6)
                P.wait("tensor", "scalar", s_d)
                P.do("tensor", lambda e: e.matmul(
                    c.ps[6][:, :], c.ones_b[0:1, :], dnvb[:],
                    start=True, stop=True), inc=1)
                bank_done(c, 6)
                bank_consume(c, 6, "vector")
                P.do("vector", lambda e: e.tensor_copy(
                    dbcs[:], c.ps[6][:, :]), inc=1)
                bank_release(c, 6, "vector")
                P.do("vector", lambda e, cb=cb, po=po, hp=hp, qc=qc:
                     e.tensor_tensor(
                         out=c.scr[po:po + 64,
                                   hp * 2048 + qc * 512:hp * 2048 + qc * 512 + 512],
                         in0=c.ps[cb][0:64, :], in1=dbcs[0:64, :],
                         op=ALU.mult), inc=1)
                dnv_bc_pe = P.e["vector"].n
            ctx_tail_v = P.e["vector"].n
            ctx_tail_s = P.e["scalar"].n
        pe_prev_hp = P.e["tensor"].n
    pe_n = P.e["tensor"].n
    P.wait("gpsimd", "tensor", pe_n)
    ldo = stage_weight_bf16(c, c.w["L3_attn_Wo"], 0, 0, 8, 8, A)
    P.wait("tensor", "gpsimd", ldo)
    P.wait("tensor", "vector")
    P.wait("tensor", "sync", nbo)
    for qc in range(NQ):
        for dct in range(8):
            b = dct % 2
            bank_acquire(c, b)
            for cc in range(8):
                P.do("tensor", lambda e, b=b, cc=cc, dct=dct, qc=qc: e.matmul(
                    c.ps[b][:, :], wtile(c, A, 8, cc, dct),
                    c.scr[:, cc * 2048 + qc * 512:cc * 2048 + qc * 512 + 512],
                    start=(cc == 0), stop=(cc == 7)),
                    inc=(1 if cc == 7 else 0))
            bank_done(c, b)
            bank_consume(c, b, "vector")
            P.do("vector", lambda e, b=b, dct=dct, qc=qc: e.scalar_tensor_tensor(
                out=xr_ap(c, dct, qc * TQ, TQ), in0=c.ps[b][:, :],
                scalar=bo_sb[:, dct:dct + 1], in1=xr_ap(c, dct, qc * TQ, TQ),
                op0=ALU.add, op1=ALU.add), inc=1)
            bank_release(c, b, "vector")
    P.barrier()
    bes.close()



def b_l1_ffn(c):
    P = c.P
    emit_ln(c)
    bes = contextlib.ExitStack()
    H = 4096
    b1_sb = bes.enter_context(c.nc.sbuf_tensor("l1f_b1", [128, 32], F32))
    b2_sb = bes.enter_context(c.nc.sbuf_tensor("l1f_b2", [128, NDC], F32))
    P.dma("sync", b1_sb[:], c.w["L1_ffn_b1"])
    P.dma("sync", b2_sb[:], c.w["L1_ffn_b2"])
    P.wait("scalar", "sync")
    P.wait("vector", "sync")

    region = [0, 8192]
    region_pe = [0, 0]
    stages_q = ([("L1_ffn_w1", 0, hs * 1024) for hs in range(4)] +
                [("L1_ffn_w2", hs * 1024, 0) for hs in range(4)])

    def l1f_stage(tag):
        w_name, k0, m0 = stages_q[tag]
        r = region[tag % 2]
        P.wait("gpsimd", "tensor", region_pe[tag % 2])
        ld = stage_weight_bf16(c, c.w[w_name], k0, m0, 8, 8, r)
        return r, ld

    for q in range(NQ):
        t0 = q * TQ
        tag = 0
        nxt = l1f_stage(0)
        for hs in range(4):
            r, ld = nxt
            P.wait("tensor", "gpsimd", ld)
            if tag + 1 < 8:
                nxt = l1f_stage(tag + 1)
            for hl in range(8):
                hc = hs * 8 + hl
                b = hc % 4
                bank_acquire(c, b)
                for dc in range(NDC):
                    P.do("tensor", lambda e, b=b, r=r, dc=dc, hl=hl, t0=t0:
                         e.matmul(c.ps[b][:, :], wtile(c, r, 8, dc, hl),
                                  xln_ap(c, dc, t0, TQ),
                                  start=(dc == 0), stop=(dc == NDC - 1)),
                         inc=(1 if dc == NDC - 1 else 0))
                bank_done(c, b)
                bank_consume(c, b, "scalar")
                P.do("scalar", lambda e, b=b, hc=hc: e.activation(
                    c.scr[:, hc * 512:hc * 512 + 512], c.ps[b][:, :], AF.Gelu,
                    bias=b1_sb[:, hc:hc + 1]), inc=1)
                bank_release(c, b, "scalar")
            region_pe[tag % 2] = P.e["tensor"].n
            tag += 1
        sc_n = P.e["scalar"].n
        for hs in range(4):
            r, ld = nxt
            P.wait("tensor", "gpsimd", ld)
            if tag + 1 < 8:
                nxt = l1f_stage(tag + 1)
            if hs == 0:
                for b in range(8):
                    bank_acquire(c, b)
                P.wait("tensor", "scalar", sc_n)
            for dc in range(NDC):
                for hl in range(8):
                    hc = hs * 8 + hl
                    P.do("tensor", lambda e, dc=dc, r=r, hl=hl, hc=hc, hs=hs:
                         e.matmul(c.ps[dc][:, :], wtile(c, r, 8, hl, dc),
                                  c.scr[:, hc * 512:hc * 512 + 512],
                                  start=(hs == 0 and hl == 0),
                                  stop=(hs == 3 and hl == 7)),
                         inc=(1 if hl == 7 else 0))
            region_pe[tag % 2] = P.e["tensor"].n
            tag += 1
        for dc in range(NDC):
            bank_done(c, dc)
            bank_consume(c, dc, "vector")
            P.do("vector", lambda e, dc=dc, t0=t0: e.scalar_tensor_tensor(
                out=xr_ap(c, dc, t0, TQ), in0=c.ps[dc][:, :],
                scalar=b2_sb[:, dc:dc + 1], in1=xr_ap(c, dc, t0, TQ),
                op0=ALU.add, op1=ALU.add), inc=1)
            bank_release(c, dc, "vector")
    P.barrier()
    bes.close()

def b_identity(c):
    pass


def finish(c):
    P = c.P
    P.barrier()
    n = P.dma("sync", c.x_out.rearrange("(ch p) t -> p ch t", p=128),
              c.xr[:].rearrange("p (ch t) -> p ch t", ch=NDC))
    P.wait("sync", "sync", n)


def _w(name, tr=None, srckey=None):
    return (name, tr, srckey)


def _tT(a):
    return a.T


def _tB(a):
    return a.reshape(-1, 128).T


BLOCKS = [
    ("L0_attn", b_l0_attn, [
        _w("L0_attn_in_proj_w"), _w("L0_attn_out_proj_w"),
        _w("L0_attn_W_re"), _w("L0_attn_W_im"),
        _w("L0THETA", lambda a: np.ascontiguousarray(
            a.reshape(2048).reshape(16, 128).T), srckey="L0_attn_theta"),
        _w("L0TAPS", lambda a: _l0taps(a), srckey="L0_attn_conv_k"),
        _w("L0CONVB", lambda a: np.ascontiguousarray(
            np.pad(a, (0, 96)).reshape(33, 128).T), srckey="L0_attn_conv_b"),
        _w("L0SSC", lambda a: a.reshape(32, 1), srckey="L0_attn_score_scale"),
        _w("L0NSC", lambda a: a.reshape(128, 1), srckey="L0_attn_norm_scale"),
        _w("L0DERIV", lambda a: a.reshape(1, 3), srckey="L0_attn_deriv_logits"),
        _w("L0SLOPES", None, srckey="__L0SLOPES__"),
        _w("L0DIST", None), _w("L0SEL", None), _w("L0HSEL", None),
        _w("L0HSEL2", None), _w("IDENT128", None)]),
    ("L0_ffn", b_l0_ffn, [_w("L0_ffn_w_gate", _tT), _w("L0_ffn_w_up", _tT),
                          _w("L0_ffn_w_down", _tT)]),
    ("L1_attn", b_l1_attn, [
        _w("L1_attn_qkv_w", _tT), _w("L1_attn_qkv_b", _tB),
        _w("L1A_VBROW", lambda a: a[2048:3072].reshape(1, 1024),
           srckey="L1_attn_qkv_b"),
        _w("L1_attn_r_w1", _tT), _w("L1_attn_r_b1", _tB),
        _w("L1_attn_r_w2", _tT), _w("L1_attn_r_b2", lambda a: a.reshape(16, 1)),
        _w("IDENT128", None)]),
    ("L1_ffn", b_l1_ffn, [_w("L1_ffn_w1"), _w("L1_ffn_b1", _tB),
                          _w("L1_ffn_w2"), _w("L1_ffn_b2", _tB)]),
    ("L2_attn", b_l2_attn, [_w("L2_attn_w_q"), _w("L2_attn_w_k"),
                            _w("L2_attn_w_v"), _w("L2_attn_w_o")]),
    ("L2_ffn", b_l2_ffn, [_w("L2_ffn_Wr"), _w("L2_ffn_br", lambda a: a.reshape(1, 8)),
                          _w("L2_ffn_W1"), _w("L2_ffn_b1", lambda a: np.ascontiguousarray(a.reshape(8, 8, 128).transpose(0, 2, 1))),
                          _w("L2_ffn_W2"),
                          _w("L2B2T", lambda a: np.ascontiguousarray(
                              a.reshape(8, 8, 128).transpose(0, 2, 1)),
                             srckey="L2_ffn_b2"),
                          _w("IDENT128", None), _w("TRI128", None)]),
    ("L3_attn", b_l3_attn, [_w("L3_attn_Wq", _tT), _w("L3_attn_Wk", _tT),
                          _w("L3_attn_Wv", _tT), _w("L3_attn_Wo", _tT),
                          _w("L3_attn_bo", _tB), _w("IDENT128", None),
                          _w("TRI128", None)]),
    ("L3_ffn", b_l3_ffn, [_w("L3_ffn_gate_w", _tT),
                          _w("L3_ffn_expert_w", lambda a: a.transpose(0, 2, 1)),
                          _w("IDENT128", None)]),
]

GROUPS = [[0, 1, 2, 3, 4, 5, 6, 7]]


def _l0taps(a):
    ck = np.pad(a[:, 0, :], ((0, 0), (0, 96)))
    out = np.zeros((128, 132), np.float32)
    for ct in range(33):
        for k in range(4):
            out[:, ct * 4 + k] = ck[k, ct * 128:ct * 128 + 128]
    return out


def _l0dist():
    d = np.zeros((32, 2048), np.float32)
    t = np.arange(2048, dtype=np.float32)
    d[:28] = 2047.0 - t
    d[28:] = t
    return d


def _l0sel():
    s = np.zeros((32, 2048), np.float32)
    for ch in range(2048):
        s[ch // 64, ch] = 1.0
    return s


_CONSTS = {"IDENT128": lambda: np.eye(128, dtype=np.float32),
           "TRI128": lambda: np.ascontiguousarray(
               np.tril(np.ones((128, 128), np.float32)).T),
           "L0DIST": _l0dist,
           "L0SEL": _l0sel,
           "L0HSEL": lambda: (np.arange(128)[:, None] // 64 ==
                              np.arange(2)[None, :]).astype(np.float32),
           "L0HSEL2": lambda: (np.arange(128)[None, :] // 64 ==
                               np.arange(2)[:, None]).astype(np.float32)}


def _prep_host(inputs):
    arrs = {}
    for name, builder, wl in BLOCKS:
        for key, tr, srck in wl:
            kk = srck or key
            if kk == "__L0SLOPES__":
                a = np.concatenate([np.asarray(inputs["L0_attn_decay_slopes"]),
                                    np.asarray(inputs["L0_attn_anchor_slopes"])]
                                   ).reshape(32, 1)
            else:
                a = np.asarray(inputs[kk]) if kk in inputs else _CONSTS[key]()
            if tr is not None:
                a = np.ascontiguousarray(tr(a))
            else:
                a = np.ascontiguousarray(a)
            arrs[key] = a
    return arrs


def _weight_decls(block_ids, arrs):
    decl = {}
    for bi in block_ids:
        for key, tr, srck in BLOCKS[bi][2]:
            a = arrs[key]
            dt = I32 if a.dtype == np.int32 else F32
            decl[key] = (a.shape, dt)
    return decl


def make_program(block_ids, arrs):
    nc = bass.Bass("TRN2", target_bir_lowering=False)
    es = contextlib.ExitStack()
    P = Prog(nc, es)
    c = make_ctx(nc, es, P, _weight_decls(block_ids, arrs))
    mark = es.enter_context(nc.sbuf_tensor("blkmark", [1, 8], F32))
    P.do("vector", lambda e: e.memset(mark[:], 0.0), inc=1)
    for bi in block_ids:
        P.do("vector", lambda e: e.pool_avg(mark[0:1, 0:1], mark[0:1, 0:8]),
             inc=1)
        BLOCKS[bi][1](c)
    P.do("vector", lambda e: e.pool_avg(mark[0:1, 0:1], mark[0:1, 0:8]),
         inc=1)
    finish(c)
    P.emit()
    es.close()
    return nc


def kernel(**inputs):
    x = np.asarray(inputs["L0_attn_x"], np.float32)
    arrs = _prep_host(inputs)
    cur = [np.ascontiguousarray(x[b].T) for b in range(8)]
    for g in GROUPS:
        nc = make_program(g, arrs)
        need = {k for bi in g for k, _, _s in BLOCKS[bi][2]}
        in_maps = [dict({k: arrs[k] for k in need}, x_in=cur[b]) for b in range(8)]
        res = run_bass_kernel_spmd(nc, in_maps, core_ids=list(range(8)))
        cur = [np.asarray(res.results[b]["x_out"]) for b in range(8)]
    out = np.stack([c.T for c in cur]).astype(np.float32)
    return out
